# version 04
# speedup vs baseline: 1.0718x; 1.0408x over previous
import numpy as np
import ml_dtypes
from contextlib import ExitStack

import concourse.bass as bass
import concourse.mybir as mybir
from concourse.bass_utils import run_bass_kernel_spmd

F32 = mybir.dt.float32
BF16 = mybir.dt.bfloat16
AF = mybir.ActivationFunctionType
ALU = mybir.AluOpType
AX = mybir.AxisListType
BF = ml_dtypes.bfloat16
f32 = np.float32

ENGINES = ("sync", "tensor", "scalar", "vector", "gpsimd")
DEBUG_BARRIERS = False
DBG_NOHALO = False
LF_PART = 3
LF_BIAS = True
LF_NTC = 16
PAIRS = [[0, 1], [2, 3], [4, 5], [6, 7]]
ALLC = [list(range(8))]

T = 2048
D = 1024
HID = 1536
NC_D = D // 128
NC_H = HID // 128


class P:

    NDMASEM = 12

    def __init__(self, nc, ctx):
        self.nc = nc
        self.ctx = ctx
        self.ops = {e: [] for e in ENGINES}
        self.opph = {e: [] for e in ENGINES}
        self.phase = "init"
        self.semh = {}
        self.cnt = {}
        self.lastw = {e: {} for e in ENGINES}
        self.dmarr = {"sync": 0, "gpsimd": 0}

    def _app(self, eng, fn):
        self.ops[eng].append(fn)
        self.opph[eng].append(self.phase)

    def set_phase(self, name):
        self.phase = name

    def sem(self, name):
        if name not in self.semh:
            self.semh[name] = self.ctx.enter_context(self.nc.semaphore(name))
            self.cnt[name] = 0
        return self.semh[name]

    def wait(self, eng, tokens):
        for t in tokens:
            if t is None:
                continue
            s, v = t
            if v <= 0 or self.lastw[eng].get(s, 0) >= v:
                continue
            self.lastw[eng][s] = v
            h = self.sem(s)
            self._app(eng, lambda E, h=h, v=v: E.wait_ge(h, v))

    def _tok(self, sem, n):
        self.sem(sem)
        self.cnt[sem] += n
        return (sem, self.cnt[sem])

    def now(self):
        return [(s, v) for s, v in self.cnt.items() if v > 0]

    def barrier(self, eng=None):
        toks = self.now()
        for e in (ENGINES if eng is None else [eng]):
            self.wait(e, toks)

    def dma(self, q, out, in_, after=(), sem=None, **kw):
        i = self.dmarr[q]
        self.dmarr[q] = (i + 1) % self.NDMASEM
        sem = f"dma_{q}_{i}"
        self.wait(q, after)
        tok = self._tok(sem, 16)
        h = self.sem(sem)
        self._app(q,
            lambda E: E.dma_start(out=out, in_=in_, **kw).then_inc(h, 16))
        return tok

    def mm(self, psum_ap, pairs, after=(), start=True, stop=True, sem="pe"):
        self.wait("tensor", after)
        tok = self._tok(sem, 1)
        h = self.sem(sem)
        n = len(pairs)

        def f(E, pairs=pairs, n=n, start=start, stop=stop):
            ins = None
            for i, (l, r) in enumerate(pairs):
                ins = E.matmul(psum_ap, l, r, start=(start and i == 0),
                               stop=(stop and i == n - 1))
            ins.then_inc(h, 1)
        self._app("tensor", f)
        return tok

    def trans(self, psum_ap, in_ap, ident_ap, after=(), sem="pe"):
        self.wait("tensor", after)
        tok = self._tok(sem, 1)
        h = self.sem(sem)
        self._app("tensor",
            lambda E: E.transpose(psum_ap, in_ap, ident_ap).then_inc(h, 1))
        return tok

    def act(self, out, in_, func, after=(), bias=0.0, scale=1.0,
            accum_out=None, sem="sc"):
        if sem in self.cnt and self.cnt[sem] > 0:
            self.wait("scalar", [(sem, self.cnt[sem])])
        self.wait("scalar", after)
        tok = self._tok(sem, 1)
        h = self.sem(sem)
        if isinstance(bias, float):
            bias = f32(bias)
        if isinstance(scale, float):
            scale = f32(scale)
        self._app("scalar",
            lambda E: E.activation(out, in_, func, bias=bias, scale=scale,
                                   accum_out=accum_out).then_inc(h, 1))
        return tok

    def vec(self, meth, *args, after=(), sem="ve", **kw):
        if sem in self.cnt and self.cnt[sem] > 0:
            self.wait("vector", [(sem, self.cnt[sem])])
        self.wait("vector", after)
        tok = self._tok(sem, 1)
        h = self.sem(sem)

        def f(E, meth=meth, args=args, kw=kw):
            getattr(E, meth)(*args, **kw).then_inc(h, 1)
        self._app("vector", f)
        return tok

    def gps(self, meth, *args, after=(), sem="gc", **kw):
        self.wait("gpsimd", after)
        tok = self._tok(sem, 1)
        h = self.sem(sem)

        def f(E, meth=meth, args=args, kw=kw):
            getattr(E, meth)(*args, **kw).then_inc(h, 1)
        self._app("gpsimd", f)
        return tok

    def coll(self, kind, op, groups, ins, outs, after=()):
        self.wait("gpsimd", after)
        tok = self._tok("cc", 1)
        h = self.sem("cc")
        self._app("gpsimd",
            lambda E: E.collective_compute(
                kind, op, replica_groups=groups, ins=ins, outs=outs,
            ).then_inc(h, 1))
        return tok

    def finalize(self):
        nc = self.nc
        with nc.Block() as block:
            for eng in ENGINES:
                ops = self.ops[eng]
                if not ops:
                    continue
                phs = self.opph[eng]

                def runner(E, ops=ops, phs=phs):
                    cur = None
                    for fn, ph in zip(ops, phs):
                        if ph != cur:
                            if cur is not None:
                                nc.pop_named_scope(cur)
                            nc.push_named_scope(ph)
                            cur = ph
                        fn(E)
                    if cur is not None:
                        nc.pop_named_scope(cur)
                getattr(block, eng)(runner)


import itertools
_UNIQ = itertools.count()


def sb(ctx, nc, name, shape, dt=F32):
    return ctx.enter_context(
        nc.sbuf_tensor(f"{name}_u{next(_UNIQ)}", shape, dt))


def psum(ctx, nc, name, shape, dt=F32):
    return ctx.enter_context(
        nc.psum_tensor(f"{name}_u{next(_UNIQ)}", shape, dt))


class Streamer:

    def __init__(self, p, bufs):
        self.p = p
        self.bufs = bufs
        self.load_tok = [None] * len(bufs)
        self.used_tok = [None] * len(bufs)
        self.i = 0

    def load(self, in_ap, after=()):
        i = self.i % len(self.bufs)
        self.i += 1
        aft = list(after)
        if self.used_tok[i] is not None:
            aft.append(self.used_tok[i])
        self.load_tok[i] = self.p.dma(
            "gpsimd", self.bufs[i][:, :].rearrange("q (a j) -> q a j", j=128),
            in_ap, after=aft)
        return i

    def note_use(self, i, tok):
        self.used_tok[i] = tok


def slab_ap(W, m, w=128):
    return W[:, w * m:w * m + w].rearrange("(a q) j -> q a j", q=128)



def softplus_cols(p, ctx, nc, lam_cols, ld_tok):
    a1 = sb(ctx, nc, "sp_a1", [128, NC_H])
    sp8 = sb(ctx, nc, "sp8", [128, NC_H])
    sp16 = sb(ctx, nc, "sp16", [128, NC_H])
    t1 = p.act(a1[:, :], lam_cols[:, :], AF.Abs, after=[ld_tok])
    t2 = p.act(a1[:, :], a1[:, :], AF.Exp, scale=-1.0, after=[t1])
    t3 = p.act(a1[:, :], a1[:, :], AF.Ln, bias=1.0, after=[t2])
    t4 = p.act(sp8[:, :], lam_cols[:, :], AF.Relu, after=[ld_tok])
    t5 = p.vec("tensor_add", a1[:, :], a1[:, :], sp8[:, :], after=[t3, t4])
    t6 = p.vec("tensor_scalar_mul", sp8[:, :], a1[:, :], -8.0, after=[t5])
    t7 = p.vec("tensor_scalar_mul", sp16[:, :], a1[:, :], -16.0, after=[t5])
    return sp8, sp16, [t6, t7]


def make_ln_scratch(ctx, nc, tag):
    return dict(
        rb=sb(ctx, nc, f"{tag}_rb", [128, 512], BF16),
        rb2=sb(ctx, nc, f"{tag}_rb2", [128, 512], BF16),
        r0=sb(ctx, nc, f"{tag}_r0", [1, 512]),
        r1=sb(ctx, nc, f"{tag}_r1", [1, 512]),
        r2=sb(ctx, nc, f"{tag}_r2", [1, 512]),
        r3=sb(ctx, nc, f"{tag}_r3", [1, 512]),
        brow=sb(ctx, nc, f"{tag}_brow", [1, 512]),
        a_bf=sb(ctx, nc, f"{tag}_abf", [1, 512], BF16),
        b_bf=sb(ctx, nc, f"{tag}_bbf", [1, 512], BF16))


def layernorm_fm(p, scratch, r_tiles, width, ones_col, ones_row,
                 psA, psB, pA, pB, after, out_write, guard=()):
    nseg = width // 512
    rb = scratch["rb"]
    rb2 = scratch["rb2"]
    r0, r1, r2, r3 = (scratch["r0"], scratch["r1"], scratch["r2"],
                      scratch["r3"])
    brow = scratch["brow"]
    a_bf, b_bf = scratch["a_bf"], scratch["b_bf"]
    out_toks = []
    guard = list(guard)
    for seg in range(nseg):
        sl = slice(seg * 512, seg * 512 + 512)
        prev = list(after) + guard
        ta = tb = None
        for d in range(NC_D):
            tc = p.vec("tensor_copy", rb[:, :], r_tiles[d][:, sl], after=prev)
            ts = p.act(rb2[:, :], r_tiles[d][:, sl], AF.Square, after=prev)
            ta = p.mm(psA[:, :], [(ones_col[:, :], rb[:, :])],
                      after=[tc], start=(d == 0), stop=(d == NC_D - 1))
            tb = p.mm(psB[:, :], [(ones_col[:, :], rb2[:, :])],
                      after=[ts], start=(d == 0), stop=(d == NC_D - 1))
            prev = list(after) + [ta, tb]
        t1 = p.act(r0[:, :], psA[0:1, :], AF.Identity, scale=1.0 / D,
                   after=[ta, tb])
        t2 = p.act(r1[:, :], psB[0:1, :], AF.Identity, scale=1.0 / D,
                   after=[ta, tb])
        t3 = p.act(r2[:, :], r0[:, :], AF.Square, after=[t1])
        t4 = p.vec("tensor_sub", r1[:, :], r1[:, :], r2[:, :],
                   after=[t2, t3])
        t5 = p.act(r2[:, :], r1[:, :], AF.Ln, bias=1e-5, after=[t4])
        t6 = p.act(r3[:, :], r2[:, :], AF.Exp, scale=-0.5, after=[t5])
        t7 = p.vec("tensor_mul", brow[:, :], r0[:, :], r3[:, :],
                   after=[t6])
        t8 = p.vec("tensor_scalar_mul", brow[:, :], brow[:, :], -1.0,
                   after=[t7])
        t9 = p.vec("tensor_copy", a_bf[:, :], r3[:, :], after=[t6])
        t10 = p.vec("tensor_copy", b_bf[:, :], brow[:, :], after=[t8])
        bA = p.mm(pA[:, :], [(ones_row[:, :], a_bf[:, :])],
                  after=[t9] + guard)
        bB = p.mm(pB[:, :], [(ones_row[:, :], b_bf[:, :])],
                  after=[t10] + guard)
        seg_toks = []
        tn = None
        for d in range(NC_D):
            tm = p.vec("tensor_mul", r_tiles[d][:, sl], r_tiles[d][:, sl],
                       pA[:, :], after=[bA, bB])
            tn = p.vec("tensor_add", r_tiles[d][:, sl], r_tiles[d][:, sl],
                       pB[:, :], after=[tm])
            seg_toks.append(out_write(d, seg, r_tiles[d][:, sl], [tn]))
        guard = [t1, t2, tn]
        out_toks += seg_toks
    return out_toks, guard



def build_hawk(nc, ctx, p, dram):
    G = 1024
    NSEG = 2

    xfm = dram["xfm"]
    y1 = dram["y1"]

    xb = [sb(ctx, nc, f"hk_xb{k}", [128, G + 3], BF16) for k in range(NC_D)]
    wslab = [sb(ctx, nc, f"hk_wsl{i}", [128, 1024], BF16) for i in range(2)]
    lwi = [sb(ctx, nc, f"hk_lwi{i}", [128, 1536], BF16) for i in range(2)]
    lwg = [sb(ctx, nc, f"hk_lwg{i}", [128, 1536], BF16) for i in range(2)]
    wosl = [sb(ctx, nc, f"hk_wos{i}", [128, 1536], BF16) for i in range(2)]
    y_g = [sb(ctx, nc, f"hk_yg{m}", [128, G], BF16) for m in range(NC_H)]
    h_st = [sb(ctx, nc, f"hk_h{m}", [128, T], BF16) for m in range(NC_H)]
    cp_st = [sb(ctx, nc, f"hk_cp{m}", [128, T], BF16) for m in range(NC_H)]
    u_sb = sb(ctx, nc, "hk_u", [128, 515])
    tA = sb(ctx, nc, "hk_tA", [128, 512])
    tB = sb(ctx, nc, "hk_tB", [128, 512])
    tC = sb(ctx, nc, "hk_tC", [128, 512])
    tD = sb(ctx, nc, "hk_tD", [128, 512])
    zeros512 = sb(ctx, nc, "hk_z512", [128, 512])
    hcarry = sb(ctx, nc, "hk_hcar", [128, NC_H])
    cpcarry = sb(ctx, nc, "hk_cpcar", [128, NC_H])
    carrycol = sb(ctx, nc, "hk_carcol", [128, NC_H])
    lam_cols = sb(ctx, nc, "hk_lam", [128, NC_H])
    convw = sb(ctx, nc, "hk_cw", [128, NC_H * 4])
    convb = sb(ctx, nc, "hk_cb", [128, NC_H])
    nbi = sb(ctx, nc, "hk_nbi", [128, NC_H])
    nbg = sb(ctx, nc, "hk_nbg", [128, NC_H])
    is_odd = sb(ctx, nc, "hk_odd", [128, 1])
    ones_col = sb(ctx, nc, "hk_oc", [128, 1], BF16)
    ones_row = sb(ctx, nc, "hk_or", [1, 128], BF16)
    r_tiles = [sb(ctx, nc, f"hk_rt{d}", [128, 512]) for d in range(NC_D)]
    xres = sb(ctx, nc, "hk_xres", [128, 512])
    gg = sb(ctx, nc, "hk_gg", [128, 512])
    ln_scr = make_ln_scratch(ctx, nc, "hkln")

    p0 = psum(ctx, nc, "hk_p0", [128, 512])
    p1 = psum(ctx, nc, "hk_p1", [128, 512])
    p2 = psum(ctx, nc, "hk_p2", [128, 512])
    p3 = psum(ctx, nc, "hk_p3", [128, 512])
    ph3 = psum(ctx, nc, "hk_ph3", [128, 8])
    psA = psum(ctx, nc, "hk_psA", [1, 512])
    psB = psum(ctx, nc, "hk_psB", [1, 512])

    tz = p.vec("memset", zeros512[:, :], 0.0)
    p.vec("memset", hcarry[:, :], 0.0)
    p.vec("memset", cpcarry[:, :], 1.0)
    p.vec("memset", ones_col[:, :], 1.0)
    p.vec("memset", ones_row[:, :], 1.0)
    tl1 = p.dma("sync", lam_cols[:, :], dram["lam_cols"].ap())
    p.dma("sync", convw[:, :], dram["convw_cols"].ap())
    p.dma("sync", convb[:, :], dram["convb_cols"].ap())
    tl4 = p.dma("sync", is_odd[:, :], dram["is_odd"].ap())
    tb1 = p.dma("sync", nbi[:, :], dram["bi_cols"].ap())
    tb2 = p.dma("sync", nbg[:, :], dram["bg_cols"].ap())
    tn1 = p.vec("tensor_scalar_mul", nbi[:, :], nbi[:, :], -1.0, after=[tb1])
    tn2 = p.vec("tensor_scalar_mul", nbg[:, :], nbg[:, :], -1.0, after=[tb2])
    sp8, sp16, sp_toks = softplus_cols(p, ctx, nc, lam_cols, tl1)
    p.barrier("vector")

    WiT, WgT = dram["WiT"], dram["WgT"]
    lruWiT, lruWgT, WoT = dram["lruWiT"], dram["lruWgT"], dram["WoT"]

    scan_done = []

    for g in range(2):
        if g == 1:
            p.barrier()
        xcol0 = g * G
        xb_toks = [p.dma("gpsimd", xb[k][:, :],
                         xfm[128 * k:128 * k + 128, xcol0:xcol0 + G + 3])
                   for k in range(NC_D)]
        ws = Streamer(p, wslab)
        ws.load(slab_ap(WiT, 0))
        ws.load(slab_ap(WiT, 1))
        warm = p.mm(p0[:, :], [
            (wslab[0][:, 128 * k:128 * k + 128],
             xb[k][:, 3:515]) for k in range(NC_D)],
            after=[ws.load_tok[0]] + xb_toks)
        ws.note_use(0, warm)
        evac_u = evac_h = conv_last = None
        for mi, m in enumerate(range(NC_H)):
            bi = (mi % 2)
            buf = wslab[bi]
            for seg in range(NSEG):
                pr = [ws.load_tok[bi]] + xb_toks
                if evac_u is not None:
                    pr.append(evac_u)
                tmm = p.mm(p0[:, :], [
                    (buf[:, 128 * k:128 * k + 128],
                     xb[k][:, 3 + seg * 512: 3 + seg * 512 + 512])
                    for k in range(NC_D)], after=pr)
                ws.note_use(bi, tmm)
                if DBG_NOHALO:
                    evac_u = p.vec("tensor_copy", u_sb[:, 3:515], p0[:, :],
                                   after=[tmm] + ([conv_last] if conv_last else []))
                    evac_h = p.vec("memset", u_sb[:, 0:3], 0.0,
                                   after=[conv_last] if conv_last else [])
                else:
                    tm3 = p.mm(ph3[:, 0:3], [
                        (buf[:, 128 * k:128 * k + 128],
                         xb[k][:, seg * 512: seg * 512 + 3])
                        for k in range(NC_D)],
                        after=[evac_h] if evac_h else [])
                    ws.note_use(bi, tm3)
                    evac_u = p.vec("tensor_copy", u_sb[:, 3:515], p0[:, :],
                                   after=[tmm] + ([conv_last] if conv_last else []))
                    evac_h = p.vec("tensor_copy", u_sb[:, 0:3], ph3[:, 0:3],
                                   after=[tm3] + ([conv_last] if conv_last else []))
                c1 = p.vec("tensor_scalar", tA[:, :], u_sb[:, 3:515],
                           convw[:, 3 * NC_H + m:3 * NC_H + m + 1],
                           convb[:, m:m + 1], ALU.mult, ALU.add,
                           after=[evac_u, evac_h])
                c2 = p.vec("scalar_tensor_tensor", tA[:, :], u_sb[:, 2:514],
                           convw[:, 2 * NC_H + m:2 * NC_H + m + 1], tA[:, :],
                           ALU.mult, ALU.add, after=[c1])
                c3 = p.vec("scalar_tensor_tensor", tA[:, :], u_sb[:, 1:513],
                           convw[:, 1 * NC_H + m:1 * NC_H + m + 1], tA[:, :],
                           ALU.mult, ALU.add, after=[c2])
                conv_last = p.vec(
                    "scalar_tensor_tensor",
                    y_g[m][:, seg * 512:seg * 512 + 512],
                    u_sb[:, 0:512], convw[:, 0 * NC_H + m:0 * NC_H + m + 1],
                    tA[:, :], ALU.mult, ALU.add, after=[c3])
                if DEBUG_BARRIERS:
                    p.barrier()
                if "dbg_u" in dram and g == 0 and m == 0 and seg == 0:
                    p.dma("sync", dram["dbg_u"].ap(), u_sb[:, :],
                          after=[evac_u, evac_h], sem="st")
                    p.dma("gpsimd", dram["dbg_tA"].ap(),
                          y_g[0][:, 0:512], after=[conv_last], sem="st")
            if mi + 2 < NC_H:
                ws.load(slab_ap(WiT, mi + 2))
        y_done = [conv_last, evac_u]
        if "dbg_y" in dram:
            for m_ in range(NC_H):
                p.dma("sync", dram["dbg_y"][128 * m_:128 * m_ + 128,
                                            g * G:(g + 1) * G],
                      y_g[m_][:, :], after=y_done, sem="st")
            p.dma("gpsimd", dram["dbg_y2"][:, g * G:(g + 1) * G],
                  y_g[0][:, :], after=y_done, sem="st")

        si_s = Streamer(p, lwi)
        sg_s = Streamer(p, lwg)
        si_s.load(slab_ap(lruWiT, 0), after=y_done)
        sg_s.load(slab_ap(lruWgT, 0), after=y_done)
        si_s.load(slab_ap(lruWiT, 1))
        sg_s.load(slab_ap(lruWgT, 1))
        warm2 = p.mm(p1[:, :], [
            (lwi[0][:, 128 * k:128 * k + 128], y_g[k][:, 0:512])
            for k in range(NC_H)], after=[si_s.load_tok[0]] + y_done)
        si_s.note_use(0, warm2)
        warm3 = p.mm(p2[:, :], [
            (lwg[0][:, 128 * k:128 * k + 128], y_g[k][:, 0:512])
            for k in range(NC_H)], after=[sg_s.load_tok[0]] + y_done)
        sg_s.note_use(0, warm3)
        ev_i = ev_g = last_v = None
        for m in range(NC_H):
            bi = m % 2
            for seg in range(NSEG):
                gsl = slice(seg * 512, seg * 512 + 512)
                tsl = slice(g * G + seg * 512, g * G + seg * 512 + 512)
                pr = [si_s.load_tok[bi]] + y_done
                if ev_i is not None:
                    pr.append(ev_i)
                t_si = p.mm(p1[:, :], [
                    (lwi[bi][:, 128 * k:128 * k + 128], y_g[k][:, gsl])
                    for k in range(NC_H)], after=pr)
                si_s.note_use(bi, t_si)
                pr = [sg_s.load_tok[bi]] + y_done
                if ev_g is not None:
                    pr.append(ev_g)
                t_sg = p.mm(p2[:, :], [
                    (lwg[bi][:, 128 * k:128 * k + 128], y_g[k][:, gsl])
                    for k in range(NC_H)], after=pr)
                sg_s.note_use(bi, t_sg)
                lw = [last_v] if last_v else []
                e1 = p.act(tA[:, :], p1[:, :], AF.Exp, scale=-1.0,
                           bias=nbi[:, m:m + 1], after=[t_si, tn1] + lw)
                ev_i = e1
                v1 = p.vec("tensor_scalar_add", tA[:, :], tA[:, :],
                           1.0, after=[e1])
                v2 = p.vec("reciprocal", tA[:, :], tA[:, :], after=[v1])
                e2 = p.act(tB[:, :], p2[:, :], AF.Exp, scale=-1.0,
                           bias=nbg[:, m:m + 1], after=[t_sg, tn2] + lw)
                ev_g = e2
                v3 = p.vec("tensor_scalar_add", tB[:, :], tB[:, :],
                           1.0, after=[e2])
                v4 = p.vec("reciprocal", tB[:, :], tB[:, :], after=[v3])
                a1 = p.act(tC[:, :], tB[:, :], AF.Exp,
                           scale=sp8[:, m:m + 1], after=[v4] + sp_toks)
                a2 = p.act(tD[:, :], tB[:, :], AF.Exp,
                           scale=sp16[:, m:m + 1], after=[v4] + sp_toks)
                z1 = p.vec("tensor_scalar", tD[:, :], tD[:, :], -1.0,
                           1.000001, ALU.mult, ALU.add, after=[a2])
                z2 = p.act(tD[:, :], tD[:, :], AF.Ln, after=[z1])
                z3 = p.act(tD[:, :], tD[:, :], AF.Exp, scale=0.5, after=[z2])
                x1 = p.vec("tensor_mul", tB[:, :], tD[:, :], tA[:, :],
                           after=[z3, v2])
                x2 = p.vec("tensor_mul", tB[:, :], tB[:, :], y_g[m][:, gsl],
                           after=[x1])
                s1 = p.vec("tensor_tensor_scan", h_st[m][:, tsl], tC[:, :],
                           tB[:, :], hcarry[:, m:m + 1], ALU.mult, ALU.add,
                           after=[x2, a1])
                s2 = p.vec("tensor_tensor_scan", cp_st[m][:, tsl], tC[:, :],
                           zeros512[:, :], cpcarry[:, m:m + 1], ALU.mult,
                           ALU.add, after=[s1, tz])
                s3 = p.vec("tensor_copy", hcarry[:, m:m + 1],
                           h_st[m][:, tsl.stop - 1:tsl.stop], after=[s1])
                s4 = p.vec("tensor_copy", cpcarry[:, m:m + 1],
                           cp_st[m][:, tsl.stop - 1:tsl.stop], after=[s2])
                last_v = s4
                if g == 1 and seg == NSEG - 1:
                    scan_done += [s3, s4]
            if m + 2 < NC_H:
                si_s.load(slab_ap(lruWiT, m + 2))
                sg_s.load(slab_ap(lruWgT, m + 2))

    cg_in, cg_out = dram["cg_in"], dram["cg_out"]
    tpub = p.dma("gpsimd", cg_in.ap(), hcarry[:, :], after=scan_done)
    tcoll = p.coll("AllGather", ALU.bypass, PAIRS,
                   [cg_in.ap().opt()], [cg_out.ap().opt()], after=[tpub])
    tback = p.dma("gpsimd", carrycol[:, :], cg_out[0:128, :], after=[tcoll])
    tcc = p.vec("tensor_scalar_mul", carrycol[:, :], carrycol[:, :],
                is_odd[:, :], after=[tback, tl4])
    if "dbg_h" in dram:
        for m_ in range(NC_H):
            p.dma("sync", dram["dbg_h"][128 * m_:128 * m_ + 128, :],
                  h_st[m_][:, :], after=scan_done, sem="st")
            p.dma("sync", dram["dbg_cp"][128 * m_:128 * m_ + 128, :],
                  cp_st[m_][:, :], after=scan_done, sem="st")
        p.dma("sync", dram["dbg_carry"].ap(), carrycol[:, :], after=[tcc],
              sem="st")

    ln_guard = []
    for g in range(2):
        p.barrier()
        xcol0 = g * G
        xb_toks = [p.dma("gpsimd", xb[k][:, :],
                         xfm[128 * k:128 * k + 128, xcol0:xcol0 + G + 3])
                   for k in range(NC_D)]
        gs = Streamer(p, wslab)
        gs.load(slab_ap(WgT, 0))
        gs.load(slab_ap(WgT, 1))
        warm4 = p.mm(p0[:, :], [
            (wslab[0][:, 128 * k:128 * k + 128],
             xb[k][:, 3:515]) for k in range(NC_D)],
            after=[gs.load_tok[0]] + xb_toks)
        gs.note_use(0, warm4)
        ev_gate = None
        prev_f2 = None
        mm_done = []
        for m in range(NC_H):
            bi = m % 2
            for seg in range(NSEG):
                gsl = slice(seg * 512, seg * 512 + 512)
                tsl = slice(g * G + seg * 512, g * G + seg * 512 + 512)
                pr = [gs.load_tok[bi]] + xb_toks
                if ev_gate is not None:
                    pr.append(ev_gate)
                tmm = p.mm(p0[:, :], [
                    (wslab[bi][:, 128 * k:128 * k + 128],
                     xb[k][:, 3 + seg * 512: 3 + seg * 512 + 512])
                    for k in range(NC_D)], after=pr)
                gs.note_use(bi, tmm)
                ga = [tmm] + ([prev_f2] if prev_f2 else [])
                ev_gate = p.act(gg[:, :], p0[:, :], AF.Gelu, after=ga)
                f1 = p.vec("scalar_tensor_tensor", tA[:, :],
                           cp_st[m][:, tsl], carrycol[:, m:m + 1],
                           h_st[m][:, tsl], ALU.mult, ALU.add,
                           after=[tcc, ev_gate])
                f2 = p.vec("tensor_mul", y_g[m][:, gsl], gg[:, :],
                           tA[:, :], after=[f1])
                prev_f2 = f2
                mm_done.append(f2)
            if m + 2 < NC_H:
                gs.load(slab_ap(WgT, m + 2))
        if "dbg_m" in dram:
            for m_ in range(NC_H):
                p.dma("sync", dram["dbg_m"][128 * m_:128 * m_ + 128,
                                            g * G:(g + 1) * G],
                      y_g[m_][:, :], after=mm_done, sem="st")
        os_ = Streamer(p, wosl)
        for seg in range(NSEG):
            os_.load(slab_ap(WoT, 0), after=mm_done[-2:])
            os_.load(slab_ap(WoT, 1))
            warm5 = p.mm(p1[:, :], [
                (wosl[0][:, 128 * k:128 * k + 128], y_g[k][:, 0:512])
                for k in range(NC_H)],
                after=[os_.load_tok[0]] + mm_done + ln_guard)
            os_.note_use(0, warm5)
            ev_out = None
            r_done = []
            gsl = slice(seg * 512, seg * 512 + 512)
            for d in range(NC_D):
                bi = d % 2
                pr = [os_.load_tok[bi]] + mm_done + ln_guard
                if ev_out is not None:
                    pr.append(ev_out)
                tmm = p.mm(p1[:, :], [
                    (wosl[bi][:, 128 * k:128 * k + 128], y_g[k][:, gsl])
                    for k in range(NC_H)], after=pr)
                os_.note_use(bi, tmm)
                tx = p.dma("sync", xres[:, :],
                           xfm[128 * d:128 * d + 128,
                               3 + g * G + seg * 512:
                               3 + g * G + seg * 512 + 512],
                           after=[ev_out] if ev_out else [])
                ev_out = p.vec("tensor_add", r_tiles[d][:, 0:512], p1[:, :],
                               xres[:, :], after=[tmm, tx] + ln_guard)
                r_done.append(ev_out)
                if d + 2 < NC_D:
                    os_.load(slab_ap(WoT, d + 2))

            def writer(dd, _s, src_ap, toks, g=g, seg=seg):
                return p.dma("sync",
                             y1[128 * dd:128 * dd + 128,
                                g * G + seg * 512: g * G + seg * 512 + 512],
                             src_ap, after=toks, sem="st")
            stoks, ln_guard = layernorm_fm(
                p, ln_scr, r_tiles, 512, ones_col, ones_row,
                psA, psB, p2, p3, r_done, writer, guard=ln_guard)
            ln_guard = ln_guard + stoks

    p.barrier("sync")



def cols128(v):
    v = np.asarray(v, dtype=np.float32)
    return np.ascontiguousarray(v.reshape(-1, 128).T)


def pack_hawk_inputs(inputs):
    x = np.asarray(inputs["L0_attn_x"], dtype=np.float32)
    WiT = np.ascontiguousarray(np.asarray(inputs["L0_attn_Wi"], dtype=np.float32).T)
    WgT = np.ascontiguousarray(np.asarray(inputs["L0_attn_Wg"], dtype=np.float32).T)
    lruWiT = np.ascontiguousarray(np.asarray(inputs["L0_attn_lru_Wi"], dtype=np.float32).T)
    lruWgT = np.ascontiguousarray(np.asarray(inputs["L0_attn_lru_Wg"], dtype=np.float32).T)
    WoT = np.ascontiguousarray(np.asarray(inputs["L0_attn_Wo"], dtype=np.float32).T)
    cw = np.asarray(inputs["L0_attn_conv_w"], dtype=np.float32)[:, 0, :]
    convw_cols = np.ascontiguousarray(
        cw.reshape(NC_H, 128, 4).transpose(1, 2, 0).reshape(128, 4 * NC_H))
    convb_cols = cols128(inputs["L0_attn_conv_b"])
    lam_cols = cols128(inputs["L0_attn_lam"])
    bi_cols = cols128(inputs["L0_attn_lru_bi"])
    bg_cols = cols128(inputs["L0_attn_lru_bg"])

    maps = []
    for c in range(8):
        b, h = c // 2, c % 2
        own = x[b, h * T:(h + 1) * T, :].T
        if h == 0:
            halo = np.zeros((D, 3), np.float32)
        else:
            halo = x[b, h * T - 3:h * T, :].T
        xfm = np.ascontiguousarray(np.concatenate([halo, own], axis=1))
        is_odd = np.full((128, 1), float(h), np.float32)
        maps.append(dict(xfm=xfm, WiT=WiT, WgT=WgT, lruWiT=lruWiT,
                         lruWgT=lruWgT, WoT=WoT, convw_cols=convw_cols,
                         convb_cols=convb_cols, lam_cols=lam_cols,
                         bi_cols=bi_cols, bg_cols=bg_cols, is_odd=is_odd))
    return maps


def declare_hawk_dram(nc, stream_in_kind="ExternalInput",
                      stream_out_kind="ExternalOutput"):
    dram = {}
    dram["xfm"] = nc.dram_tensor("xfm", [D, T + 3], F32, kind=stream_in_kind)
    for nm, shp in [("WiT", [D, HID]), ("WgT", [D, HID]),
                    ("lruWiT", [HID, HID]), ("lruWgT", [HID, HID]),
                    ("WoT", [HID, D]), ("convw_cols", [128, 4 * NC_H]),
                    ("convb_cols", [128, NC_H]), ("lam_cols", [128, NC_H]),
                    ("bi_cols", [128, NC_H]), ("bg_cols", [128, NC_H]),
                    ("is_odd", [128, 1])]:
        dram[nm] = nc.dram_tensor(nm, shp, F32, kind="ExternalInput")
    dram["y1"] = nc.dram_tensor("y1", [D, T], F32, kind=stream_out_kind)
    dram["cg_in"] = nc.dram_tensor("cg_in", [128, NC_H], F32)
    dram["cg_out"] = nc.dram_tensor("cg_out", [256, NC_H], F32)
    return dram


def declare_hawk_dbg(nc, dram):
    for nm, shp in [("dbg_y", [HID, T]), ("dbg_h", [HID, T]),
                    ("dbg_cp", [HID, T]), ("dbg_m", [HID, T]),
                    ("dbg_carry", [128, NC_H])]:
        dram[nm] = nc.dram_tensor(nm, shp, BF16 if nm != "dbg_carry" else F32,
                                  kind="ExternalOutput")
    dram["dbg_u"] = nc.dram_tensor("dbg_u", [128, 515], F32,
                                   kind="ExternalOutput")
    dram["dbg_y2"] = nc.dram_tensor("dbg_y2", [128, T], BF16,
                                    kind="ExternalOutput")
    dram["dbg_tA"] = nc.dram_tensor("dbg_tA", [128, 512], F32,
                                    kind="ExternalOutput")


def run_hawk(inputs, dbg=False):
    nc = bass.Bass("TRN2", target_bir_lowering=False)
    with ExitStack() as ctx:
        dram = declare_hawk_dram(nc)
        if dbg:
            declare_hawk_dbg(nc, dram)
        p = P(nc, ctx)
        build_hawk(nc, ctx, p, dram)
        p.finalize()
    maps = pack_hawk_inputs(inputs)
    return run_bass_kernel_spmd(nc, maps, core_ids=list(range(8)))


def _gather_fm(res, key):
    out = np.zeros((4, 4096, D), np.float32)
    for c in range(8):
        b, h = c // 2, c % 2
        out[b, h * T:(h + 1) * T, :] = res.results[c][key].T
    return out



def build_ffn_attn(nc, ctx, p, dram):
    NW = 4
    W = 512
    stream_in = dram["fa_in"]
    stream_out = dram["fa_out"]
    wqkv = dram["fa_wqkv"]
    wo = dram["fa_wo"]

    with ExitStack() as c1:
        xb = [sb(c1, nc, f"fa_xb{k}", [128, T + 1], BF16) for k in range(NC_D)]
        kblk = [sb(c1, nc, f"fa_k{h}", [128, 2560], BF16) for h in range(8)]
        qblk = [sb(c1, nc, f"fa_q{h}", [128, T], BF16) for h in range(8)]
        vt = [sb(c1, nc, f"fa_v{i}", [128, 520], BF16) for i in range(20)]
        otile = [sb(c1, nc, f"fa_o{i}", [128, T], BF16) for i in range(4)]
        pmax = sb(c1, nc, "fa_pmax", [1, 40])
        qcs = sb(c1, nc, "fa_qcs_sb", [128, T], BF16)
        qss = sb(c1, nc, "fa_qss_sb", [128, T], BF16)
        kci = sb(c1, nc, "fa_kci_sb", [128, 2560], BF16)
        ksi = sb(c1, nc, "fa_ksi_sb", [128, 2560], BF16)
        perm = sb(c1, nc, "fa_perm_sb", [128, 128], BF16)
        mask0 = sb(c1, nc, "fa_m0_sb", [128, 512], BF16)
        mask4 = sb(c1, nc, "fa_m4_sb", [128, 512], BF16)
        hd_a = sb(c1, nc, "fa_hda_sb", [128, 1], BF16)
        hd_b = sb(c1, nc, "fa_hdb_sb", [128, 1], BF16)
        ind_a = sb(c1, nc, "fa_inda_sb", [1, 128], BF16)
        ind_b = sb(c1, nc, "fa_indb_sb", [1, 128], BF16)
        ones64r = sb(c1, nc, "fa_o64r", [1, 64], BF16)
        ones64c = sb(c1, nc, "fa_o64c", [64, 1], BF16)
        qs8col = sb(c1, nc, "fa_qs8_sb", [128, 1])
        kscol = sb(c1, nc, "fa_ks_sb", [128, 1])
        is_odd = sb(c1, nc, "fa_odd_sb", [128, 1])
        qraw = sb(c1, nc, "fa_qraw", [128, 512])
        qsc1 = sb(c1, nc, "fa_qsc1", [128, 512])
        qsc2 = sb(c1, nc, "fa_qsc2", [128, 512])
        qsqb = sb(c1, nc, "fa_qsqb", [128, 512], BF16)
        qnb = sb(c1, nc, "fa_qnb", [128, 512], BF16)
        rowA = sb(c1, nc, "fa_rowA", [1, 1024])
        rowB = sb(c1, nc, "fa_rowB", [1, 1024])
        rowBf = sb(c1, nc, "fa_rowBf", [1, 1024], BF16)
        mcol = sb(c1, nc, "fa_mcol", [64, 1])
        mcolb = sb(c1, nc, "fa_mcolb", [64, 1], BF16)
        halocol = sb(c1, nc, "fa_hcol", [128, 8])
        probs = [sb(c1, nc, f"fa_pr{d}", [128, 512], BF16) for d in range(5)]
        invb = sb(c1, nc, "fa_invb", [1, 512], BF16)
        invf = sb(c1, nc, "fa_invf", [1, 512])
        osc = sb(c1, nc, "fa_osc", [128, 512])
        xres = sb(c1, nc, "fa_xres", [128, 512])
        wsl = [sb(c1, nc, f"fa_wsl{i}", [128, 1024], BF16) for i in range(2)]
        wvsl = sb(c1, nc, "fa_wvsl", [128, 4096], BF16)
        wosl = [sb(c1, nc, f"fa_wos{i}", [128, 512], BF16) for i in range(2)]

        p0 = psum(c1, nc, "fa_p0", [128, 512])
        p1 = psum(c1, nc, "fa_p1", [128, 512])
        p2 = psum(c1, nc, "fa_p2", [128, 512])
        pav = psum(c1, nc, "fa_pav", [128, 512])
        pbc = psum(c1, nc, "fa_pbc", [128, 512])
        prow = psum(c1, nc, "fa_prow", [1, 1024])

        ct = []
        ct.append(p.dma("gpsimd", qcs[:, :], dram["fa_qcs"].ap()))
        ct.append(p.dma("gpsimd", qss[:, :], dram["fa_qss"].ap()))
        ct.append(p.dma("gpsimd", kci[:, :], dram["fa_kci"].ap()))
        ct.append(p.dma("gpsimd", ksi[:, :], dram["fa_ksi"].ap()))
        ct.append(p.dma("sync", perm[:, :], dram["fa_perm"].ap()))
        ct.append(p.dma("sync", mask0[:, :], dram["fa_mask0"].ap()))
        ct.append(p.dma("sync", mask4[:, :], dram["fa_mask4"].ap()))
        ct.append(p.dma("sync", hd_a[:, :], dram["fa_hda"].ap()))
        ct.append(p.dma("sync", hd_b[:, :], dram["fa_hdb"].ap()))
        ct.append(p.dma("sync", ind_a[:, :], dram["fa_inda"].ap()))
        ct.append(p.dma("sync", ind_b[:, :], dram["fa_indb"].ap()))
        ct.append(p.dma("sync", qs8col[:, :], dram["fa_qs8"].ap()))
        ct.append(p.dma("sync", kscol[:, :], dram["fa_ks"].ap()))
        ct.append(p.dma("sync", is_odd[:, :], dram["fa_odd"].ap()))
        tone = p.vec("memset", ones64r[:, :], 1.0)
        tone2 = p.vec("memset", ones64c[:, :], 1.0)
        ct.append(tone)
        for h in range(8):
            ct.append(p.dma("gpsimd", kblk[h][65:66, :], dram["fa_kpad"].ap()))
            ct.append(p.dma("sync", qblk[h][65:66, :], dram["fa_neg1"].ap()))
            ct.append(p.vec("memset", kblk[h][64:65, :], 1.0))
        xb_toks = []
        for k in range(NC_D):
            xb_toks.append(p.dma("gpsimd", xb[k][:, 1:T + 1],
                                 stream_in[128 * k:128 * k + 128, :]))
        thc = p.dma("gpsimd", halocol[:, :], dram["fa_cgat"][0:128, :],
                    after=[dram["fa_cgat_tok"]])
        for k in range(NC_D):
            xb_toks.append(p.vec("tensor_scalar_mul", xb[k][:, 0:1],
                                 halocol[:, k:k + 1], is_odd[:, :],
                                 after=[thc, ct[13]]))
        p.barrier("vector")

        def xsh(k, c0, n):
            if k < 4:
                return xb[k][:, 1 + c0:1 + c0 + n]
            return xb[k][:, c0:c0 + n]

        ws = Streamer(p, wsl)
        ws.load(slab_ap(wqkv, 0), after=[])
        warm = p.mm(p0[:, :], [
            (wsl[0][:, 128 * k:128 * k + 128], xsh(k, 0, 512))
            for k in range(NC_D)], after=[ws.load_tok[0]] + xb_toks)
        ws.note_use(0, warm)
        ws.load(slab_ap(wqkv, 1))

        def qk_pipeline(F, g, is_q, ev_prev):
            bi = F % 2
            gsl = slice(g * 512, g * 512 + 512)
            pr = [ws.load_tok[bi]] + xb_toks
            if ev_prev[0] is not None:
                pr.append(ev_prev[0])
            tmm = p.mm(p0[:, :], [
                (wsl[bi][:, 128 * k:128 * k + 128], xsh(k, g * 512, 512))
                for k in range(NC_D)], after=pr)
            ws.note_use(bi, tmm)
            ev = p.act(qraw[:, :], p0[:, :], AF.Identity,
                       after=[tmm] + ev_prev[1:])
            ev_prev[0] = ev
            sq = p.act(qsqb[:, :], qraw[:, :], AF.Square, after=[ev])
            n1 = p.mm(prow[0:1, 0:512], [(hd_a[:, :], qsqb[:, :])],
                      after=[sq] + ct)
            n2 = p.mm(prow[0:1, 512:1024], [(hd_b[:, :], qsqb[:, :])],
                      after=[sq] + ct)
            li = p.act(rowA[0:1, :], prow[0:1, :], AF.Ln, bias=1e-24,
                       after=[n1, n2])
            iv = p.act(rowA[0:1, :], rowA[0:1, :], AF.Exp, scale=-0.5,
                       after=[li])
            ivb = p.vec("tensor_copy", rowBf[0:1, :], rowA[0:1, :],
                        after=[iv])
            b1 = p.mm(pbc[:, :], [(ind_a[:, :], rowBf[0:1, 0:512])],
                      after=[ivb], stop=False)
            b2 = p.mm(pbc[:, :], [(ind_b[:, :], rowBf[0:1, 512:1024])],
                      after=[ivb], start=False)
            scol = qs8col if is_q else kscol
            v1 = p.vec("tensor_scalar_mul", qsc1[:, :], qraw[:, :],
                       scol[:, :], after=[ev] + ct)
            del ev_prev[1:]
            ev_prev.append(v1)
            v2 = p.vec("tensor_copy", qnb[:, :], qsc1[:, :], after=[v1])
            rp = p.mm(p1[:, :], [(perm[:, :], qnb[:, :])], after=[v2] + ct)
            ctab = qcs if is_q else kci
            stab = qss if is_q else ksi
            cview = ctab[:, gsl] if is_q else ctab[:, 512 + g * 512:512 + g * 512 + 512]
            sview = stab[:, gsl] if is_q else stab[:, 512 + g * 512:512 + g * 512 + 512]
            v3 = p.vec("tensor_mul", qsc1[:, :], qsc1[:, :], cview,
                       after=[v2] + ct)
            v4 = p.vec("tensor_mul", qsc2[:, :], p1[:, :], sview,
                       after=[rp] + ct)
            v5 = p.vec("tensor_add", qsc1[:, :], qsc1[:, :], qsc2[:, :],
                       after=[v3, v4])
            h0 = 2 * (F % 4)
            if is_q:
                dsts = [qblk[h0][0:64, gsl], qblk[h0 + 1][0:64, gsl]]
            else:
                ksl = slice(512 + g * 512, 512 + g * 512 + 512)
                dsts = [kblk[h0][0:64, ksl], kblk[h0 + 1][0:64, ksl]]
            v6 = p.vec("tensor_mul", qsc1[:, :], qsc1[:, :], pbc[:, :],
                       after=[v5, b1, b2])
            c1t = p.vec("tensor_copy", dsts[0], qsc1[0:64, :], after=[v6])
            c2t = p.vec("tensor_copy", dsts[1], qsc1[64:128, :], after=[v6])
            toks = [c1t, c2t, v6]
            if not is_q:
                pass
            return toks

        ev_prev = [None]
        qk_done = []
        for F in range(8):
            is_q = F < 4
            for g in range(4):
                qk_done += qk_pipeline(F, g, is_q, ev_prev)
            if F + 2 < 8:
                ws.load(slab_ap(wqkv, F + 2))

        kn_done = []

        wvt = p.dma("gpsimd", wvsl[:, :].rearrange("q (a j) -> q a j", j=512),
                    wqkv[:, 1024:1536].rearrange("(a q) j -> q a j", q=128))
        ev_v = None
        v_done = []
        for tc in range(16):
            pr = [wvt] + xb_toks
            if ev_v is not None:
                pr.append(ev_v)
            tmm = p.mm(p2[:, :], [
                (xsh(k, 128 * tc, 128), wvsl[:, 512 * k:512 * k + 512])
                for k in range(NC_D)], after=pr)
            ev_v = p.act(
                vt[4 + tc][:, :].rearrange("q (h c) -> q h c", c=65)[:, :, 0:64],
                p2[:, :].rearrange("q (h c) -> q h c", c=64), AF.Identity,
                after=[tmm])
            mz = p.vec("memset",
                       vt[4 + tc][:, :].rearrange("q (h c) -> q h c", c=65)[:, :, 64:65],
                       1.0)
            v_done += [ev_v, mz]

        kpub, kgat = dram["fa_kpub"], dram["fa_kgat"]
        vpub, vgat = dram["fa_vpub"], dram["fa_vgat"]
        pub_toks = []
        for h in range(8):
            pub_toks.append(p.dma("gpsimd", kpub[64 * h:64 * h + 64, :],
                                  kblk[h][0:64, 2048:2560],
                                  after=qk_done))
        for i in range(4):
            pub_toks.append(p.dma("gpsimd", vpub[128 * i:128 * i + 128, :],
                                  vt[16 + i][:, :], after=v_done))
        ck = p.coll("AllGather", ALU.bypass, PAIRS, [kpub.ap().opt()],
                    [kgat.ap().opt()], after=pub_toks)
        cv = p.coll("AllGather", ALU.bypass, PAIRS, [vpub.ap().opt()],
                    [vgat.ap().opt()], after=pub_toks)
        back_toks = []
        for h in range(8):
            back_toks.append(p.dma("gpsimd", kblk[h][0:64, 0:512],
                                   kgat[64 * h:64 * h + 64, :], after=[ck]))
        for i in range(4):
            back_toks.append(p.dma("gpsimd", vt[i][:, :],
                                   vgat[128 * i:128 * i + 128, :], after=[cv]))
        hz = []
        for h in range(8):
            hz.append(p.vec("tensor_scalar_mul", kblk[h][0:64, 0:512],
                            kblk[h][0:64, 0:512], is_odd[0:64, :],
                            after=back_toks + ct))
        kn_done += hz

        p.barrier()
        att_done = []
        for w in range(NW):
            for h in range(8):
                qsl = slice(512 * w, 512 * w + 512)
                t1 = p.vec("reduce_max", mcol[0:64, :],
                           kblk[h][0:64, 512 * w:512 * w + 1024], AX.X,
                           apply_absolute_value=True, after=kn_done)
                t2 = p.vec("tensor_copy", mcolb[0:64, :], mcol[0:64, :],
                           after=[t1])
                sq = p.act(qsqb[0:64, :], qblk[h][0:64, qsl], AF.Abs,
                           after=qk_done + att_done[-1:])
                nm = p.mm(prow[0:1, 0:512], [(mcolb[0:64, :], qsqb[0:64, :])],
                          after=[sq, t2] + att_done[-1:])
                wb_ = p.vec("tensor_scalar_mul", qblk[h][64:65, qsl],
                            prow[0:1, 0:512], -1.0, after=[nm])
                ev_toks = []
                for d in range(5):
                    pd = [p0, p1, p2, pav, pbc][d]
                    mm_toks = []
                    for t_ in range(4):
                        kc = 512 * w + 128 * (t_ + d)
                        mt = p.mm(pd[:, 128 * t_:128 * t_ + 128],
                                  [(kblk[h][0:66, kc:kc + 128],
                                    qblk[h][0:66, 512 * w + 128 * t_:
                                            512 * w + 128 * t_ + 128])],
                                  after=[wb_] + att_done[-1:])
                        mm_toks.append(mt)
                    ev = p.act(probs[d][:, :], pd[:, :], AF.Exp,
                               after=mm_toks)
                    ev_toks.append(ev)
                m0 = p.vec("tensor_mul", probs[0][:, :], probs[0][:, :],
                           mask0[:, :], after=[ev_toks[0]] + ct)
                m4 = p.vec("tensor_mul", probs[4][:, :], probs[4][:, :],
                           mask4[:, :], after=[ev_toks[4]] + ct)
                av_toks = []
                for t_ in range(4):
                    pairs = []
                    for d in range(5):
                        kc = 512 * w + 128 * (t_ + d)
                        pairs.append((vt[kc // 128][:, 65 * h:65 * h + 65],
                                      probs[d][:, 128 * t_:128 * t_ + 128]))
                    at = p.mm(pav[0:65, 128 * t_:128 * t_ + 128], pairs,
                              after=ev_toks + [m0, m4])
                    av_toks.append(at)
                rc = p.vec("reciprocal", invf[0:1, :], pav[64:65, :],
                           after=av_toks)
                rb = p.vec("tensor_copy", invb[0:1, :], invf[0:1, :],
                           after=[rc])
                bc = p.mm(pbc[0:64, :], [(ones64r[:, :], invb[0:1, :])],
                          after=[rb])
                eo = p.act(osc[0:64, :], pav[0:64, :], AF.Identity,
                           after=[bc])
                fm = p.vec("tensor_mul",
                           otile[h // 2][64 * (h % 2):64 * (h % 2) + 64, qsl],
                           osc[0:64, :], pbc[0:64, :], after=[eo, bc])
                att_done.append(fm)

        if "fa_dbgo" in dram:
            for i in range(4):
                p.dma("sync", dram["fa_dbgo"][128 * i:128 * i + 128, :],
                      otile[i][:, :], after=att_done, sem="st")
            for h in range(8):
                p.dma("sync", dram["fa_dbgk"][64 * h:64 * h + 64, :],
                      kblk[h][0:64, 0:1024], after=kn_done + qk_done, sem="st")
            p.dma("sync", dram["fa_dbgv"].ap(), vt[0][:, :],
                  after=back_toks, sem="st")
        p.barrier()
        os_ = Streamer(p, wosl)
        os_.load(slab_ap(wo, 0))
        warm2 = p.mm(p0[:, :], [
            (wosl[0][:, 128 * k:128 * k + 128], otile[k][:, 0:512])
            for k in range(4)], after=[os_.load_tok[0]] + att_done)
        os_.note_use(0, warm2)
        os_.load(slab_ap(wo, 1))
        ev_out = None
        st_prev = []
        for d in range(NC_D):
            bi = d % 2
            for g in range(4):
                gsl = slice(g * 512, g * 512 + 512)
                pr = [os_.load_tok[bi]] + att_done
                if ev_out is not None:
                    pr.append(ev_out)
                tmm = p.mm(p0[:, :], [
                    (wosl[bi][:, 128 * k:128 * k + 128], otile[k][:, gsl])
                    for k in range(4)], after=pr)
                os_.note_use(bi, tmm)
                tx = p.dma("sync", xres[:, :],
                           stream_in[128 * d:128 * d + 128, gsl],
                           after=[ev_out] if ev_out else [])
                ev_out = p.vec("tensor_add", osc[:, :], p0[:, :], xres[:, :],
                               after=[tmm, tx] + st_prev)
                st_prev = [p.dma("sync",
                                 stream_out[128 * d:128 * d + 128, gsl],
                                 osc[:, :], after=[ev_out], sem="st")]
            if d + 2 < NC_D:
                os_.load(slab_ap(wo, d + 2))
        p.barrier("sync")
    return


def make_ffn_attn_consts(inputs, sub):
    DH = 64
    WIN = 512
    n = 4096
    half = np.arange(0, DH, 2, dtype=np.float64)
    inv_freq = 1.0 / 10000.0 ** (half / DH)
    base = (half + 0.4 * DH) / (1.4 * DH)

    def tables(t0, ntok):
        t = np.arange(t0, t0 + ntok, dtype=np.float64)
        fr = t[:, None] * inv_freq[None, :]
        fr = np.concatenate([fr, fr], 1)
        power = (t - n // 2) / (WIN // 2)
        sc = base[None, :] ** power[:, None]
        sc = np.concatenate([sc, sc], 1)
        cos, sin = np.cos(fr), np.sin(fr)
        return cos, sin, sc

    def fm(a):
        b = np.ascontiguousarray(a.T.astype(np.float32))
        return np.concatenate([b, b], 0)

    qs = np.asarray(inputs["L0_ffn_q_scale"], np.float32)[sub]
    ks = np.asarray(inputs["L0_ffn_k_scale"], np.float32)[sub]
    qs8 = np.tile(qs * 8.0, 2)[:, None].astype(np.float32)
    ksc = np.tile(ks, 2)[:, None].astype(np.float32)
    P = np.zeros((128, 128), np.float32)
    for i in range(128):
        bb = (i // 64) * 64
        j = i % 64
        if j < 32:
            P[bb + j + 32, i] = -1.0
        else:
            P[bb + j - 32, i] = 1.0
    ql = np.arange(512) % 128
    r = np.arange(128)
    mask0 = (r[:, None] >= ql[None, :]).astype(np.float32)
    mask4 = (r[:, None] <= ql[None, :]).astype(np.float32)
    hd_a = np.zeros((128, 1), np.float32); hd_a[0:64] = 1
    hd_b = np.zeros((128, 1), np.float32); hd_b[64:128] = 1
    ind_a = np.zeros((1, 128), np.float32); ind_a[0, 0:64] = 1
    ind_b = np.zeros((1, 128), np.float32); ind_b[0, 64:128] = 1
    neg1 = np.full((1, T), -1.0, np.float32)
    shared = dict(fa_perm=P.astype(BF), fa_mask0=mask0.astype(BF),
                  fa_mask4=mask4.astype(BF), fa_hda=hd_a.astype(BF),
                  fa_hdb=hd_b.astype(BF), fa_inda=ind_a.astype(BF),
                  fa_indb=ind_b.astype(BF), fa_qs8=qs8, fa_ks=ksc,
                  fa_neg1=neg1.astype(BF))
    per_core = []
    for c in range(8):
        b, h = c // 2, c % 2
        S0 = h * T
        cos, sin, sc = tables(S0, T)
        qcs = fm(cos * sc); qss = fm(sin * sc)
        cos, sin, sc = tables(S0 - 512, 2560)
        kci = fm(cos / sc); ksi = fm(sin / sc)
        kpad = np.zeros((1, 2560), np.float32)
        if h == 0:
            kpad[0, 0:512] = 60000.0
        per_core.append(dict(fa_qcs=qcs, fa_qss=qss, fa_kci=kci, fa_ksi=ksi,
                             fa_kpad=kpad,
                             fa_odd=np.full((128, 1), float(h), np.float32)))
    return shared, per_core


def declare_ffn_attn_dram(nc, p, dbg=False, pfx="", in_h=None, out_h=None):
    dram = {}
    dram["fa_in"] = in_h if in_h is not None else nc.dram_tensor(
        pfx + "fa_in", [D, T], F32, kind="ExternalInput")
    dram["fa_out"] = out_h if out_h is not None else nc.dram_tensor(
        pfx + "fa_out", [D, T], F32, kind="ExternalOutput")
    dram["fa_wqkv"] = nc.dram_tensor(pfx + "fa_wqkv", [1024, 1536], F32,
                                     kind="ExternalInput")
    dram["fa_wo"] = nc.dram_tensor(pfx + "fa_wo", [512, 1024], F32,
                                   kind="ExternalInput")
    for nm, shp, dt in [("fa_qcs", [128, T], F32), ("fa_qss", [128, T], F32),
                        ("fa_kci", [128, 2560], F32),
                        ("fa_ksi", [128, 2560], F32),
                        ("fa_perm", [128, 128], BF16),
                        ("fa_mask0", [128, 512], BF16),
                        ("fa_mask4", [128, 512], BF16),
                        ("fa_hda", [128, 1], BF16), ("fa_hdb", [128, 1], BF16),
                        ("fa_inda", [1, 128], BF16),
                        ("fa_indb", [1, 128], BF16),
                        ("fa_qs8", [128, 1], F32), ("fa_ks", [128, 1], F32),
                        ("fa_odd", [128, 1], F32),
                        ("fa_kpad", [1, 2560], F32),
                        ("fa_neg1", [1, T], BF16)]:
        dram[nm] = nc.dram_tensor(pfx + nm, shp, dt, kind="ExternalInput")
    dram["fa_kpub"] = nc.dram_tensor(pfx + "fa_kpub", [512, 512], BF16)
    dram["fa_kgat"] = nc.dram_tensor(pfx + "fa_kgat", [1024, 512], BF16)
    dram["fa_vpub"] = nc.dram_tensor(pfx + "fa_vpub", [512, 520], BF16)
    dram["fa_vgat"] = nc.dram_tensor(pfx + "fa_vgat", [1024, 520], BF16)
    if dbg:
        dram["fa_dbgo"] = nc.dram_tensor("fa_dbgo", [512, T], BF16,
                                         kind="ExternalOutput")
        dram["fa_dbgk"] = nc.dram_tensor("fa_dbgk", [512, 1024], BF16,
                                         kind="ExternalOutput")
        dram["fa_dbgv"] = nc.dram_tensor("fa_dbgv", [128, 520], BF16,
                                         kind="ExternalOutput")
    dram["fa_cpub"] = nc.dram_tensor(pfx + "fa_cpub", [128, 8], F32)
    dram["fa_cgat"] = nc.dram_tensor(pfx + "fa_cgat", [256, 8], F32)
    return dram


def build_ffn_attn_prologue(nc, ctx, p, dram):
    with ExitStack() as c1:
        col = sb(c1, nc, "fa_pcol", [128, 8])
        toks = []
        for k in range(NC_D):
            toks.append(p.dma("sync", col[:, k:k + 1],
                              dram["fa_in"][128 * k:128 * k + 128,
                                            T - 1:T],
                              allow_slow_non_contiguous=True))
        tp = p.dma("gpsimd", dram["fa_cpub"].ap(), col[:, :], after=toks)
        tc = p.coll("AllGather", ALU.bypass, PAIRS, [dram["fa_cpub"].ap().opt()],
                    [dram["fa_cgat"].ap().opt()], after=[tp])
        dram["fa_cgat_tok"] = tc


def run_ffn_attn(inputs, sub, stream_in_full):
    nc = bass.Bass("TRN2", target_bir_lowering=False)
    with ExitStack() as ctx:
        p = P(nc, ctx)
        dram = declare_ffn_attn_dram(nc, p)
        build_ffn_attn_prologue(nc, ctx, p, dram)
        build_ffn_attn(nc, ctx, p, dram)
        p.finalize()
    shared, per_core = make_ffn_attn_consts(inputs, sub)
    wqkv = np.ascontiguousarray(np.asarray(inputs["L0_ffn_wqkv"], np.float32)[sub])
    wo = np.ascontiguousarray(np.asarray(inputs["L0_ffn_wo"], np.float32)[sub])
    maps = []
    for c in range(8):
        b, h = c // 2, c % 2
        fa_in = np.ascontiguousarray(
            stream_in_full[b, h * T:(h + 1) * T, :].T.astype(np.float32))
        m = dict(fa_in=fa_in, fa_wqkv=wqkv, fa_wo=wo)
        m.update(shared)
        m.update(per_core[c])
        maps.append(m)
    return run_bass_kernel_spmd(nc, maps, core_ids=list(range(8)))



NJ = 22

def build_ffn_ffn(nc, ctx, p, dram, final_ln=False):
    stream_in = dram["ff_in"]
    stream_out = dram["ff_out"]
    w1a, w1g, w2p = dram["ff_w1a"], dram["ff_w1g"], dram["ff_w2p"]

    with ExitStack() as c1:
        h1 = [sb(c1, nc, f"ff_h1{k}", [128, T], BF16) for k in range(NC_D)]
        sqt = sb(c1, nc, "ff_sq", [128, 512], BF16)
        t1t = sb(c1, nc, "ff_t1", [128, 512])
        glt = sb(c1, nc, "ff_gl", [128, 512], BF16)
        att = sb(c1, nc, "ff_at", [128, 512], BF16)
        xres = sb(c1, nc, "ff_xres", [128, 512])
        xres2 = sb(c1, nc, "ff_xres2", [128, 512])
        osc = sb(c1, nc, "ff_osc", [128, 512])
        halocol = sb(c1, nc, "ff_hcol", [128, 8])
        is_odd = sb(c1, nc, "ff_odd_sb", [128, 1])
        lg = sb(c1, nc, "ff_lg_sb", [128, NC_D])
        lb = sb(c1, nc, "ff_lb_sb", [128, NC_D])
        b1a = sb(c1, nc, "ff_b1a_sb", [128, NJ])
        b1g = sb(c1, nc, "ff_b1g_sb", [128, NJ])
        b2c = sb(c1, nc, "ff_b2_sb", [128, NC_D])
        ones_col = sb(c1, nc, "ff_oc", [128, 1], BF16)
        ones_row = sb(c1, nc, "ff_or", [1, 128], BF16)
        r_tiles = [sb(c1, nc, f"ff_rt{d}", [128, 512]) for d in range(NC_D)]
        ln_scr = make_ln_scratch(c1, nc, "ffln")
        wsl_a = [sb(c1, nc, f"ff_wa{i}", [128, 1024], BF16) for i in range(2)]
        wsl_g = [sb(c1, nc, f"ff_wg{i}", [128, 1024], BF16) for i in range(2)]
        wsl_2 = [sb(c1, nc, f"ff_w2{i}", [128, 2816], BF16) for i in range(2)]

        p0 = psum(c1, nc, "ff_p0", [128, 512])
        p1 = psum(c1, nc, "ff_p1", [128, 512])
        pA = psum(c1, nc, "ff_pA", [128, 512])
        pB = psum(c1, nc, "ff_pB", [128, 512])
        psA = psum(c1, nc, "ff_psA", [1, 512])
        psB = psum(c1, nc, "ff_psB", [1, 512])

        ct = []
        ct.append(p.dma("sync", is_odd[:, :], dram["ff_odd"].ap()))
        ct.append(p.dma("sync", lg[:, :], dram["ff_lg"].ap()))
        ct.append(p.dma("sync", lb[:, :], dram["ff_lb"].ap()))
        ct.append(p.dma("sync", b1a[:, :], dram["ff_b1a"].ap()))
        ct.append(p.dma("sync", b1g[:, :], dram["ff_b1g"].ap()))
        ct.append(p.dma("sync", b2c[:, :], dram["ff_b2"].ap()))
        ct.append(p.vec("memset", ones_col[:, :], 1.0))
        ct.append(p.vec("memset", ones_row[:, :], 1.0))

        with ExitStack() as c2:
            xb = [sb(c2, nc, f"ff_xb{k}", [128, T + 1], BF16)
                  for k in range(NC_D)]
            xb_toks = []
            for k in range(NC_D):
                xb_toks.append(p.dma("gpsimd", xb[k][:, 1:T + 1],
                                     stream_in[128 * k:128 * k + 128, :]))
            thc = p.dma("gpsimd", halocol[:, :], dram["ff_cgat"][0:128, :],
                        after=[dram["ff_cgat_tok"]])
            for k in range(NC_D):
                xb_toks.append(p.vec("tensor_scalar_mul", xb[k][:, 0:1],
                                     halocol[:, k:k + 1], is_odd[:, :],
                                     after=[thc, ct[0]]))
            p.barrier("vector")

            def xsh(k, c0, n):
                if k < 4:
                    return xb[k][:, 1 + c0:1 + c0 + n]
                return xb[k][:, c0:c0 + n]

            warm = p.mm(psA[:, :], [(ones_col[:, :], xsh(k, 0, 512))
                                    for k in range(NC_D)],
                        after=xb_toks + ct)
            prev = [warm]
            for g in range(4):
                gsl = slice(g * 512, g * 512 + 512)
                ta = tb = None
                for d in range(NC_D):
                    sq = p.act(sqt[:, :], xsh(d, g * 512, 512), AF.Square,
                               after=xb_toks + prev)
                    ta = p.mm(psA[:, :], [(ones_col[:, :],
                                           xsh(d, g * 512, 512))],
                              after=xb_toks + prev, start=(d == 0),
                              stop=(d == NC_D - 1))
                    tb = p.mm(psB[:, :], [(ones_col[:, :], sqt[:, :])],
                              after=[sq], start=(d == 0),
                              stop=(d == NC_D - 1))
                    prev = [ta, tb]
                r0, r1_, r2, r3 = (ln_scr["r0"], ln_scr["r1"], ln_scr["r2"],
                                   ln_scr["r3"])
                brow, a_bf, b_bf = (ln_scr["brow"], ln_scr["a_bf"],
                                    ln_scr["b_bf"])
                t1 = p.act(r0[:, :], psA[0:1, :], AF.Identity, scale=1.0 / D,
                           after=[ta, tb])
                t2 = p.act(r1_[:, :], psB[0:1, :], AF.Identity,
                           scale=1.0 / D, after=[ta, tb])
                t3 = p.act(r2[:, :], r0[:, :], AF.Square, after=[t1])
                t4 = p.vec("tensor_sub", r1_[:, :], r1_[:, :], r2[:, :],
                           after=[t2, t3])
                t5 = p.act(r2[:, :], r1_[:, :], AF.Ln, bias=1e-5, after=[t4])
                t6 = p.act(r3[:, :], r2[:, :], AF.Exp, scale=-0.5,
                           after=[t5])
                t7 = p.vec("tensor_mul", brow[:, :], r0[:, :], r3[:, :],
                           after=[t6])
                t8 = p.vec("tensor_scalar_mul", brow[:, :], brow[:, :],
                           -1.0, after=[t7])
                t9 = p.vec("tensor_copy", a_bf[:, :], r3[:, :], after=[t6])
                t10 = p.vec("tensor_copy", b_bf[:, :], brow[:, :],
                            after=[t8])
                bA = p.mm(pA[:, :], [(ones_row[:, :], a_bf[:, :])],
                          after=[t9] + prev)
                bB = p.mm(pB[:, :], [(ones_row[:, :], b_bf[:, :])],
                          after=[t10])
                ht = None
                for d in range(NC_D):
                    v1 = p.vec("tensor_mul", t1t[:, :],
                               xsh(d, g * 512, 512), pA[:, :],
                               after=[bA, bB] + ([ht] if ht else []))
                    v2 = p.vec("tensor_add", t1t[:, :], t1t[:, :], pB[:, :],
                               after=[v1])
                    ht = p.act(h1[d][:, gsl], t1t[:, :], AF.Identity,
                               scale=lg[:, d:d + 1], bias=lb[:, d:d + 1],
                               after=[v2] + ct)
                prev = [ta, tb, ht, bA, bB]
            h1_done = list(prev)
            if "ff_dbgh" in dram:
                for k in range(NC_D):
                    p.dma("sync", dram["ff_dbgh"][128 * k:128 * k + 128, :],
                          h1[k][:, :], after=h1_done, sem="st")
            p.barrier()

        ag = [sb(c1, nc, f"ff_ag{j}", [128, T], BF16) for j in range(NJ)]
        sa = Streamer(p, wsl_a)
        sg_ = Streamer(p, wsl_g)
        sa.load(slab_ap(w1a, 0))
        sg_.load(slab_ap(w1g, 0))
        warm2 = p.mm(p0[:, :], [
            (wsl_a[0][:, 128 * k:128 * k + 128], h1[k][:, 0:512])
            for k in range(NC_D)], after=[sa.load_tok[0]] + h1_done)
        sa.note_use(0, warm2)
        sa.load(slab_ap(w1a, 1))
        sg_.load(slab_ap(w1g, 1))
        ev_a = ev_g = mul_prev = None
        ag_done = []
        for j in range(NJ):
            bi = j % 2
            for g in range(4):
                gsl = slice(g * 512, g * 512 + 512)
                pr = [sa.load_tok[bi]] + h1_done
                if ev_a is not None:
                    pr.append(ev_a)
                tma = p.mm(p0[:, :], [
                    (wsl_a[bi][:, 128 * k:128 * k + 128], h1[k][:, gsl])
                    for k in range(NC_D)], after=pr)
                sa.note_use(bi, tma)
                pr = [sg_.load_tok[bi]] + h1_done
                if ev_g is not None:
                    pr.append(ev_g)
                tmg = p.mm(p1[:, :], [
                    (wsl_g[bi][:, 128 * k:128 * k + 128], h1[k][:, gsl])
                    for k in range(NC_D)], after=pr)
                sg_.note_use(bi, tmg)
                ga = [tma] + ([mul_prev] if mul_prev else [])
                ev_a = p.act(att[:, :], p0[:, :], AF.Identity,
                             bias=b1a[:, j:j + 1], after=ga + ct)
                gg_ = [tmg] + ([mul_prev] if mul_prev else [])
                ev_g = p.act(glt[:, :], p1[:, :], AF.Gelu,
                             bias=b1g[:, j:j + 1], after=gg_ + ct)
                mul_prev = p.vec("tensor_mul", ag[j][:, gsl], att[:, :],
                                 glt[:, :], after=[ev_a, ev_g])
                ag_done.append(mul_prev)
            if j + 2 < NJ:
                sa.load(slab_ap(w1a, j + 2))
                sg_.load(slab_ap(w1g, j + 2))

        if "ff_dbga" in dram:
            for j in range(4):
                p.dma("sync", dram["ff_dbga"][128 * j:128 * j + 128, :],
                      ag[j][:, :], after=ag_done, sem="st")
        s2 = Streamer(p, wsl_2)
        s2.load(slab_ap(w2p, 0), after=ag_done[-2:])
        warm3 = p.mm(p1[:, :], [
            (wsl_2[0][:, 128 * k:128 * k + 128], ag[k][:, 0:512])
            for k in range(NJ)], after=[s2.load_tok[0]] + ag_done)
        s2.note_use(0, warm3)
        s2.load(slab_ap(w2p, 1))
        ev_out = None
        st_prev = []
        for d in range(NC_D):
            bi = d % 2
            for g in range(4):
                gsl = slice(g * 512, g * 512 + 512)
                pr = [s2.load_tok[bi]] + ag_done
                if ev_out is not None:
                    pr.append(ev_out)
                tmm = p.mm(p1[:, :], [
                    (wsl_2[bi][:, 128 * k:128 * k + 128], ag[k][:, gsl])
                    for k in range(NJ)], after=pr)
                s2.note_use(bi, tmm)
                tx = p.dma("sync", xres[:, :],
                           stream_in[128 * d:128 * d + 128, gsl],
                           after=[ev_out] if ev_out else [])
                eb = p.act(osc[:, :], p1[:, :], AF.Identity,
                           bias=b2c[:, d:d + 1],
                           after=[tmm] + ct + st_prev)
                ev_out = p.vec("tensor_add", osc[:, :], osc[:, :],
                               xres[:, :], after=[eb, tx] + st_prev)
                st_prev = [p.dma("sync",
                                 stream_out[128 * d:128 * d + 128, gsl],
                                 osc[:, :], after=[ev_out], sem="st")]
            if d + 2 < NC_D:
                s2.load(slab_ap(w2p, d + 2))
        p.barrier()

        if final_ln:
            out2 = dram["ff_out2"]
            ln_guard = []
            for g in range(4):
                gsl = slice(g * 512, g * 512 + 512)
                r_done = []
                rx_prev = []
                for d in range(NC_D):
                    t1_ = p.dma("sync", xres[:, :],
                                stream_out[128 * d:128 * d + 128, gsl],
                                after=rx_prev + st_prev)
                    t2_ = p.dma("sync", xres2[:, :],
                                dram["ff_res2"][128 * d:128 * d + 128, gsl],
                                after=rx_prev)
                    rv = p.vec("tensor_add", r_tiles[d][:, :], xres[:, :],
                               xres2[:, :], after=[t1_, t2_] + ln_guard)
                    rx_prev = [rv]
                    r_done.append(rv)

                def writer(dd, _s, src_ap, toks, g=g):
                    return p.dma("sync",
                                 out2[128 * dd:128 * dd + 128, gsl],
                                 src_ap, after=toks, sem="st")
                stoks, ln_guard = layernorm_fm(
                    p, ln_scr, r_tiles, 512, ones_col, ones_row,
                    psA, psB, pA, pB, r_done, writer, guard=ln_guard)
                ln_guard = ln_guard + stoks
        p.barrier("sync")


def declare_ffn_ffn_dram(nc, final_ln=False, dbg=False, pfx="", in_h=None,
                         out_h=None, res2_h=None, out2_h=None):
    dram = {}
    dram["ff_in"] = in_h if in_h is not None else nc.dram_tensor(
        pfx + "ff_in", [D, T], F32, kind="ExternalInput")
    dram["ff_out"] = out_h if out_h is not None else nc.dram_tensor(
        pfx + "ff_out", [D, T], F32, kind="ExternalOutput")
    for nm, shp in [("ff_w1a", [1024, 2816]), ("ff_w1g", [1024, 2816]),
                    ("ff_w2p", [2816, 1024]), ("ff_lg", [128, NC_D]),
                    ("ff_lb", [128, NC_D]), ("ff_b1a", [128, NJ]),
                    ("ff_b1g", [128, NJ]), ("ff_b2", [128, NC_D]),
                    ("ff_odd", [128, 1])]:
        dram[nm] = nc.dram_tensor(pfx + nm, shp, F32, kind="ExternalInput")
    if final_ln:
        dram["ff_res2"] = res2_h if res2_h is not None else nc.dram_tensor(
            pfx + "ff_res2", [D, T], F32, kind="ExternalInput")
        dram["ff_out2"] = out2_h if out2_h is not None else nc.dram_tensor(
            pfx + "ff_out2", [D, T], F32, kind="ExternalOutput")
    if dbg:
        dram["ff_dbgh"] = nc.dram_tensor("ff_dbgh", [D, T], BF16,
                                         kind="ExternalOutput")
        dram["ff_dbga"] = nc.dram_tensor("ff_dbga", [512, T], BF16,
                                         kind="ExternalOutput")
    dram["ff_cpub"] = nc.dram_tensor(pfx + "ff_cpub", [128, 8], F32)
    dram["ff_cgat"] = nc.dram_tensor(pfx + "ff_cgat", [256, 8], F32)
    return dram


def build_col_prologue(nc, ctx, p, dram, in_name, pub, gat, tokname):
    with ExitStack() as c1:
        col = sb(c1, nc, tokname + "_col", [128, 8])
        toks = []
        for k in range(NC_D):
            toks.append(p.dma("sync", col[:, k:k + 1],
                              dram[in_name][128 * k:128 * k + 128, T - 1:T],
                              allow_slow_non_contiguous=True))
        tp = p.dma("gpsimd", dram[pub].ap(), col[:, :], after=toks)
        tc = p.coll("AllGather", ALU.bypass, PAIRS, [dram[pub].ap().opt()],
                    [dram[gat].ap().opt()], after=[tp])
        dram[tokname] = tc


def pack_ffn_ffn_inputs(inputs, sub, stream_in_full, res2_full=None):
    w1 = np.asarray(inputs["L0_ffn_w1"], np.float32)[sub]
    b1 = np.asarray(inputs["L0_ffn_b1"], np.float32)[sub]
    w2 = np.asarray(inputs["L0_ffn_w2"], np.float32)[sub]
    b2 = np.asarray(inputs["L0_ffn_b2"], np.float32)[sub]
    lgv = np.asarray(inputs["L0_ffn_ln_g"], np.float32)[sub]
    lbv = np.asarray(inputs["L0_ffn_ln_b"], np.float32)[sub]
    HH = 2730
    PAD = NJ * 128
    w1a = np.zeros((1024, PAD), np.float32); w1a[:, :HH] = w1[:, :HH]
    w1g = np.zeros((1024, PAD), np.float32); w1g[:, :HH] = w1[:, HH:]
    w2p = np.zeros((PAD, 1024), np.float32); w2p[:HH] = w2
    b1a = np.zeros(PAD, np.float32); b1a[:HH] = b1[:HH]
    b1g = np.zeros(PAD, np.float32); b1g[:HH] = b1[HH:]
    shared = dict(ff_w1a=w1a, ff_w1g=w1g, ff_w2p=w2p,
                  ff_b1a=cols128(b1a), ff_b1g=cols128(b1g),
                  ff_b2=cols128(b2), ff_lg=cols128(lgv), ff_lb=cols128(lbv))
    maps = []
    for c in range(8):
        b, h = c // 2, c % 2
        m = dict(shared)
        m["ff_in"] = np.ascontiguousarray(
            stream_in_full[b, h * T:(h + 1) * T, :].T.astype(np.float32))
        if res2_full is not None:
            m["ff_res2"] = np.ascontiguousarray(
                res2_full[b, h * T:(h + 1) * T, :].T.astype(np.float32))
        m["ff_odd"] = np.full((128, 1), float(h), np.float32)
        maps.append(m)
    return maps


def run_ffn_ffn(inputs, sub, stream_in_full, res2_full=None):
    final_ln = res2_full is not None
    nc = bass.Bass("TRN2", target_bir_lowering=False)
    with ExitStack() as ctx:
        p = P(nc, ctx)
        dram = declare_ffn_ffn_dram(nc, final_ln)
        build_col_prologue(nc, ctx, p, dram, "ff_in", "ff_cpub", "ff_cgat",
                           "ff_cgat_tok")
        build_ffn_ffn(nc, ctx, p, dram, final_ln)
        p.finalize()
    maps = pack_ffn_ffn_inputs(inputs, sub, stream_in_full, res2_full)
    return run_bass_kernel_spmd(nc, maps, core_ids=list(range(8)))



def build_l1_attn(nc, ctx, p, dram):
    stream_in = dram["la_in"]
    stream_out = dram["la_out"]
    winT = dram["la_winT"]
    woutT = dram["la_woutT"]
    winGT = dram["la_winGT"]
    woutGT = dram["la_woutGT"]
    S8 = 1.0 / np.sqrt(128.0)
    SG = 1.0 / np.sqrt(256.0)

    with ExitStack() as c1:
        otile = [sb(c1, nc, f"la_o{d}", [128, T], BF16) for d in range(NC_D)]
        meansf = [sb(c1, nc, f"la_mn{d}", [128, 32]) for d in range(NC_D)]
        mall = [sb(c1, nc, f"la_ma{d}", [128, 64], BF16) for d in range(NC_D)]
        bql = sb(c1, nc, "la_bql_s", [128, NC_D])
        bkl = sb(c1, nc, "la_bkl_s", [128, NC_D])
        bvl_r = sb(c1, nc, "la_bvl_s", [1, 1024], BF16)
        bol = sb(c1, nc, "la_bol_s", [128, NC_D])
        bqg = sb(c1, nc, "la_bqg_s", [128, NC_D])
        bkg = sb(c1, nc, "la_bkg_s", [128, NC_D])
        bvg_r = sb(c1, nc, "la_bvg_s", [1, 1024], BF16)
        bog = sb(c1, nc, "la_bog_s", [128, NC_D])
        is_odd = sb(c1, nc, "la_odd_sb", [128, 1])
        is_even = sb(c1, nc, "la_evn_sb", [128, 1])
        ones_col = sb(c1, nc, "la_oc", [128, 1], BF16)
        ones_row = sb(c1, nc, "la_or", [1, 128], BF16)
        ones64c = sb(c1, nc, "la_o64c", [64, 1], BF16)
        ones64r = sb(c1, nc, "la_o64r", [1, 64], BF16)
        mcols = sb(c1, nc, "la_mcols", [128, 8])
        mcolsb = sb(c1, nc, "la_mcolsb", [128, 8], BF16)
        qabs = sb(c1, nc, "la_qabs", [128, 512], BF16)
        negB = sb(c1, nc, "la_negB", [1, 512], BF16)
        probs = sb(c1, nc, "la_probs", [128, 512], BF16)
        invb = sb(c1, nc, "la_invb", [1, 512], BF16)
        osc = sb(c1, nc, "la_osc", [128, 512])
        xres = sb(c1, nc, "la_xres", [128, 512])
        gsel = sb(c1, nc, "la_gsel", [128, 32])
        gselb = sb(c1, nc, "la_gselb", [128, 32], BF16)
        ident = sb(c1, nc, "la_id", [128, 128], BF16)
        gT = [sb(c1, nc, f"la_gT{d}", [32, 128], BF16) for d in range(NC_D)]

        p0 = psum(c1, nc, "la_p0", [128, 512])
        p1 = psum(c1, nc, "la_p1", [128, 512])
        p2 = psum(c1, nc, "la_p2", [128, 512])
        pav = psum(c1, nc, "la_pav", [128, 512])
        pbc = psum(c1, nc, "la_pbc", [128, 512])
        prow = psum(c1, nc, "la_prow", [1, 512])
        psA_ = psum(c1, nc, "la_psA", [1, 512])
        ptb = psum(c1, nc, "la_ptb", [128, 128], BF16)

        ct = []
        ct.append(p.dma("sync", is_odd[:, :], dram["la_odd"].ap()))
        ct.append(p.dma("sync", ident[:, :], dram["la_ident"].ap()))
        for nm, t in [("la_bql", bql), ("la_bkl", bkl), ("la_bol", bol),
                      ("la_bqg", bqg), ("la_bkg", bkg), ("la_bog", bog)]:
            ct.append(p.dma("sync", t[:, :], dram[nm].ap()))
        ct.append(p.dma("gpsimd", bvl_r[:, :], dram["la_bvl"].ap()))
        ct.append(p.dma("gpsimd", bvg_r[:, :], dram["la_bvg"].ap()))
        ct.append(p.vec("memset", ones_col[:, :], 1.0))
        ct.append(p.vec("memset", ones_row[:, :], 1.0))
        ct.append(p.vec("memset", ones64c[:, :], 1.0))
        ct.append(p.vec("memset", ones64r[:, :], 1.0))
        ct.append(p.vec("tensor_scalar", is_even[:, :], is_odd[:, :], -1.0,
                        1.0, ALU.mult, ALU.add, after=[ct[0]]))
        ct.append(p.vec("tensor_scalar_mul", bql[:, :], bql[:, :], S8,
                        after=ct[3:4]))
        ct.append(p.vec("tensor_scalar_mul", bqg[:, :], bqg[:, :], SG,
                        after=ct[6:7]))

        with ExitStack() as c2:
            qh = [sb(c2, nc, f"la_q{h}", [128, T], BF16) for h in range(8)]
            kh = [sb(c2, nc, f"la_k{h}", [128, T], BF16) for h in range(8)]
            vt = [sb(c2, nc, f"la_v{i}", [128, 1024], BF16)
                  for i in range(16)]
            wsl = [sb(c2, nc, f"la_wsl{i}", [128, 1024], BF16)
                   for i in range(2)]
            wvsl = sb(c2, nc, "la_wvsl", [128, 4096], BF16)
            c3 = ExitStack()
            xb = [sb(c3, nc, f"la_xb{k}", [128, T], BF16)
                  for k in range(NC_D)]

            xb_toks = [p.dma("gpsimd", xb[k][:, :],
                             stream_in[128 * k:128 * k + 128, :])
                      for k in range(NC_D)]
            p.barrier("vector")

            ws = Streamer(p, wsl)
            ws.load(slab_ap(winT, 0))
            warm = p.mm(p0[:, :], [
                (wsl[0][:, 128 * k:128 * k + 128], xb[k][:, 0:512])
                for k in range(NC_D)], after=[ws.load_tok[0]] + xb_toks)
            ws.note_use(0, warm)
            ws.load(slab_ap(winT, 1))
            qk_done = []
            ev = None
            for F in range(16):
                bi = F % 2
                is_q = F < 8
                h = F % 8
                for g in range(4):
                    gsl = slice(g * 512, g * 512 + 512)
                    pr = [ws.load_tok[bi]] + xb_toks
                    if ev is not None:
                        pr.append(ev)
                    tmm = p.mm(p0[:, :], [
                        (wsl[bi][:, 128 * k:128 * k + 128], xb[k][:, gsl])
                        for k in range(NC_D)], after=pr)
                    ws.note_use(bi, tmm)
                    dst = (qh if is_q else kh)[h][:, gsl]
                    bias = (bql if is_q else bkl)[:, h:h + 1]
                    ev = p.act(dst, p0[:, :], AF.Identity,
                               scale=(S8 if is_q else 1.0), bias=bias,
                               after=[tmm] + ct)
                    qk_done.append(ev)
                if F + 2 < 16:
                    ws.load(slab_ap(winT, F + 2))
            ev_v = None
            v_done = []
            for half in range(2):
                wvt = p.dma("gpsimd",
                            wvsl[:, :].rearrange("q (a j) -> q a j", j=512),
                            winT[:, 2048 + 512 * half:2560 + 512 * half]
                            .rearrange("(a q) j -> q a j", q=128),
                            after=v_done[-1:])
                for tc in range(16):
                    pr = [wvt] + xb_toks + ct
                    if ev_v is not None:
                        pr.append(ev_v)
                    tmm = p.mm(p2[:, :], [
                        (ones_row[:, :],
                         bvl_r[0:1, 512 * half:512 * half + 512])] + [
                        (xb[k][:, 128 * tc:128 * tc + 128],
                         wvsl[:, 512 * k:512 * k + 512])
                        for k in range(NC_D)], after=pr)
                    ev_v = p.act(vt[tc][:, 512 * half:512 * half + 512],
                                 p2[:, :], AF.Identity, after=[tmm])
                    v_done.append(ev_v)
            mn_done = []
            for d in range(NC_D):
                t1 = p.vec("reduce_sum", meansf[d][:, :],
                           xb[d][:, :].rearrange("q (s t) -> q s t", t=64),
                           AX.X, after=xb_toks)
                t2 = p.vec("tensor_scalar_mul", meansf[d][:, :],
                           meansf[d][:, :], 1.0 / 64.0, after=[t1])
                mn_done.append(p.dma("gpsimd", dram["la_mpub"][:, 32 * d:32 * d + 32],
                                     meansf[d][:, :], after=[t2]))
            cm = p.coll("AllGather", ALU.bypass, PAIRS,
                        [dram["la_mpub"].ap().opt()],
                        [dram["la_mgat"].ap().opt()], after=mn_done)
            ma_toks = []
            for d in range(NC_D):
                ma_toks.append(p.dma("gpsimd", mall[d][:, 0:32],
                                     dram["la_mgat"][0:128, 32 * d:32 * d + 32],
                                     after=[cm]))
                ma_toks.append(p.dma("gpsimd", mall[d][:, 32:64],
                                     dram["la_mgat"][128:256, 32 * d:32 * d + 32],
                                     after=[cm]))

            p.barrier()
            c3.close()
            att_done = []
            for h in range(8):
                for sgg in range(4):
                    gsl = slice(512 * sgg, 512 * sgg + 512)
                    sq = p.act(qabs[:, :], qh[h][:, gsl], AF.Abs,
                               after=qk_done + att_done[-1:])
                    rm_toks = []
                    for s8 in range(8):
                        t1 = p.vec("reduce_max", mcols[:, s8:s8 + 1],
                                   kh[h][:, 512 * sgg + 64 * s8:
                                         512 * sgg + 64 * s8 + 64],
                                   AX.X, apply_absolute_value=True,
                                   after=qk_done)
                        rm_toks.append(t1)
                    cv = p.vec("tensor_copy", mcolsb[:, :], mcols[:, :],
                               after=rm_toks)
                    nb_toks = []
                    for s8 in range(8):
                        t2 = p.mm(prow[0:1, 64 * s8:64 * s8 + 64],
                                  [(mcolsb[:, s8:s8 + 1],
                                    qabs[:, 64 * s8:64 * s8 + 64])],
                                  after=[cv, sq] + att_done[-1:])
                        nb_toks.append(t2)
                    nbv = p.vec("tensor_scalar_mul", negB[0:1, :],
                                prow[0:1, 0:512], -1.0, after=nb_toks)
                    sc_toks = []
                    for s8 in range(8):
                        ssl = slice(512 * sgg + 64 * s8,
                                    512 * sgg + 64 * s8 + 64)
                        m1 = p.mm(p0[0:64, 64 * s8:64 * s8 + 64],
                                  [(kh[h][:, ssl], qh[h][:, ssl])],
                                  after=[nbv] + att_done[-1:], stop=False)
                        m2 = p.mm(p0[0:64, 64 * s8:64 * s8 + 64],
                                  [(ones64r[:, :],
                                    negB[0:1, 64 * s8:64 * s8 + 64])],
                                  start=False)
                        sc_toks.append(m2)
                    mz = p.vec("memset", probs[:, :], 0.0,
                               after=att_done[-1:])
                    ev_toks = []
                    for s8 in range(8):
                        seg = 8 * sgg + s8
                        rows = slice(64 * (seg % 2), 64 * (seg % 2) + 64)
                        ev1 = p.act(probs[rows, 64 * s8:64 * s8 + 64],
                                    p0[0:64, 64 * s8:64 * s8 + 64], AF.Exp,
                                    after=sc_toks + [mz])
                        ev_toks.append(ev1)
                    evp = ev_toks[-1]
                    dn_toks = []
                    for s8 in range(8):
                        d1 = p.mm(prow[0:1, 64 * s8:64 * s8 + 64],
                                  [(ones_col[:, :],
                                    probs[:, 64 * s8:64 * s8 + 64])],
                                  after=ev_toks + [mz])
                        dn_toks.append(d1)
                    rc = p.vec("reciprocal", osc[0:1, 0:512],
                               prow[0:1, 0:512], after=dn_toks)
                    rb = p.vec("tensor_copy", invb[0:1, :], osc[0:1, 0:512],
                               after=[rc])
                    bc = p.mm(pbc[:, :], [(ones_row[:, :], invb[0:1, :])],
                              after=[rb])
                    av_toks = []
                    for s8 in range(8):
                        seg = 8 * sgg + s8
                        vtile = vt[seg // 2]
                        a1 = p.mm(pav[:, 64 * s8:64 * s8 + 64],
                                  [(vtile[:, 128 * h:128 * h + 128],
                                    probs[:, 64 * s8:64 * s8 + 64])],
                                  after=ev_toks)
                        av_toks.append(a1)
                    eo = p.act(osc[:, :], pav[:, :], AF.Identity,
                               after=av_toks + [bc])
                    fm = p.vec("tensor_mul", otile[h][:, gsl], osc[:, :],
                               pbc[:, :], after=[eo])
                    att_done.append(fm)

            p.barrier()
            os_ = Streamer(p, wsl)
            os_.load(slab_ap(woutT, 0))
            warm2 = p.mm(p0[:, :], [
                (wsl[0][:, 128 * k:128 * k + 128], otile[k][:, 0:512])
                for k in range(NC_D)], after=[os_.load_tok[0]] + att_done)
            os_.note_use(0, warm2)
            os_.load(slab_ap(woutT, 1))
            ev_out = None
            loc_st = []
            for d in range(NC_D):
                bi = d % 2
                for g in range(4):
                    gsl = slice(g * 512, g * 512 + 512)
                    pr = [os_.load_tok[bi]] + att_done
                    if ev_out is not None:
                        pr.append(ev_out)
                    tmm = p.mm(p0[:, :], [
                        (wsl[bi][:, 128 * k:128 * k + 128],
                         otile[k][:, gsl]) for k in range(NC_D)], after=pr)
                    os_.note_use(bi, tmm)
                    tx = p.dma("sync", xres[:, :],
                               stream_in[128 * d:128 * d + 128, gsl],
                               after=[ev_out] if ev_out else [])
                    eb = p.act(osc[:, :], p0[:, :], AF.Identity,
                               bias=bol[:, d:d + 1],
                               after=[tmm] + ct + loc_st)
                    ev_out = p.vec("tensor_add", osc[:, :],
                                   osc[:, :], xres[:, :],
                                   after=[eb, tx] + loc_st)
                    loc_st = [p.dma("sync",
                                    dram["la_loc"][128 * d:128 * d + 128, gsl],
                                    osc[:, :], after=[ev_out], sem="st")]
                    if "la_dbgl" in dram:
                        loc_st.append(p.dma(
                            "sync",
                            dram["la_dbgl"][128 * d:128 * d + 128, gsl],
                            osc[:, :], after=[ev_out], sem="st"))
                if d + 2 < NC_D:
                    os_.load(slab_ap(woutT, d + 2))
            local_done = [ev_out] + loc_st

            p.barrier()
            mb_toks = []
            qg = [sb(c2, nc, f"la_qg{d}", [128, 64], BF16)
                  for d in range(NC_D)]
            kg = [sb(c2, nc, f"la_kg{d}", [128, 64], BF16)
                  for d in range(NC_D)]
            vg = sb(c2, nc, "la_vg", [64, 1024], BF16)
            og = [sb(c2, nc, f"la_og{d}", [128, 64], BF16)
                  for d in range(NC_D)]
            globf = [sb(c2, nc, f"la_gf{d}", [128, 64]) for d in range(NC_D)]
            for d in range(NC_D):
                mb_toks.append(p.vec("tensor_copy", mall[d][:, :],
                                     mall[d][:, :], after=ma_toks))
            gw = Streamer(p, wsl)
            gw.load(slab_ap(winGT, 0), after=att_done)
            warm3 = p.mm(p0[:, 0:64], [
                (wsl[0][:, 128 * k:128 * k + 128], mall[k][:, :])
                for k in range(NC_D)], after=[gw.load_tok[0]] + mb_toks)
            gw.note_use(0, warm3)
            gw.load(slab_ap(winGT, 1))
            ev = None
            gqk_done = []
            for F in range(16):
                bi = F % 2
                is_q = F < 8
                h = F % 8
                pr = [gw.load_tok[bi]] + mb_toks
                if ev is not None:
                    pr.append(ev)
                tmm = p.mm(p0[:, 0:64], [
                    (wsl[bi][:, 128 * k:128 * k + 128], mall[k][:, :])
                    for k in range(NC_D)], after=pr)
                gw.note_use(bi, tmm)
                dst = (qg if is_q else kg)[h][:, :]
                bias = (bqg if is_q else bkg)[:, h:h + 1]
                ev = p.act(dst, p0[:, 0:64], AF.Identity,
                           scale=(SG if is_q else 1.0), bias=bias,
                           after=[tmm] + ct)
                gqk_done.append(ev)
                if F + 2 < 16:
                    gw.load(slab_ap(winGT, F + 2))
            ev_v = None
            for half in range(2):
                wvg = p.dma("gpsimd",
                            wvsl[:, :].rearrange("q (a j) -> q a j", j=512),
                            winGT[:, 2048 + 512 * half:2560 + 512 * half]
                            .rearrange("(a q) j -> q a j", q=128),
                            after=(att_done + v_done) if half == 0
                            else [ev_v])
                pr = [wvg] + mb_toks + ct
                if ev_v is not None:
                    pr.append(ev_v)
                tmm = p.mm(p2[0:64, :], [
                    (ones_row[:, 0:64],
                     bvg_r[0:1, 512 * half:512 * half + 512])] + [
                    (mall[k][:, :], wvsl[:, 512 * k:512 * k + 512])
                    for k in range(NC_D)], after=pr)
                ev_v = p.act(vg[:, 512 * half:512 * half + 512], p2[0:64, :],
                             AF.Identity, after=[tmm])
            gv_done = [ev_v]
            g_att = []
            for h in range(4):
                sq = p.act(qabs[:, 0:64], qg[2 * h][:, :], AF.Abs,
                           after=gqk_done + g_att[-1:])
                sq2 = p.act(qabs[:, 64:128], qg[2 * h + 1][:, :], AF.Abs,
                            after=gqk_done)
                r1t = p.vec("reduce_max", mcols[:, 0:1], kg[2 * h][:, :],
                            AX.X, apply_absolute_value=True, after=gqk_done)
                r2t = p.vec("reduce_max", mcols[:, 1:2], kg[2 * h + 1][:, :],
                            AX.X, apply_absolute_value=True, after=gqk_done)
                cv = p.vec("tensor_copy", mcolsb[:, 0:2], mcols[:, 0:2],
                           after=[r1t, r2t])
                b1t = p.mm(prow[0:1, 0:64],
                           [(mcolsb[:, 0:1], qabs[:, 0:64]),
                            (mcolsb[:, 1:2], qabs[:, 64:128])],
                           after=[cv, sq, sq2] + g_att[-1:])
                nbv = p.vec("tensor_scalar_mul", negB[0:1, 0:64],
                            prow[0:1, 0:64], -1.0, after=[b1t])
                m1 = p.mm(p0[0:64, 0:64],
                          [(kg[2 * h][:, :], qg[2 * h][:, :]),
                           (kg[2 * h + 1][:, :], qg[2 * h + 1][:, :])],
                          after=[nbv] + g_att[-1:], stop=False)
                m2 = p.mm(p0[0:64, 0:64],
                          [(ones64r[:, :], negB[0:1, 0:64])], start=False)
                evp = p.act(probs[0:64, 0:64], p0[0:64, 0:64], AF.Exp,
                            after=[m2])
                d1 = p.mm(prow[0:1, 0:64],
                          [(ones64c[:, :], probs[0:64, 0:64])], after=[evp])
                rc = p.vec("reciprocal", osc[0:1, 0:64], prow[0:1, 0:64],
                           after=[d1])
                rb = p.vec("tensor_copy", invb[0:1, 0:64], osc[0:1, 0:64],
                           after=[rc])
                bc = p.mm(pbc[:, 0:64], [(ones_row[:, :], invb[0:1, 0:64])],
                          after=[rb] + g_att[-1:])
                prev_eo = []
                for half in range(2):
                    dd = 2 * h + half
                    a1 = p.mm(pav[:, 0:64],
                              [(vg[:, 128 * dd:128 * dd + 128],
                                probs[0:64, 0:64])],
                              after=[evp] + gv_done + prev_eo)
                    eo = p.act(osc[:, 0:64], pav[:, 0:64], AF.Identity,
                               after=[a1, bc])
                    prev_eo = [eo]
                    fm = p.vec("tensor_mul", og[dd][:, :], osc[:, 0:64],
                               pbc[:, 0:64], after=[eo])
                    g_att.append(fm)
            gow = Streamer(p, wsl)
            gow.load(slab_ap(woutGT, 0))
            gow.load(slab_ap(woutGT, 1))
            gw_done = []
            ev = None
            for d in range(NC_D):
                bi = d % 2
                pr = [gow.load_tok[bi]] + g_att
                if ev is not None:
                    pr.append(ev)
                tmm = p.mm(p0[:, 0:64], [
                    (wsl[bi][:, 128 * k:128 * k + 128], og[k][:, :])
                    for k in range(NC_D)], after=pr)
                gow.note_use(bi, tmm)
                ev = p.act(globf[d][:, :], p0[:, 0:64], AF.Identity,
                           bias=bog[:, d:d + 1], after=[tmm] + ct)
                gw_done.append(ev)
                if d + 2 < NC_D:
                    gow.load(slab_ap(woutGT, d + 2))

            if "la_dbgg" in dram:
                for d in range(NC_D):
                    p.dma("sync", dram["la_dbgg"][128 * d:128 * d + 128, :],
                          globf[d][:, :], after=gw_done, sem="st")
                    p.dma("gpsimd",
                          dram["la_dbgm"][128 * d:128 * d + 128, :],
                          mall[d][:, :], after=ma_toks + mb_toks, sem="st")

            if "la_dbgq" in dram:
                for d in range(NC_D):
                    p.dma("sync", dram["la_dbgq"][128 * d:128 * d + 128, :],
                          qg[d][:, :], after=gqk_done, sem="st")
                    p.dma("sync", dram["la_dbgk"][128 * d:128 * d + 128, :],
                          kg[d][:, :], after=gqk_done, sem="st")
                    p.dma("sync", dram["la_dbgog"][128 * d:128 * d + 128, :],
                          og[d][:, :], after=g_att, sem="st")
                p.dma("sync", dram["la_dbgv"].ap(), vg[:, :], after=gv_done,
                      sem="st")
            gT_done = []
            for d in range(NC_D):
                t1 = p.vec("tensor_scalar_mul", gsel[:, :],
                           globf[d][:, 0:32], is_even[:, :],
                           after=gw_done + ct)
                t2 = p.vec("scalar_tensor_tensor", gsel[:, :],
                           globf[d][:, 32:64], is_odd[:, :], gsel[:, :],
                           ALU.mult, ALU.add, after=[t1])
                t3 = p.vec("tensor_copy", gselb[:, :], gsel[:, :],
                           after=[t2])
                tt = p.trans(ptb[0:32, 0:128], gselb[:, 0:32], ident[:, :],
                             after=[t3] + gT_done)
                te = p.vec("tensor_copy", gT[d][:, :], ptb[0:32, 0:128],
                           after=[tt])
                gT_done.append(te)

        p.barrier()
        R32 = sb(c1, nc, "la_R32_s", [32, T], BF16)
        ln_scr = make_ln_scratch(c1, nc, "laln")
        r_tiles = [sb(c1, nc, f"la_rt{d}", [128, 512]) for d in range(NC_D)]
        tr32 = p.dma("sync", R32[:, :], dram["la_R32"].ap())
        lng = [tr32]
        for g in range(4):
            gsl = slice(512 * g, 512 * g + 512)
            r_done = []
            rv = None
            for d in range(NC_D):
                tl = p.dma("sync", r_tiles[d][:, :],
                           dram["la_loc"][128 * d:128 * d + 128, gsl],
                           after=lng)
                rp = p.mm(pbc[:, :], [(gT[d][0:32, :], R32[0:32, gsl])],
                          after=gT_done + lng + [tr32] +
                          ([rv] if rv else []))
                rv = p.vec("tensor_add", r_tiles[d][:, :],
                           r_tiles[d][:, :], pbc[:, :], after=[tl, rp] + lng)
                r_done.append(rv)

            def writer(dd, _s, src_ap, toks, g=g):
                return p.dma("sync",
                             stream_out[128 * dd:128 * dd + 128, gsl],
                             src_ap, after=toks, sem="st")
            stoks, lng = layernorm_fm(
                p, ln_scr, r_tiles, 512, ones_col, ones_row,
                psA_, prow, p1, pav, r_done, writer, guard=lng)
            lng = lng + stoks
        p.barrier("sync")


def declare_l1_attn_dram(nc, dbg=False, in_h=None, out_h=None):
    dram = {}
    dram["la_in"] = in_h if in_h is not None else nc.dram_tensor(
        "la_in", [D, T], F32, kind="ExternalInput")
    dram["la_out"] = out_h if out_h is not None else nc.dram_tensor(
        "la_out", [D, T], F32, kind="ExternalOutput")
    for nm, shp, dt in [("la_winT", [1024, 3072], F32),
                        ("la_woutT", [1024, 1024], F32),
                        ("la_winGT", [1024, 3072], F32),
                        ("la_woutGT", [1024, 1024], F32),
                        ("la_bql", [128, NC_D], F32),
                        ("la_bkl", [128, NC_D], F32),
                        ("la_bvl", [1, 1024], F32),
                        ("la_bol", [128, NC_D], F32),
                        ("la_bqg", [128, NC_D], F32),
                        ("la_bkg", [128, NC_D], F32),
                        ("la_bvg", [1, 1024], F32),
                        ("la_bog", [128, NC_D], F32),
                        ("la_odd", [128, 1], F32),
                        ("la_ident", [128, 128], BF16),
                        ("la_R32", [32, T], BF16)]:
        dram[nm] = nc.dram_tensor(nm, shp, dt, kind="ExternalInput")
    dram["la_loc"] = nc.dram_tensor("la_loc", [D, T], F32)
    if dbg:
        dram["la_dbgl"] = nc.dram_tensor("la_dbgl", [D, T], F32,
                                         kind="ExternalOutput")
        dram["la_dbgg"] = nc.dram_tensor("la_dbgg", [D, 64], F32,
                                         kind="ExternalOutput")
        dram["la_dbgm"] = nc.dram_tensor("la_dbgm", [D, 64], F32,
                                         kind="ExternalOutput")
        dram["la_dbgq"] = nc.dram_tensor("la_dbgq", [D, 64], BF16,
                                         kind="ExternalOutput")
        dram["la_dbgk"] = nc.dram_tensor("la_dbgk", [D, 64], BF16,
                                         kind="ExternalOutput")
        dram["la_dbgv"] = nc.dram_tensor("la_dbgv", [64, 1024], BF16,
                                         kind="ExternalOutput")
        dram["la_dbgog"] = nc.dram_tensor("la_dbgog", [D, 64], BF16,
                                          kind="ExternalOutput")
    dram["la_mpub"] = nc.dram_tensor("la_mpub", [128, 256], F32)
    dram["la_mgat"] = nc.dram_tensor("la_mgat", [256, 256], F32)
    return dram


def pack_l1_attn_inputs(inputs, stream_in_full):
    winT = np.ascontiguousarray(
        np.asarray(inputs["L1_attn_w_in_local"], np.float32).T)
    woutT = np.ascontiguousarray(
        np.asarray(inputs["L1_attn_w_out_local"], np.float32).T)
    winGT = np.ascontiguousarray(
        np.asarray(inputs["L1_attn_w_in_global"], np.float32).T)
    woutGT = np.ascontiguousarray(
        np.asarray(inputs["L1_attn_w_out_global"], np.float32).T)
    bl = np.asarray(inputs["L1_attn_b_in_local"], np.float32)
    bg = np.asarray(inputs["L1_attn_b_in_global"], np.float32)
    R32 = np.zeros((32, T), np.float32)
    for col in range(T):
        R32[col // 64, col] = 1.0
    shared = dict(la_winT=winT, la_woutT=woutT, la_winGT=winGT,
                  la_woutGT=woutGT,
                  la_bql=cols128(bl[0:1024]), la_bkl=cols128(bl[1024:2048]),
                  la_bvl=bl[2048:3072][None, :],
                  la_bol=cols128(inputs["L1_attn_b_out_local"]),
                  la_bqg=cols128(bg[0:1024]), la_bkg=cols128(bg[1024:2048]),
                  la_bvg=bg[2048:3072][None, :],
                  la_bog=cols128(inputs["L1_attn_b_out_global"]),
                  la_ident=np.eye(128, dtype=np.float32).astype(BF),
                  la_R32=R32.astype(BF))
    maps = []
    for c in range(8):
        b, h = c // 2, c % 2
        m = dict(shared)
        m["la_in"] = np.ascontiguousarray(
            stream_in_full[b, h * T:(h + 1) * T, :].T.astype(np.float32))
        m["la_odd"] = np.full((128, 1), float(h), np.float32)
        maps.append(m)
    return maps


def run_l1_attn(inputs, stream_in_full):
    nc = bass.Bass("TRN2", target_bir_lowering=False)
    with ExitStack() as ctx:
        p = P(nc, ctx)
        dram = declare_l1_attn_dram(nc)
        build_l1_attn(nc, ctx, p, dram)
        p.finalize()
    maps = pack_l1_attn_inputs(inputs, stream_in_full)
    return run_bass_kernel_spmd(nc, maps, core_ids=list(range(8)))



def build_l1_ffn(nc, ctx, p, dram):
    stream_in = dram["lf_in"]
    out_fm = dram["lf_out"]

    with ExitStack() as c1:
        ident = sb(c1, nc, "lf_id", [128, 128], BF16)
        ones_col = sb(c1, nc, "lf_oc", [128, 1], BF16)
        ones_row = sb(c1, nc, "lf_or", [1, 128], BF16)
        o32r = sb(c1, nc, "lf_o32r", [1, 32], BF16)
        phib = sb(c1, nc, "lf_phib_s", [1, 2048], BF16)
        R64 = sb(c1, nc, "lf_R64_s", [64, T], BF16)
        R32C = sb(c1, nc, "lf_R32C_s", [32, T], BF16)

        p0 = psum(c1, nc, "lf_p0", [128, 512])
        p1 = psum(c1, nc, "lf_p1", [128, 512])
        p2 = psum(c1, nc, "lf_p2", [128, 512])
        p3 = psum(c1, nc, "lf_p3", [128, 512])
        ptb = psum(c1, nc, "lf_ptb", [128, 128], BF16)
        ptbB = psum(c1, nc, "lf_ptbB", [128, 128], BF16)

        ct = []
        ct.append(p.dma("sync", ident[:, :], dram["lf_ident"].ap()))
        ct.append(p.dma("sync", R64[:, :], dram["lf_R64"].ap()))
        ct.append(p.dma("sync", R32C[:, :], dram["lf_R32C"].ap()))
        ct.append(p.dma("gpsimd", phib[:, :], dram["lf_phib"].ap()))
        ct.append(p.vec("memset", ones_col[:, :], 1.0))
        ct.append(p.vec("memset", ones_row[:, :], 1.0))
        ct.append(p.vec("memset", o32r[:, :], 1.0))

        with ExitStack() as c2:
            Dt = [sb(c2, nc, f"lf_D{tc}", [128, T], BF16) for tc in range(16)]
            xt = [sb(c2, nc, f"lf_xt{tc}", [128, 1024], BF16)
                  for tc in range(16)]
            phis = [sb(c2, nc, f"lf_phi{j}", [128, 4096], BF16)
                    for j in range(4)]
            logit_p = [sb(c2, nc, f"lf_logit{i}", [128, T])
                       for i in range(2)]
            expc_p = [sb(c2, nc, f"lf_expc{i}", [128, T], BF16)
                      for i in range(2)]
            mrow_p = [sb(c2, nc, f"lf_mrow{i}", [128, 64]) for i in range(2)]
            mrowb_p = [sb(c2, nc, f"lf_mrowb{i}", [128, 64], BF16)
                       for i in range(2)]
            mT_p = [sb(c2, nc, f"lf_mT{i}", [64, 128], BF16)
                    for i in range(2)]
            srow_p = [sb(c2, nc, f"lf_srow{i}", [128, 64]) for i in range(2)]
            scr_p = [sb(c2, nc, f"lf_scr{i}", [128, 512]) for i in range(2)]
            xstage_p = [sb(c2, nc, f"lf_xst{i}", [128, 512])
                        for i in range(2)]
            logit = logit_p[0]
            scr = scr_p[0]
            ctstage_p = [sb(c2, nc, f"lf_cts{i}", [128, 256], BF16)
                         for i in range(2)]
            ptbC = psum(c2, nc, "lf_ptbC", [128, 128], BF16)
            ptbD = psum(c2, nc, "lf_ptbD", [128, 128], BF16)
            ptb_p = [ptb, ptbB]
            ptbX_p = [ptbC, ptbD]
            p0_p = [p0, p2]
            p1_p = [p1, p3]
            c3 = ExitStack()
            xb = [sb(c3, nc, f"lf_xb{k}", [128, T], BF16)
                  for k in range(NC_D)]

            xb_toks = [p.dma("gpsimd", xb[k][:, :],
                             stream_in[128 * k:128 * k + 128, :])
                       for k in range(NC_D)]
            ph_toks = [p.dma("gpsimd",
                             phis[j][:, :].rearrange("q (a j) -> q a j",
                                                     j=512),
                             dram["lf_phiT"][:, 512 * j:512 * j + 512]
                             .rearrange("(a q) j -> q a j", q=128))
                       for j in range(4)]
            p.barrier("vector")
            if LF_PART == -1:
                for k in range(NC_D):
                    vv = p.vec("tensor_copy", logit[:, 0:T], xb[k][:, :],
                               after=xb_toks)
                    p.dma("sync", dram["lf_out"][128 * k:128 * k + 128, :],
                          logit[:, 0:T], after=[vv], sem="st")
                c3.close()
                p.barrier("sync")
                return
            warm = p.mm(p1[0:1, :], [(ones_col[:, :], xb[k][:, 0:512])
                                     for k in range(NC_D)],
                        after=xb_toks + ct)
            warm2 = p.mm(p0[:, :], [
                (xb[k][:, 0:128], phis[0][:, 512 * k:512 * k + 512])
                for k in range(NC_D)], after=ph_toks + [warm])

            prev_ev = [[warm2], [warm2]]
            for tc in range(16):
                for d in range(NC_D):
                    q = (tc * NC_D + d) % 2
                    tt = p.trans(ptb_p[q][:, :],
                                 xb[d][:, 128 * tc:128 * tc + 128],
                                 ident[:, :], after=xb_toks + ct + prev_ev[q])
                    te = p.vec("tensor_copy", xt[tc][:, 128 * d:128 * d + 128],
                               ptb_p[q][:, :], after=[tt])
                    prev_ev[q] = [te]
            xt_done = prev_ev[0] + prev_ev[1]
            if LF_PART == -2:
                for k in range(NC_D):
                    p.dma("gpsimd", dram["lf_out"][128 * k:128 * k + 128,
                                                   0:1024],
                          xt[k][:, :], after=xt_done, sem="st")
                c3.close()
                p.barrier("gpsimd")
                return

            Dt_done = []
            Dt_done_p = [[], []]
            Ct_st_p = [[], []]
            lg_last_p = [[], []]
            for tc in range(LF_NTC):
                par = tc % 2
                logit = logit_p[par]
                expc = expc_p[par]
                mrow = mrow_p[par]
                mrowb = mrowb_p[par]
                mT = mT_p[par]
                srow = srow_p[par]
                scr = scr_p[par]
                ptb = ptb_p[par]
                p0 = p0_p[par]
                p1 = p1_p[par]
                Ct_st = Ct_st_p[par]
                lg_toks = []
                for j in range(4):
                    bias_pairs = ([(ones_row[:, :],
                                    phib[0:1, 512 * j:512 * j + 512])]
                                  if LF_BIAS else [])
                    mm_t = p.mm(p0[:, :], bias_pairs + [
                        (xb[k][:, 128 * tc:128 * tc + 128],
                         phis[j][:, 512 * k:512 * k + 512])
                        for k in range(NC_D)],
                        after=ph_toks + xb_toks + Dt_done_p[par][-1:] +
                        Ct_st[-1:] + (lg_toks[-1:] or lg_last_p[par]))
                    ev = p.act(logit[:, 512 * j:512 * j + 512], p0[:, :],
                               AF.Identity, after=[mm_t] + Dt_done_p[par][-1:])
                    lg_toks.append(ev)
                lg_last_p[par] = lg_toks[-1:]
                if LF_PART == 0:
                    for j4 in range(4):
                        Ct_st.append(p.dma(
                            "sync",
                            dram["lf_out"][0:128,
                                           512 * j4:512 * j4 + 512],
                            logit[:, 512 * j4:512 * j4 + 512],
                            after=lg_toks, sem="st"))
                    Dt_done = lg_toks
                    continue
                m1 = p.vec("reduce_max", mrow[:, :],
                           logit[:, :].rearrange("q (a b) -> q a b", b=32),
                           AX.X, negate=True, after=lg_toks)
                m2 = p.vec("tensor_copy", mrowb[:, :], mrow[:, :],
                           after=[m1])
                t1 = p.trans(ptb[0:64, :], mrowb[:, :], ident[:, :],
                             after=[m2])
                t2 = p.vec("tensor_copy", mT[:, :], ptb[0:64, 0:128],
                           after=[t1])
                sub_toks = []
                for j in range(4):
                    rp = p.mm(p1[:, :], [(mT[:, :],
                                          R64[:, 512 * j:512 * j + 512])],
                              after=[t2, ct[1]] + sub_toks[-1:])
                    sv = p.vec("tensor_add", scr[:, :],
                               logit[:, 512 * j:512 * j + 512], p1[:, :],
                               after=[rp] + lg_toks)
                    ev = p.act(Dt[tc][:, 512 * j:512 * j + 512], scr[:, :],
                               AF.Exp, after=[sv])
                    sub_toks.append(ev)
                s1 = p.vec("reduce_sum", srow[:, :],
                           Dt[tc][:, :].rearrange("q (a b) -> q a b", b=32),
                           AX.X, after=sub_toks)
                s2 = p.vec("reciprocal", srow[:, :], srow[:, :], after=[s1])
                s3 = p.vec("tensor_copy", mrowb[:, :], srow[:, :],
                           after=[s2])
                t3 = p.trans(ptb[0:64, :], mrowb[:, :], ident[:, :],
                             after=[s3])
                t4 = p.vec("tensor_copy", mT[:, :], ptb[0:64, 0:128],
                           after=[t3])
                dn_toks = []
                for j in range(4):
                    rp = p.mm(p1[:, :], [(mT[:, :],
                                          R64[:, 512 * j:512 * j + 512])],
                              after=[t4] + dn_toks[-1:])
                    dv = p.vec("tensor_mul", Dt[tc][:, 512 * j:512 * j + 512],
                               Dt[tc][:, 512 * j:512 * j + 512], p1[:, :],
                               after=[rp] + sub_toks)
                    dn_toks.append(dv)
                Dt_done = dn_toks
                Dt_done_p[par] = dn_toks

                cm1 = p.vec("reduce_max", mrow[:, 0:32],
                            logit[:, :].rearrange("q (a b) -> q b a", b=32),
                            AX.X, negate=True, after=lg_toks + Ct_st)
                cm2 = p.vec("tensor_copy", mrowb[:, 0:32], mrow[:, 0:32],
                            after=[cm1])
                ct1 = p.trans(ptb[0:32, :], mrowb[:, 0:32], ident[:, :],
                              after=[cm2])
                ct2 = p.vec("tensor_copy", mT[0:32, :], ptb[0:32, 0:128],
                            after=[ct1])
                csub = []
                for j in range(4):
                    rp = p.mm(p1[:, :], [(mT[0:32, :],
                                          R32C[:, 512 * j:512 * j + 512])],
                              after=[ct2, ct[2]] + csub[-1:])
                    sv = p.vec("tensor_add", scr[:, :],
                               logit[:, 512 * j:512 * j + 512], p1[:, :],
                               after=[rp] + lg_toks)
                    ev = p.act(expc[:, 512 * j:512 * j + 512], scr[:, :],
                               AF.Exp, after=[sv])
                    csub.append(ev)
                cs1 = p.vec("reduce_sum", srow[:, 0:32],
                            expc[:, :].rearrange("q (a b) -> q b a", b=32),
                            AX.X, after=csub)
                cs2 = p.vec("reciprocal", srow[:, 0:32], srow[:, 0:32],
                            after=[cs1])
                cs3 = p.vec("tensor_copy", mrowb[:, 0:32], srow[:, 0:32],
                            after=[cs2])
                ct3 = p.trans(ptb[0:32, :], mrowb[:, 0:32], ident[:, :],
                              after=[cs3])
                ct4 = p.vec("tensor_copy", mT[0:32, :], ptb[0:32, 0:128],
                            after=[ct3])
                cdn = []
                for j in range(4):
                    rp = p.mm(p1[:, :], [(mT[0:32, :],
                                          R32C[:, 512 * j:512 * j + 512])],
                              after=[ct4] + cdn[-1:])
                    dv = p.vec("tensor_mul", expc[:, 512 * j:512 * j + 512],
                               expc[:, 512 * j:512 * j + 512], p1[:, :],
                               after=[rp] + csub)
                    cdn.append(dv)
                PTs = [ptb, ptbX_p[par]]
                cts = ctstage_p[par]
                tes = []
                for npc in range(16):
                    tt = p.trans(PTs[npc % 2][:, :],
                                 expc[:, 128 * npc:128 * npc + 128],
                                 ident[:, :],
                                 after=cdn + tes[-2:-1])
                    te = p.vec("tensor_copy",
                               cts[:, 128 * (npc % 2):128 * (npc % 2) + 128],
                               PTs[npc % 2][:, :],
                               after=[tt] + Ct_st[-2:-1])
                    tes.append(te)
                    Ct_st.append(p.dma(
                        "sync",
                        dram["lf_Ct"][128 * npc:128 * npc + 128,
                                      128 * tc:128 * tc + 128],
                        cts[:, 128 * (npc % 2):128 * (npc % 2) + 128],
                        after=[te], sem="st"))

            if LF_PART == 0:
                c3.close()
                p.barrier("sync")
                p.barrier("gpsimd")
                return
            if "lf_dbgd" in dram:
                p.dma("sync", dram["lf_dbgd"].ap(), Dt[0][:, :],
                      after=Dt_done, sem="st")

            dn_all = Dt_done_p[0][-1:] + Dt_done_p[1][-1:] + \
                lg_last_p[0] + lg_last_p[1] + Ct_st_p[0][-1:] + \
                Ct_st_p[1][-1:]
            xp_st = [[], []]
            ev_x = [[], []]
            for npc in range(16):
                for half in range(2):
                    q = (2 * npc + half) % 2
                    pr = Dt_done + xt_done + dn_all + ev_x[q][-1:] + \
                        xp_st[q][-1:]
                    tmm = p.mm(p0_p[q][:, :], [
                        (Dt[tc2][:, 128 * npc:128 * npc + 128],
                         xt[tc2][:, 512 * half:512 * half + 512])
                        for tc2 in range(16)], after=pr)
                    ev_x[q].append(p.act(
                        xstage_p[q][:, :],
                        p0_p[q][:, :], AF.Identity, after=[tmm]))
                    xp_st[q].append(p.dma(
                        "gpsimd",
                        dram["lf_Xp"][128 * npc:128 * npc + 128,
                                      512 * half:512 * half + 512],
                        xstage_p[q][:, :], after=ev_x[q][-1:], sem="st"))
            xp_st = xp_st[0] + xp_st[1]
            c3.close()

        if LF_PART < 2:
            p.barrier("sync")
            p.barrier("gpsimd")
            return
        rs = p.coll("ReduceScatter", ALU.add, ALLC,
                    [dram["lf_Xp"].ap().opt()], [dram["lf_Xs"].ap().opt()],
                    after=xp_st)

        with ExitStack() as c2:
            Xe = [sb(c2, nc, f"lf_Xe{i}", [128, 1024]) for i in range(2)]
            Xeb = [sb(c2, nc, f"lf_Xeb{i}", [128, 1024], BF16)
                   for i in range(2)]
            XeT = [sb(c2, nc, f"lf_XeT{d}", [128, 256], BF16)
                   for d in range(NC_D)]
            col1 = sb(c2, nc, "lf_col1", [128, 1])
            col2 = sb(c2, nc, "lf_col2", [128, 1])
            sqs = sb(c2, nc, "lf_sqs", [128, 1024])
            ebrow = sb(c2, nc, "lf_ebrow", [1, 8192], BF16)
            ytst = sb(c2, nc, "lf_ytst", [128, 1024], BF16)
            ewsl = [sb(c2, nc, f"lf_ew{i}", [128, 8192], BF16)
                    for i in range(2)]

            xs_toks = [p.dma("sync", Xe[i][:, :],
                             dram["lf_Xs"][128 * i:128 * i + 128, :],
                             after=[rs]) for i in range(2)]
            teb = p.dma("gpsimd", ebrow[:, :],
                        dram["lf_eb"].ap().rearrange("a j -> (a j)")
                        .unsqueeze(0), after=[rs])
            ln_toks = []
            for i in range(2):
                cl = p.vec("tensor_scalar", Xe[i][:, :], Xe[i][:, :],
                           65000.0, -33000.0, ALU.min, ALU.max,
                           after=xs_toks)
                r1 = p.vec("reduce_sum", col1[:, :], Xe[i][:, :], AX.X,
                           after=[cl])
                r2 = p.vec("tensor_scalar_mul", col1[:, :], col1[:, :],
                           1.0 / 1024.0, after=[r1])
                c1_ = p.vec("tensor_scalar_sub", Xe[i][:, :], Xe[i][:, :],
                            col1[:, :], after=[r2])
                sq = p.act(sqs[:, :], Xe[i][:, :], AF.Square, after=[c1_])
                r3 = p.vec("reduce_sum", col2[:, :], sqs[:, :], AX.X,
                           after=[sq])
                r4 = p.act(col2[:, :], col2[:, :], AF.Ln, bias=1e-5,
                           scale=1.0 / 1024.0, after=[r3])
                r5 = p.act(col2[:, :], col2[:, :], AF.Exp, scale=-0.5,
                           after=[r4])
                mu = p.vec("tensor_scalar_mul", Xe[i][:, :], Xe[i][:, :],
                           col2[:, :], after=[r5])
                th = p.act(Xeb[i][:, :], Xe[i][:, :], AF.Tanh, after=[mu])
                ln_toks.append(th)
            if "lf_dbgx" in dram:
                for i in range(2):
                    p.dma("sync", dram["lf_dbgx"][128 * i:128 * i + 128, :],
                          Xeb[i][:, :], after=ln_toks, sem="st")
            tr_toks = []
            for i in range(2):
                for d in range(NC_D):
                    tt = p.trans(ptb[:, :],
                                 Xeb[i][:, 128 * d:128 * d + 128],
                                 ident[:, :], after=ln_toks + tr_toks[-1:])
                    te = p.vec("tensor_copy",
                               XeT[d][:, 128 * i:128 * i + 128], ptb[:, :],
                               after=[tt])
                    tr_toks.append(te)
            ew_tok = [None, None]
            ew_use = [None, None]

            def ew_load(e):
                i = e % 2
                aft = [ew_use[i]] if ew_use[i] else []
                ew_tok[i] = p.dma(
                    "gpsimd",
                    ewsl[i][:, :].rearrange("q (a j) -> q a j", j=1024),
                    dram["lf_ew"][e, :, :].rearrange("(a q) j -> q a j",
                                                     q=128), after=aft)

            ew_load(0)
            warm3 = p.mm(p3[0:32, :], [
                (XeT[k][:, 0:32], ewsl[0][:, 1024 * k:1024 * k + 512])
                for k in range(NC_D)], after=[ew_tok[0]] + tr_toks)
            ew_use[0] = warm3
            ew_load(1)
            yt_st = []
            ev_y = None
            for e in range(8):
                bi = e % 2
                for half in range(2):
                    pr = [ew_tok[bi], teb] + tr_toks
                    if ev_y is not None:
                        pr.append(ev_y)
                    tmm = p.mm(p3[0:32, :], [
                        (o32r[:, :],
                         ebrow[0:1, 1024 * e + 512 * half:
                               1024 * e + 512 * half + 512])] + [
                        (XeT[k][:, 32 * e:32 * e + 32],
                         ewsl[bi][:, 1024 * k + 512 * half:
                                  1024 * k + 512 * half + 512])
                        for k in range(NC_D)], after=pr)
                    ew_use[bi] = tmm
                    ev_y = p.act(ytst[32 * (e % 4):32 * (e % 4) + 32,
                                      512 * half:512 * half + 512],
                                 p3[0:32, :], AF.Identity, after=[tmm])
                if e % 4 == 3:
                    yt_st.append(p.dma("gpsimd",
                                       dram["lf_Ytp"][128 * (e // 4):
                                                      128 * (e // 4) + 128, :],
                                       ytst[:, :], after=[ev_y], sem="st"))
                    if "lf_dbgy" in dram:
                        yt_st.append(p.dma(
                            "gpsimd",
                            dram["lf_dbgy"][128 * (e // 4):
                                            128 * (e // 4) + 128, :],
                            ytst[:, :], after=[ev_y], sem="st"))
                    ev_y = yt_st[-1]
                if e + 2 < 8:
                    ew_load(e + 2)
            ag = p.coll("AllGather", ALU.bypass, ALLC,
                        [dram["lf_Ytp"].ap().opt()],
                        [dram["lf_Ytg"].ap().opt()], after=yt_st)

        if LF_PART < 3:
            p.barrier("sync")
            p.barrier("gpsimd")
            return
        with ExitStack() as c2:
            yt = [sb(c2, nc, f"lf_yt{i}", [128, 1024], BF16)
                  for i in range(16)]
            Ctt = [sb(c2, nc, f"lf_ct{i}", [128, T], BF16)
                   for i in range(16)]
            xres_p = [sb(c2, nc, f"lf_xres{i}", [128, 512])
                      for i in range(2)]
            osc = sb(c2, nc, "lf_osc", [128, 512])
            ln_scr = make_ln_scratch(c2, nc, "lfln")
            psA = psum(c2, nc, "lf_psA", [1, 512])
            psB = psum(c2, nc, "lf_psB", [1, 512])
            r_tiles = [sb(c2, nc, f"lf_rt{d}", [128, 512])
                       for d in range(NC_D)]
            yt_toks = [p.dma("gpsimd", yt[i][:, :],
                             dram["lf_Ytg"][128 * i:128 * i + 128, :],
                             after=[ag]) for i in range(16)]
            ct_toks = [p.dma("sync", Ctt[i][:, :],
                             dram["lf_Ct"][128 * i:128 * i + 128, :],
                             after=p.now())
                       for i in range(16)]
            if "lf_dbgc" in dram:
                p.dma("sync", dram["lf_dbgc"].ap(), Ctt[0][:, :],
                      after=ct_toks, sem="st")
            warm4 = p.mm(p0[:, :], [(yt[k][:, 0:128], Ctt[k][:, 0:512])
                                    for k in range(16)],
                         after=yt_toks + ct_toks)
            lng = [warm4]
            pC = [p0, p3]
            for g in range(4):
                gsl = slice(512 * g, 512 * g + 512)
                r_done = []
                ev_q = [[], []]
                tx_q = [[], []]
                for d in range(NC_D):
                    q = d % 2
                    pr = yt_toks + ct_toks + lng + ev_q[q][-1:]
                    tmm = p.mm(pC[q][:, :], [
                        (yt[k][:, 128 * d:128 * d + 128], Ctt[k][:, gsl])
                        for k in range(16)], after=pr)
                    tx = p.dma("sync", xres_p[q][:, :],
                               stream_in[128 * d:128 * d + 128, gsl],
                               after=ev_q[q][-1:])
                    ev_o = p.vec("tensor_add", r_tiles[d][:, :],
                                 pC[q][:, :], xres_p[q][:, :],
                                 after=[tmm, tx] + lng)
                    ev_q[q].append(ev_o)
                    r_done.append(ev_o)

                if g == 0 and "lf_dbgr" in dram:
                    for dd in range(NC_D):
                        p.dma("sync",
                              dram["lf_dbgr"][128 * dd:128 * dd + 128, :],
                              r_tiles[dd][:, :], after=r_done, sem="st")

                def writer(dd, _s, src_ap, toks, g=g):
                    return p.dma("sync",
                                 out_fm[128 * dd:128 * dd + 128, gsl],
                                 src_ap, after=toks, sem="st")
                stoks, lng = layernorm_fm(
                    p, ln_scr, r_tiles, 512, ones_col, ones_row,
                    psA, psB, p1, p2, r_done, writer, guard=lng)
                lng = lng + stoks
        p.barrier("sync")


def declare_l1_ffn_dram(nc, dbg=False, in_h=None):
    dram = {}
    dram["lf_in"] = in_h if in_h is not None else nc.dram_tensor(
        "lf_in", [D, T], F32, kind="ExternalInput")
    dram["lf_out"] = nc.dram_tensor("lf_out", [D, T], F32,
                                    kind="ExternalOutput")
    for nm, shp, dt in [("lf_phiT", [1024, 2048], F32),
                        ("lf_phib", [1, 2048], F32),
                        ("lf_ew", [8, 1024, 1024], F32),
                        ("lf_eb", [8, 1024], F32),
                        ("lf_ident", [128, 128], BF16),
                        ("lf_R64", [64, T], BF16),
                        ("lf_R32C", [32, T], BF16)]:
        dram[nm] = nc.dram_tensor(nm, shp, dt, kind="ExternalInput")
    if dbg:
        dram["lf_dbgx"] = nc.dram_tensor("lf_dbgx", [256, 1024], BF16,
                                         kind="ExternalOutput")
        dram["lf_dbgy"] = nc.dram_tensor("lf_dbgy", [256, 1024], BF16,
                                         kind="ExternalOutput")
        dram["lf_dbgc"] = nc.dram_tensor("lf_dbgc", [128, T], BF16,
                                         kind="ExternalOutput")
        dram["lf_dbge"] = nc.dram_tensor("lf_dbge", [128, T], BF16,
                                         kind="ExternalOutput")
        dram["lf_dbgr"] = nc.dram_tensor("lf_dbgr", [D, 512], F32,
                                         kind="ExternalOutput")
        dram["lf_dbgd"] = nc.dram_tensor("lf_dbgd", [128, T], BF16,
                                         kind="ExternalOutput")
    dram["lf_Ct"] = nc.dram_tensor("lf_Ct", [T, T], BF16)
    dram["lf_Xp"] = nc.dram_tensor("lf_Xp", [2048, 1024], F32)
    dram["lf_Xs"] = nc.dram_tensor("lf_Xs", [256, 1024], F32)
    dram["lf_Ytp"] = nc.dram_tensor("lf_Ytp", [256, 1024], BF16)
    dram["lf_Ytg"] = nc.dram_tensor("lf_Ytg", [2048, 1024], BF16)
    return dram


def pack_l1_ffn_inputs(inputs, stream_in_full):
    phiT = np.ascontiguousarray(
        np.asarray(inputs["L1_ffn_phi_w"], np.float32).T)
    phib = np.asarray(inputs["L1_ffn_phi_b"], np.float32)[None, :]
    ew = np.asarray(inputs["L1_ffn_expert_w"], np.float32)
    eb = np.asarray(inputs["L1_ffn_expert_b"], np.float32)
    R64 = np.zeros((64, T), np.float32)
    for col in range(T):
        R64[col // 32, col] = 1.0
    R32C = np.zeros((32, T), np.float32)
    for col in range(T):
        R32C[col % 32, col] = 1.0
    shared = dict(lf_phiT=phiT, lf_phib=phib,
                  lf_ident=np.eye(128, dtype=np.float32).astype(BF),
                  lf_R64=R64.astype(BF), lf_R32C=R32C.astype(BF))
    maps = []
    for c in range(8):
        b, h = c // 2, c % 2
        m = dict(shared)
        m["lf_in"] = np.ascontiguousarray(
            stream_in_full[b, h * T:(h + 1) * T, :].T.astype(np.float32))
        m["lf_ew"] = np.ascontiguousarray(ew[8 * c:8 * c + 8])
        m["lf_eb"] = np.ascontiguousarray(eb[8 * c:8 * c + 8])
        maps.append(m)
    return maps


def run_l1_ffn(inputs, stream_in_full):
    nc = bass.Bass("TRN2", target_bir_lowering=False)
    with ExitStack() as ctx:
        p = P(nc, ctx)
        dram = declare_l1_ffn_dram(nc)
        build_l1_ffn(nc, ctx, p, dram)
        p.finalize()
    maps = pack_l1_ffn_inputs(inputs, stream_in_full)
    return run_bass_kernel_spmd(nc, maps, core_ids=list(range(8)))


def kernel_unfused(**inputs):
    res = run_hawk(inputs)
    y1 = _gather_fm(res, "y1")
    res = run_ffn_attn(inputs, 0, y1)
    x2 = _gather_fm(res, "fa_out")
    res = run_ffn_ffn(inputs, 0, x2)
    x3 = _gather_fm(res, "ff_out")
    res = run_ffn_attn(inputs, 1, x3)
    x4 = _gather_fm(res, "fa_out")
    res = run_ffn_ffn(inputs, 1, x4, res2_full=y1)
    y2 = _gather_fm(res, "ff_out2")
    res = run_l1_attn(inputs, y2)
    y3 = _gather_fm(res, "la_out")
    res = run_l1_ffn(inputs, y3)
    return _gather_fm(res, "lf_out")



def build_fused():
    nc = bass.Bass("TRN2", target_bir_lowering=False)
    with ExitStack() as ctx:
        p = P(nc, ctx)
        senti = sb(ctx, nc, "senti", [1, 1])

        def sentinel(i):
            p.vec("memset", senti[0:1, 0:1], 601.0 + i)
        dram_h = declare_hawk_dram(nc, stream_out_kind="Internal")
        y1 = dram_h["y1"]
        p.set_phase("hawk")
        sentinel(0)
        with ExitStack() as ch:
            build_hawk(nc, ch, p, dram_h)
        p.barrier()

        p.set_phase("fa0")
        sentinel(1)
        s_x2 = nc.dram_tensor("s_x2", [D, T], F32)
        d = declare_ffn_attn_dram(nc, p, pfx="a0_", in_h=y1, out_h=s_x2)
        build_ffn_attn_prologue(nc, ctx, p, d)
        build_ffn_attn(nc, ctx, p, d)
        p.barrier()

        p.set_phase("ff0")
        sentinel(2)
        s_x3 = nc.dram_tensor("s_x3", [D, T], F32)
        d = declare_ffn_ffn_dram(nc, False, pfx="f0_", in_h=s_x2, out_h=s_x3)
        build_col_prologue(nc, ctx, p, d, "ff_in", "ff_cpub", "ff_cgat",
                           "ff_cgat_tok")
        build_ffn_ffn(nc, ctx, p, d, False)
        p.barrier()

        p.set_phase("fa1")
        sentinel(3)
        s_x4 = nc.dram_tensor("s_x4", [D, T], F32)
        d = declare_ffn_attn_dram(nc, p, pfx="a1_", in_h=s_x3, out_h=s_x4)
        build_ffn_attn_prologue(nc, ctx, p, d)
        build_ffn_attn(nc, ctx, p, d)
        p.barrier()

        p.set_phase("ff1")
        sentinel(4)
        s_y2 = nc.dram_tensor("s_y2", [D, T], F32)
        s_x5 = nc.dram_tensor("s_x5", [D, T], F32)
        d = declare_ffn_ffn_dram(nc, True, pfx="f1_", in_h=s_x4, out_h=s_x5,
                                 res2_h=y1, out2_h=s_y2)
        build_col_prologue(nc, ctx, p, d, "ff_in", "ff_cpub", "ff_cgat",
                           "ff_cgat_tok")
        build_ffn_ffn(nc, ctx, p, d, True)
        p.barrier()

        p.set_phase("la")
        sentinel(5)
        s_y3 = nc.dram_tensor("s_y3", [D, T], F32)
        d = declare_l1_attn_dram(nc, in_h=s_y2, out_h=s_y3)
        build_l1_attn(nc, ctx, p, d)
        p.barrier()

        p.set_phase("lf")
        sentinel(6)
        d = declare_l1_ffn_dram(nc, in_h=s_y3)
        build_l1_ffn(nc, ctx, p, d)
        p.finalize()
    return nc


def pack_fused_inputs(inputs):
    maps = pack_hawk_inputs(inputs)
    Z = np.zeros((4, 4096, D), np.float32)
    for sub, pfx in ((0, "a0_"), (1, "a1_")):
        shared, per_core = make_ffn_attn_consts(inputs, sub)
        wqkv = np.ascontiguousarray(
            np.asarray(inputs["L0_ffn_wqkv"], np.float32)[sub])
        wo = np.ascontiguousarray(
            np.asarray(inputs["L0_ffn_wo"], np.float32)[sub])
        for c in range(8):
            m = dict(fa_wqkv=wqkv, fa_wo=wo)
            m.update(shared)
            m.update(per_core[c])
            maps[c].update({pfx + k: v for k, v in m.items()})
    for sub, pfx in ((0, "f0_"), (1, "f1_")):
        fmaps = pack_ffn_ffn_inputs(inputs, sub, Z)
        for c in range(8):
            maps[c].update({pfx + k: v for k, v in fmaps[c].items()
                            if k not in ("ff_in", "ff_res2")})
    lmaps = pack_l1_attn_inputs(inputs, Z)
    for c in range(8):
        maps[c].update({k: v for k, v in lmaps[c].items() if k != "la_in"})
    fmaps = pack_l1_ffn_inputs(inputs, Z)
    for c in range(8):
        maps[c].update({k: v for k, v in fmaps[c].items() if k != "lf_in"})
    return maps


def kernel(**inputs):
    nc = build_fused()
    maps = pack_fused_inputs(inputs)
    res = run_bass_kernel_spmd(nc, maps, core_ids=list(range(8)))
    return _gather_fm(res, "lf_out")


# version 06
# speedup vs baseline: 1.0818x; 1.0093x over previous
import numpy as np
import ml_dtypes
from contextlib import ExitStack

import concourse.bass as bass
import concourse.mybir as mybir
from concourse.bass_utils import run_bass_kernel_spmd

F32 = mybir.dt.float32
BF16 = mybir.dt.bfloat16
AF = mybir.ActivationFunctionType
ALU = mybir.AluOpType
AX = mybir.AxisListType
BF = ml_dtypes.bfloat16
f32 = np.float32

ENGINES = ("sync", "tensor", "scalar", "vector", "gpsimd")
DEBUG_BARRIERS = False
DBG_NOHALO = False
LF_PART = 3
LF_BIAS = True
LF_NTC = 16
PAIRS = [[0, 1], [2, 3], [4, 5], [6, 7]]
ALLC = [list(range(8))]

T = 2048
D = 1024
HID = 1536
NC_D = D // 128
NC_H = HID // 128


class P:

    NDMASEM = 12

    def __init__(self, nc, ctx):
        self.nc = nc
        self.ctx = ctx
        self.ops = {e: [] for e in ENGINES}
        self.opph = {e: [] for e in ENGINES}
        self.phase = "init"
        self.semh = {}
        self.cnt = {}
        self.lastw = {e: {} for e in ENGINES}
        self.dmarr = {"sync": 0, "gpsimd": 0}

    def _app(self, eng, fn):
        self.ops[eng].append(fn)
        self.opph[eng].append(self.phase)

    def set_phase(self, name):
        self.phase = name

    def sem(self, name):
        if name not in self.semh:
            self.semh[name] = self.ctx.enter_context(self.nc.semaphore(name))
            self.cnt[name] = 0
        return self.semh[name]

    def wait(self, eng, tokens):
        for t in tokens:
            if t is None:
                continue
            s, v = t
            if v <= 0 or self.lastw[eng].get(s, 0) >= v:
                continue
            self.lastw[eng][s] = v
            h = self.sem(s)
            self._app(eng, lambda E, h=h, v=v: E.wait_ge(h, v))

    def _tok(self, sem, n):
        self.sem(sem)
        self.cnt[sem] += n
        return (sem, self.cnt[sem])

    def now(self):
        return [(s, v) for s, v in self.cnt.items() if v > 0]

    def barrier(self, eng=None):
        toks = self.now()
        for e in (ENGINES if eng is None else [eng]):
            self.wait(e, toks)

    def dma(self, q, out, in_, after=(), sem=None, **kw):
        i = self.dmarr[q]
        self.dmarr[q] = (i + 1) % self.NDMASEM
        sem = f"dma_{q}_{i}"
        self.wait(q, after)
        tok = self._tok(sem, 16)
        h = self.sem(sem)
        self._app(q,
            lambda E: E.dma_start(out=out, in_=in_, **kw).then_inc(h, 16))
        return tok

    def mm(self, psum_ap, pairs, after=(), start=True, stop=True, sem="pe"):
        self.wait("tensor", after)
        tok = self._tok(sem, 1)
        h = self.sem(sem)
        n = len(pairs)

        def f(E, pairs=pairs, n=n, start=start, stop=stop):
            ins = None
            for i, (l, r) in enumerate(pairs):
                ins = E.matmul(psum_ap, l, r, start=(start and i == 0),
                               stop=(stop and i == n - 1))
            ins.then_inc(h, 1)
        self._app("tensor", f)
        return tok

    def trans(self, psum_ap, in_ap, ident_ap, after=(), sem="pe"):
        self.wait("tensor", after)
        tok = self._tok(sem, 1)
        h = self.sem(sem)
        self._app("tensor",
            lambda E: E.transpose(psum_ap, in_ap, ident_ap).then_inc(h, 1))
        return tok

    def act(self, out, in_, func, after=(), bias=0.0, scale=1.0,
            accum_out=None, sem="sc"):
        if sem in self.cnt and self.cnt[sem] > 0:
            self.wait("scalar", [(sem, self.cnt[sem])])
        self.wait("scalar", after)
        tok = self._tok(sem, 1)
        h = self.sem(sem)
        if isinstance(bias, float):
            bias = f32(bias)
        if isinstance(scale, float):
            scale = f32(scale)
        self._app("scalar",
            lambda E: E.activation(out, in_, func, bias=bias, scale=scale,
                                   accum_out=accum_out).then_inc(h, 1))
        return tok

    def vec(self, meth, *args, after=(), sem="ve", **kw):
        if sem in self.cnt and self.cnt[sem] > 0:
            self.wait("vector", [(sem, self.cnt[sem])])
        self.wait("vector", after)
        tok = self._tok(sem, 1)
        h = self.sem(sem)

        def f(E, meth=meth, args=args, kw=kw):
            getattr(E, meth)(*args, **kw).then_inc(h, 1)
        self._app("vector", f)
        return tok

    def gps(self, meth, *args, after=(), sem="gc", **kw):
        self.wait("gpsimd", after)
        tok = self._tok(sem, 1)
        h = self.sem(sem)

        def f(E, meth=meth, args=args, kw=kw):
            getattr(E, meth)(*args, **kw).then_inc(h, 1)
        self._app("gpsimd", f)
        return tok

    def coll(self, kind, op, groups, ins, outs, after=()):
        self.wait("gpsimd", after)
        tok = self._tok("cc", 1)
        h = self.sem("cc")
        self._app("gpsimd",
            lambda E: E.collective_compute(
                kind, op, replica_groups=groups, ins=ins, outs=outs,
            ).then_inc(h, 1))
        return tok

    def finalize(self):
        nc = self.nc
        with nc.Block() as block:
            for eng in ENGINES:
                ops = self.ops[eng]
                if not ops:
                    continue
                phs = self.opph[eng]

                def runner(E, ops=ops, phs=phs):
                    cur = None
                    for fn, ph in zip(ops, phs):
                        if ph != cur:
                            if cur is not None:
                                nc.pop_named_scope(cur)
                            nc.push_named_scope(ph)
                            cur = ph
                        fn(E)
                    if cur is not None:
                        nc.pop_named_scope(cur)
                getattr(block, eng)(runner)


import itertools
_UNIQ = itertools.count()


def sb(ctx, nc, name, shape, dt=F32):
    return ctx.enter_context(
        nc.sbuf_tensor(f"{name}_u{next(_UNIQ)}", shape, dt))


def psum(ctx, nc, name, shape, dt=F32):
    return ctx.enter_context(
        nc.psum_tensor(f"{name}_u{next(_UNIQ)}", shape, dt))


class Streamer:

    def __init__(self, p, bufs):
        self.p = p
        self.bufs = bufs
        self.load_tok = [None] * len(bufs)
        self.used_tok = [None] * len(bufs)
        self.i = 0

    def load(self, in_ap, after=()):
        i = self.i % len(self.bufs)
        self.i += 1
        aft = list(after)
        if self.used_tok[i] is not None:
            aft.append(self.used_tok[i])
        self.load_tok[i] = self.p.dma(
            "gpsimd", self.bufs[i][:, :].rearrange("q (a j) -> q a j", j=128),
            in_ap, after=aft)
        return i

    def note_use(self, i, tok):
        self.used_tok[i] = tok


def slab_ap(W, m, w=128):
    return W[:, w * m:w * m + w].rearrange("(a q) j -> q a j", q=128)



def softplus_cols(p, ctx, nc, lam_cols, ld_tok):
    a1 = sb(ctx, nc, "sp_a1", [128, NC_H])
    sp8 = sb(ctx, nc, "sp8", [128, NC_H])
    sp16 = sb(ctx, nc, "sp16", [128, NC_H])
    t1 = p.act(a1[:, :], lam_cols[:, :], AF.Abs, after=[ld_tok])
    t2 = p.act(a1[:, :], a1[:, :], AF.Exp, scale=-1.0, after=[t1])
    t3 = p.act(a1[:, :], a1[:, :], AF.Ln, bias=1.0, after=[t2])
    t4 = p.act(sp8[:, :], lam_cols[:, :], AF.Relu, after=[ld_tok])
    t5 = p.vec("tensor_add", a1[:, :], a1[:, :], sp8[:, :], after=[t3, t4])
    t6 = p.vec("tensor_scalar_mul", sp8[:, :], a1[:, :], -8.0, after=[t5])
    t7 = p.vec("tensor_scalar_mul", sp16[:, :], a1[:, :], -16.0, after=[t5])
    return sp8, sp16, [t6, t7]


def make_ln_scratch(ctx, nc, tag):
    return dict(
        rb=sb(ctx, nc, f"{tag}_rb", [128, 512], BF16),
        rb2=sb(ctx, nc, f"{tag}_rb2", [128, 512], BF16),
        r0=sb(ctx, nc, f"{tag}_r0", [1, 512]),
        r1=sb(ctx, nc, f"{tag}_r1", [1, 512]),
        r2=sb(ctx, nc, f"{tag}_r2", [1, 512]),
        r3=sb(ctx, nc, f"{tag}_r3", [1, 512]),
        brow=sb(ctx, nc, f"{tag}_brow", [1, 512]),
        a_bf=sb(ctx, nc, f"{tag}_abf", [1, 512], BF16),
        b_bf=sb(ctx, nc, f"{tag}_bbf", [1, 512], BF16))


def layernorm_fm(p, scratch, r_tiles, width, ones_col, ones_row,
                 psA, psB, pA, pB, after, out_write, guard=()):
    nseg = width // 512
    rb = scratch["rb"]
    rb2 = scratch["rb2"]
    r0, r1, r2, r3 = (scratch["r0"], scratch["r1"], scratch["r2"],
                      scratch["r3"])
    brow = scratch["brow"]
    a_bf, b_bf = scratch["a_bf"], scratch["b_bf"]
    out_toks = []
    guard = list(guard)
    for seg in range(nseg):
        sl = slice(seg * 512, seg * 512 + 512)
        prev = list(after) + guard
        ta = tb = None
        for d in range(NC_D):
            tc = p.vec("tensor_copy", rb[:, :], r_tiles[d][:, sl], after=prev)
            ts = p.act(rb2[:, :], r_tiles[d][:, sl], AF.Square, after=prev)
            ta = p.mm(psA[:, :], [(ones_col[:, :], rb[:, :])],
                      after=[tc], start=(d == 0), stop=(d == NC_D - 1))
            tb = p.mm(psB[:, :], [(ones_col[:, :], rb2[:, :])],
                      after=[ts], start=(d == 0), stop=(d == NC_D - 1))
            prev = list(after) + [ta, tb]
        t1 = p.act(r0[:, :], psA[0:1, :], AF.Identity, scale=1.0 / D,
                   after=[ta, tb])
        t2 = p.act(r1[:, :], psB[0:1, :], AF.Identity, scale=1.0 / D,
                   after=[ta, tb])
        t3 = p.act(r2[:, :], r0[:, :], AF.Square, after=[t1])
        t4 = p.vec("tensor_sub", r1[:, :], r1[:, :], r2[:, :],
                   after=[t2, t3])
        t5 = p.act(r2[:, :], r1[:, :], AF.Ln, bias=1e-5, after=[t4])
        t6 = p.act(r3[:, :], r2[:, :], AF.Exp, scale=-0.5, after=[t5])
        t7 = p.vec("tensor_mul", brow[:, :], r0[:, :], r3[:, :],
                   after=[t6])
        t8 = p.vec("tensor_scalar_mul", brow[:, :], brow[:, :], -1.0,
                   after=[t7])
        t9 = p.vec("tensor_copy", a_bf[:, :], r3[:, :], after=[t6])
        t10 = p.vec("tensor_copy", b_bf[:, :], brow[:, :], after=[t8])
        bA = p.mm(pA[:, :], [(ones_row[:, :], a_bf[:, :])],
                  after=[t9] + guard)
        bB = p.mm(pB[:, :], [(ones_row[:, :], b_bf[:, :])],
                  after=[t10] + guard)
        seg_toks = []
        tn = None
        for d in range(NC_D):
            tm = p.vec("tensor_mul", r_tiles[d][:, sl], r_tiles[d][:, sl],
                       pA[:, :], after=[bA, bB])
            tn = p.vec("tensor_add", r_tiles[d][:, sl], r_tiles[d][:, sl],
                       pB[:, :], after=[tm])
            seg_toks.append(out_write(d, seg, r_tiles[d][:, sl], [tn]))
        guard = [t1, t2, tn]
        out_toks += seg_toks
    return out_toks, guard



def build_hawk(nc, ctx, p, dram):
    G = 1024
    NSEG = 2

    xfm = dram["xfm"]
    y1 = dram["y1"]

    xb = [sb(ctx, nc, f"hk_xb{k}", [128, G + 3], BF16) for k in range(NC_D)]
    wslab = [sb(ctx, nc, f"hk_wsl{i}", [128, 1024], BF16) for i in range(2)]
    lwi = [sb(ctx, nc, f"hk_lwi{i}", [128, 1536], BF16) for i in range(2)]
    lwg = [sb(ctx, nc, f"hk_lwg{i}", [128, 1536], BF16) for i in range(2)]
    wosl = [sb(ctx, nc, f"hk_wos{i}", [128, 1536], BF16) for i in range(2)]
    y_g = [sb(ctx, nc, f"hk_yg{m}", [128, G], BF16) for m in range(NC_H)]
    h_st = [sb(ctx, nc, f"hk_h{m}", [128, T], BF16) for m in range(NC_H)]
    cp_st = [sb(ctx, nc, f"hk_cp{m}", [128, T], BF16) for m in range(NC_H)]
    u_sb = sb(ctx, nc, "hk_u", [128, 515])
    tA = sb(ctx, nc, "hk_tA", [128, 512])
    tB = sb(ctx, nc, "hk_tB", [128, 512])
    tC = sb(ctx, nc, "hk_tC", [128, 512])
    tD = sb(ctx, nc, "hk_tD", [128, 512])
    zeros512 = sb(ctx, nc, "hk_z512", [128, 512])
    hcarry = sb(ctx, nc, "hk_hcar", [128, NC_H])
    cpcarry = sb(ctx, nc, "hk_cpcar", [128, NC_H])
    carrycol = sb(ctx, nc, "hk_carcol", [128, NC_H])
    lam_cols = sb(ctx, nc, "hk_lam", [128, NC_H])
    convw = sb(ctx, nc, "hk_cw", [128, NC_H * 4])
    convb = sb(ctx, nc, "hk_cb", [128, NC_H])
    nbi = sb(ctx, nc, "hk_nbi", [128, NC_H])
    nbg = sb(ctx, nc, "hk_nbg", [128, NC_H])
    is_odd = sb(ctx, nc, "hk_odd", [128, 1])
    ones_col = sb(ctx, nc, "hk_oc", [128, 1], BF16)
    ones_row = sb(ctx, nc, "hk_or", [1, 128], BF16)
    r_tiles = [sb(ctx, nc, f"hk_rt{d}", [128, 512]) for d in range(NC_D)]
    xres = sb(ctx, nc, "hk_xres", [128, 512])
    gg = sb(ctx, nc, "hk_gg", [128, 512])
    ln_scr = make_ln_scratch(ctx, nc, "hkln")

    p0 = psum(ctx, nc, "hk_p0", [128, 512])
    p1 = psum(ctx, nc, "hk_p1", [128, 512])
    p2 = psum(ctx, nc, "hk_p2", [128, 512])
    p3 = psum(ctx, nc, "hk_p3", [128, 512])
    ph3 = psum(ctx, nc, "hk_ph3", [128, 8])
    psA = psum(ctx, nc, "hk_psA", [1, 512])
    psB = psum(ctx, nc, "hk_psB", [1, 512])

    tz = p.vec("memset", zeros512[:, :], 0.0)
    p.vec("memset", hcarry[:, :], 0.0)
    p.vec("memset", cpcarry[:, :], 1.0)
    p.vec("memset", ones_col[:, :], 1.0)
    p.vec("memset", ones_row[:, :], 1.0)
    tl1 = p.dma("sync", lam_cols[:, :], dram["lam_cols"].ap())
    p.dma("sync", convw[:, :], dram["convw_cols"].ap())
    p.dma("sync", convb[:, :], dram["convb_cols"].ap())
    tl4 = p.dma("sync", is_odd[:, :], dram["is_odd"].ap())
    tb1 = p.dma("sync", nbi[:, :], dram["bi_cols"].ap())
    tb2 = p.dma("sync", nbg[:, :], dram["bg_cols"].ap())
    tn1 = p.vec("tensor_scalar_mul", nbi[:, :], nbi[:, :], -1.0, after=[tb1])
    tn2 = p.vec("tensor_scalar_mul", nbg[:, :], nbg[:, :], -1.0, after=[tb2])
    sp8, sp16, sp_toks = softplus_cols(p, ctx, nc, lam_cols, tl1)
    p.barrier("vector")

    WiT, WgT = dram["WiT"], dram["WgT"]
    lruWiT, lruWgT, WoT = dram["lruWiT"], dram["lruWgT"], dram["WoT"]

    scan_done = []

    for g in range(2):
        if g == 1:
            p.barrier()
        xcol0 = g * G
        xb_toks = [p.dma("gpsimd", xb[k][:, :],
                         xfm[128 * k:128 * k + 128, xcol0:xcol0 + G + 3])
                   for k in range(NC_D)]
        ws = Streamer(p, wslab)
        ws.load(slab_ap(WiT, 0))
        ws.load(slab_ap(WiT, 1))
        warm = p.mm(p0[:, :], [
            (wslab[0][:, 128 * k:128 * k + 128],
             xb[k][:, 3:515]) for k in range(NC_D)],
            after=[ws.load_tok[0]] + xb_toks)
        ws.note_use(0, warm)
        evac_u = evac_h = conv_last = None
        for mi, m in enumerate(range(NC_H)):
            bi = (mi % 2)
            buf = wslab[bi]
            for seg in range(NSEG):
                pr = [ws.load_tok[bi]] + xb_toks
                if evac_u is not None:
                    pr.append(evac_u)
                tmm = p.mm(p0[:, :], [
                    (buf[:, 128 * k:128 * k + 128],
                     xb[k][:, 3 + seg * 512: 3 + seg * 512 + 512])
                    for k in range(NC_D)], after=pr)
                ws.note_use(bi, tmm)
                if DBG_NOHALO:
                    evac_u = p.vec("tensor_copy", u_sb[:, 3:515], p0[:, :],
                                   after=[tmm] + ([conv_last] if conv_last else []))
                    evac_h = p.vec("memset", u_sb[:, 0:3], 0.0,
                                   after=[conv_last] if conv_last else [])
                else:
                    tm3 = p.mm(ph3[:, 0:3], [
                        (buf[:, 128 * k:128 * k + 128],
                         xb[k][:, seg * 512: seg * 512 + 3])
                        for k in range(NC_D)],
                        after=[evac_h] if evac_h else [])
                    ws.note_use(bi, tm3)
                    evac_u = p.vec("tensor_copy", u_sb[:, 3:515], p0[:, :],
                                   after=[tmm] + ([conv_last] if conv_last else []))
                    evac_h = p.vec("tensor_copy", u_sb[:, 0:3], ph3[:, 0:3],
                                   after=[tm3] + ([conv_last] if conv_last else []))
                c1 = p.vec("tensor_scalar", tA[:, :], u_sb[:, 3:515],
                           convw[:, 3 * NC_H + m:3 * NC_H + m + 1],
                           convb[:, m:m + 1], ALU.mult, ALU.add,
                           after=[evac_u, evac_h])
                c2 = p.vec("scalar_tensor_tensor", tA[:, :], u_sb[:, 2:514],
                           convw[:, 2 * NC_H + m:2 * NC_H + m + 1], tA[:, :],
                           ALU.mult, ALU.add, after=[c1])
                c3 = p.vec("scalar_tensor_tensor", tA[:, :], u_sb[:, 1:513],
                           convw[:, 1 * NC_H + m:1 * NC_H + m + 1], tA[:, :],
                           ALU.mult, ALU.add, after=[c2])
                conv_last = p.vec(
                    "scalar_tensor_tensor",
                    y_g[m][:, seg * 512:seg * 512 + 512],
                    u_sb[:, 0:512], convw[:, 0 * NC_H + m:0 * NC_H + m + 1],
                    tA[:, :], ALU.mult, ALU.add, after=[c3])
                if DEBUG_BARRIERS:
                    p.barrier()
                if "dbg_u" in dram and g == 0 and m == 0 and seg == 0:
                    p.dma("sync", dram["dbg_u"].ap(), u_sb[:, :],
                          after=[evac_u, evac_h], sem="st")
                    p.dma("gpsimd", dram["dbg_tA"].ap(),
                          y_g[0][:, 0:512], after=[conv_last], sem="st")
            if mi + 2 < NC_H:
                ws.load(slab_ap(WiT, mi + 2))
        y_done = [conv_last, evac_u]
        if "dbg_y" in dram:
            for m_ in range(NC_H):
                p.dma("sync", dram["dbg_y"][128 * m_:128 * m_ + 128,
                                            g * G:(g + 1) * G],
                      y_g[m_][:, :], after=y_done, sem="st")
            p.dma("gpsimd", dram["dbg_y2"][:, g * G:(g + 1) * G],
                  y_g[0][:, :], after=y_done, sem="st")

        si_s = Streamer(p, lwi)
        sg_s = Streamer(p, lwg)
        si_s.load(slab_ap(lruWiT, 0), after=y_done)
        sg_s.load(slab_ap(lruWgT, 0), after=y_done)
        si_s.load(slab_ap(lruWiT, 1))
        sg_s.load(slab_ap(lruWgT, 1))
        warm2 = p.mm(p1[:, :], [
            (lwi[0][:, 128 * k:128 * k + 128], y_g[k][:, 0:512])
            for k in range(NC_H)], after=[si_s.load_tok[0]] + y_done)
        si_s.note_use(0, warm2)
        warm3 = p.mm(p2[:, :], [
            (lwg[0][:, 128 * k:128 * k + 128], y_g[k][:, 0:512])
            for k in range(NC_H)], after=[sg_s.load_tok[0]] + y_done)
        sg_s.note_use(0, warm3)
        ev_i = ev_g = last_v = None
        for m in range(NC_H):
            bi = m % 2
            for seg in range(NSEG):
                gsl = slice(seg * 512, seg * 512 + 512)
                tsl = slice(g * G + seg * 512, g * G + seg * 512 + 512)
                pr = [si_s.load_tok[bi]] + y_done
                if ev_i is not None:
                    pr.append(ev_i)
                t_si = p.mm(p1[:, :], [
                    (lwi[bi][:, 128 * k:128 * k + 128], y_g[k][:, gsl])
                    for k in range(NC_H)], after=pr)
                si_s.note_use(bi, t_si)
                pr = [sg_s.load_tok[bi]] + y_done
                if ev_g is not None:
                    pr.append(ev_g)
                t_sg = p.mm(p2[:, :], [
                    (lwg[bi][:, 128 * k:128 * k + 128], y_g[k][:, gsl])
                    for k in range(NC_H)], after=pr)
                sg_s.note_use(bi, t_sg)
                lw = [last_v] if last_v else []
                e1 = p.act(tA[:, :], p1[:, :], AF.Exp, scale=-1.0,
                           bias=nbi[:, m:m + 1], after=[t_si, tn1] + lw)
                ev_i = e1
                v1 = p.vec("tensor_scalar_add", tA[:, :], tA[:, :],
                           1.0, after=[e1])
                v2 = p.vec("reciprocal", tA[:, :], tA[:, :], after=[v1])
                e2 = p.act(tB[:, :], p2[:, :], AF.Exp, scale=-1.0,
                           bias=nbg[:, m:m + 1], after=[t_sg, tn2] + lw)
                ev_g = e2
                v3 = p.vec("tensor_scalar_add", tB[:, :], tB[:, :],
                           1.0, after=[e2])
                v4 = p.vec("reciprocal", tB[:, :], tB[:, :], after=[v3])
                a1 = p.act(tC[:, :], tB[:, :], AF.Exp,
                           scale=sp8[:, m:m + 1], after=[v4] + sp_toks)
                a2 = p.act(tD[:, :], tB[:, :], AF.Exp,
                           scale=sp16[:, m:m + 1], after=[v4] + sp_toks)
                z1 = p.vec("tensor_scalar", tD[:, :], tD[:, :], -1.0,
                           1.000001, ALU.mult, ALU.add, after=[a2])
                z2 = p.act(tD[:, :], tD[:, :], AF.Ln, after=[z1])
                z3 = p.act(tD[:, :], tD[:, :], AF.Exp, scale=0.5, after=[z2])
                x1 = p.vec("tensor_mul", tB[:, :], tD[:, :], tA[:, :],
                           after=[z3, v2])
                x2 = p.vec("tensor_mul", tB[:, :], tB[:, :], y_g[m][:, gsl],
                           after=[x1])
                s1 = p.vec("tensor_tensor_scan", h_st[m][:, tsl], tC[:, :],
                           tB[:, :], hcarry[:, m:m + 1], ALU.mult, ALU.add,
                           after=[x2, a1])
                s2 = p.vec("tensor_tensor_scan", cp_st[m][:, tsl], tC[:, :],
                           zeros512[:, :], cpcarry[:, m:m + 1], ALU.mult,
                           ALU.add, after=[s1, tz])
                s3 = p.vec("tensor_copy", hcarry[:, m:m + 1],
                           h_st[m][:, tsl.stop - 1:tsl.stop], after=[s1])
                s4 = p.vec("tensor_copy", cpcarry[:, m:m + 1],
                           cp_st[m][:, tsl.stop - 1:tsl.stop], after=[s2])
                last_v = s4
                if g == 1 and seg == NSEG - 1:
                    scan_done += [s3, s4]
            if m + 2 < NC_H:
                si_s.load(slab_ap(lruWiT, m + 2))
                sg_s.load(slab_ap(lruWgT, m + 2))

    cg_in, cg_out = dram["cg_in"], dram["cg_out"]
    tpub = p.dma("gpsimd", cg_in.ap(), hcarry[:, :], after=scan_done)
    tcoll = p.coll("AllGather", ALU.bypass, PAIRS,
                   [cg_in.ap().opt()], [cg_out.ap().opt()], after=[tpub])
    tback = p.dma("gpsimd", carrycol[:, :], cg_out[0:128, :], after=[tcoll])
    tcc = p.vec("tensor_scalar_mul", carrycol[:, :], carrycol[:, :],
                is_odd[:, :], after=[tback, tl4])
    if "dbg_h" in dram:
        for m_ in range(NC_H):
            p.dma("sync", dram["dbg_h"][128 * m_:128 * m_ + 128, :],
                  h_st[m_][:, :], after=scan_done, sem="st")
            p.dma("sync", dram["dbg_cp"][128 * m_:128 * m_ + 128, :],
                  cp_st[m_][:, :], after=scan_done, sem="st")
        p.dma("sync", dram["dbg_carry"].ap(), carrycol[:, :], after=[tcc],
              sem="st")

    ln_guard = []
    for g in range(2):
        p.barrier()
        xcol0 = g * G
        xb_toks = [p.dma("gpsimd", xb[k][:, :],
                         xfm[128 * k:128 * k + 128, xcol0:xcol0 + G + 3])
                   for k in range(NC_D)]
        gs = Streamer(p, wslab)
        gs.load(slab_ap(WgT, 0))
        gs.load(slab_ap(WgT, 1))
        warm4 = p.mm(p0[:, :], [
            (wslab[0][:, 128 * k:128 * k + 128],
             xb[k][:, 3:515]) for k in range(NC_D)],
            after=[gs.load_tok[0]] + xb_toks)
        gs.note_use(0, warm4)
        ev_gate = None
        prev_f2 = None
        mm_done = []
        for m in range(NC_H):
            bi = m % 2
            for seg in range(NSEG):
                gsl = slice(seg * 512, seg * 512 + 512)
                tsl = slice(g * G + seg * 512, g * G + seg * 512 + 512)
                pr = [gs.load_tok[bi]] + xb_toks
                if ev_gate is not None:
                    pr.append(ev_gate)
                tmm = p.mm(p0[:, :], [
                    (wslab[bi][:, 128 * k:128 * k + 128],
                     xb[k][:, 3 + seg * 512: 3 + seg * 512 + 512])
                    for k in range(NC_D)], after=pr)
                gs.note_use(bi, tmm)
                ga = [tmm] + ([prev_f2] if prev_f2 else [])
                ev_gate = p.act(gg[:, :], p0[:, :], AF.Gelu, after=ga)
                f1 = p.vec("scalar_tensor_tensor", tA[:, :],
                           cp_st[m][:, tsl], carrycol[:, m:m + 1],
                           h_st[m][:, tsl], ALU.mult, ALU.add,
                           after=[tcc, ev_gate])
                f2 = p.vec("tensor_mul", y_g[m][:, gsl], gg[:, :],
                           tA[:, :], after=[f1])
                prev_f2 = f2
                mm_done.append(f2)
            if m + 2 < NC_H:
                gs.load(slab_ap(WgT, m + 2))
        if "dbg_m" in dram:
            for m_ in range(NC_H):
                p.dma("sync", dram["dbg_m"][128 * m_:128 * m_ + 128,
                                            g * G:(g + 1) * G],
                      y_g[m_][:, :], after=mm_done, sem="st")
        os_ = Streamer(p, wosl)
        for seg in range(NSEG):
            os_.load(slab_ap(WoT, 0), after=mm_done[-2:])
            os_.load(slab_ap(WoT, 1))
            warm5 = p.mm(p1[:, :], [
                (wosl[0][:, 128 * k:128 * k + 128], y_g[k][:, 0:512])
                for k in range(NC_H)],
                after=[os_.load_tok[0]] + mm_done + ln_guard)
            os_.note_use(0, warm5)
            ev_out = None
            r_done = []
            gsl = slice(seg * 512, seg * 512 + 512)
            for d in range(NC_D):
                bi = d % 2
                pr = [os_.load_tok[bi]] + mm_done + ln_guard
                if ev_out is not None:
                    pr.append(ev_out)
                tmm = p.mm(p1[:, :], [
                    (wosl[bi][:, 128 * k:128 * k + 128], y_g[k][:, gsl])
                    for k in range(NC_H)], after=pr)
                os_.note_use(bi, tmm)
                tx = p.dma("sync", xres[:, :],
                           xfm[128 * d:128 * d + 128,
                               3 + g * G + seg * 512:
                               3 + g * G + seg * 512 + 512],
                           after=[ev_out] if ev_out else [])
                ev_out = p.vec("tensor_add", r_tiles[d][:, 0:512], p1[:, :],
                               xres[:, :], after=[tmm, tx] + ln_guard)
                r_done.append(ev_out)
                if d + 2 < NC_D:
                    os_.load(slab_ap(WoT, d + 2))

            def writer(dd, _s, src_ap, toks, g=g, seg=seg):
                return p.dma("sync",
                             y1[128 * dd:128 * dd + 128,
                                g * G + seg * 512: g * G + seg * 512 + 512],
                             src_ap, after=toks, sem="st")
            stoks, ln_guard = layernorm_fm(
                p, ln_scr, r_tiles, 512, ones_col, ones_row,
                psA, psB, p2, p3, r_done, writer, guard=ln_guard)
            ln_guard = ln_guard + stoks

    p.barrier("sync")



def cols128(v):
    v = np.asarray(v, dtype=np.float32)
    return np.ascontiguousarray(v.reshape(-1, 128).T)


def pack_hawk_inputs(inputs):
    x = np.asarray(inputs["L0_attn_x"], dtype=np.float32)
    WiT = np.ascontiguousarray(np.asarray(inputs["L0_attn_Wi"], dtype=np.float32).T)
    WgT = np.ascontiguousarray(np.asarray(inputs["L0_attn_Wg"], dtype=np.float32).T)
    lruWiT = np.ascontiguousarray(np.asarray(inputs["L0_attn_lru_Wi"], dtype=np.float32).T)
    lruWgT = np.ascontiguousarray(np.asarray(inputs["L0_attn_lru_Wg"], dtype=np.float32).T)
    WoT = np.ascontiguousarray(np.asarray(inputs["L0_attn_Wo"], dtype=np.float32).T)
    cw = np.asarray(inputs["L0_attn_conv_w"], dtype=np.float32)[:, 0, :]
    convw_cols = np.ascontiguousarray(
        cw.reshape(NC_H, 128, 4).transpose(1, 2, 0).reshape(128, 4 * NC_H))
    convb_cols = cols128(inputs["L0_attn_conv_b"])
    lam_cols = cols128(inputs["L0_attn_lam"])
    bi_cols = cols128(inputs["L0_attn_lru_bi"])
    bg_cols = cols128(inputs["L0_attn_lru_bg"])

    maps = []
    for c in range(8):
        b, h = c // 2, c % 2
        own = x[b, h * T:(h + 1) * T, :].T
        if h == 0:
            halo = np.zeros((D, 3), np.float32)
        else:
            halo = x[b, h * T - 3:h * T, :].T
        xfm = np.ascontiguousarray(np.concatenate([halo, own], axis=1))
        is_odd = np.full((128, 1), float(h), np.float32)
        maps.append(dict(xfm=xfm, WiT=WiT, WgT=WgT, lruWiT=lruWiT,
                         lruWgT=lruWgT, WoT=WoT, convw_cols=convw_cols,
                         convb_cols=convb_cols, lam_cols=lam_cols,
                         bi_cols=bi_cols, bg_cols=bg_cols, is_odd=is_odd))
    return maps


def declare_hawk_dram(nc, stream_in_kind="ExternalInput",
                      stream_out_kind="ExternalOutput"):
    dram = {}
    dram["xfm"] = nc.dram_tensor("xfm", [D, T + 3], F32, kind=stream_in_kind)
    for nm, shp in [("WiT", [D, HID]), ("WgT", [D, HID]),
                    ("lruWiT", [HID, HID]), ("lruWgT", [HID, HID]),
                    ("WoT", [HID, D]), ("convw_cols", [128, 4 * NC_H]),
                    ("convb_cols", [128, NC_H]), ("lam_cols", [128, NC_H]),
                    ("bi_cols", [128, NC_H]), ("bg_cols", [128, NC_H]),
                    ("is_odd", [128, 1])]:
        dram[nm] = nc.dram_tensor(nm, shp, F32, kind="ExternalInput")
    dram["y1"] = nc.dram_tensor("y1", [D, T], F32, kind=stream_out_kind)
    dram["cg_in"] = nc.dram_tensor("cg_in", [128, NC_H], F32)
    dram["cg_out"] = nc.dram_tensor("cg_out", [256, NC_H], F32)
    return dram


def declare_hawk_dbg(nc, dram):
    for nm, shp in [("dbg_y", [HID, T]), ("dbg_h", [HID, T]),
                    ("dbg_cp", [HID, T]), ("dbg_m", [HID, T]),
                    ("dbg_carry", [128, NC_H])]:
        dram[nm] = nc.dram_tensor(nm, shp, BF16 if nm != "dbg_carry" else F32,
                                  kind="ExternalOutput")
    dram["dbg_u"] = nc.dram_tensor("dbg_u", [128, 515], F32,
                                   kind="ExternalOutput")
    dram["dbg_y2"] = nc.dram_tensor("dbg_y2", [128, T], BF16,
                                    kind="ExternalOutput")
    dram["dbg_tA"] = nc.dram_tensor("dbg_tA", [128, 512], F32,
                                    kind="ExternalOutput")


def run_hawk(inputs, dbg=False):
    nc = bass.Bass("TRN2", target_bir_lowering=False)
    with ExitStack() as ctx:
        dram = declare_hawk_dram(nc)
        if dbg:
            declare_hawk_dbg(nc, dram)
        p = P(nc, ctx)
        build_hawk(nc, ctx, p, dram)
        p.finalize()
    maps = pack_hawk_inputs(inputs)
    return run_bass_kernel_spmd(nc, maps, core_ids=list(range(8)))


def _gather_fm(res, key):
    out = np.zeros((4, 4096, D), np.float32)
    for c in range(8):
        b, h = c // 2, c % 2
        out[b, h * T:(h + 1) * T, :] = res.results[c][key].T
    return out



def build_ffn_attn(nc, ctx, p, dram):
    NW = 4
    W = 512
    stream_in = dram["fa_in"]
    stream_out = dram["fa_out"]
    wqkv = dram["fa_wqkv"]
    wo = dram["fa_wo"]

    with ExitStack() as c1:
        xb = [sb(c1, nc, f"fa_xb{k}", [128, T + 1], BF16) for k in range(NC_D)]
        kblk = [sb(c1, nc, f"fa_k{h}", [128, 2560], BF16) for h in range(8)]
        qblk = [sb(c1, nc, f"fa_q{h}", [128, T], BF16) for h in range(8)]
        vt = [sb(c1, nc, f"fa_v{i}", [128, 520], BF16) for i in range(20)]
        otile = [sb(c1, nc, f"fa_o{i}", [128, T], BF16) for i in range(4)]
        pmax = sb(c1, nc, "fa_pmax", [1, 40])
        qcs = sb(c1, nc, "fa_qcs_sb", [128, T], BF16)
        qss = sb(c1, nc, "fa_qss_sb", [128, T], BF16)
        kci = sb(c1, nc, "fa_kci_sb", [128, 2560], BF16)
        ksi = sb(c1, nc, "fa_ksi_sb", [128, 2560], BF16)
        perm = sb(c1, nc, "fa_perm_sb", [128, 128], BF16)
        mask0 = sb(c1, nc, "fa_m0_sb", [128, 512], BF16)
        mask4 = sb(c1, nc, "fa_m4_sb", [128, 512], BF16)
        hd_a = sb(c1, nc, "fa_hda_sb", [128, 1], BF16)
        hd_b = sb(c1, nc, "fa_hdb_sb", [128, 1], BF16)
        ind_a = sb(c1, nc, "fa_inda_sb", [1, 128], BF16)
        ind_b = sb(c1, nc, "fa_indb_sb", [1, 128], BF16)
        ones64r = sb(c1, nc, "fa_o64r", [1, 64], BF16)
        ones64c = sb(c1, nc, "fa_o64c", [64, 1], BF16)
        qs8col = sb(c1, nc, "fa_qs8_sb", [128, 1])
        kscol = sb(c1, nc, "fa_ks_sb", [128, 1])
        is_odd = sb(c1, nc, "fa_odd_sb", [128, 1])
        qraw = sb(c1, nc, "fa_qraw", [128, 512])
        qsc1 = sb(c1, nc, "fa_qsc1", [128, 512])
        qsc2 = sb(c1, nc, "fa_qsc2", [128, 512])
        qsqb = sb(c1, nc, "fa_qsqb", [128, 512], BF16)
        qnb = sb(c1, nc, "fa_qnb", [128, 512], BF16)
        rowA = sb(c1, nc, "fa_rowA", [1, 1024])
        rowB = sb(c1, nc, "fa_rowB", [1, 1024])
        rowBf = sb(c1, nc, "fa_rowBf", [1, 1024], BF16)
        mcol = sb(c1, nc, "fa_mcol", [128, 1])
        mcolb = sb(c1, nc, "fa_mcolb", [128, 1], BF16)
        halocol = sb(c1, nc, "fa_hcol", [128, 8])
        probs = [sb(c1, nc, f"fa_pr{d}", [128, 512], BF16) for d in range(5)]
        invb = sb(c1, nc, "fa_invb", [1, 512], BF16)
        invf = sb(c1, nc, "fa_invf", [1, 512])
        osc = sb(c1, nc, "fa_osc", [128, 512])
        xres = sb(c1, nc, "fa_xres", [128, 512])
        wsl = [sb(c1, nc, f"fa_wsl{i}", [128, 1024], BF16) for i in range(2)]
        wvsl = sb(c1, nc, "fa_wvsl", [128, 4096], BF16)
        wosl = [sb(c1, nc, f"fa_wos{i}", [128, 512], BF16) for i in range(2)]

        p0 = psum(c1, nc, "fa_p0", [128, 512])
        p1 = psum(c1, nc, "fa_p1", [128, 512])
        p2 = psum(c1, nc, "fa_p2", [128, 512])
        pav = psum(c1, nc, "fa_pav", [128, 512])
        pbc = psum(c1, nc, "fa_pbc", [128, 512])
        pC = psum(c1, nc, "fa_pC", [128, 512])
        prow = psum(c1, nc, "fa_prow", [1, 1024])

        ct = []
        ct.append(p.dma("gpsimd", qcs[:, :], dram["fa_qcs"].ap()))
        ct.append(p.dma("gpsimd", qss[:, :], dram["fa_qss"].ap()))
        ct.append(p.dma("gpsimd", kci[:, :], dram["fa_kci"].ap()))
        ct.append(p.dma("gpsimd", ksi[:, :], dram["fa_ksi"].ap()))
        ct.append(p.dma("sync", perm[:, :], dram["fa_perm"].ap()))
        ct.append(p.dma("sync", mask0[:, :], dram["fa_mask0"].ap()))
        ct.append(p.dma("sync", mask4[:, :], dram["fa_mask4"].ap()))
        ct.append(p.dma("sync", hd_a[:, :], dram["fa_hda"].ap()))
        ct.append(p.dma("sync", hd_b[:, :], dram["fa_hdb"].ap()))
        ct.append(p.dma("sync", ind_a[:, :], dram["fa_inda"].ap()))
        ct.append(p.dma("sync", ind_b[:, :], dram["fa_indb"].ap()))
        ct.append(p.dma("sync", qs8col[:, :], dram["fa_qs8"].ap()))
        ct.append(p.dma("sync", kscol[:, :], dram["fa_ks"].ap()))
        ct.append(p.dma("sync", is_odd[:, :], dram["fa_odd"].ap()))
        tone = p.vec("memset", ones64r[:, :], 1.0)
        tone2 = p.vec("memset", ones64c[:, :], 1.0)
        ct.append(tone)
        for h in range(8):
            ct.append(p.dma("gpsimd", kblk[h][65:66, :], dram["fa_kpad"].ap()))
            ct.append(p.dma("sync", qblk[h][65:66, :], dram["fa_neg1"].ap()))
            ct.append(p.vec("memset", kblk[h][64:65, :], 1.0))
        xb_toks = []
        for k in range(NC_D):
            xb_toks.append(p.dma("gpsimd", xb[k][:, 1:T + 1],
                                 stream_in[128 * k:128 * k + 128, :]))
        thc = p.dma("gpsimd", halocol[:, :], dram["fa_cgat"][0:128, :],
                    after=[dram["fa_cgat_tok"]])
        for k in range(NC_D):
            xb_toks.append(p.vec("tensor_scalar_mul", xb[k][:, 0:1],
                                 halocol[:, k:k + 1], is_odd[:, :],
                                 after=[thc, ct[13]]))
        p.barrier("vector")

        def xsh(k, c0, n):
            if k < 4:
                return xb[k][:, 1 + c0:1 + c0 + n]
            return xb[k][:, c0:c0 + n]

        ws = Streamer(p, wsl)
        ws.load(slab_ap(wqkv, 0), after=[])
        warm = p.mm(p0[:, :], [
            (wsl[0][:, 128 * k:128 * k + 128], xsh(k, 0, 512))
            for k in range(NC_D)], after=[ws.load_tok[0]] + xb_toks)
        ws.note_use(0, warm)
        ws.load(slab_ap(wqkv, 1))

        def qk_pipeline(F, g, is_q, ev_prev):
            bi = F % 2
            gsl = slice(g * 512, g * 512 + 512)
            pr = [ws.load_tok[bi]] + xb_toks
            if ev_prev[0] is not None:
                pr.append(ev_prev[0])
            tmm = p.mm(p0[:, :], [
                (wsl[bi][:, 128 * k:128 * k + 128], xsh(k, g * 512, 512))
                for k in range(NC_D)], after=pr)
            ws.note_use(bi, tmm)
            ev = p.act(qraw[:, :], p0[:, :], AF.Identity,
                       after=[tmm] + ev_prev[1:])
            ev_prev[0] = ev
            sq = p.act(qsqb[:, :], qraw[:, :], AF.Square, after=[ev])
            n1 = p.mm(prow[0:1, 0:512], [(hd_a[:, :], qsqb[:, :])],
                      after=[sq] + ct)
            n2 = p.mm(prow[0:1, 512:1024], [(hd_b[:, :], qsqb[:, :])],
                      after=[sq] + ct)
            li = p.act(rowA[0:1, :], prow[0:1, :], AF.Ln, bias=1e-24,
                       after=[n1, n2])
            iv = p.act(rowA[0:1, :], rowA[0:1, :], AF.Exp, scale=-0.5,
                       after=[li])
            ivb = p.vec("tensor_copy", rowBf[0:1, :], rowA[0:1, :],
                        after=[iv])
            b1 = p.mm(pbc[:, :], [(ind_a[:, :], rowBf[0:1, 0:512])],
                      after=[ivb], stop=False)
            b2 = p.mm(pbc[:, :], [(ind_b[:, :], rowBf[0:1, 512:1024])],
                      after=[ivb], start=False)
            scol = qs8col if is_q else kscol
            v1 = p.vec("tensor_scalar_mul", qsc1[:, :], qraw[:, :],
                       scol[:, :], after=[ev] + ct)
            del ev_prev[1:]
            ev_prev.append(v1)
            v2 = p.vec("tensor_copy", qnb[:, :], qsc1[:, :], after=[v1])
            rp = p.mm(p1[:, :], [(perm[:, :], qnb[:, :])], after=[v2] + ct)
            ctab = qcs if is_q else kci
            stab = qss if is_q else ksi
            cview = ctab[:, gsl] if is_q else ctab[:, 512 + g * 512:512 + g * 512 + 512]
            sview = stab[:, gsl] if is_q else stab[:, 512 + g * 512:512 + g * 512 + 512]
            v3 = p.vec("tensor_mul", qsc1[:, :], qsc1[:, :], cview,
                       after=[v2] + ct)
            v4 = p.vec("tensor_mul", qsc2[:, :], p1[:, :], sview,
                       after=[rp] + ct)
            v5 = p.vec("tensor_add", qsc1[:, :], qsc1[:, :], qsc2[:, :],
                       after=[v3, v4])
            h0 = 2 * (F % 4)
            if is_q:
                dsts = [qblk[h0][0:64, gsl], qblk[h0 + 1][0:64, gsl]]
            else:
                ksl = slice(512 + g * 512, 512 + g * 512 + 512)
                dsts = [kblk[h0][0:64, ksl], kblk[h0 + 1][0:64, ksl]]
            v6 = p.vec("tensor_mul", qsc1[:, :], qsc1[:, :], pbc[:, :],
                       after=[v5, b1, b2])
            c1t = p.vec("tensor_copy", dsts[0], qsc1[0:64, :], after=[v6])
            c2t = p.vec("tensor_copy", dsts[1], qsc1[64:128, :], after=[v6])
            toks = [c1t, c2t, v6]
            if not is_q:
                pass
            return toks

        ev_prev = [None]
        qk_done = []
        for F in range(8):
            is_q = F < 4
            for g in range(4):
                qk_done += qk_pipeline(F, g, is_q, ev_prev)
            if F + 2 < 8:
                ws.load(slab_ap(wqkv, F + 2))

        kn_done = []

        wvt = p.dma("gpsimd", wvsl[:, :].rearrange("q (a j) -> q a j", j=512),
                    wqkv[:, 1024:1536].rearrange("(a q) j -> q a j", q=128))
        ev_v = None
        v_done = []
        for tc in range(16):
            pr = [wvt] + xb_toks
            if ev_v is not None:
                pr.append(ev_v)
            tmm = p.mm(p2[:, :], [
                (xsh(k, 128 * tc, 128), wvsl[:, 512 * k:512 * k + 512])
                for k in range(NC_D)], after=pr)
            ev_v = p.act(
                vt[4 + tc][:, :].rearrange("q (h c) -> q h c", c=65)[:, :, 0:64],
                p2[:, :].rearrange("q (h c) -> q h c", c=64), AF.Identity,
                after=[tmm])
            mz = p.vec("memset",
                       vt[4 + tc][:, :].rearrange("q (h c) -> q h c", c=65)[:, :, 64:65],
                       1.0)
            v_done += [ev_v, mz]

        kpub, kgat = dram["fa_kpub"], dram["fa_kgat"]
        vpub, vgat = dram["fa_vpub"], dram["fa_vgat"]
        pub_toks = []
        for h in range(8):
            pub_toks.append(p.dma("gpsimd", kpub[64 * h:64 * h + 64, :],
                                  kblk[h][0:64, 2048:2560],
                                  after=qk_done))
        for i in range(4):
            pub_toks.append(p.dma("gpsimd", vpub[128 * i:128 * i + 128, :],
                                  vt[16 + i][:, :], after=v_done))
        ck = p.coll("AllGather", ALU.bypass, PAIRS, [kpub.ap().opt()],
                    [kgat.ap().opt()], after=pub_toks)
        cv = p.coll("AllGather", ALU.bypass, PAIRS, [vpub.ap().opt()],
                    [vgat.ap().opt()], after=pub_toks)
        back_toks = []
        for h in range(8):
            back_toks.append(p.dma("gpsimd", kblk[h][0:64, 0:512],
                                   kgat[64 * h:64 * h + 64, :], after=[ck]))
        for i in range(4):
            back_toks.append(p.dma("gpsimd", vt[i][:, :],
                                   vgat[128 * i:128 * i + 128, :], after=[cv]))
        hz = []
        for h in range(8):
            hz.append(p.vec("tensor_scalar_mul", kblk[h][0:64, 0:512],
                            kblk[h][0:64, 0:512], is_odd[0:64, :],
                            after=back_toks + ct))
        kn_done += hz

        p.barrier()
        att_done = []
        wb_tok = {}
        bnd_prev = []
        for w in range(NW):
            for h in range(8):
                i = w * 8 + h
                b0 = 64 * (i % 2)
                qsl = slice(512 * w, 512 * w + 512)
                t1 = p.vec("reduce_max", mcol[b0:b0 + 64, :],
                           kblk[h][0:64, 512 * w:512 * w + 1024], AX.X,
                           apply_absolute_value=True,
                           after=kn_done + bnd_prev[-2:-1])
                t2 = p.vec("tensor_copy", mcolb[b0:b0 + 64, :],
                           mcol[b0:b0 + 64, :], after=[t1])
                sq = p.act(qsqb[b0:b0 + 64, :], qblk[h][0:64, qsl], AF.Abs,
                           after=qk_done + bnd_prev[-2:-1])
                nm = p.mm(prow[0:1, 512 * (i % 2):512 * (i % 2) + 512],
                          [(mcolb[b0:b0 + 64, :], qsqb[b0:b0 + 64, :])],
                          after=[sq, t2] + bnd_prev[-2:-1])
                wb_ = p.vec("tensor_scalar_mul", qblk[h][64:65, qsl],
                            prow[0:1, 512 * (i % 2):512 * (i % 2) + 512],
                            -1.0, after=[nm])
                wb_tok[(w, h)] = wb_
                bnd_prev.append(wb_)

        AB = [[p0, p1], [pav, pbc]]
        CCb = [p2, pC]
        probs_q = [probs, probs]
        osc_q = [osc, qsc1]
        invf_q = [invf, qsc2[0:1, :]]
        invb_q = [invb, qnb[0:1, :]]
        ev_bank_last = {(q, b): [] for q in (0, 1) for b in (0, 1)}
        av_last = [[], []]
        fm_last = [[], []]
        eo_last = [[], []]
        pend_tail = None
        for w in range(NW):
            for h in range(8):
                i = w * 8 + h
                q = i % 2
                A, B = AB[q]
                C = CCb[q]
                prb = probs_q[q]
                qsl = slice(512 * w, 512 * w + 512)
                ev_toks = []
                for d in range(5):
                    pd = A if d % 2 == 0 else B
                    bkey = (q, d % 2)
                    mm_toks = []
                    for t_ in range(4):
                        kc = 512 * w + 128 * (t_ + d)
                        aft = [wb_tok[(w, h)]] + ev_bank_last[bkey][-1:]
                        if d == 0:
                            aft += fm_last[q]
                        mt = p.mm(pd[:, 128 * t_:128 * t_ + 128],
                                  [(kblk[h][0:66, kc:kc + 128],
                                    qblk[h][0:66, 512 * w + 128 * t_:
                                            512 * w + 128 * t_ + 128])],
                                  after=aft)
                        mm_toks.append(mt)
                    ev = p.act(prb[d][:, :], pd[:, :], AF.Exp,
                               after=mm_toks + av_last[0] + av_last[1])
                    ev_bank_last[bkey] = [ev]
                    ev_toks.append(ev)
                m0 = p.vec("tensor_mul", prb[0][:, :], prb[0][:, :],
                           mask0[:, :], after=[ev_toks[0]] + ct)
                m4 = p.vec("tensor_mul", prb[4][:, :], prb[4][:, :],
                           mask4[:, :], after=[ev_toks[4]] + ct)
                av_toks = []
                for t_ in range(4):
                    pairs = []
                    for d in range(5):
                        kc = 512 * w + 128 * (t_ + d)
                        pairs.append((vt[kc // 128][:, 65 * h:65 * h + 65],
                                      prb[d][:, 128 * t_:128 * t_ + 128]))
                    at = p.mm(C[0:65, 128 * t_:128 * t_ + 128], pairs,
                              after=ev_toks + [m0, m4] + eo_last[q])
                    av_toks.append(at)
                av_last[q] = av_toks[-1:]
                rc = p.vec("reciprocal", invf_q[q][0:1, :], C[64:65, :],
                           after=av_toks)
                rb = p.vec("tensor_copy", invb_q[q][0:1, :],
                           invf_q[q][0:1, :], after=[rc])

                def tail(q=q, h=h, w=w, qsl=qsl, rb=rb, A=A, C=C,
                         eA=ev_bank_last[(q, 0)]):
                    bc = p.mm(A[0:64, :],
                              [(ones64r[:, :], invb_q[q][0:1, :])],
                              after=[rb] + eA)
                    eo = p.act(osc_q[q][0:64, :], C[0:64, :], AF.Identity,
                               after=[bc])
                    fm = p.vec("tensor_mul",
                               otile[h // 2][64 * (h % 2):
                                             64 * (h % 2) + 64, qsl],
                               osc_q[q][0:64, :], A[0:64, :],
                               after=[eo, bc])
                    fm_last[q] = [fm]
                    eo_last[q] = [eo]
                    att_done.append(fm)
                if pend_tail is not None:
                    pend_tail()
                pend_tail = tail
        pend_tail()

        if "fa_dbgo" in dram:
            for i in range(4):
                p.dma("sync", dram["fa_dbgo"][128 * i:128 * i + 128, :],
                      otile[i][:, :], after=att_done, sem="st")
            for h in range(8):
                p.dma("sync", dram["fa_dbgk"][64 * h:64 * h + 64, :],
                      kblk[h][0:64, 0:1024], after=kn_done + qk_done, sem="st")
            p.dma("sync", dram["fa_dbgv"].ap(), vt[0][:, :],
                  after=back_toks, sem="st")
        p.barrier()
        os_ = Streamer(p, wosl)
        os_.load(slab_ap(wo, 0))
        warm2 = p.mm(p0[:, :], [
            (wosl[0][:, 128 * k:128 * k + 128], otile[k][:, 0:512])
            for k in range(4)], after=[os_.load_tok[0]] + att_done)
        os_.note_use(0, warm2)
        os_.load(slab_ap(wo, 1))
        ev_out = None
        st_prev = []
        for d in range(NC_D):
            bi = d % 2
            for g in range(4):
                gsl = slice(g * 512, g * 512 + 512)
                pr = [os_.load_tok[bi]] + att_done
                if ev_out is not None:
                    pr.append(ev_out)
                tmm = p.mm(p0[:, :], [
                    (wosl[bi][:, 128 * k:128 * k + 128], otile[k][:, gsl])
                    for k in range(4)], after=pr)
                os_.note_use(bi, tmm)
                tx = p.dma("sync", xres[:, :],
                           stream_in[128 * d:128 * d + 128, gsl],
                           after=[ev_out] if ev_out else [])
                ev_out = p.vec("tensor_add", osc[:, :], p0[:, :], xres[:, :],
                               after=[tmm, tx] + st_prev)
                st_prev = [p.dma("sync",
                                 stream_out[128 * d:128 * d + 128, gsl],
                                 osc[:, :], after=[ev_out], sem="st")]
            if d + 2 < NC_D:
                os_.load(slab_ap(wo, d + 2))
        p.barrier("sync")
    return


def make_ffn_attn_consts(inputs, sub):
    DH = 64
    WIN = 512
    n = 4096
    half = np.arange(0, DH, 2, dtype=np.float64)
    inv_freq = 1.0 / 10000.0 ** (half / DH)
    base = (half + 0.4 * DH) / (1.4 * DH)

    def tables(t0, ntok):
        t = np.arange(t0, t0 + ntok, dtype=np.float64)
        fr = t[:, None] * inv_freq[None, :]
        fr = np.concatenate([fr, fr], 1)
        power = (t - n // 2) / (WIN // 2)
        sc = base[None, :] ** power[:, None]
        sc = np.concatenate([sc, sc], 1)
        cos, sin = np.cos(fr), np.sin(fr)
        return cos, sin, sc

    def fm(a):
        b = np.ascontiguousarray(a.T.astype(np.float32))
        return np.concatenate([b, b], 0)

    qs = np.asarray(inputs["L0_ffn_q_scale"], np.float32)[sub]
    ks = np.asarray(inputs["L0_ffn_k_scale"], np.float32)[sub]
    qs8 = np.tile(qs * 8.0, 2)[:, None].astype(np.float32)
    ksc = np.tile(ks, 2)[:, None].astype(np.float32)
    P = np.zeros((128, 128), np.float32)
    for i in range(128):
        bb = (i // 64) * 64
        j = i % 64
        if j < 32:
            P[bb + j + 32, i] = -1.0
        else:
            P[bb + j - 32, i] = 1.0
    ql = np.arange(512) % 128
    r = np.arange(128)
    mask0 = (r[:, None] >= ql[None, :]).astype(np.float32)
    mask4 = (r[:, None] <= ql[None, :]).astype(np.float32)
    hd_a = np.zeros((128, 1), np.float32); hd_a[0:64] = 1
    hd_b = np.zeros((128, 1), np.float32); hd_b[64:128] = 1
    ind_a = np.zeros((1, 128), np.float32); ind_a[0, 0:64] = 1
    ind_b = np.zeros((1, 128), np.float32); ind_b[0, 64:128] = 1
    neg1 = np.full((1, T), -1.0, np.float32)
    shared = dict(fa_perm=P.astype(BF), fa_mask0=mask0.astype(BF),
                  fa_mask4=mask4.astype(BF), fa_hda=hd_a.astype(BF),
                  fa_hdb=hd_b.astype(BF), fa_inda=ind_a.astype(BF),
                  fa_indb=ind_b.astype(BF), fa_qs8=qs8, fa_ks=ksc,
                  fa_neg1=neg1.astype(BF))
    per_core = []
    for c in range(8):
        b, h = c // 2, c % 2
        S0 = h * T
        cos, sin, sc = tables(S0, T)
        qcs = fm(cos * sc); qss = fm(sin * sc)
        cos, sin, sc = tables(S0 - 512, 2560)
        kci = fm(cos / sc); ksi = fm(sin / sc)
        kpad = np.zeros((1, 2560), np.float32)
        if h == 0:
            kpad[0, 0:512] = 60000.0
        per_core.append(dict(fa_qcs=qcs, fa_qss=qss, fa_kci=kci, fa_ksi=ksi,
                             fa_kpad=kpad,
                             fa_odd=np.full((128, 1), float(h), np.float32)))
    return shared, per_core


def declare_ffn_attn_dram(nc, p, dbg=False, pfx="", in_h=None, out_h=None):
    dram = {}
    dram["fa_in"] = in_h if in_h is not None else nc.dram_tensor(
        pfx + "fa_in", [D, T], F32, kind="ExternalInput")
    dram["fa_out"] = out_h if out_h is not None else nc.dram_tensor(
        pfx + "fa_out", [D, T], F32, kind="ExternalOutput")
    dram["fa_wqkv"] = nc.dram_tensor(pfx + "fa_wqkv", [1024, 1536], F32,
                                     kind="ExternalInput")
    dram["fa_wo"] = nc.dram_tensor(pfx + "fa_wo", [512, 1024], F32,
                                   kind="ExternalInput")
    for nm, shp, dt in [("fa_qcs", [128, T], F32), ("fa_qss", [128, T], F32),
                        ("fa_kci", [128, 2560], F32),
                        ("fa_ksi", [128, 2560], F32),
                        ("fa_perm", [128, 128], BF16),
                        ("fa_mask0", [128, 512], BF16),
                        ("fa_mask4", [128, 512], BF16),
                        ("fa_hda", [128, 1], BF16), ("fa_hdb", [128, 1], BF16),
                        ("fa_inda", [1, 128], BF16),
                        ("fa_indb", [1, 128], BF16),
                        ("fa_qs8", [128, 1], F32), ("fa_ks", [128, 1], F32),
                        ("fa_odd", [128, 1], F32),
                        ("fa_kpad", [1, 2560], F32),
                        ("fa_neg1", [1, T], BF16)]:
        dram[nm] = nc.dram_tensor(pfx + nm, shp, dt, kind="ExternalInput")
    dram["fa_kpub"] = nc.dram_tensor(pfx + "fa_kpub", [512, 512], BF16)
    dram["fa_kgat"] = nc.dram_tensor(pfx + "fa_kgat", [1024, 512], BF16)
    dram["fa_vpub"] = nc.dram_tensor(pfx + "fa_vpub", [512, 520], BF16)
    dram["fa_vgat"] = nc.dram_tensor(pfx + "fa_vgat", [1024, 520], BF16)
    if dbg:
        dram["fa_dbgo"] = nc.dram_tensor("fa_dbgo", [512, T], BF16,
                                         kind="ExternalOutput")
        dram["fa_dbgk"] = nc.dram_tensor("fa_dbgk", [512, 1024], BF16,
                                         kind="ExternalOutput")
        dram["fa_dbgv"] = nc.dram_tensor("fa_dbgv", [128, 520], BF16,
                                         kind="ExternalOutput")
    dram["fa_cpub"] = nc.dram_tensor(pfx + "fa_cpub", [128, 8], F32)
    dram["fa_cgat"] = nc.dram_tensor(pfx + "fa_cgat", [256, 8], F32)
    return dram


def build_ffn_attn_prologue(nc, ctx, p, dram):
    with ExitStack() as c1:
        col = sb(c1, nc, "fa_pcol", [128, 8])
        toks = []
        for k in range(NC_D):
            toks.append(p.dma("sync", col[:, k:k + 1],
                              dram["fa_in"][128 * k:128 * k + 128,
                                            T - 1:T],
                              allow_slow_non_contiguous=True))
        tp = p.dma("gpsimd", dram["fa_cpub"].ap(), col[:, :], after=toks)
        tc = p.coll("AllGather", ALU.bypass, PAIRS, [dram["fa_cpub"].ap().opt()],
                    [dram["fa_cgat"].ap().opt()], after=[tp])
        dram["fa_cgat_tok"] = tc


def run_ffn_attn(inputs, sub, stream_in_full):
    nc = bass.Bass("TRN2", target_bir_lowering=False)
    with ExitStack() as ctx:
        p = P(nc, ctx)
        dram = declare_ffn_attn_dram(nc, p)
        build_ffn_attn_prologue(nc, ctx, p, dram)
        build_ffn_attn(nc, ctx, p, dram)
        p.finalize()
    shared, per_core = make_ffn_attn_consts(inputs, sub)
    wqkv = np.ascontiguousarray(np.asarray(inputs["L0_ffn_wqkv"], np.float32)[sub])
    wo = np.ascontiguousarray(np.asarray(inputs["L0_ffn_wo"], np.float32)[sub])
    maps = []
    for c in range(8):
        b, h = c // 2, c % 2
        fa_in = np.ascontiguousarray(
            stream_in_full[b, h * T:(h + 1) * T, :].T.astype(np.float32))
        m = dict(fa_in=fa_in, fa_wqkv=wqkv, fa_wo=wo)
        m.update(shared)
        m.update(per_core[c])
        maps.append(m)
    return run_bass_kernel_spmd(nc, maps, core_ids=list(range(8)))



NJ = 22

def build_ffn_ffn(nc, ctx, p, dram, final_ln=False):
    stream_in = dram["ff_in"]
    stream_out = dram["ff_out"]
    w1a, w1g, w2p = dram["ff_w1a"], dram["ff_w1g"], dram["ff_w2p"]

    with ExitStack() as c1:
        h1 = [sb(c1, nc, f"ff_h1{k}", [128, T], BF16) for k in range(NC_D)]
        sqt = sb(c1, nc, "ff_sq", [128, 512], BF16)
        t1t = sb(c1, nc, "ff_t1", [128, 512])
        glt = sb(c1, nc, "ff_gl", [128, 512], BF16)
        att = sb(c1, nc, "ff_at", [128, 512], BF16)
        xres = sb(c1, nc, "ff_xres", [128, 512])
        xres2 = sb(c1, nc, "ff_xres2", [128, 512])
        osc = sb(c1, nc, "ff_osc", [128, 512])
        halocol = sb(c1, nc, "ff_hcol", [128, 8])
        is_odd = sb(c1, nc, "ff_odd_sb", [128, 1])
        lg = sb(c1, nc, "ff_lg_sb", [128, NC_D])
        lb = sb(c1, nc, "ff_lb_sb", [128, NC_D])
        b1a = sb(c1, nc, "ff_b1a_sb", [128, NJ])
        b1g = sb(c1, nc, "ff_b1g_sb", [128, NJ])
        b2c = sb(c1, nc, "ff_b2_sb", [128, NC_D])
        ones_col = sb(c1, nc, "ff_oc", [128, 1], BF16)
        ones_row = sb(c1, nc, "ff_or", [1, 128], BF16)
        r_tiles = [sb(c1, nc, f"ff_rt{d}", [128, 512]) for d in range(NC_D)]
        ln_scr = make_ln_scratch(c1, nc, "ffln")
        wsl_a = [sb(c1, nc, f"ff_wa{i}", [128, 1024], BF16) for i in range(2)]
        wsl_g = [sb(c1, nc, f"ff_wg{i}", [128, 1024], BF16) for i in range(2)]
        wsl_2 = [sb(c1, nc, f"ff_w2{i}", [128, 2816], BF16) for i in range(2)]

        p0 = psum(c1, nc, "ff_p0", [128, 512])
        p1 = psum(c1, nc, "ff_p1", [128, 512])
        pA = psum(c1, nc, "ff_pA", [128, 512])
        pB = psum(c1, nc, "ff_pB", [128, 512])
        psA = psum(c1, nc, "ff_psA", [1, 512])
        psB = psum(c1, nc, "ff_psB", [1, 512])

        ct = []
        ct.append(p.dma("sync", is_odd[:, :], dram["ff_odd"].ap()))
        ct.append(p.dma("sync", lg[:, :], dram["ff_lg"].ap()))
        ct.append(p.dma("sync", lb[:, :], dram["ff_lb"].ap()))
        ct.append(p.dma("sync", b1a[:, :], dram["ff_b1a"].ap()))
        ct.append(p.dma("sync", b1g[:, :], dram["ff_b1g"].ap()))
        ct.append(p.dma("sync", b2c[:, :], dram["ff_b2"].ap()))
        ct.append(p.vec("memset", ones_col[:, :], 1.0))
        ct.append(p.vec("memset", ones_row[:, :], 1.0))

        with ExitStack() as c2:
            xb = [sb(c2, nc, f"ff_xb{k}", [128, T + 1], BF16)
                  for k in range(NC_D)]
            xb_toks = []
            for k in range(NC_D):
                xb_toks.append(p.dma("gpsimd", xb[k][:, 1:T + 1],
                                     stream_in[128 * k:128 * k + 128, :]))
            thc = p.dma("gpsimd", halocol[:, :], dram["ff_cgat"][0:128, :],
                        after=[dram["ff_cgat_tok"]])
            for k in range(NC_D):
                xb_toks.append(p.vec("tensor_scalar_mul", xb[k][:, 0:1],
                                     halocol[:, k:k + 1], is_odd[:, :],
                                     after=[thc, ct[0]]))
            p.barrier("vector")

            def xsh(k, c0, n):
                if k < 4:
                    return xb[k][:, 1 + c0:1 + c0 + n]
                return xb[k][:, c0:c0 + n]

            warm = p.mm(psA[:, :], [(ones_col[:, :], xsh(k, 0, 512))
                                    for k in range(NC_D)],
                        after=xb_toks + ct)
            prev = [warm]
            for g in range(4):
                gsl = slice(g * 512, g * 512 + 512)
                ta = tb = None
                for d in range(NC_D):
                    sq = p.act(sqt[:, :], xsh(d, g * 512, 512), AF.Square,
                               after=xb_toks + prev)
                    ta = p.mm(psA[:, :], [(ones_col[:, :],
                                           xsh(d, g * 512, 512))],
                              after=xb_toks + prev, start=(d == 0),
                              stop=(d == NC_D - 1))
                    tb = p.mm(psB[:, :], [(ones_col[:, :], sqt[:, :])],
                              after=[sq], start=(d == 0),
                              stop=(d == NC_D - 1))
                    prev = [ta, tb]
                r0, r1_, r2, r3 = (ln_scr["r0"], ln_scr["r1"], ln_scr["r2"],
                                   ln_scr["r3"])
                brow, a_bf, b_bf = (ln_scr["brow"], ln_scr["a_bf"],
                                    ln_scr["b_bf"])
                t1 = p.act(r0[:, :], psA[0:1, :], AF.Identity, scale=1.0 / D,
                           after=[ta, tb])
                t2 = p.act(r1_[:, :], psB[0:1, :], AF.Identity,
                           scale=1.0 / D, after=[ta, tb])
                t3 = p.act(r2[:, :], r0[:, :], AF.Square, after=[t1])
                t4 = p.vec("tensor_sub", r1_[:, :], r1_[:, :], r2[:, :],
                           after=[t2, t3])
                t5 = p.act(r2[:, :], r1_[:, :], AF.Ln, bias=1e-5, after=[t4])
                t6 = p.act(r3[:, :], r2[:, :], AF.Exp, scale=-0.5,
                           after=[t5])
                t7 = p.vec("tensor_mul", brow[:, :], r0[:, :], r3[:, :],
                           after=[t6])
                t8 = p.vec("tensor_scalar_mul", brow[:, :], brow[:, :],
                           -1.0, after=[t7])
                t9 = p.vec("tensor_copy", a_bf[:, :], r3[:, :], after=[t6])
                t10 = p.vec("tensor_copy", b_bf[:, :], brow[:, :],
                            after=[t8])
                bA = p.mm(pA[:, :], [(ones_row[:, :], a_bf[:, :])],
                          after=[t9] + prev)
                bB = p.mm(pB[:, :], [(ones_row[:, :], b_bf[:, :])],
                          after=[t10])
                ht = None
                for d in range(NC_D):
                    v1 = p.vec("tensor_mul", t1t[:, :],
                               xsh(d, g * 512, 512), pA[:, :],
                               after=[bA, bB] + ([ht] if ht else []))
                    v2 = p.vec("tensor_add", t1t[:, :], t1t[:, :], pB[:, :],
                               after=[v1])
                    ht = p.act(h1[d][:, gsl], t1t[:, :], AF.Identity,
                               scale=lg[:, d:d + 1], bias=lb[:, d:d + 1],
                               after=[v2] + ct)
                prev = [ta, tb, ht, bA, bB]
            h1_done = list(prev)
            if "ff_dbgh" in dram:
                for k in range(NC_D):
                    p.dma("sync", dram["ff_dbgh"][128 * k:128 * k + 128, :],
                          h1[k][:, :], after=h1_done, sem="st")
            p.barrier()

        ag = [sb(c1, nc, f"ff_ag{j}", [128, T], BF16) for j in range(NJ)]
        sa = Streamer(p, wsl_a)
        sg_ = Streamer(p, wsl_g)
        sa.load(slab_ap(w1a, 0))
        sg_.load(slab_ap(w1g, 0))
        warm2 = p.mm(p0[:, :], [
            (wsl_a[0][:, 128 * k:128 * k + 128], h1[k][:, 0:512])
            for k in range(NC_D)], after=[sa.load_tok[0]] + h1_done)
        sa.note_use(0, warm2)
        sa.load(slab_ap(w1a, 1))
        sg_.load(slab_ap(w1g, 1))
        ev_a = ev_g = mul_prev = None
        ag_done = []
        for j in range(NJ):
            bi = j % 2
            for g in range(4):
                gsl = slice(g * 512, g * 512 + 512)
                pr = [sa.load_tok[bi]] + h1_done
                if ev_a is not None:
                    pr.append(ev_a)
                tma = p.mm(p0[:, :], [
                    (wsl_a[bi][:, 128 * k:128 * k + 128], h1[k][:, gsl])
                    for k in range(NC_D)], after=pr)
                sa.note_use(bi, tma)
                pr = [sg_.load_tok[bi]] + h1_done
                if ev_g is not None:
                    pr.append(ev_g)
                tmg = p.mm(p1[:, :], [
                    (wsl_g[bi][:, 128 * k:128 * k + 128], h1[k][:, gsl])
                    for k in range(NC_D)], after=pr)
                sg_.note_use(bi, tmg)
                ga = [tma] + ([mul_prev] if mul_prev else [])
                ev_a = p.act(att[:, :], p0[:, :], AF.Identity,
                             bias=b1a[:, j:j + 1], after=ga + ct)
                gg_ = [tmg] + ([mul_prev] if mul_prev else [])
                ev_g = p.act(glt[:, :], p1[:, :], AF.Gelu,
                             bias=b1g[:, j:j + 1], after=gg_ + ct)
                mul_prev = p.vec("tensor_mul", ag[j][:, gsl], att[:, :],
                                 glt[:, :], after=[ev_a, ev_g])
                ag_done.append(mul_prev)
            if j + 2 < NJ:
                sa.load(slab_ap(w1a, j + 2))
                sg_.load(slab_ap(w1g, j + 2))

        if "ff_dbga" in dram:
            for j in range(4):
                p.dma("sync", dram["ff_dbga"][128 * j:128 * j + 128, :],
                      ag[j][:, :], after=ag_done, sem="st")
        s2 = Streamer(p, wsl_2)
        s2.load(slab_ap(w2p, 0), after=ag_done[-2:])
        warm3 = p.mm(p1[:, :], [
            (wsl_2[0][:, 128 * k:128 * k + 128], ag[k][:, 0:512])
            for k in range(NJ)], after=[s2.load_tok[0]] + ag_done)
        s2.note_use(0, warm3)
        s2.load(slab_ap(w2p, 1))
        ev_out = None
        st_prev = []
        for d in range(NC_D):
            bi = d % 2
            for g in range(4):
                gsl = slice(g * 512, g * 512 + 512)
                pr = [s2.load_tok[bi]] + ag_done
                if ev_out is not None:
                    pr.append(ev_out)
                tmm = p.mm(p1[:, :], [
                    (wsl_2[bi][:, 128 * k:128 * k + 128], ag[k][:, gsl])
                    for k in range(NJ)], after=pr)
                s2.note_use(bi, tmm)
                tx = p.dma("sync", xres[:, :],
                           stream_in[128 * d:128 * d + 128, gsl],
                           after=[ev_out] if ev_out else [])
                eb = p.act(osc[:, :], p1[:, :], AF.Identity,
                           bias=b2c[:, d:d + 1],
                           after=[tmm] + ct + st_prev)
                ev_out = p.vec("tensor_add", osc[:, :], osc[:, :],
                               xres[:, :], after=[eb, tx] + st_prev)
                st_prev = [p.dma("sync",
                                 stream_out[128 * d:128 * d + 128, gsl],
                                 osc[:, :], after=[ev_out], sem="st")]
            if d + 2 < NC_D:
                s2.load(slab_ap(w2p, d + 2))
        p.barrier()

        if final_ln:
            out2 = dram["ff_out2"]
            ln_guard = []
            for g in range(4):
                gsl = slice(g * 512, g * 512 + 512)
                r_done = []
                rx_prev = []
                for d in range(NC_D):
                    t1_ = p.dma("sync", xres[:, :],
                                stream_out[128 * d:128 * d + 128, gsl],
                                after=rx_prev + st_prev)
                    t2_ = p.dma("sync", xres2[:, :],
                                dram["ff_res2"][128 * d:128 * d + 128, gsl],
                                after=rx_prev)
                    rv = p.vec("tensor_add", r_tiles[d][:, :], xres[:, :],
                               xres2[:, :], after=[t1_, t2_] + ln_guard)
                    rx_prev = [rv]
                    r_done.append(rv)

                def writer(dd, _s, src_ap, toks, g=g):
                    return p.dma("sync",
                                 out2[128 * dd:128 * dd + 128, gsl],
                                 src_ap, after=toks, sem="st")
                stoks, ln_guard = layernorm_fm(
                    p, ln_scr, r_tiles, 512, ones_col, ones_row,
                    psA, psB, pA, pB, r_done, writer, guard=ln_guard)
                ln_guard = ln_guard + stoks
        p.barrier("sync")


def declare_ffn_ffn_dram(nc, final_ln=False, dbg=False, pfx="", in_h=None,
                         out_h=None, res2_h=None, out2_h=None):
    dram = {}
    dram["ff_in"] = in_h if in_h is not None else nc.dram_tensor(
        pfx + "ff_in", [D, T], F32, kind="ExternalInput")
    dram["ff_out"] = out_h if out_h is not None else nc.dram_tensor(
        pfx + "ff_out", [D, T], F32, kind="ExternalOutput")
    for nm, shp in [("ff_w1a", [1024, 2816]), ("ff_w1g", [1024, 2816]),
                    ("ff_w2p", [2816, 1024]), ("ff_lg", [128, NC_D]),
                    ("ff_lb", [128, NC_D]), ("ff_b1a", [128, NJ]),
                    ("ff_b1g", [128, NJ]), ("ff_b2", [128, NC_D]),
                    ("ff_odd", [128, 1])]:
        dram[nm] = nc.dram_tensor(pfx + nm, shp, F32, kind="ExternalInput")
    if final_ln:
        dram["ff_res2"] = res2_h if res2_h is not None else nc.dram_tensor(
            pfx + "ff_res2", [D, T], F32, kind="ExternalInput")
        dram["ff_out2"] = out2_h if out2_h is not None else nc.dram_tensor(
            pfx + "ff_out2", [D, T], F32, kind="ExternalOutput")
    if dbg:
        dram["ff_dbgh"] = nc.dram_tensor("ff_dbgh", [D, T], BF16,
                                         kind="ExternalOutput")
        dram["ff_dbga"] = nc.dram_tensor("ff_dbga", [512, T], BF16,
                                         kind="ExternalOutput")
    dram["ff_cpub"] = nc.dram_tensor(pfx + "ff_cpub", [128, 8], F32)
    dram["ff_cgat"] = nc.dram_tensor(pfx + "ff_cgat", [256, 8], F32)
    return dram


def build_col_prologue(nc, ctx, p, dram, in_name, pub, gat, tokname):
    with ExitStack() as c1:
        col = sb(c1, nc, tokname + "_col", [128, 8])
        toks = []
        for k in range(NC_D):
            toks.append(p.dma("sync", col[:, k:k + 1],
                              dram[in_name][128 * k:128 * k + 128, T - 1:T],
                              allow_slow_non_contiguous=True))
        tp = p.dma("gpsimd", dram[pub].ap(), col[:, :], after=toks)
        tc = p.coll("AllGather", ALU.bypass, PAIRS, [dram[pub].ap().opt()],
                    [dram[gat].ap().opt()], after=[tp])
        dram[tokname] = tc


def pack_ffn_ffn_inputs(inputs, sub, stream_in_full, res2_full=None):
    w1 = np.asarray(inputs["L0_ffn_w1"], np.float32)[sub]
    b1 = np.asarray(inputs["L0_ffn_b1"], np.float32)[sub]
    w2 = np.asarray(inputs["L0_ffn_w2"], np.float32)[sub]
    b2 = np.asarray(inputs["L0_ffn_b2"], np.float32)[sub]
    lgv = np.asarray(inputs["L0_ffn_ln_g"], np.float32)[sub]
    lbv = np.asarray(inputs["L0_ffn_ln_b"], np.float32)[sub]
    HH = 2730
    PAD = NJ * 128
    w1a = np.zeros((1024, PAD), np.float32); w1a[:, :HH] = w1[:, :HH]
    w1g = np.zeros((1024, PAD), np.float32); w1g[:, :HH] = w1[:, HH:]
    w2p = np.zeros((PAD, 1024), np.float32); w2p[:HH] = w2
    b1a = np.zeros(PAD, np.float32); b1a[:HH] = b1[:HH]
    b1g = np.zeros(PAD, np.float32); b1g[:HH] = b1[HH:]
    shared = dict(ff_w1a=w1a, ff_w1g=w1g, ff_w2p=w2p,
                  ff_b1a=cols128(b1a), ff_b1g=cols128(b1g),
                  ff_b2=cols128(b2), ff_lg=cols128(lgv), ff_lb=cols128(lbv))
    maps = []
    for c in range(8):
        b, h = c // 2, c % 2
        m = dict(shared)
        m["ff_in"] = np.ascontiguousarray(
            stream_in_full[b, h * T:(h + 1) * T, :].T.astype(np.float32))
        if res2_full is not None:
            m["ff_res2"] = np.ascontiguousarray(
                res2_full[b, h * T:(h + 1) * T, :].T.astype(np.float32))
        m["ff_odd"] = np.full((128, 1), float(h), np.float32)
        maps.append(m)
    return maps


def run_ffn_ffn(inputs, sub, stream_in_full, res2_full=None):
    final_ln = res2_full is not None
    nc = bass.Bass("TRN2", target_bir_lowering=False)
    with ExitStack() as ctx:
        p = P(nc, ctx)
        dram = declare_ffn_ffn_dram(nc, final_ln)
        build_col_prologue(nc, ctx, p, dram, "ff_in", "ff_cpub", "ff_cgat",
                           "ff_cgat_tok")
        build_ffn_ffn(nc, ctx, p, dram, final_ln)
        p.finalize()
    maps = pack_ffn_ffn_inputs(inputs, sub, stream_in_full, res2_full)
    return run_bass_kernel_spmd(nc, maps, core_ids=list(range(8)))



def build_l1_attn(nc, ctx, p, dram):
    stream_in = dram["la_in"]
    stream_out = dram["la_out"]
    winT = dram["la_winT"]
    woutT = dram["la_woutT"]
    winGT = dram["la_winGT"]
    woutGT = dram["la_woutGT"]
    S8 = 1.0 / np.sqrt(128.0)
    SG = 1.0 / np.sqrt(256.0)

    with ExitStack() as c1:
        otile = [sb(c1, nc, f"la_o{d}", [128, T], BF16) for d in range(NC_D)]
        meansf = [sb(c1, nc, f"la_mn{d}", [128, 32]) for d in range(NC_D)]
        mall = [sb(c1, nc, f"la_ma{d}", [128, 64], BF16) for d in range(NC_D)]
        bql = sb(c1, nc, "la_bql_s", [128, NC_D])
        bkl = sb(c1, nc, "la_bkl_s", [128, NC_D])
        bvl_r = sb(c1, nc, "la_bvl_s", [1, 1024], BF16)
        bol = sb(c1, nc, "la_bol_s", [128, NC_D])
        bqg = sb(c1, nc, "la_bqg_s", [128, NC_D])
        bkg = sb(c1, nc, "la_bkg_s", [128, NC_D])
        bvg_r = sb(c1, nc, "la_bvg_s", [1, 1024], BF16)
        bog = sb(c1, nc, "la_bog_s", [128, NC_D])
        is_odd = sb(c1, nc, "la_odd_sb", [128, 1])
        is_even = sb(c1, nc, "la_evn_sb", [128, 1])
        ones_col = sb(c1, nc, "la_oc", [128, 1], BF16)
        ones_row = sb(c1, nc, "la_or", [1, 128], BF16)
        ones64c = sb(c1, nc, "la_o64c", [64, 1], BF16)
        ones64r = sb(c1, nc, "la_o64r", [1, 64], BF16)
        mcols = sb(c1, nc, "la_mcols", [128, 8])
        mcolsb = sb(c1, nc, "la_mcolsb", [128, 8], BF16)
        qabs = sb(c1, nc, "la_qabs", [128, 512], BF16)
        negB = sb(c1, nc, "la_negB", [1, 512], BF16)
        probs = sb(c1, nc, "la_probs", [128, 512], BF16)
        invb = sb(c1, nc, "la_invb", [1, 512], BF16)
        osc = sb(c1, nc, "la_osc", [128, 512])
        xres = sb(c1, nc, "la_xres", [128, 512])
        gsel = sb(c1, nc, "la_gsel", [128, 32])
        gselb = sb(c1, nc, "la_gselb", [128, 32], BF16)
        ident = sb(c1, nc, "la_id", [128, 128], BF16)
        gT = [sb(c1, nc, f"la_gT{d}", [32, 128], BF16) for d in range(NC_D)]

        p0 = psum(c1, nc, "la_p0", [128, 512])
        p1 = psum(c1, nc, "la_p1", [128, 512])
        p2 = psum(c1, nc, "la_p2", [128, 512])
        pav = psum(c1, nc, "la_pav", [128, 512])
        pbc = psum(c1, nc, "la_pbc", [128, 512])
        prow = psum(c1, nc, "la_prow", [1, 512])
        psA_ = psum(c1, nc, "la_psA", [1, 512])
        ptb = psum(c1, nc, "la_ptb", [128, 128], BF16)

        ct = []
        ct.append(p.dma("sync", is_odd[:, :], dram["la_odd"].ap()))
        ct.append(p.dma("sync", ident[:, :], dram["la_ident"].ap()))
        for nm, t in [("la_bql", bql), ("la_bkl", bkl), ("la_bol", bol),
                      ("la_bqg", bqg), ("la_bkg", bkg), ("la_bog", bog)]:
            ct.append(p.dma("sync", t[:, :], dram[nm].ap()))
        ct.append(p.dma("gpsimd", bvl_r[:, :], dram["la_bvl"].ap()))
        ct.append(p.dma("gpsimd", bvg_r[:, :], dram["la_bvg"].ap()))
        ct.append(p.vec("memset", ones_col[:, :], 1.0))
        ct.append(p.vec("memset", ones_row[:, :], 1.0))
        ct.append(p.vec("memset", ones64c[:, :], 1.0))
        ct.append(p.vec("memset", ones64r[:, :], 1.0))
        ct.append(p.vec("tensor_scalar", is_even[:, :], is_odd[:, :], -1.0,
                        1.0, ALU.mult, ALU.add, after=[ct[0]]))
        ct.append(p.vec("tensor_scalar_mul", bql[:, :], bql[:, :], S8,
                        after=ct[3:4]))
        ct.append(p.vec("tensor_scalar_mul", bqg[:, :], bqg[:, :], SG,
                        after=ct[6:7]))

        with ExitStack() as c2:
            qh = [sb(c2, nc, f"la_q{h}", [128, T], BF16) for h in range(8)]
            kh = [sb(c2, nc, f"la_k{h}", [128, T], BF16) for h in range(8)]
            vt = [sb(c2, nc, f"la_v{i}", [128, 1024], BF16)
                  for i in range(16)]
            wsl = [sb(c2, nc, f"la_wsl{i}", [128, 1024], BF16)
                   for i in range(2)]
            wvsl = sb(c2, nc, "la_wvsl", [128, 4096], BF16)
            c3 = ExitStack()
            xb = [sb(c3, nc, f"la_xb{k}", [128, T], BF16)
                  for k in range(NC_D)]

            xb_toks = [p.dma("gpsimd", xb[k][:, :],
                             stream_in[128 * k:128 * k + 128, :])
                      for k in range(NC_D)]
            p.barrier("vector")

            ws = Streamer(p, wsl)
            ws.load(slab_ap(winT, 0))
            warm = p.mm(p0[:, :], [
                (wsl[0][:, 128 * k:128 * k + 128], xb[k][:, 0:512])
                for k in range(NC_D)], after=[ws.load_tok[0]] + xb_toks)
            ws.note_use(0, warm)
            ws.load(slab_ap(winT, 1))
            qk_done = []
            ev = None
            for F in range(16):
                bi = F % 2
                is_q = F < 8
                h = F % 8
                for g in range(4):
                    gsl = slice(g * 512, g * 512 + 512)
                    pr = [ws.load_tok[bi]] + xb_toks
                    if ev is not None:
                        pr.append(ev)
                    tmm = p.mm(p0[:, :], [
                        (wsl[bi][:, 128 * k:128 * k + 128], xb[k][:, gsl])
                        for k in range(NC_D)], after=pr)
                    ws.note_use(bi, tmm)
                    dst = (qh if is_q else kh)[h][:, gsl]
                    bias = (bql if is_q else bkl)[:, h:h + 1]
                    ev = p.act(dst, p0[:, :], AF.Identity,
                               scale=(S8 if is_q else 1.0), bias=bias,
                               after=[tmm] + ct)
                    qk_done.append(ev)
                if F + 2 < 16:
                    ws.load(slab_ap(winT, F + 2))
            ev_v = None
            v_done = []
            for half in range(2):
                wvt = p.dma("gpsimd",
                            wvsl[:, :].rearrange("q (a j) -> q a j", j=512),
                            winT[:, 2048 + 512 * half:2560 + 512 * half]
                            .rearrange("(a q) j -> q a j", q=128),
                            after=v_done[-1:])
                for tc in range(16):
                    pr = [wvt] + xb_toks + ct
                    if ev_v is not None:
                        pr.append(ev_v)
                    tmm = p.mm(p2[:, :], [
                        (ones_row[:, :],
                         bvl_r[0:1, 512 * half:512 * half + 512])] + [
                        (xb[k][:, 128 * tc:128 * tc + 128],
                         wvsl[:, 512 * k:512 * k + 512])
                        for k in range(NC_D)], after=pr)
                    ev_v = p.act(vt[tc][:, 512 * half:512 * half + 512],
                                 p2[:, :], AF.Identity, after=[tmm])
                    v_done.append(ev_v)
            mn_done = []
            for d in range(NC_D):
                t1 = p.vec("reduce_sum", meansf[d][:, :],
                           xb[d][:, :].rearrange("q (s t) -> q s t", t=64),
                           AX.X, after=xb_toks)
                t2 = p.vec("tensor_scalar_mul", meansf[d][:, :],
                           meansf[d][:, :], 1.0 / 64.0, after=[t1])
                mn_done.append(p.dma("gpsimd", dram["la_mpub"][:, 32 * d:32 * d + 32],
                                     meansf[d][:, :], after=[t2]))
            cm = p.coll("AllGather", ALU.bypass, PAIRS,
                        [dram["la_mpub"].ap().opt()],
                        [dram["la_mgat"].ap().opt()], after=mn_done)
            ma_toks = []
            for d in range(NC_D):
                ma_toks.append(p.dma("gpsimd", mall[d][:, 0:32],
                                     dram["la_mgat"][0:128, 32 * d:32 * d + 32],
                                     after=[cm]))
                ma_toks.append(p.dma("gpsimd", mall[d][:, 32:64],
                                     dram["la_mgat"][128:256, 32 * d:32 * d + 32],
                                     after=[cm]))

            p.barrier()
            c3.close()
            att_done = []
            for h in range(8):
                for sgg in range(4):
                    gsl = slice(512 * sgg, 512 * sgg + 512)
                    sq = p.act(qabs[:, :], qh[h][:, gsl], AF.Abs,
                               after=qk_done + att_done[-1:])
                    rm_toks = []
                    for s8 in range(8):
                        t1 = p.vec("reduce_max", mcols[:, s8:s8 + 1],
                                   kh[h][:, 512 * sgg + 64 * s8:
                                         512 * sgg + 64 * s8 + 64],
                                   AX.X, apply_absolute_value=True,
                                   after=qk_done)
                        rm_toks.append(t1)
                    cv = p.vec("tensor_copy", mcolsb[:, :], mcols[:, :],
                               after=rm_toks)
                    nb_toks = []
                    for s8 in range(8):
                        t2 = p.mm(prow[0:1, 64 * s8:64 * s8 + 64],
                                  [(mcolsb[:, s8:s8 + 1],
                                    qabs[:, 64 * s8:64 * s8 + 64])],
                                  after=[cv, sq] + att_done[-1:])
                        nb_toks.append(t2)
                    nbv = p.vec("tensor_scalar_mul", negB[0:1, :],
                                prow[0:1, 0:512], -1.0, after=nb_toks)
                    sc_toks = []
                    for s8 in range(8):
                        ssl = slice(512 * sgg + 64 * s8,
                                    512 * sgg + 64 * s8 + 64)
                        m1 = p.mm(p0[0:64, 64 * s8:64 * s8 + 64],
                                  [(kh[h][:, ssl], qh[h][:, ssl])],
                                  after=[nbv] + att_done[-1:], stop=False)
                        m2 = p.mm(p0[0:64, 64 * s8:64 * s8 + 64],
                                  [(ones64r[:, :],
                                    negB[0:1, 64 * s8:64 * s8 + 64])],
                                  start=False)
                        sc_toks.append(m2)
                    mz = p.vec("memset", probs[:, :], 0.0,
                               after=att_done[-1:])
                    ev_toks = []
                    for s8 in range(8):
                        seg = 8 * sgg + s8
                        rows = slice(64 * (seg % 2), 64 * (seg % 2) + 64)
                        ev1 = p.act(probs[rows, 64 * s8:64 * s8 + 64],
                                    p0[0:64, 64 * s8:64 * s8 + 64], AF.Exp,
                                    after=sc_toks + [mz])
                        ev_toks.append(ev1)
                    evp = ev_toks[-1]
                    dn_toks = []
                    for s8 in range(8):
                        d1 = p.mm(prow[0:1, 64 * s8:64 * s8 + 64],
                                  [(ones_col[:, :],
                                    probs[:, 64 * s8:64 * s8 + 64])],
                                  after=ev_toks + [mz])
                        dn_toks.append(d1)
                    rc = p.vec("reciprocal", osc[0:1, 0:512],
                               prow[0:1, 0:512], after=dn_toks)
                    rb = p.vec("tensor_copy", invb[0:1, :], osc[0:1, 0:512],
                               after=[rc])
                    bc = p.mm(pbc[:, :], [(ones_row[:, :], invb[0:1, :])],
                              after=[rb])
                    av_toks = []
                    for s8 in range(8):
                        seg = 8 * sgg + s8
                        vtile = vt[seg // 2]
                        a1 = p.mm(pav[:, 64 * s8:64 * s8 + 64],
                                  [(vtile[:, 128 * h:128 * h + 128],
                                    probs[:, 64 * s8:64 * s8 + 64])],
                                  after=ev_toks)
                        av_toks.append(a1)
                    eo = p.act(osc[:, :], pav[:, :], AF.Identity,
                               after=av_toks + [bc])
                    fm = p.vec("tensor_mul", otile[h][:, gsl], osc[:, :],
                               pbc[:, :], after=[eo])
                    att_done.append(fm)

            p.barrier()
            os_ = Streamer(p, wsl)
            os_.load(slab_ap(woutT, 0))
            warm2 = p.mm(p0[:, :], [
                (wsl[0][:, 128 * k:128 * k + 128], otile[k][:, 0:512])
                for k in range(NC_D)], after=[os_.load_tok[0]] + att_done)
            os_.note_use(0, warm2)
            os_.load(slab_ap(woutT, 1))
            ev_out = None
            loc_st = []
            for d in range(NC_D):
                bi = d % 2
                for g in range(4):
                    gsl = slice(g * 512, g * 512 + 512)
                    pr = [os_.load_tok[bi]] + att_done
                    if ev_out is not None:
                        pr.append(ev_out)
                    tmm = p.mm(p0[:, :], [
                        (wsl[bi][:, 128 * k:128 * k + 128],
                         otile[k][:, gsl]) for k in range(NC_D)], after=pr)
                    os_.note_use(bi, tmm)
                    tx = p.dma("sync", xres[:, :],
                               stream_in[128 * d:128 * d + 128, gsl],
                               after=[ev_out] if ev_out else [])
                    eb = p.act(osc[:, :], p0[:, :], AF.Identity,
                               bias=bol[:, d:d + 1],
                               after=[tmm] + ct + loc_st)
                    ev_out = p.vec("tensor_add", osc[:, :],
                                   osc[:, :], xres[:, :],
                                   after=[eb, tx] + loc_st)
                    loc_st = [p.dma("sync",
                                    dram["la_loc"][128 * d:128 * d + 128, gsl],
                                    osc[:, :], after=[ev_out], sem="st")]
                    if "la_dbgl" in dram:
                        loc_st.append(p.dma(
                            "sync",
                            dram["la_dbgl"][128 * d:128 * d + 128, gsl],
                            osc[:, :], after=[ev_out], sem="st"))
                if d + 2 < NC_D:
                    os_.load(slab_ap(woutT, d + 2))
            local_done = [ev_out] + loc_st

            p.barrier()
            mb_toks = []
            qg = [sb(c2, nc, f"la_qg{d}", [128, 64], BF16)
                  for d in range(NC_D)]
            kg = [sb(c2, nc, f"la_kg{d}", [128, 64], BF16)
                  for d in range(NC_D)]
            vg = sb(c2, nc, "la_vg", [64, 1024], BF16)
            og = [sb(c2, nc, f"la_og{d}", [128, 64], BF16)
                  for d in range(NC_D)]
            globf = [sb(c2, nc, f"la_gf{d}", [128, 64]) for d in range(NC_D)]
            for d in range(NC_D):
                mb_toks.append(p.vec("tensor_copy", mall[d][:, :],
                                     mall[d][:, :], after=ma_toks))
            gw = Streamer(p, wsl)
            gw.load(slab_ap(winGT, 0), after=att_done)
            warm3 = p.mm(p0[:, 0:64], [
                (wsl[0][:, 128 * k:128 * k + 128], mall[k][:, :])
                for k in range(NC_D)], after=[gw.load_tok[0]] + mb_toks)
            gw.note_use(0, warm3)
            gw.load(slab_ap(winGT, 1))
            ev = None
            gqk_done = []
            for F in range(16):
                bi = F % 2
                is_q = F < 8
                h = F % 8
                pr = [gw.load_tok[bi]] + mb_toks
                if ev is not None:
                    pr.append(ev)
                tmm = p.mm(p0[:, 0:64], [
                    (wsl[bi][:, 128 * k:128 * k + 128], mall[k][:, :])
                    for k in range(NC_D)], after=pr)
                gw.note_use(bi, tmm)
                dst = (qg if is_q else kg)[h][:, :]
                bias = (bqg if is_q else bkg)[:, h:h + 1]
                ev = p.act(dst, p0[:, 0:64], AF.Identity,
                           scale=(SG if is_q else 1.0), bias=bias,
                           after=[tmm] + ct)
                gqk_done.append(ev)
                if F + 2 < 16:
                    gw.load(slab_ap(winGT, F + 2))
            ev_v = None
            for half in range(2):
                wvg = p.dma("gpsimd",
                            wvsl[:, :].rearrange("q (a j) -> q a j", j=512),
                            winGT[:, 2048 + 512 * half:2560 + 512 * half]
                            .rearrange("(a q) j -> q a j", q=128),
                            after=(att_done + v_done) if half == 0
                            else [ev_v])
                pr = [wvg] + mb_toks + ct
                if ev_v is not None:
                    pr.append(ev_v)
                tmm = p.mm(p2[0:64, :], [
                    (ones_row[:, 0:64],
                     bvg_r[0:1, 512 * half:512 * half + 512])] + [
                    (mall[k][:, :], wvsl[:, 512 * k:512 * k + 512])
                    for k in range(NC_D)], after=pr)
                ev_v = p.act(vg[:, 512 * half:512 * half + 512], p2[0:64, :],
                             AF.Identity, after=[tmm])
            gv_done = [ev_v]
            g_att = []
            for h in range(4):
                sq = p.act(qabs[:, 0:64], qg[2 * h][:, :], AF.Abs,
                           after=gqk_done + g_att[-1:])
                sq2 = p.act(qabs[:, 64:128], qg[2 * h + 1][:, :], AF.Abs,
                            after=gqk_done)
                r1t = p.vec("reduce_max", mcols[:, 0:1], kg[2 * h][:, :],
                            AX.X, apply_absolute_value=True, after=gqk_done)
                r2t = p.vec("reduce_max", mcols[:, 1:2], kg[2 * h + 1][:, :],
                            AX.X, apply_absolute_value=True, after=gqk_done)
                cv = p.vec("tensor_copy", mcolsb[:, 0:2], mcols[:, 0:2],
                           after=[r1t, r2t])
                b1t = p.mm(prow[0:1, 0:64],
                           [(mcolsb[:, 0:1], qabs[:, 0:64]),
                            (mcolsb[:, 1:2], qabs[:, 64:128])],
                           after=[cv, sq, sq2] + g_att[-1:])
                nbv = p.vec("tensor_scalar_mul", negB[0:1, 0:64],
                            prow[0:1, 0:64], -1.0, after=[b1t])
                m1 = p.mm(p0[0:64, 0:64],
                          [(kg[2 * h][:, :], qg[2 * h][:, :]),
                           (kg[2 * h + 1][:, :], qg[2 * h + 1][:, :])],
                          after=[nbv] + g_att[-1:], stop=False)
                m2 = p.mm(p0[0:64, 0:64],
                          [(ones64r[:, :], negB[0:1, 0:64])], start=False)
                evp = p.act(probs[0:64, 0:64], p0[0:64, 0:64], AF.Exp,
                            after=[m2])
                d1 = p.mm(prow[0:1, 0:64],
                          [(ones64c[:, :], probs[0:64, 0:64])], after=[evp])
                rc = p.vec("reciprocal", osc[0:1, 0:64], prow[0:1, 0:64],
                           after=[d1])
                rb = p.vec("tensor_copy", invb[0:1, 0:64], osc[0:1, 0:64],
                           after=[rc])
                bc = p.mm(pbc[:, 0:64], [(ones_row[:, :], invb[0:1, 0:64])],
                          after=[rb] + g_att[-1:])
                prev_eo = []
                for half in range(2):
                    dd = 2 * h + half
                    a1 = p.mm(pav[:, 0:64],
                              [(vg[:, 128 * dd:128 * dd + 128],
                                probs[0:64, 0:64])],
                              after=[evp] + gv_done + prev_eo)
                    eo = p.act(osc[:, 0:64], pav[:, 0:64], AF.Identity,
                               after=[a1, bc])
                    prev_eo = [eo]
                    fm = p.vec("tensor_mul", og[dd][:, :], osc[:, 0:64],
                               pbc[:, 0:64], after=[eo])
                    g_att.append(fm)
            gow = Streamer(p, wsl)
            gow.load(slab_ap(woutGT, 0))
            gow.load(slab_ap(woutGT, 1))
            gw_done = []
            ev = None
            for d in range(NC_D):
                bi = d % 2
                pr = [gow.load_tok[bi]] + g_att
                if ev is not None:
                    pr.append(ev)
                tmm = p.mm(p0[:, 0:64], [
                    (wsl[bi][:, 128 * k:128 * k + 128], og[k][:, :])
                    for k in range(NC_D)], after=pr)
                gow.note_use(bi, tmm)
                ev = p.act(globf[d][:, :], p0[:, 0:64], AF.Identity,
                           bias=bog[:, d:d + 1], after=[tmm] + ct)
                gw_done.append(ev)
                if d + 2 < NC_D:
                    gow.load(slab_ap(woutGT, d + 2))

            if "la_dbgg" in dram:
                for d in range(NC_D):
                    p.dma("sync", dram["la_dbgg"][128 * d:128 * d + 128, :],
                          globf[d][:, :], after=gw_done, sem="st")
                    p.dma("gpsimd",
                          dram["la_dbgm"][128 * d:128 * d + 128, :],
                          mall[d][:, :], after=ma_toks + mb_toks, sem="st")

            if "la_dbgq" in dram:
                for d in range(NC_D):
                    p.dma("sync", dram["la_dbgq"][128 * d:128 * d + 128, :],
                          qg[d][:, :], after=gqk_done, sem="st")
                    p.dma("sync", dram["la_dbgk"][128 * d:128 * d + 128, :],
                          kg[d][:, :], after=gqk_done, sem="st")
                    p.dma("sync", dram["la_dbgog"][128 * d:128 * d + 128, :],
                          og[d][:, :], after=g_att, sem="st")
                p.dma("sync", dram["la_dbgv"].ap(), vg[:, :], after=gv_done,
                      sem="st")
            gT_done = []
            for d in range(NC_D):
                t1 = p.vec("tensor_scalar_mul", gsel[:, :],
                           globf[d][:, 0:32], is_even[:, :],
                           after=gw_done + ct)
                t2 = p.vec("scalar_tensor_tensor", gsel[:, :],
                           globf[d][:, 32:64], is_odd[:, :], gsel[:, :],
                           ALU.mult, ALU.add, after=[t1])
                t3 = p.vec("tensor_copy", gselb[:, :], gsel[:, :],
                           after=[t2])
                tt = p.trans(ptb[0:32, 0:128], gselb[:, 0:32], ident[:, :],
                             after=[t3] + gT_done)
                te = p.vec("tensor_copy", gT[d][:, :], ptb[0:32, 0:128],
                           after=[tt])
                gT_done.append(te)

        p.barrier()
        R32 = sb(c1, nc, "la_R32_s", [32, T], BF16)
        ln_scr = make_ln_scratch(c1, nc, "laln")
        r_tiles = [sb(c1, nc, f"la_rt{d}", [128, 512]) for d in range(NC_D)]
        tr32 = p.dma("sync", R32[:, :], dram["la_R32"].ap())
        lng = [tr32]
        for g in range(4):
            gsl = slice(512 * g, 512 * g + 512)
            r_done = []
            rv = None
            for d in range(NC_D):
                tl = p.dma("sync", r_tiles[d][:, :],
                           dram["la_loc"][128 * d:128 * d + 128, gsl],
                           after=lng)
                rp = p.mm(pbc[:, :], [(gT[d][0:32, :], R32[0:32, gsl])],
                          after=gT_done + lng + [tr32] +
                          ([rv] if rv else []))
                rv = p.vec("tensor_add", r_tiles[d][:, :],
                           r_tiles[d][:, :], pbc[:, :], after=[tl, rp] + lng)
                r_done.append(rv)

            def writer(dd, _s, src_ap, toks, g=g):
                return p.dma("sync",
                             stream_out[128 * dd:128 * dd + 128, gsl],
                             src_ap, after=toks, sem="st")
            stoks, lng = layernorm_fm(
                p, ln_scr, r_tiles, 512, ones_col, ones_row,
                psA_, prow, p1, pav, r_done, writer, guard=lng)
            lng = lng + stoks
        p.barrier("sync")


def declare_l1_attn_dram(nc, dbg=False, in_h=None, out_h=None):
    dram = {}
    dram["la_in"] = in_h if in_h is not None else nc.dram_tensor(
        "la_in", [D, T], F32, kind="ExternalInput")
    dram["la_out"] = out_h if out_h is not None else nc.dram_tensor(
        "la_out", [D, T], F32, kind="ExternalOutput")
    for nm, shp, dt in [("la_winT", [1024, 3072], F32),
                        ("la_woutT", [1024, 1024], F32),
                        ("la_winGT", [1024, 3072], F32),
                        ("la_woutGT", [1024, 1024], F32),
                        ("la_bql", [128, NC_D], F32),
                        ("la_bkl", [128, NC_D], F32),
                        ("la_bvl", [1, 1024], F32),
                        ("la_bol", [128, NC_D], F32),
                        ("la_bqg", [128, NC_D], F32),
                        ("la_bkg", [128, NC_D], F32),
                        ("la_bvg", [1, 1024], F32),
                        ("la_bog", [128, NC_D], F32),
                        ("la_odd", [128, 1], F32),
                        ("la_ident", [128, 128], BF16),
                        ("la_R32", [32, T], BF16)]:
        dram[nm] = nc.dram_tensor(nm, shp, dt, kind="ExternalInput")
    dram["la_loc"] = nc.dram_tensor("la_loc", [D, T], F32)
    if dbg:
        dram["la_dbgl"] = nc.dram_tensor("la_dbgl", [D, T], F32,
                                         kind="ExternalOutput")
        dram["la_dbgg"] = nc.dram_tensor("la_dbgg", [D, 64], F32,
                                         kind="ExternalOutput")
        dram["la_dbgm"] = nc.dram_tensor("la_dbgm", [D, 64], F32,
                                         kind="ExternalOutput")
        dram["la_dbgq"] = nc.dram_tensor("la_dbgq", [D, 64], BF16,
                                         kind="ExternalOutput")
        dram["la_dbgk"] = nc.dram_tensor("la_dbgk", [D, 64], BF16,
                                         kind="ExternalOutput")
        dram["la_dbgv"] = nc.dram_tensor("la_dbgv", [64, 1024], BF16,
                                         kind="ExternalOutput")
        dram["la_dbgog"] = nc.dram_tensor("la_dbgog", [D, 64], BF16,
                                          kind="ExternalOutput")
    dram["la_mpub"] = nc.dram_tensor("la_mpub", [128, 256], F32)
    dram["la_mgat"] = nc.dram_tensor("la_mgat", [256, 256], F32)
    return dram


def pack_l1_attn_inputs(inputs, stream_in_full):
    winT = np.ascontiguousarray(
        np.asarray(inputs["L1_attn_w_in_local"], np.float32).T)
    woutT = np.ascontiguousarray(
        np.asarray(inputs["L1_attn_w_out_local"], np.float32).T)
    winGT = np.ascontiguousarray(
        np.asarray(inputs["L1_attn_w_in_global"], np.float32).T)
    woutGT = np.ascontiguousarray(
        np.asarray(inputs["L1_attn_w_out_global"], np.float32).T)
    bl = np.asarray(inputs["L1_attn_b_in_local"], np.float32)
    bg = np.asarray(inputs["L1_attn_b_in_global"], np.float32)
    R32 = np.zeros((32, T), np.float32)
    for col in range(T):
        R32[col // 64, col] = 1.0
    shared = dict(la_winT=winT, la_woutT=woutT, la_winGT=winGT,
                  la_woutGT=woutGT,
                  la_bql=cols128(bl[0:1024]), la_bkl=cols128(bl[1024:2048]),
                  la_bvl=bl[2048:3072][None, :],
                  la_bol=cols128(inputs["L1_attn_b_out_local"]),
                  la_bqg=cols128(bg[0:1024]), la_bkg=cols128(bg[1024:2048]),
                  la_bvg=bg[2048:3072][None, :],
                  la_bog=cols128(inputs["L1_attn_b_out_global"]),
                  la_ident=np.eye(128, dtype=np.float32).astype(BF),
                  la_R32=R32.astype(BF))
    maps = []
    for c in range(8):
        b, h = c // 2, c % 2
        m = dict(shared)
        m["la_in"] = np.ascontiguousarray(
            stream_in_full[b, h * T:(h + 1) * T, :].T.astype(np.float32))
        m["la_odd"] = np.full((128, 1), float(h), np.float32)
        maps.append(m)
    return maps


def run_l1_attn(inputs, stream_in_full):
    nc = bass.Bass("TRN2", target_bir_lowering=False)
    with ExitStack() as ctx:
        p = P(nc, ctx)
        dram = declare_l1_attn_dram(nc)
        build_l1_attn(nc, ctx, p, dram)
        p.finalize()
    maps = pack_l1_attn_inputs(inputs, stream_in_full)
    return run_bass_kernel_spmd(nc, maps, core_ids=list(range(8)))



def build_l1_ffn(nc, ctx, p, dram):
    stream_in = dram["lf_in"]
    out_fm = dram["lf_out"]

    with ExitStack() as c1:
        ident = sb(c1, nc, "lf_id", [128, 128], BF16)
        ones_col = sb(c1, nc, "lf_oc", [128, 1], BF16)
        ones_row = sb(c1, nc, "lf_or", [1, 128], BF16)
        o32r = sb(c1, nc, "lf_o32r", [1, 32], BF16)
        phib = sb(c1, nc, "lf_phib_s", [1, 2048], BF16)
        R64 = sb(c1, nc, "lf_R64_s", [64, T], BF16)
        R32C = sb(c1, nc, "lf_R32C_s", [32, T], BF16)

        p0 = psum(c1, nc, "lf_p0", [128, 512])
        p1 = psum(c1, nc, "lf_p1", [128, 512])
        p2 = psum(c1, nc, "lf_p2", [128, 512])
        p3 = psum(c1, nc, "lf_p3", [128, 512])
        ptb = psum(c1, nc, "lf_ptb", [128, 128], BF16)
        ptbB = psum(c1, nc, "lf_ptbB", [128, 128], BF16)

        ct = []
        ct.append(p.dma("sync", ident[:, :], dram["lf_ident"].ap()))
        ct.append(p.dma("sync", R64[:, :], dram["lf_R64"].ap()))
        ct.append(p.dma("sync", R32C[:, :], dram["lf_R32C"].ap()))
        ct.append(p.dma("gpsimd", phib[:, :], dram["lf_phib"].ap()))
        ct.append(p.vec("memset", ones_col[:, :], 1.0))
        ct.append(p.vec("memset", ones_row[:, :], 1.0))
        ct.append(p.vec("memset", o32r[:, :], 1.0))

        with ExitStack() as c2:
            Dt = [sb(c2, nc, f"lf_D{tc}", [128, T], BF16) for tc in range(16)]
            xt = [sb(c2, nc, f"lf_xt{tc}", [128, 1024], BF16)
                  for tc in range(16)]
            phis = [sb(c2, nc, f"lf_phi{j}", [128, 4096], BF16)
                    for j in range(4)]
            logit_p = [sb(c2, nc, f"lf_logit{i}", [128, T])
                       for i in range(2)]
            expc_p = [sb(c2, nc, f"lf_expc{i}", [128, T], BF16)
                      for i in range(2)]
            mrow_p = [sb(c2, nc, f"lf_mrow{i}", [128, 64]) for i in range(2)]
            mrowb_p = [sb(c2, nc, f"lf_mrowb{i}", [128, 64], BF16)
                       for i in range(2)]
            mT_p = [sb(c2, nc, f"lf_mT{i}", [64, 128], BF16)
                    for i in range(2)]
            srow_p = [sb(c2, nc, f"lf_srow{i}", [128, 64]) for i in range(2)]
            scr_p = [sb(c2, nc, f"lf_scr{i}", [128, 512]) for i in range(2)]
            xstage_p = [sb(c2, nc, f"lf_xst{i}", [128, 512])
                        for i in range(2)]
            logit = logit_p[0]
            scr = scr_p[0]
            ctstage_p = [sb(c2, nc, f"lf_cts{i}", [128, 256], BF16)
                         for i in range(2)]
            ptbC = psum(c2, nc, "lf_ptbC", [128, 128], BF16)
            ptbD = psum(c2, nc, "lf_ptbD", [128, 128], BF16)
            ptb_p = [ptb, ptbB]
            ptbX_p = [ptbC, ptbD]
            p0_p = [p0, p2]
            p1_p = [p1, p3]
            c3 = ExitStack()
            xb = [sb(c3, nc, f"lf_xb{k}", [128, T], BF16)
                  for k in range(NC_D)]

            xb_toks = [p.dma("gpsimd", xb[k][:, :],
                             stream_in[128 * k:128 * k + 128, :])
                       for k in range(NC_D)]
            ph_toks = [p.dma("gpsimd",
                             phis[j][:, :].rearrange("q (a j) -> q a j",
                                                     j=512),
                             dram["lf_phiT"][:, 512 * j:512 * j + 512]
                             .rearrange("(a q) j -> q a j", q=128))
                       for j in range(4)]
            p.barrier("vector")
            if LF_PART == -1:
                for k in range(NC_D):
                    vv = p.vec("tensor_copy", logit[:, 0:T], xb[k][:, :],
                               after=xb_toks)
                    p.dma("sync", dram["lf_out"][128 * k:128 * k + 128, :],
                          logit[:, 0:T], after=[vv], sem="st")
                c3.close()
                p.barrier("sync")
                return
            warm = p.mm(p1[0:1, :], [(ones_col[:, :], xb[k][:, 0:512])
                                     for k in range(NC_D)],
                        after=xb_toks + ct)
            warm2 = p.mm(p0[:, :], [
                (xb[k][:, 0:128], phis[0][:, 512 * k:512 * k + 512])
                for k in range(NC_D)], after=ph_toks + [warm])

            prev_ev = [[warm2], [warm2]]
            for tc in range(16):
                for d in range(NC_D):
                    q = (tc * NC_D + d) % 2
                    tt = p.trans(ptb_p[q][:, :],
                                 xb[d][:, 128 * tc:128 * tc + 128],
                                 ident[:, :], after=xb_toks + ct + prev_ev[q])
                    te = p.vec("tensor_copy", xt[tc][:, 128 * d:128 * d + 128],
                               ptb_p[q][:, :], after=[tt])
                    prev_ev[q] = [te]
            xt_done = prev_ev[0] + prev_ev[1]
            if LF_PART == -2:
                for k in range(NC_D):
                    p.dma("gpsimd", dram["lf_out"][128 * k:128 * k + 128,
                                                   0:1024],
                          xt[k][:, :], after=xt_done, sem="st")
                c3.close()
                p.barrier("gpsimd")
                return

            Dt_done = []
            Dt_done_p = [[], []]
            Ct_st_p = [[], []]
            lg_last_p = [[], []]
            for tc in range(LF_NTC):
                par = tc % 2
                logit = logit_p[par]
                expc = expc_p[par]
                mrow = mrow_p[par]
                mrowb = mrowb_p[par]
                mT = mT_p[par]
                srow = srow_p[par]
                scr = scr_p[par]
                ptb = ptb_p[par]
                p0 = p0_p[par]
                p1 = p1_p[par]
                Ct_st = Ct_st_p[par]
                lg_toks = []
                for j in range(4):
                    bias_pairs = ([(ones_row[:, :],
                                    phib[0:1, 512 * j:512 * j + 512])]
                                  if LF_BIAS else [])
                    mm_t = p.mm(p0[:, :], bias_pairs + [
                        (xb[k][:, 128 * tc:128 * tc + 128],
                         phis[j][:, 512 * k:512 * k + 512])
                        for k in range(NC_D)],
                        after=ph_toks + xb_toks + Dt_done_p[par][-1:] +
                        Ct_st[-1:] + (lg_toks[-1:] or lg_last_p[par]))
                    ev = p.act(logit[:, 512 * j:512 * j + 512], p0[:, :],
                               AF.Identity, after=[mm_t] + Dt_done_p[par][-1:])
                    lg_toks.append(ev)
                lg_last_p[par] = lg_toks[-1:]
                if LF_PART == 0:
                    for j4 in range(4):
                        Ct_st.append(p.dma(
                            "sync",
                            dram["lf_out"][0:128,
                                           512 * j4:512 * j4 + 512],
                            logit[:, 512 * j4:512 * j4 + 512],
                            after=lg_toks, sem="st"))
                    Dt_done = lg_toks
                    continue
                m1 = p.vec("reduce_max", mrow[:, :],
                           logit[:, :].rearrange("q (a b) -> q a b", b=32),
                           AX.X, negate=True, after=lg_toks)
                m2 = p.vec("tensor_copy", mrowb[:, :], mrow[:, :],
                           after=[m1])
                t1 = p.trans(ptb[0:64, :], mrowb[:, :], ident[:, :],
                             after=[m2])
                t2 = p.vec("tensor_copy", mT[:, :], ptb[0:64, 0:128],
                           after=[t1])
                sub_toks = []
                for j in range(4):
                    rp = p.mm(p1[:, :], [(mT[:, :],
                                          R64[:, 512 * j:512 * j + 512])],
                              after=[t2, ct[1]] + sub_toks[-1:])
                    sv = p.vec("tensor_add", scr[:, :],
                               logit[:, 512 * j:512 * j + 512], p1[:, :],
                               after=[rp] + lg_toks)
                    ev = p.act(Dt[tc][:, 512 * j:512 * j + 512], scr[:, :],
                               AF.Exp, after=[sv])
                    sub_toks.append(ev)
                s1 = p.vec("reduce_sum", srow[:, :],
                           Dt[tc][:, :].rearrange("q (a b) -> q a b", b=32),
                           AX.X, after=sub_toks)
                s2 = p.vec("reciprocal", srow[:, :], srow[:, :], after=[s1])
                s3 = p.vec("tensor_copy", mrowb[:, :], srow[:, :],
                           after=[s2])
                t3 = p.trans(ptb[0:64, :], mrowb[:, :], ident[:, :],
                             after=[s3])
                t4 = p.vec("tensor_copy", mT[:, :], ptb[0:64, 0:128],
                           after=[t3])
                dn_toks = []
                for j in range(4):
                    rp = p.mm(p1[:, :], [(mT[:, :],
                                          R64[:, 512 * j:512 * j + 512])],
                              after=[t4] + dn_toks[-1:])
                    dv = p.vec("tensor_mul", Dt[tc][:, 512 * j:512 * j + 512],
                               Dt[tc][:, 512 * j:512 * j + 512], p1[:, :],
                               after=[rp] + sub_toks)
                    dn_toks.append(dv)
                Dt_done = dn_toks
                Dt_done_p[par] = dn_toks

                cm1 = p.vec("reduce_max", mrow[:, 0:32],
                            logit[:, :].rearrange("q (a b) -> q b a", b=32),
                            AX.X, negate=True, after=lg_toks + Ct_st)
                cm2 = p.vec("tensor_copy", mrowb[:, 0:32], mrow[:, 0:32],
                            after=[cm1])
                ct1 = p.trans(ptb[0:32, :], mrowb[:, 0:32], ident[:, :],
                              after=[cm2])
                ct2 = p.vec("tensor_copy", mT[0:32, :], ptb[0:32, 0:128],
                            after=[ct1])
                csub = []
                for j in range(4):
                    rp = p.mm(p1[:, :], [(mT[0:32, :],
                                          R32C[:, 512 * j:512 * j + 512])],
                              after=[ct2, ct[2]] + csub[-1:])
                    sv = p.vec("tensor_add", scr[:, :],
                               logit[:, 512 * j:512 * j + 512], p1[:, :],
                               after=[rp] + lg_toks)
                    ev = p.act(expc[:, 512 * j:512 * j + 512], scr[:, :],
                               AF.Exp, after=[sv])
                    csub.append(ev)
                cs1 = p.vec("reduce_sum", srow[:, 0:32],
                            expc[:, :].rearrange("q (a b) -> q b a", b=32),
                            AX.X, after=csub)
                cs2 = p.vec("reciprocal", srow[:, 0:32], srow[:, 0:32],
                            after=[cs1])
                cs3 = p.vec("tensor_copy", mrowb[:, 0:32], srow[:, 0:32],
                            after=[cs2])
                ct3 = p.trans(ptb[0:32, :], mrowb[:, 0:32], ident[:, :],
                              after=[cs3])
                ct4 = p.vec("tensor_copy", mT[0:32, :], ptb[0:32, 0:128],
                            after=[ct3])
                cdn = []
                for j in range(4):
                    rp = p.mm(p1[:, :], [(mT[0:32, :],
                                          R32C[:, 512 * j:512 * j + 512])],
                              after=[ct4] + cdn[-1:])
                    dv = p.vec("tensor_mul", expc[:, 512 * j:512 * j + 512],
                               expc[:, 512 * j:512 * j + 512], p1[:, :],
                               after=[rp] + csub)
                    cdn.append(dv)
                PTs = [ptb, ptbX_p[par]]
                cts = ctstage_p[par]
                tes = []
                for npc in range(16):
                    tt = p.trans(PTs[npc % 2][:, :],
                                 expc[:, 128 * npc:128 * npc + 128],
                                 ident[:, :],
                                 after=cdn + tes[-2:-1])
                    te = p.vec("tensor_copy",
                               cts[:, 128 * (npc % 2):128 * (npc % 2) + 128],
                               PTs[npc % 2][:, :],
                               after=[tt] + Ct_st[-2:-1])
                    tes.append(te)
                    Ct_st.append(p.dma(
                        "sync",
                        dram["lf_Ct"][128 * npc:128 * npc + 128,
                                      128 * tc:128 * tc + 128],
                        cts[:, 128 * (npc % 2):128 * (npc % 2) + 128],
                        after=[te], sem="st"))

            if LF_PART == 0:
                c3.close()
                p.barrier("sync")
                p.barrier("gpsimd")
                return
            if "lf_dbgd" in dram:
                p.dma("sync", dram["lf_dbgd"].ap(), Dt[0][:, :],
                      after=Dt_done, sem="st")

            dn_all = Dt_done_p[0][-1:] + Dt_done_p[1][-1:] + \
                lg_last_p[0] + lg_last_p[1] + Ct_st_p[0][-1:] + \
                Ct_st_p[1][-1:]
            xp_st = [[], []]
            ev_x = [[], []]
            for npc in range(16):
                for half in range(2):
                    q = (2 * npc + half) % 2
                    pr = Dt_done + xt_done + dn_all + ev_x[q][-1:] + \
                        xp_st[q][-1:]
                    tmm = p.mm(p0_p[q][:, :], [
                        (Dt[tc2][:, 128 * npc:128 * npc + 128],
                         xt[tc2][:, 512 * half:512 * half + 512])
                        for tc2 in range(16)], after=pr)
                    ev_x[q].append(p.act(
                        xstage_p[q][:, :],
                        p0_p[q][:, :], AF.Identity, after=[tmm]))
                    xp_st[q].append(p.dma(
                        "gpsimd",
                        dram["lf_Xp"][128 * npc:128 * npc + 128,
                                      512 * half:512 * half + 512],
                        xstage_p[q][:, :], after=ev_x[q][-1:], sem="st"))
            xp_st = xp_st[0] + xp_st[1]
            c3.close()

        if LF_PART < 2:
            p.barrier("sync")
            p.barrier("gpsimd")
            return
        rs = p.coll("ReduceScatter", ALU.add, ALLC,
                    [dram["lf_Xp"].ap().opt()], [dram["lf_Xs"].ap().opt()],
                    after=xp_st)

        with ExitStack() as c2:
            Xe = [sb(c2, nc, f"lf_Xe{i}", [128, 1024]) for i in range(2)]
            Xeb = [sb(c2, nc, f"lf_Xeb{i}", [128, 1024], BF16)
                   for i in range(2)]
            XeT = [sb(c2, nc, f"lf_XeT{d}", [128, 256], BF16)
                   for d in range(NC_D)]
            col1 = sb(c2, nc, "lf_col1", [128, 1])
            col2 = sb(c2, nc, "lf_col2", [128, 1])
            sqs = sb(c2, nc, "lf_sqs", [128, 1024])
            ebrow = sb(c2, nc, "lf_ebrow", [1, 8192], BF16)
            ytst = sb(c2, nc, "lf_ytst", [128, 1024], BF16)
            ewsl = [sb(c2, nc, f"lf_ew{i}", [128, 8192], BF16)
                    for i in range(2)]

            xs_toks = [p.dma("sync", Xe[i][:, :],
                             dram["lf_Xs"][128 * i:128 * i + 128, :],
                             after=[rs]) for i in range(2)]
            teb = p.dma("gpsimd", ebrow[:, :],
                        dram["lf_eb"].ap().rearrange("a j -> (a j)")
                        .unsqueeze(0), after=[rs])
            ln_toks = []
            for i in range(2):
                cl = p.vec("tensor_scalar", Xe[i][:, :], Xe[i][:, :],
                           65000.0, -33000.0, ALU.min, ALU.max,
                           after=xs_toks)
                r1 = p.vec("reduce_sum", col1[:, :], Xe[i][:, :], AX.X,
                           after=[cl])
                r2 = p.vec("tensor_scalar_mul", col1[:, :], col1[:, :],
                           1.0 / 1024.0, after=[r1])
                c1_ = p.vec("tensor_scalar_sub", Xe[i][:, :], Xe[i][:, :],
                            col1[:, :], after=[r2])
                sq = p.act(sqs[:, :], Xe[i][:, :], AF.Square, after=[c1_])
                r3 = p.vec("reduce_sum", col2[:, :], sqs[:, :], AX.X,
                           after=[sq])
                r4 = p.act(col2[:, :], col2[:, :], AF.Ln, bias=1e-5,
                           scale=1.0 / 1024.0, after=[r3])
                r5 = p.act(col2[:, :], col2[:, :], AF.Exp, scale=-0.5,
                           after=[r4])
                mu = p.vec("tensor_scalar_mul", Xe[i][:, :], Xe[i][:, :],
                           col2[:, :], after=[r5])
                th = p.act(Xeb[i][:, :], Xe[i][:, :], AF.Tanh, after=[mu])
                ln_toks.append(th)
            if "lf_dbgx" in dram:
                for i in range(2):
                    p.dma("sync", dram["lf_dbgx"][128 * i:128 * i + 128, :],
                          Xeb[i][:, :], after=ln_toks, sem="st")
            tr_toks = []
            for i in range(2):
                for d in range(NC_D):
                    tt = p.trans(ptb[:, :],
                                 Xeb[i][:, 128 * d:128 * d + 128],
                                 ident[:, :], after=ln_toks + tr_toks[-1:])
                    te = p.vec("tensor_copy",
                               XeT[d][:, 128 * i:128 * i + 128], ptb[:, :],
                               after=[tt])
                    tr_toks.append(te)
            ew_tok = [None, None]
            ew_use = [None, None]

            def ew_load(e):
                i = e % 2
                aft = [ew_use[i]] if ew_use[i] else []
                ew_tok[i] = p.dma(
                    "gpsimd",
                    ewsl[i][:, :].rearrange("q (a j) -> q a j", j=1024),
                    dram["lf_ew"][e, :, :].rearrange("(a q) j -> q a j",
                                                     q=128), after=aft)

            ew_load(0)
            warm3 = p.mm(p3[0:32, :], [
                (XeT[k][:, 0:32], ewsl[0][:, 1024 * k:1024 * k + 512])
                for k in range(NC_D)], after=[ew_tok[0]] + tr_toks)
            ew_use[0] = warm3
            ew_load(1)
            yt_st = []
            ev_y = None
            for e in range(8):
                bi = e % 2
                for half in range(2):
                    pr = [ew_tok[bi], teb] + tr_toks
                    if ev_y is not None:
                        pr.append(ev_y)
                    tmm = p.mm(p3[0:32, :], [
                        (o32r[:, :],
                         ebrow[0:1, 1024 * e + 512 * half:
                               1024 * e + 512 * half + 512])] + [
                        (XeT[k][:, 32 * e:32 * e + 32],
                         ewsl[bi][:, 1024 * k + 512 * half:
                                  1024 * k + 512 * half + 512])
                        for k in range(NC_D)], after=pr)
                    ew_use[bi] = tmm
                    ev_y = p.act(ytst[32 * (e % 4):32 * (e % 4) + 32,
                                      512 * half:512 * half + 512],
                                 p3[0:32, :], AF.Identity, after=[tmm])
                if e % 4 == 3:
                    yt_st.append(p.dma("gpsimd",
                                       dram["lf_Ytp"][128 * (e // 4):
                                                      128 * (e // 4) + 128, :],
                                       ytst[:, :], after=[ev_y], sem="st"))
                    if "lf_dbgy" in dram:
                        yt_st.append(p.dma(
                            "gpsimd",
                            dram["lf_dbgy"][128 * (e // 4):
                                            128 * (e // 4) + 128, :],
                            ytst[:, :], after=[ev_y], sem="st"))
                    ev_y = yt_st[-1]
                if e + 2 < 8:
                    ew_load(e + 2)
            ag = p.coll("AllGather", ALU.bypass, ALLC,
                        [dram["lf_Ytp"].ap().opt()],
                        [dram["lf_Ytg"].ap().opt()], after=yt_st)

        if LF_PART < 3:
            p.barrier("sync")
            p.barrier("gpsimd")
            return
        with ExitStack() as c2:
            yt = [sb(c2, nc, f"lf_yt{i}", [128, 1024], BF16)
                  for i in range(16)]
            Ctt = [sb(c2, nc, f"lf_ct{i}", [128, T], BF16)
                   for i in range(16)]
            xres_p = [sb(c2, nc, f"lf_xres{i}", [128, 512])
                      for i in range(2)]
            osc = sb(c2, nc, "lf_osc", [128, 512])
            ln_scr = make_ln_scratch(c2, nc, "lfln")
            psA = psum(c2, nc, "lf_psA", [1, 512])
            psB = psum(c2, nc, "lf_psB", [1, 512])
            r_tiles = [sb(c2, nc, f"lf_rt{d}", [128, 512])
                       for d in range(NC_D)]
            yt_toks = [p.dma("gpsimd", yt[i][:, :],
                             dram["lf_Ytg"][128 * i:128 * i + 128, :],
                             after=[ag]) for i in range(16)]
            ct_toks = [p.dma("sync", Ctt[i][:, :],
                             dram["lf_Ct"][128 * i:128 * i + 128, :],
                             after=p.now())
                       for i in range(16)]
            if "lf_dbgc" in dram:
                p.dma("sync", dram["lf_dbgc"].ap(), Ctt[0][:, :],
                      after=ct_toks, sem="st")
            warm4 = p.mm(p0[:, :], [(yt[k][:, 0:128], Ctt[k][:, 0:512])
                                    for k in range(16)],
                         after=yt_toks + ct_toks)
            lng = [warm4]
            pC = [p0, p3]
            for g in range(4):
                gsl = slice(512 * g, 512 * g + 512)
                r_done = []
                ev_q = [[], []]
                tx_q = [[], []]
                for d in range(NC_D):
                    q = d % 2
                    pr = yt_toks + ct_toks + lng + ev_q[q][-1:]
                    tmm = p.mm(pC[q][:, :], [
                        (yt[k][:, 128 * d:128 * d + 128], Ctt[k][:, gsl])
                        for k in range(16)], after=pr)
                    tx = p.dma("sync", xres_p[q][:, :],
                               stream_in[128 * d:128 * d + 128, gsl],
                               after=ev_q[q][-1:])
                    ev_o = p.vec("tensor_add", r_tiles[d][:, :],
                                 pC[q][:, :], xres_p[q][:, :],
                                 after=[tmm, tx] + lng)
                    ev_q[q].append(ev_o)
                    r_done.append(ev_o)

                if g == 0 and "lf_dbgr" in dram:
                    for dd in range(NC_D):
                        p.dma("sync",
                              dram["lf_dbgr"][128 * dd:128 * dd + 128, :],
                              r_tiles[dd][:, :], after=r_done, sem="st")

                def writer(dd, _s, src_ap, toks, g=g):
                    return p.dma("sync",
                                 out_fm[128 * dd:128 * dd + 128, gsl],
                                 src_ap, after=toks, sem="st")
                stoks, lng = layernorm_fm(
                    p, ln_scr, r_tiles, 512, ones_col, ones_row,
                    psA, psB, p1, p2, r_done, writer, guard=lng)
                lng = lng + stoks
        p.barrier("sync")


def declare_l1_ffn_dram(nc, dbg=False, in_h=None):
    dram = {}
    dram["lf_in"] = in_h if in_h is not None else nc.dram_tensor(
        "lf_in", [D, T], F32, kind="ExternalInput")
    dram["lf_out"] = nc.dram_tensor("lf_out", [D, T], F32,
                                    kind="ExternalOutput")
    for nm, shp, dt in [("lf_phiT", [1024, 2048], F32),
                        ("lf_phib", [1, 2048], F32),
                        ("lf_ew", [8, 1024, 1024], F32),
                        ("lf_eb", [8, 1024], F32),
                        ("lf_ident", [128, 128], BF16),
                        ("lf_R64", [64, T], BF16),
                        ("lf_R32C", [32, T], BF16)]:
        dram[nm] = nc.dram_tensor(nm, shp, dt, kind="ExternalInput")
    if dbg:
        dram["lf_dbgx"] = nc.dram_tensor("lf_dbgx", [256, 1024], BF16,
                                         kind="ExternalOutput")
        dram["lf_dbgy"] = nc.dram_tensor("lf_dbgy", [256, 1024], BF16,
                                         kind="ExternalOutput")
        dram["lf_dbgc"] = nc.dram_tensor("lf_dbgc", [128, T], BF16,
                                         kind="ExternalOutput")
        dram["lf_dbge"] = nc.dram_tensor("lf_dbge", [128, T], BF16,
                                         kind="ExternalOutput")
        dram["lf_dbgr"] = nc.dram_tensor("lf_dbgr", [D, 512], F32,
                                         kind="ExternalOutput")
        dram["lf_dbgd"] = nc.dram_tensor("lf_dbgd", [128, T], BF16,
                                         kind="ExternalOutput")
    dram["lf_Ct"] = nc.dram_tensor("lf_Ct", [T, T], BF16)
    dram["lf_Xp"] = nc.dram_tensor("lf_Xp", [2048, 1024], F32)
    dram["lf_Xs"] = nc.dram_tensor("lf_Xs", [256, 1024], F32)
    dram["lf_Ytp"] = nc.dram_tensor("lf_Ytp", [256, 1024], BF16)
    dram["lf_Ytg"] = nc.dram_tensor("lf_Ytg", [2048, 1024], BF16)
    return dram


def pack_l1_ffn_inputs(inputs, stream_in_full):
    phiT = np.ascontiguousarray(
        np.asarray(inputs["L1_ffn_phi_w"], np.float32).T)
    phib = np.asarray(inputs["L1_ffn_phi_b"], np.float32)[None, :]
    ew = np.asarray(inputs["L1_ffn_expert_w"], np.float32)
    eb = np.asarray(inputs["L1_ffn_expert_b"], np.float32)
    R64 = np.zeros((64, T), np.float32)
    for col in range(T):
        R64[col // 32, col] = 1.0
    R32C = np.zeros((32, T), np.float32)
    for col in range(T):
        R32C[col % 32, col] = 1.0
    shared = dict(lf_phiT=phiT, lf_phib=phib,
                  lf_ident=np.eye(128, dtype=np.float32).astype(BF),
                  lf_R64=R64.astype(BF), lf_R32C=R32C.astype(BF))
    maps = []
    for c in range(8):
        b, h = c // 2, c % 2
        m = dict(shared)
        m["lf_in"] = np.ascontiguousarray(
            stream_in_full[b, h * T:(h + 1) * T, :].T.astype(np.float32))
        m["lf_ew"] = np.ascontiguousarray(ew[8 * c:8 * c + 8])
        m["lf_eb"] = np.ascontiguousarray(eb[8 * c:8 * c + 8])
        maps.append(m)
    return maps


def run_l1_ffn(inputs, stream_in_full):
    nc = bass.Bass("TRN2", target_bir_lowering=False)
    with ExitStack() as ctx:
        p = P(nc, ctx)
        dram = declare_l1_ffn_dram(nc)
        build_l1_ffn(nc, ctx, p, dram)
        p.finalize()
    maps = pack_l1_ffn_inputs(inputs, stream_in_full)
    return run_bass_kernel_spmd(nc, maps, core_ids=list(range(8)))


def kernel_unfused(**inputs):
    res = run_hawk(inputs)
    y1 = _gather_fm(res, "y1")
    res = run_ffn_attn(inputs, 0, y1)
    x2 = _gather_fm(res, "fa_out")
    res = run_ffn_ffn(inputs, 0, x2)
    x3 = _gather_fm(res, "ff_out")
    res = run_ffn_attn(inputs, 1, x3)
    x4 = _gather_fm(res, "fa_out")
    res = run_ffn_ffn(inputs, 1, x4, res2_full=y1)
    y2 = _gather_fm(res, "ff_out2")
    res = run_l1_attn(inputs, y2)
    y3 = _gather_fm(res, "la_out")
    res = run_l1_ffn(inputs, y3)
    return _gather_fm(res, "lf_out")



def build_fused():
    nc = bass.Bass("TRN2", target_bir_lowering=False)
    with ExitStack() as ctx:
        p = P(nc, ctx)
        senti = sb(ctx, nc, "senti", [1, 1])

        def sentinel(i):
            p.vec("memset", senti[0:1, 0:1], 601.0 + i)
        dram_h = declare_hawk_dram(nc, stream_out_kind="Internal")
        y1 = dram_h["y1"]
        p.set_phase("hawk")
        sentinel(0)
        with ExitStack() as ch:
            build_hawk(nc, ch, p, dram_h)
        p.barrier()

        p.set_phase("fa0")
        sentinel(1)
        s_x2 = nc.dram_tensor("s_x2", [D, T], F32)
        d = declare_ffn_attn_dram(nc, p, pfx="a0_", in_h=y1, out_h=s_x2)
        build_ffn_attn_prologue(nc, ctx, p, d)
        build_ffn_attn(nc, ctx, p, d)
        p.barrier()

        p.set_phase("ff0")
        sentinel(2)
        s_x3 = nc.dram_tensor("s_x3", [D, T], F32)
        d = declare_ffn_ffn_dram(nc, False, pfx="f0_", in_h=s_x2, out_h=s_x3)
        build_col_prologue(nc, ctx, p, d, "ff_in", "ff_cpub", "ff_cgat",
                           "ff_cgat_tok")
        build_ffn_ffn(nc, ctx, p, d, False)
        p.barrier()

        p.set_phase("fa1")
        sentinel(3)
        s_x4 = nc.dram_tensor("s_x4", [D, T], F32)
        d = declare_ffn_attn_dram(nc, p, pfx="a1_", in_h=s_x3, out_h=s_x4)
        build_ffn_attn_prologue(nc, ctx, p, d)
        build_ffn_attn(nc, ctx, p, d)
        p.barrier()

        p.set_phase("ff1")
        sentinel(4)
        s_y2 = nc.dram_tensor("s_y2", [D, T], F32)
        s_x5 = nc.dram_tensor("s_x5", [D, T], F32)
        d = declare_ffn_ffn_dram(nc, True, pfx="f1_", in_h=s_x4, out_h=s_x5,
                                 res2_h=y1, out2_h=s_y2)
        build_col_prologue(nc, ctx, p, d, "ff_in", "ff_cpub", "ff_cgat",
                           "ff_cgat_tok")
        build_ffn_ffn(nc, ctx, p, d, True)
        p.barrier()

        p.set_phase("la")
        sentinel(5)
        s_y3 = nc.dram_tensor("s_y3", [D, T], F32)
        d = declare_l1_attn_dram(nc, in_h=s_y2, out_h=s_y3)
        build_l1_attn(nc, ctx, p, d)
        p.barrier()

        p.set_phase("lf")
        sentinel(6)
        d = declare_l1_ffn_dram(nc, in_h=s_y3)
        build_l1_ffn(nc, ctx, p, d)
        p.finalize()
    return nc


def pack_fused_inputs(inputs):
    maps = pack_hawk_inputs(inputs)
    Z = np.zeros((4, 4096, D), np.float32)
    for sub, pfx in ((0, "a0_"), (1, "a1_")):
        shared, per_core = make_ffn_attn_consts(inputs, sub)
        wqkv = np.ascontiguousarray(
            np.asarray(inputs["L0_ffn_wqkv"], np.float32)[sub])
        wo = np.ascontiguousarray(
            np.asarray(inputs["L0_ffn_wo"], np.float32)[sub])
        for c in range(8):
            m = dict(fa_wqkv=wqkv, fa_wo=wo)
            m.update(shared)
            m.update(per_core[c])
            maps[c].update({pfx + k: v for k, v in m.items()})
    for sub, pfx in ((0, "f0_"), (1, "f1_")):
        fmaps = pack_ffn_ffn_inputs(inputs, sub, Z)
        for c in range(8):
            maps[c].update({pfx + k: v for k, v in fmaps[c].items()
                            if k not in ("ff_in", "ff_res2")})
    lmaps = pack_l1_attn_inputs(inputs, Z)
    for c in range(8):
        maps[c].update({k: v for k, v in lmaps[c].items() if k != "la_in"})
    fmaps = pack_l1_ffn_inputs(inputs, Z)
    for c in range(8):
        maps[c].update({k: v for k, v in fmaps[c].items() if k != "lf_in"})
    return maps


def kernel(**inputs):
    nc = build_fused()
    maps = pack_fused_inputs(inputs)
    res = run_bass_kernel_spmd(nc, maps, core_ids=list(range(8)))
    return _gather_fm(res, "lf_out")


# version 07
# speedup vs baseline: 1.0913x; 1.0088x over previous
import numpy as np
import ml_dtypes
from contextlib import ExitStack

import concourse.bass as bass
import concourse.mybir as mybir
from concourse.bass_utils import run_bass_kernel_spmd

F32 = mybir.dt.float32
BF16 = mybir.dt.bfloat16
AF = mybir.ActivationFunctionType
ALU = mybir.AluOpType
AX = mybir.AxisListType
BF = ml_dtypes.bfloat16
f32 = np.float32

ENGINES = ("sync", "tensor", "scalar", "vector", "gpsimd")
DEBUG_BARRIERS = False
DBG_NOHALO = False
LF_PART = 3
LF_BIAS = True
LF_NTC = 16
PAIRS = [[0, 1], [2, 3], [4, 5], [6, 7]]
ALLC = [list(range(8))]

T = 2048
D = 1024
HID = 1536
NC_D = D // 128
NC_H = HID // 128


class P:

    NDMASEM = 12

    def __init__(self, nc, ctx):
        self.nc = nc
        self.ctx = ctx
        self.ops = {e: [] for e in ENGINES}
        self.opph = {e: [] for e in ENGINES}
        self.phase = "init"
        self.semh = {}
        self.cnt = {}
        self.lastw = {e: {} for e in ENGINES}
        self.dmarr = {"sync": 0, "gpsimd": 0}

    def _app(self, eng, fn):
        self.ops[eng].append(fn)
        self.opph[eng].append(self.phase)

    def set_phase(self, name):
        self.phase = name

    def sem(self, name):
        if name not in self.semh:
            self.semh[name] = self.ctx.enter_context(self.nc.semaphore(name))
            self.cnt[name] = 0
        return self.semh[name]

    def wait(self, eng, tokens):
        for t in tokens:
            if t is None:
                continue
            s, v = t
            if v <= 0 or self.lastw[eng].get(s, 0) >= v:
                continue
            self.lastw[eng][s] = v
            h = self.sem(s)
            self._app(eng, lambda E, h=h, v=v: E.wait_ge(h, v))

    def _tok(self, sem, n):
        self.sem(sem)
        self.cnt[sem] += n
        return (sem, self.cnt[sem])

    def now(self):
        return [(s, v) for s, v in self.cnt.items() if v > 0]

    def barrier(self, eng=None):
        toks = self.now()
        for e in (ENGINES if eng is None else [eng]):
            self.wait(e, toks)

    def dma(self, q, out, in_, after=(), sem=None, **kw):
        i = self.dmarr[q]
        self.dmarr[q] = (i + 1) % self.NDMASEM
        sem = f"dma_{q}_{i}"
        self.wait(q, after)
        tok = self._tok(sem, 16)
        h = self.sem(sem)
        self._app(q,
            lambda E: E.dma_start(out=out, in_=in_, **kw).then_inc(h, 16))
        return tok

    def mm(self, psum_ap, pairs, after=(), start=True, stop=True, sem="pe"):
        self.wait("tensor", after)
        tok = self._tok(sem, 1)
        h = self.sem(sem)
        n = len(pairs)

        def f(E, pairs=pairs, n=n, start=start, stop=stop):
            ins = None
            for i, (l, r) in enumerate(pairs):
                ins = E.matmul(psum_ap, l, r, start=(start and i == 0),
                               stop=(stop and i == n - 1))
            ins.then_inc(h, 1)
        self._app("tensor", f)
        return tok

    def trans(self, psum_ap, in_ap, ident_ap, after=(), sem="pe"):
        self.wait("tensor", after)
        tok = self._tok(sem, 1)
        h = self.sem(sem)
        self._app("tensor",
            lambda E: E.transpose(psum_ap, in_ap, ident_ap).then_inc(h, 1))
        return tok

    def act(self, out, in_, func, after=(), bias=0.0, scale=1.0,
            accum_out=None, sem="sc"):
        if sem in self.cnt and self.cnt[sem] > 0:
            self.wait("scalar", [(sem, self.cnt[sem])])
        self.wait("scalar", after)
        tok = self._tok(sem, 1)
        h = self.sem(sem)
        if isinstance(bias, float):
            bias = f32(bias)
        if isinstance(scale, float):
            scale = f32(scale)
        self._app("scalar",
            lambda E: E.activation(out, in_, func, bias=bias, scale=scale,
                                   accum_out=accum_out).then_inc(h, 1))
        return tok

    def vec(self, meth, *args, after=(), sem="ve", **kw):
        if sem in self.cnt and self.cnt[sem] > 0:
            self.wait("vector", [(sem, self.cnt[sem])])
        self.wait("vector", after)
        tok = self._tok(sem, 1)
        h = self.sem(sem)

        def f(E, meth=meth, args=args, kw=kw):
            getattr(E, meth)(*args, **kw).then_inc(h, 1)
        self._app("vector", f)
        return tok

    def gps(self, meth, *args, after=(), sem="gc", **kw):
        self.wait("gpsimd", after)
        tok = self._tok(sem, 1)
        h = self.sem(sem)

        def f(E, meth=meth, args=args, kw=kw):
            getattr(E, meth)(*args, **kw).then_inc(h, 1)
        self._app("gpsimd", f)
        return tok

    def coll(self, kind, op, groups, ins, outs, after=()):
        self.wait("gpsimd", after)
        tok = self._tok("cc", 1)
        h = self.sem("cc")
        self._app("gpsimd",
            lambda E: E.collective_compute(
                kind, op, replica_groups=groups, ins=ins, outs=outs,
            ).then_inc(h, 1))
        return tok

    def finalize(self):
        nc = self.nc
        with nc.Block() as block:
            for eng in ENGINES:
                ops = self.ops[eng]
                if not ops:
                    continue
                phs = self.opph[eng]

                def runner(E, ops=ops, phs=phs):
                    cur = None
                    for fn, ph in zip(ops, phs):
                        if ph != cur:
                            if cur is not None:
                                nc.pop_named_scope(cur)
                            nc.push_named_scope(ph)
                            cur = ph
                        fn(E)
                    if cur is not None:
                        nc.pop_named_scope(cur)
                getattr(block, eng)(runner)


import itertools
_UNIQ = itertools.count()


def sb(ctx, nc, name, shape, dt=F32):
    return ctx.enter_context(
        nc.sbuf_tensor(f"{name}_u{next(_UNIQ)}", shape, dt))


def psum(ctx, nc, name, shape, dt=F32):
    return ctx.enter_context(
        nc.psum_tensor(f"{name}_u{next(_UNIQ)}", shape, dt))


class Streamer:

    def __init__(self, p, bufs):
        self.p = p
        self.bufs = bufs
        self.load_tok = [None] * len(bufs)
        self.used_tok = [None] * len(bufs)
        self.i = 0

    def load(self, in_ap, after=()):
        i = self.i % len(self.bufs)
        self.i += 1
        aft = list(after)
        if self.used_tok[i] is not None:
            aft.append(self.used_tok[i])
        self.load_tok[i] = self.p.dma(
            "gpsimd", self.bufs[i][:, :].rearrange("q (a j) -> q a j", j=128),
            in_ap, after=aft)
        return i

    def note_use(self, i, tok):
        self.used_tok[i] = tok


def slab_ap(W, m, w=128):
    return W[:, w * m:w * m + w].rearrange("(a q) j -> q a j", q=128)



def softplus_cols(p, ctx, nc, lam_cols, ld_tok):
    a1 = sb(ctx, nc, "sp_a1", [128, NC_H])
    sp8 = sb(ctx, nc, "sp8", [128, NC_H])
    sp16 = sb(ctx, nc, "sp16", [128, NC_H])
    t1 = p.act(a1[:, :], lam_cols[:, :], AF.Abs, after=[ld_tok])
    t2 = p.act(a1[:, :], a1[:, :], AF.Exp, scale=-1.0, after=[t1])
    t3 = p.act(a1[:, :], a1[:, :], AF.Ln, bias=1.0, after=[t2])
    t4 = p.act(sp8[:, :], lam_cols[:, :], AF.Relu, after=[ld_tok])
    t5 = p.vec("tensor_add", a1[:, :], a1[:, :], sp8[:, :], after=[t3, t4])
    t6 = p.vec("tensor_scalar_mul", sp8[:, :], a1[:, :], -8.0, after=[t5])
    t7 = p.vec("tensor_scalar_mul", sp16[:, :], a1[:, :], -16.0, after=[t5])
    return sp8, sp16, [t6, t7]


def make_ln_scratch(ctx, nc, tag):
    return dict(
        rb=sb(ctx, nc, f"{tag}_rb", [128, 512], BF16),
        rb2=sb(ctx, nc, f"{tag}_rb2", [128, 512], BF16),
        r0=sb(ctx, nc, f"{tag}_r0", [1, 512]),
        r1=sb(ctx, nc, f"{tag}_r1", [1, 512]),
        r2=sb(ctx, nc, f"{tag}_r2", [1, 512]),
        r3=sb(ctx, nc, f"{tag}_r3", [1, 512]),
        brow=sb(ctx, nc, f"{tag}_brow", [1, 512]),
        a_bf=sb(ctx, nc, f"{tag}_abf", [1, 512], BF16),
        b_bf=sb(ctx, nc, f"{tag}_bbf", [1, 512], BF16))


def layernorm_fm(p, scratch, r_tiles, width, ones_col, ones_row,
                 psA, psB, pA, pB, after, out_write, guard=()):
    nseg = width // 512
    rb = scratch["rb"]
    rb2 = scratch["rb2"]
    r0, r1, r2, r3 = (scratch["r0"], scratch["r1"], scratch["r2"],
                      scratch["r3"])
    brow = scratch["brow"]
    a_bf, b_bf = scratch["a_bf"], scratch["b_bf"]
    out_toks = []
    guard = list(guard)
    for seg in range(nseg):
        sl = slice(seg * 512, seg * 512 + 512)
        prev = list(after) + guard
        ta = tb = None
        for d in range(NC_D):
            tc = p.vec("tensor_copy", rb[:, :], r_tiles[d][:, sl], after=prev)
            ts = p.act(rb2[:, :], r_tiles[d][:, sl], AF.Square, after=prev)
            ta = p.mm(psA[:, :], [(ones_col[:, :], rb[:, :])],
                      after=[tc], start=(d == 0), stop=(d == NC_D - 1))
            tb = p.mm(psB[:, :], [(ones_col[:, :], rb2[:, :])],
                      after=[ts], start=(d == 0), stop=(d == NC_D - 1))
            prev = list(after) + [ta, tb]
        t1 = p.act(r0[:, :], psA[0:1, :], AF.Identity, scale=1.0 / D,
                   after=[ta, tb])
        t2 = p.act(r1[:, :], psB[0:1, :], AF.Identity, scale=1.0 / D,
                   after=[ta, tb])
        t3 = p.act(r2[:, :], r0[:, :], AF.Square, after=[t1])
        t4 = p.vec("tensor_sub", r1[:, :], r1[:, :], r2[:, :],
                   after=[t2, t3])
        t5 = p.act(r2[:, :], r1[:, :], AF.Ln, bias=1e-5, after=[t4])
        t6 = p.act(r3[:, :], r2[:, :], AF.Exp, scale=-0.5, after=[t5])
        t7 = p.vec("tensor_mul", brow[:, :], r0[:, :], r3[:, :],
                   after=[t6])
        t8 = p.vec("tensor_scalar_mul", brow[:, :], brow[:, :], -1.0,
                   after=[t7])
        t9 = p.vec("tensor_copy", a_bf[:, :], r3[:, :], after=[t6])
        t10 = p.vec("tensor_copy", b_bf[:, :], brow[:, :], after=[t8])
        bA = p.mm(pA[:, :], [(ones_row[:, :], a_bf[:, :])],
                  after=[t9] + guard)
        bB = p.mm(pB[:, :], [(ones_row[:, :], b_bf[:, :])],
                  after=[t10] + guard)
        seg_toks = []
        tn = None
        for d in range(NC_D):
            tm = p.vec("tensor_mul", r_tiles[d][:, sl], r_tiles[d][:, sl],
                       pA[:, :], after=[bA, bB])
            tn = p.vec("tensor_add", r_tiles[d][:, sl], r_tiles[d][:, sl],
                       pB[:, :], after=[tm])
            seg_toks.append(out_write(d, seg, r_tiles[d][:, sl], [tn]))
        guard = [t1, t2, tn]
        out_toks += seg_toks
    return out_toks, guard



def build_hawk(nc, ctx, p, dram):
    G = 1024
    NSEG = 2

    xfm = dram["xfm"]
    y1 = dram["y1"]

    xb = [sb(ctx, nc, f"hk_xb{k}", [128, G + 3], BF16) for k in range(NC_D)]
    wslab = [sb(ctx, nc, f"hk_wsl{i}", [128, 1024], BF16) for i in range(2)]
    lwi = [sb(ctx, nc, f"hk_lwi{i}", [128, 1536], BF16) for i in range(2)]
    lwg = [sb(ctx, nc, f"hk_lwg{i}", [128, 1536], BF16) for i in range(2)]
    wosl = [sb(ctx, nc, f"hk_wos{i}", [128, 1536], BF16) for i in range(2)]
    y_g = [sb(ctx, nc, f"hk_yg{m}", [128, G], BF16) for m in range(NC_H)]
    h_st = [sb(ctx, nc, f"hk_h{m}", [128, T], BF16) for m in range(NC_H)]
    cp_st = [sb(ctx, nc, f"hk_cp{m}", [128, T], BF16) for m in range(NC_H)]
    u_sb = sb(ctx, nc, "hk_u", [128, 515])
    tA = sb(ctx, nc, "hk_tA", [128, 512])
    tB = sb(ctx, nc, "hk_tB", [128, 512])
    tC = sb(ctx, nc, "hk_tC", [128, 512])
    tD = sb(ctx, nc, "hk_tD", [128, 512])
    zeros512 = sb(ctx, nc, "hk_z512", [128, 512])
    hcarry = sb(ctx, nc, "hk_hcar", [128, NC_H])
    cpcarry = sb(ctx, nc, "hk_cpcar", [128, NC_H])
    carrycol = sb(ctx, nc, "hk_carcol", [128, NC_H])
    lam_cols = sb(ctx, nc, "hk_lam", [128, NC_H])
    convw = sb(ctx, nc, "hk_cw", [128, NC_H * 4])
    convb = sb(ctx, nc, "hk_cb", [128, NC_H])
    nbi = sb(ctx, nc, "hk_nbi", [128, NC_H])
    nbg = sb(ctx, nc, "hk_nbg", [128, NC_H])
    is_odd = sb(ctx, nc, "hk_odd", [128, 1])
    ones_col = sb(ctx, nc, "hk_oc", [128, 1], BF16)
    ones_row = sb(ctx, nc, "hk_or", [1, 128], BF16)
    r_tiles = [sb(ctx, nc, f"hk_rt{d}", [128, 512]) for d in range(NC_D)]
    xres = sb(ctx, nc, "hk_xres", [128, 512])
    gg = sb(ctx, nc, "hk_gg", [128, 512])
    ln_scr = make_ln_scratch(ctx, nc, "hkln")

    p0 = psum(ctx, nc, "hk_p0", [128, 512])
    p1 = psum(ctx, nc, "hk_p1", [128, 512])
    p2 = psum(ctx, nc, "hk_p2", [128, 512])
    p3 = psum(ctx, nc, "hk_p3", [128, 512])
    ph3 = psum(ctx, nc, "hk_ph3", [128, 8])
    psA = psum(ctx, nc, "hk_psA", [1, 512])
    psB = psum(ctx, nc, "hk_psB", [1, 512])

    tz = p.vec("memset", zeros512[:, :], 0.0)
    p.vec("memset", hcarry[:, :], 0.0)
    p.vec("memset", cpcarry[:, :], 1.0)
    p.vec("memset", ones_col[:, :], 1.0)
    p.vec("memset", ones_row[:, :], 1.0)
    tl1 = p.dma("sync", lam_cols[:, :], dram["lam_cols"].ap())
    p.dma("sync", convw[:, :], dram["convw_cols"].ap())
    p.dma("sync", convb[:, :], dram["convb_cols"].ap())
    tl4 = p.dma("sync", is_odd[:, :], dram["is_odd"].ap())
    tb1 = p.dma("sync", nbi[:, :], dram["bi_cols"].ap())
    tb2 = p.dma("sync", nbg[:, :], dram["bg_cols"].ap())
    tn1 = p.vec("tensor_scalar_mul", nbi[:, :], nbi[:, :], -1.0, after=[tb1])
    tn2 = p.vec("tensor_scalar_mul", nbg[:, :], nbg[:, :], -1.0, after=[tb2])
    sp8, sp16, sp_toks = softplus_cols(p, ctx, nc, lam_cols, tl1)
    p.barrier("vector")

    WiT, WgT = dram["WiT"], dram["WgT"]
    lruWiT, lruWgT, WoT = dram["lruWiT"], dram["lruWgT"], dram["WoT"]

    scan_done = []

    for g in range(2):
        if g == 1:
            p.barrier()
        xcol0 = g * G
        xb_toks = [p.dma("gpsimd", xb[k][:, :],
                         xfm[128 * k:128 * k + 128, xcol0:xcol0 + G + 3])
                   for k in range(NC_D)]
        ws = Streamer(p, wslab)
        ws.load(slab_ap(WiT, 0))
        ws.load(slab_ap(WiT, 1))
        warm = p.mm(p0[:, :], [
            (wslab[0][:, 128 * k:128 * k + 128],
             xb[k][:, 3:515]) for k in range(NC_D)],
            after=[ws.load_tok[0]] + xb_toks)
        ws.note_use(0, warm)
        evac_u = evac_h = conv_last = None
        for mi, m in enumerate(range(NC_H)):
            bi = (mi % 2)
            buf = wslab[bi]
            for seg in range(NSEG):
                pr = [ws.load_tok[bi]] + xb_toks
                if evac_u is not None:
                    pr.append(evac_u)
                tmm = p.mm(p0[:, :], [
                    (buf[:, 128 * k:128 * k + 128],
                     xb[k][:, 3 + seg * 512: 3 + seg * 512 + 512])
                    for k in range(NC_D)], after=pr)
                ws.note_use(bi, tmm)
                if DBG_NOHALO:
                    evac_u = p.vec("tensor_copy", u_sb[:, 3:515], p0[:, :],
                                   after=[tmm] + ([conv_last] if conv_last else []))
                    evac_h = p.vec("memset", u_sb[:, 0:3], 0.0,
                                   after=[conv_last] if conv_last else [])
                else:
                    tm3 = p.mm(ph3[:, 0:3], [
                        (buf[:, 128 * k:128 * k + 128],
                         xb[k][:, seg * 512: seg * 512 + 3])
                        for k in range(NC_D)],
                        after=[evac_h] if evac_h else [])
                    ws.note_use(bi, tm3)
                    evac_u = p.vec("tensor_copy", u_sb[:, 3:515], p0[:, :],
                                   after=[tmm] + ([conv_last] if conv_last else []))
                    evac_h = p.vec("tensor_copy", u_sb[:, 0:3], ph3[:, 0:3],
                                   after=[tm3] + ([conv_last] if conv_last else []))
                c1 = p.vec("tensor_scalar", tA[:, :], u_sb[:, 3:515],
                           convw[:, 3 * NC_H + m:3 * NC_H + m + 1],
                           convb[:, m:m + 1], ALU.mult, ALU.add,
                           after=[evac_u, evac_h])
                c2 = p.vec("scalar_tensor_tensor", tA[:, :], u_sb[:, 2:514],
                           convw[:, 2 * NC_H + m:2 * NC_H + m + 1], tA[:, :],
                           ALU.mult, ALU.add, after=[c1])
                c3 = p.vec("scalar_tensor_tensor", tA[:, :], u_sb[:, 1:513],
                           convw[:, 1 * NC_H + m:1 * NC_H + m + 1], tA[:, :],
                           ALU.mult, ALU.add, after=[c2])
                conv_last = p.vec(
                    "scalar_tensor_tensor",
                    y_g[m][:, seg * 512:seg * 512 + 512],
                    u_sb[:, 0:512], convw[:, 0 * NC_H + m:0 * NC_H + m + 1],
                    tA[:, :], ALU.mult, ALU.add, after=[c3])
                if DEBUG_BARRIERS:
                    p.barrier()
                if "dbg_u" in dram and g == 0 and m == 0 and seg == 0:
                    p.dma("sync", dram["dbg_u"].ap(), u_sb[:, :],
                          after=[evac_u, evac_h], sem="st")
                    p.dma("gpsimd", dram["dbg_tA"].ap(),
                          y_g[0][:, 0:512], after=[conv_last], sem="st")
            if mi + 2 < NC_H:
                ws.load(slab_ap(WiT, mi + 2))
        y_done = [conv_last, evac_u]
        if "dbg_y" in dram:
            for m_ in range(NC_H):
                p.dma("sync", dram["dbg_y"][128 * m_:128 * m_ + 128,
                                            g * G:(g + 1) * G],
                      y_g[m_][:, :], after=y_done, sem="st")
            p.dma("gpsimd", dram["dbg_y2"][:, g * G:(g + 1) * G],
                  y_g[0][:, :], after=y_done, sem="st")

        si_s = Streamer(p, lwi)
        sg_s = Streamer(p, lwg)
        si_s.load(slab_ap(lruWiT, 0), after=y_done)
        sg_s.load(slab_ap(lruWgT, 0), after=y_done)
        si_s.load(slab_ap(lruWiT, 1))
        sg_s.load(slab_ap(lruWgT, 1))
        warm2 = p.mm(p1[:, :], [
            (lwi[0][:, 128 * k:128 * k + 128], y_g[k][:, 0:512])
            for k in range(NC_H)], after=[si_s.load_tok[0]] + y_done)
        si_s.note_use(0, warm2)
        warm3 = p.mm(p2[:, :], [
            (lwg[0][:, 128 * k:128 * k + 128], y_g[k][:, 0:512])
            for k in range(NC_H)], after=[sg_s.load_tok[0]] + y_done)
        sg_s.note_use(0, warm3)
        ev_i = ev_g = last_v = None
        for m in range(NC_H):
            bi = m % 2
            for seg in range(NSEG):
                gsl = slice(seg * 512, seg * 512 + 512)
                tsl = slice(g * G + seg * 512, g * G + seg * 512 + 512)
                pr = [si_s.load_tok[bi]] + y_done
                if ev_i is not None:
                    pr.append(ev_i)
                t_si = p.mm(p1[:, :], [
                    (lwi[bi][:, 128 * k:128 * k + 128], y_g[k][:, gsl])
                    for k in range(NC_H)], after=pr)
                si_s.note_use(bi, t_si)
                pr = [sg_s.load_tok[bi]] + y_done
                if ev_g is not None:
                    pr.append(ev_g)
                t_sg = p.mm(p2[:, :], [
                    (lwg[bi][:, 128 * k:128 * k + 128], y_g[k][:, gsl])
                    for k in range(NC_H)], after=pr)
                sg_s.note_use(bi, t_sg)
                lw = [last_v] if last_v else []
                e1 = p.act(tA[:, :], p1[:, :], AF.Exp, scale=-1.0,
                           bias=nbi[:, m:m + 1], after=[t_si, tn1] + lw)
                ev_i = e1
                v1 = p.vec("tensor_scalar_add", tA[:, :], tA[:, :],
                           1.0, after=[e1])
                v2 = p.vec("reciprocal", tA[:, :], tA[:, :], after=[v1])
                e2 = p.act(tB[:, :], p2[:, :], AF.Exp, scale=-1.0,
                           bias=nbg[:, m:m + 1], after=[t_sg, tn2] + lw)
                ev_g = e2
                v3 = p.vec("tensor_scalar_add", tB[:, :], tB[:, :],
                           1.0, after=[e2])
                v4 = p.vec("reciprocal", tB[:, :], tB[:, :], after=[v3])
                a1 = p.act(tC[:, :], tB[:, :], AF.Exp,
                           scale=sp8[:, m:m + 1], after=[v4] + sp_toks)
                a2 = p.act(tD[:, :], tB[:, :], AF.Exp,
                           scale=sp16[:, m:m + 1], after=[v4] + sp_toks)
                z1 = p.vec("tensor_scalar", tD[:, :], tD[:, :], -1.0,
                           1.000001, ALU.mult, ALU.add, after=[a2])
                z2 = p.act(tD[:, :], tD[:, :], AF.Ln, after=[z1])
                z3 = p.act(tD[:, :], tD[:, :], AF.Exp, scale=0.5, after=[z2])
                x1 = p.vec("tensor_mul", tB[:, :], tD[:, :], tA[:, :],
                           after=[z3, v2])
                x2 = p.vec("tensor_mul", tB[:, :], tB[:, :], y_g[m][:, gsl],
                           after=[x1])
                s1 = p.vec("tensor_tensor_scan", h_st[m][:, tsl], tC[:, :],
                           tB[:, :], hcarry[:, m:m + 1], ALU.mult, ALU.add,
                           after=[x2, a1])
                s2 = p.vec("tensor_tensor_scan", cp_st[m][:, tsl], tC[:, :],
                           zeros512[:, :], cpcarry[:, m:m + 1], ALU.mult,
                           ALU.add, after=[s1, tz])
                s3 = p.vec("tensor_copy", hcarry[:, m:m + 1],
                           h_st[m][:, tsl.stop - 1:tsl.stop], after=[s1])
                s4 = p.vec("tensor_copy", cpcarry[:, m:m + 1],
                           cp_st[m][:, tsl.stop - 1:tsl.stop], after=[s2])
                last_v = s4
                if g == 1 and seg == NSEG - 1:
                    scan_done += [s3, s4]
            if m + 2 < NC_H:
                si_s.load(slab_ap(lruWiT, m + 2))
                sg_s.load(slab_ap(lruWgT, m + 2))

    cg_in, cg_out = dram["cg_in"], dram["cg_out"]
    tpub = p.dma("gpsimd", cg_in.ap(), hcarry[:, :], after=scan_done)
    tcoll = p.coll("AllGather", ALU.bypass, PAIRS,
                   [cg_in.ap().opt()], [cg_out.ap().opt()], after=[tpub])
    tback = p.dma("gpsimd", carrycol[:, :], cg_out[0:128, :], after=[tcoll])
    tcc = p.vec("tensor_scalar_mul", carrycol[:, :], carrycol[:, :],
                is_odd[:, :], after=[tback, tl4])
    if "dbg_h" in dram:
        for m_ in range(NC_H):
            p.dma("sync", dram["dbg_h"][128 * m_:128 * m_ + 128, :],
                  h_st[m_][:, :], after=scan_done, sem="st")
            p.dma("sync", dram["dbg_cp"][128 * m_:128 * m_ + 128, :],
                  cp_st[m_][:, :], after=scan_done, sem="st")
        p.dma("sync", dram["dbg_carry"].ap(), carrycol[:, :], after=[tcc],
              sem="st")

    ln_guard = []
    for g in range(2):
        p.barrier()
        xcol0 = g * G
        xb_toks = [p.dma("gpsimd", xb[k][:, :],
                         xfm[128 * k:128 * k + 128, xcol0:xcol0 + G + 3])
                   for k in range(NC_D)]
        gs = Streamer(p, wslab)
        gs.load(slab_ap(WgT, 0))
        gs.load(slab_ap(WgT, 1))
        warm4 = p.mm(p0[:, :], [
            (wslab[0][:, 128 * k:128 * k + 128],
             xb[k][:, 3:515]) for k in range(NC_D)],
            after=[gs.load_tok[0]] + xb_toks)
        gs.note_use(0, warm4)
        ev_gate = None
        prev_f2 = None
        mm_done = []
        for m in range(NC_H):
            bi = m % 2
            for seg in range(NSEG):
                gsl = slice(seg * 512, seg * 512 + 512)
                tsl = slice(g * G + seg * 512, g * G + seg * 512 + 512)
                pr = [gs.load_tok[bi]] + xb_toks
                if ev_gate is not None:
                    pr.append(ev_gate)
                tmm = p.mm(p0[:, :], [
                    (wslab[bi][:, 128 * k:128 * k + 128],
                     xb[k][:, 3 + seg * 512: 3 + seg * 512 + 512])
                    for k in range(NC_D)], after=pr)
                gs.note_use(bi, tmm)
                ga = [tmm] + ([prev_f2] if prev_f2 else [])
                ev_gate = p.act(gg[:, :], p0[:, :], AF.Gelu, after=ga)
                f1 = p.vec("scalar_tensor_tensor", tA[:, :],
                           cp_st[m][:, tsl], carrycol[:, m:m + 1],
                           h_st[m][:, tsl], ALU.mult, ALU.add,
                           after=[tcc, ev_gate])
                f2 = p.vec("tensor_mul", y_g[m][:, gsl], gg[:, :],
                           tA[:, :], after=[f1])
                prev_f2 = f2
                mm_done.append(f2)
            if m + 2 < NC_H:
                gs.load(slab_ap(WgT, m + 2))
        if "dbg_m" in dram:
            for m_ in range(NC_H):
                p.dma("sync", dram["dbg_m"][128 * m_:128 * m_ + 128,
                                            g * G:(g + 1) * G],
                      y_g[m_][:, :], after=mm_done, sem="st")
        os_ = Streamer(p, wosl)
        for seg in range(NSEG):
            os_.load(slab_ap(WoT, 0), after=mm_done[-2:])
            os_.load(slab_ap(WoT, 1))
            warm5 = p.mm(p1[:, :], [
                (wosl[0][:, 128 * k:128 * k + 128], y_g[k][:, 0:512])
                for k in range(NC_H)],
                after=[os_.load_tok[0]] + mm_done + ln_guard)
            os_.note_use(0, warm5)
            ev_out = None
            r_done = []
            gsl = slice(seg * 512, seg * 512 + 512)
            for d in range(NC_D):
                bi = d % 2
                pr = [os_.load_tok[bi]] + mm_done + ln_guard
                if ev_out is not None:
                    pr.append(ev_out)
                tmm = p.mm(p1[:, :], [
                    (wosl[bi][:, 128 * k:128 * k + 128], y_g[k][:, gsl])
                    for k in range(NC_H)], after=pr)
                os_.note_use(bi, tmm)
                tx = p.dma("sync", xres[:, :],
                           xfm[128 * d:128 * d + 128,
                               3 + g * G + seg * 512:
                               3 + g * G + seg * 512 + 512],
                           after=[ev_out] if ev_out else [])
                ev_out = p.vec("tensor_add", r_tiles[d][:, 0:512], p1[:, :],
                               xres[:, :], after=[tmm, tx] + ln_guard)
                r_done.append(ev_out)
                if d + 2 < NC_D:
                    os_.load(slab_ap(WoT, d + 2))

            def writer(dd, _s, src_ap, toks, g=g, seg=seg):
                return p.dma("sync",
                             y1[128 * dd:128 * dd + 128,
                                g * G + seg * 512: g * G + seg * 512 + 512],
                             src_ap, after=toks, sem="st")
            stoks, ln_guard = layernorm_fm(
                p, ln_scr, r_tiles, 512, ones_col, ones_row,
                psA, psB, p2, p3, r_done, writer, guard=ln_guard)
            ln_guard = ln_guard + stoks

    p.barrier("sync")



def cols128(v):
    v = np.asarray(v, dtype=np.float32)
    return np.ascontiguousarray(v.reshape(-1, 128).T)


def pack_hawk_inputs(inputs):
    x = np.asarray(inputs["L0_attn_x"], dtype=np.float32)
    WiT = np.ascontiguousarray(np.asarray(inputs["L0_attn_Wi"], dtype=np.float32).T)
    WgT = np.ascontiguousarray(np.asarray(inputs["L0_attn_Wg"], dtype=np.float32).T)
    lruWiT = np.ascontiguousarray(np.asarray(inputs["L0_attn_lru_Wi"], dtype=np.float32).T)
    lruWgT = np.ascontiguousarray(np.asarray(inputs["L0_attn_lru_Wg"], dtype=np.float32).T)
    WoT = np.ascontiguousarray(np.asarray(inputs["L0_attn_Wo"], dtype=np.float32).T)
    cw = np.asarray(inputs["L0_attn_conv_w"], dtype=np.float32)[:, 0, :]
    convw_cols = np.ascontiguousarray(
        cw.reshape(NC_H, 128, 4).transpose(1, 2, 0).reshape(128, 4 * NC_H))
    convb_cols = cols128(inputs["L0_attn_conv_b"])
    lam_cols = cols128(inputs["L0_attn_lam"])
    bi_cols = cols128(inputs["L0_attn_lru_bi"])
    bg_cols = cols128(inputs["L0_attn_lru_bg"])

    maps = []
    for c in range(8):
        b, h = c // 2, c % 2
        own = x[b, h * T:(h + 1) * T, :].T
        if h == 0:
            halo = np.zeros((D, 3), np.float32)
        else:
            halo = x[b, h * T - 3:h * T, :].T
        xfm = np.ascontiguousarray(np.concatenate([halo, own], axis=1))
        is_odd = np.full((128, 1), float(h), np.float32)
        maps.append(dict(xfm=xfm, WiT=WiT, WgT=WgT, lruWiT=lruWiT,
                         lruWgT=lruWgT, WoT=WoT, convw_cols=convw_cols,
                         convb_cols=convb_cols, lam_cols=lam_cols,
                         bi_cols=bi_cols, bg_cols=bg_cols, is_odd=is_odd))
    return maps


def declare_hawk_dram(nc, stream_in_kind="ExternalInput",
                      stream_out_kind="ExternalOutput"):
    dram = {}
    dram["xfm"] = nc.dram_tensor("xfm", [D, T + 3], F32, kind=stream_in_kind)
    for nm, shp in [("WiT", [D, HID]), ("WgT", [D, HID]),
                    ("lruWiT", [HID, HID]), ("lruWgT", [HID, HID]),
                    ("WoT", [HID, D]), ("convw_cols", [128, 4 * NC_H]),
                    ("convb_cols", [128, NC_H]), ("lam_cols", [128, NC_H]),
                    ("bi_cols", [128, NC_H]), ("bg_cols", [128, NC_H]),
                    ("is_odd", [128, 1])]:
        dram[nm] = nc.dram_tensor(nm, shp, F32, kind="ExternalInput")
    dram["y1"] = nc.dram_tensor("y1", [D, T], F32, kind=stream_out_kind)
    dram["cg_in"] = nc.dram_tensor("cg_in", [128, NC_H], F32)
    dram["cg_out"] = nc.dram_tensor("cg_out", [256, NC_H], F32)
    return dram


def declare_hawk_dbg(nc, dram):
    for nm, shp in [("dbg_y", [HID, T]), ("dbg_h", [HID, T]),
                    ("dbg_cp", [HID, T]), ("dbg_m", [HID, T]),
                    ("dbg_carry", [128, NC_H])]:
        dram[nm] = nc.dram_tensor(nm, shp, BF16 if nm != "dbg_carry" else F32,
                                  kind="ExternalOutput")
    dram["dbg_u"] = nc.dram_tensor("dbg_u", [128, 515], F32,
                                   kind="ExternalOutput")
    dram["dbg_y2"] = nc.dram_tensor("dbg_y2", [128, T], BF16,
                                    kind="ExternalOutput")
    dram["dbg_tA"] = nc.dram_tensor("dbg_tA", [128, 512], F32,
                                    kind="ExternalOutput")


def run_hawk(inputs, dbg=False):
    nc = bass.Bass("TRN2", target_bir_lowering=False)
    with ExitStack() as ctx:
        dram = declare_hawk_dram(nc)
        if dbg:
            declare_hawk_dbg(nc, dram)
        p = P(nc, ctx)
        build_hawk(nc, ctx, p, dram)
        p.finalize()
    maps = pack_hawk_inputs(inputs)
    return run_bass_kernel_spmd(nc, maps, core_ids=list(range(8)))


def _gather_fm(res, key):
    out = np.zeros((4, 4096, D), np.float32)
    for c in range(8):
        b, h = c // 2, c % 2
        out[b, h * T:(h + 1) * T, :] = res.results[c][key].T
    return out



def build_ffn_attn(nc, ctx, p, dram):
    NW = 4
    W = 512
    stream_in = dram["fa_in"]
    stream_out = dram["fa_out"]
    wqkv = dram["fa_wqkv"]
    wo = dram["fa_wo"]

    with ExitStack() as c1:
        xb = [sb(c1, nc, f"fa_xb{k}", [128, T + 1], BF16) for k in range(NC_D)]
        kblk = [sb(c1, nc, f"fa_k{h}", [128, 2560], BF16) for h in range(8)]
        qblk = [sb(c1, nc, f"fa_q{h}", [128, T], BF16) for h in range(8)]
        vt = [sb(c1, nc, f"fa_v{i}", [128, 520], BF16) for i in range(20)]
        otile = [sb(c1, nc, f"fa_o{i}", [128, T], BF16) for i in range(4)]
        pmax = sb(c1, nc, "fa_pmax", [1, 40])
        qcs = sb(c1, nc, "fa_qcs_sb", [128, T], BF16)
        qss = sb(c1, nc, "fa_qss_sb", [128, T], BF16)
        kci = sb(c1, nc, "fa_kci_sb", [128, 2560], BF16)
        ksi = sb(c1, nc, "fa_ksi_sb", [128, 2560], BF16)
        perm = sb(c1, nc, "fa_perm_sb", [128, 128], BF16)
        mask0 = sb(c1, nc, "fa_m0_sb", [128, 512], BF16)
        mask4 = sb(c1, nc, "fa_m4_sb", [128, 512], BF16)
        hd_a = sb(c1, nc, "fa_hda_sb", [128, 1], BF16)
        hd_b = sb(c1, nc, "fa_hdb_sb", [128, 1], BF16)
        ind_a = sb(c1, nc, "fa_inda_sb", [1, 128], BF16)
        ind_b = sb(c1, nc, "fa_indb_sb", [1, 128], BF16)
        ones64r = sb(c1, nc, "fa_o64r", [1, 64], BF16)
        ones64c = sb(c1, nc, "fa_o64c", [64, 1], BF16)
        qs8col = sb(c1, nc, "fa_qs8_sb", [128, 1])
        kscol = sb(c1, nc, "fa_ks_sb", [128, 1])
        is_odd = sb(c1, nc, "fa_odd_sb", [128, 1])
        qraw = sb(c1, nc, "fa_qraw", [128, 512])
        qsc1 = sb(c1, nc, "fa_qsc1", [128, 512])
        qsc2 = sb(c1, nc, "fa_qsc2", [128, 512])
        qsqb = sb(c1, nc, "fa_qsqb", [128, 512], BF16)
        qnb = sb(c1, nc, "fa_qnb", [128, 512], BF16)
        rowA = sb(c1, nc, "fa_rowA", [1, 1024])
        rowB = sb(c1, nc, "fa_rowB", [1, 1024])
        rowBf = sb(c1, nc, "fa_rowBf", [1, 1024], BF16)
        mcol = sb(c1, nc, "fa_mcol", [128, 1])
        mcolb = sb(c1, nc, "fa_mcolb", [128, 1], BF16)
        halocol = sb(c1, nc, "fa_hcol", [128, 8])
        probs = [sb(c1, nc, f"fa_pr{d}", [128, 512], BF16) for d in range(5)]
        invb = sb(c1, nc, "fa_invb", [1, 512], BF16)
        invf = sb(c1, nc, "fa_invf", [1, 512])
        osc = sb(c1, nc, "fa_osc", [128, 512])
        xres = sb(c1, nc, "fa_xres", [128, 512])
        wsl = [sb(c1, nc, f"fa_wsl{i}", [128, 1024], BF16) for i in range(2)]
        wvsl = sb(c1, nc, "fa_wvsl", [128, 4096], BF16)
        wosl = [sb(c1, nc, f"fa_wos{i}", [128, 512], BF16) for i in range(2)]

        p0 = psum(c1, nc, "fa_p0", [128, 512])
        p1 = psum(c1, nc, "fa_p1", [128, 512])
        p2 = psum(c1, nc, "fa_p2", [128, 512])
        pav = psum(c1, nc, "fa_pav", [128, 512])
        pbc = psum(c1, nc, "fa_pbc", [128, 512])
        pC = psum(c1, nc, "fa_pC", [128, 512])
        prow = psum(c1, nc, "fa_prow", [1, 1024])

        ct = []
        ct.append(p.dma("gpsimd", qcs[:, :], dram["fa_qcs"].ap()))
        ct.append(p.dma("gpsimd", qss[:, :], dram["fa_qss"].ap()))
        ct.append(p.dma("gpsimd", kci[:, :], dram["fa_kci"].ap()))
        ct.append(p.dma("gpsimd", ksi[:, :], dram["fa_ksi"].ap()))
        ct.append(p.dma("sync", perm[:, :], dram["fa_perm"].ap()))
        ct.append(p.dma("sync", mask0[:, :], dram["fa_mask0"].ap()))
        ct.append(p.dma("sync", mask4[:, :], dram["fa_mask4"].ap()))
        ct.append(p.dma("sync", hd_a[:, :], dram["fa_hda"].ap()))
        ct.append(p.dma("sync", hd_b[:, :], dram["fa_hdb"].ap()))
        ct.append(p.dma("sync", ind_a[:, :], dram["fa_inda"].ap()))
        ct.append(p.dma("sync", ind_b[:, :], dram["fa_indb"].ap()))
        ct.append(p.dma("sync", qs8col[:, :], dram["fa_qs8"].ap()))
        ct.append(p.dma("sync", kscol[:, :], dram["fa_ks"].ap()))
        ct.append(p.dma("sync", is_odd[:, :], dram["fa_odd"].ap()))
        tone = p.vec("memset", ones64r[:, :], 1.0)
        tone2 = p.vec("memset", ones64c[:, :], 1.0)
        ct.append(tone)
        for h in range(8):
            ct.append(p.dma("gpsimd", kblk[h][65:66, :], dram["fa_kpad"].ap()))
            ct.append(p.dma("sync", qblk[h][65:66, :], dram["fa_neg1"].ap()))
            ct.append(p.vec("memset", kblk[h][64:65, :], 1.0))
        xb_toks = []
        for k in range(NC_D):
            xb_toks.append(p.dma("gpsimd", xb[k][:, 1:T + 1],
                                 stream_in[128 * k:128 * k + 128, :]))
        thc = p.dma("gpsimd", halocol[:, :], dram["fa_cgat"][0:128, :],
                    after=[dram["fa_cgat_tok"]])
        for k in range(NC_D):
            xb_toks.append(p.vec("tensor_scalar_mul", xb[k][:, 0:1],
                                 halocol[:, k:k + 1], is_odd[:, :],
                                 after=[thc, ct[13]]))
        p.barrier("vector")

        def xsh(k, c0, n):
            if k < 4:
                return xb[k][:, 1 + c0:1 + c0 + n]
            return xb[k][:, c0:c0 + n]

        ws = Streamer(p, wsl)
        ws.load(slab_ap(wqkv, 0), after=[])
        warm = p.mm(p0[:, :], [
            (wsl[0][:, 128 * k:128 * k + 128], xsh(k, 0, 512))
            for k in range(NC_D)], after=[ws.load_tok[0]] + xb_toks)
        ws.note_use(0, warm)
        ws.load(slab_ap(wqkv, 1))

        def qk_pipeline(F, g, is_q, ev_prev):
            bi = F % 2
            gsl = slice(g * 512, g * 512 + 512)
            pr = [ws.load_tok[bi]] + xb_toks
            if ev_prev[0] is not None:
                pr.append(ev_prev[0])
            tmm = p.mm(p0[:, :], [
                (wsl[bi][:, 128 * k:128 * k + 128], xsh(k, g * 512, 512))
                for k in range(NC_D)], after=pr)
            ws.note_use(bi, tmm)
            ev = p.act(qraw[:, :], p0[:, :], AF.Identity,
                       after=[tmm] + ev_prev[1:])
            ev_prev[0] = ev
            sq = p.act(qsqb[:, :], qraw[:, :], AF.Square, after=[ev])
            n1 = p.mm(prow[0:1, 0:512], [(hd_a[:, :], qsqb[:, :])],
                      after=[sq] + ct)
            n2 = p.mm(prow[0:1, 512:1024], [(hd_b[:, :], qsqb[:, :])],
                      after=[sq] + ct)
            li = p.act(rowA[0:1, :], prow[0:1, :], AF.Ln, bias=1e-24,
                       after=[n1, n2])
            iv = p.act(rowA[0:1, :], rowA[0:1, :], AF.Exp, scale=-0.5,
                       after=[li])
            ivb = p.vec("tensor_copy", rowBf[0:1, :], rowA[0:1, :],
                        after=[iv])
            b1 = p.mm(pbc[:, :], [(ind_a[:, :], rowBf[0:1, 0:512])],
                      after=[ivb], stop=False)
            b2 = p.mm(pbc[:, :], [(ind_b[:, :], rowBf[0:1, 512:1024])],
                      after=[ivb], start=False)
            scol = qs8col if is_q else kscol
            v1 = p.vec("tensor_scalar_mul", qsc1[:, :], qraw[:, :],
                       scol[:, :], after=[ev] + ct)
            del ev_prev[1:]
            ev_prev.append(v1)
            v2 = p.vec("tensor_copy", qnb[:, :], qsc1[:, :], after=[v1])
            rp = p.mm(p1[:, :], [(perm[:, :], qnb[:, :])], after=[v2] + ct)
            ctab = qcs if is_q else kci
            stab = qss if is_q else ksi
            cview = ctab[:, gsl] if is_q else ctab[:, 512 + g * 512:512 + g * 512 + 512]
            sview = stab[:, gsl] if is_q else stab[:, 512 + g * 512:512 + g * 512 + 512]
            v3 = p.vec("tensor_mul", qsc1[:, :], qsc1[:, :], cview,
                       after=[v2] + ct)
            v4 = p.vec("tensor_mul", qsc2[:, :], p1[:, :], sview,
                       after=[rp] + ct)
            v5 = p.vec("tensor_add", qsc1[:, :], qsc1[:, :], qsc2[:, :],
                       after=[v3, v4])
            h0 = 2 * (F % 4)
            if is_q:
                dsts = [qblk[h0][0:64, gsl], qblk[h0 + 1][0:64, gsl]]
            else:
                ksl = slice(512 + g * 512, 512 + g * 512 + 512)
                dsts = [kblk[h0][0:64, ksl], kblk[h0 + 1][0:64, ksl]]
            v6 = p.vec("tensor_mul", qsc1[:, :], qsc1[:, :], pbc[:, :],
                       after=[v5, b1, b2])
            c1t = p.vec("tensor_copy", dsts[0], qsc1[0:64, :], after=[v6])
            c2t = p.vec("tensor_copy", dsts[1], qsc1[64:128, :], after=[v6])
            toks = [c1t, c2t, v6]
            if not is_q:
                pass
            return toks

        ev_prev = [None]
        qk_done = []
        for F in range(8):
            is_q = F < 4
            for g in range(4):
                qk_done += qk_pipeline(F, g, is_q, ev_prev)
            if F + 2 < 8:
                ws.load(slab_ap(wqkv, F + 2))

        kn_done = []

        wvt = p.dma("gpsimd", wvsl[:, :].rearrange("q (a j) -> q a j", j=512),
                    wqkv[:, 1024:1536].rearrange("(a q) j -> q a j", q=128))
        ev_v = None
        v_done = []
        for tc in range(16):
            pr = [wvt] + xb_toks
            if ev_v is not None:
                pr.append(ev_v)
            tmm = p.mm(p2[:, :], [
                (xsh(k, 128 * tc, 128), wvsl[:, 512 * k:512 * k + 512])
                for k in range(NC_D)], after=pr)
            ev_v = p.act(
                vt[4 + tc][:, :].rearrange("q (h c) -> q h c", c=65)[:, :, 0:64],
                p2[:, :].rearrange("q (h c) -> q h c", c=64), AF.Identity,
                after=[tmm])
            mz = p.vec("memset",
                       vt[4 + tc][:, :].rearrange("q (h c) -> q h c", c=65)[:, :, 64:65],
                       1.0)
            v_done += [ev_v, mz]

        kpub, kgat = dram["fa_kpub"], dram["fa_kgat"]
        vpub, vgat = dram["fa_vpub"], dram["fa_vgat"]
        pub_toks = []
        for h in range(8):
            pub_toks.append(p.dma("gpsimd", kpub[64 * h:64 * h + 64, :],
                                  kblk[h][0:64, 2048:2560],
                                  after=qk_done))
        for i in range(4):
            pub_toks.append(p.dma("gpsimd", vpub[128 * i:128 * i + 128, :],
                                  vt[16 + i][:, :], after=v_done))
        ck = p.coll("AllGather", ALU.bypass, PAIRS, [kpub.ap().opt()],
                    [kgat.ap().opt()], after=pub_toks)
        cv = p.coll("AllGather", ALU.bypass, PAIRS, [vpub.ap().opt()],
                    [vgat.ap().opt()], after=pub_toks)
        back_toks = []
        for h in range(8):
            back_toks.append(p.dma("gpsimd", kblk[h][0:64, 0:512],
                                   kgat[64 * h:64 * h + 64, :], after=[ck]))
        for i in range(4):
            back_toks.append(p.dma("gpsimd", vt[i][:, :],
                                   vgat[128 * i:128 * i + 128, :], after=[cv]))
        hz = []
        for h in range(8):
            hz.append(p.vec("tensor_scalar_mul", kblk[h][0:64, 0:512],
                            kblk[h][0:64, 0:512], is_odd[0:64, :],
                            after=back_toks + ct))
        kn_done += hz

        p.barrier()
        att_done = []
        wb_tok = {}
        bnd_prev = []
        for w in range(NW):
            for h in range(8):
                i = w * 8 + h
                b0 = 64 * (i % 2)
                qsl = slice(512 * w, 512 * w + 512)
                t1 = p.vec("reduce_max", mcol[b0:b0 + 64, :],
                           kblk[h][0:64, 512 * w:512 * w + 1024], AX.X,
                           apply_absolute_value=True,
                           after=kn_done + bnd_prev[-2:-1])
                t2 = p.vec("tensor_copy", mcolb[b0:b0 + 64, :],
                           mcol[b0:b0 + 64, :], after=[t1])
                sq = p.act(qsqb[b0:b0 + 64, :], qblk[h][0:64, qsl], AF.Abs,
                           after=qk_done + bnd_prev[-2:-1])
                nm = p.mm(prow[0:1, 512 * (i % 2):512 * (i % 2) + 512],
                          [(mcolb[b0:b0 + 64, :], qsqb[b0:b0 + 64, :])],
                          after=[sq, t2] + bnd_prev[-2:-1])
                wb_ = p.vec("tensor_scalar_mul", qblk[h][64:65, qsl],
                            prow[0:1, 512 * (i % 2):512 * (i % 2) + 512],
                            -1.0, after=[nm])
                wb_tok[(w, h)] = wb_
                bnd_prev.append(wb_)

        AB = [[p0, p1], [pav, pbc]]
        CCb = [p2, pC]
        probs_q = [probs, probs]
        osc_q = [osc, qsc1]
        invf_q = [invf, qsc2[0:1, :]]
        invb_q = [invb, qnb[0:1, :]]
        ev_bank_last = {(q, b): [] for q in (0, 1) for b in (0, 1)}
        av_last = [[], []]
        fm_last = [[], []]
        eo_last = [[], []]
        pend_tail = None
        for w in range(NW):
            for h in range(8):
                i = w * 8 + h
                q = i % 2
                A, B = AB[q]
                C = CCb[q]
                prb = probs_q[q]
                qsl = slice(512 * w, 512 * w + 512)
                ev_toks = []
                for d in range(5):
                    pd = A if d % 2 == 0 else B
                    bkey = (q, d % 2)
                    mm_toks = []
                    for t_ in range(4):
                        kc = 512 * w + 128 * (t_ + d)
                        aft = [wb_tok[(w, h)]] + ev_bank_last[bkey][-1:]
                        if d == 0:
                            aft += fm_last[q]
                        mt = p.mm(pd[:, 128 * t_:128 * t_ + 128],
                                  [(kblk[h][0:66, kc:kc + 128],
                                    qblk[h][0:66, 512 * w + 128 * t_:
                                            512 * w + 128 * t_ + 128])],
                                  after=aft)
                        mm_toks.append(mt)
                    ev = p.act(prb[d][:, :], pd[:, :], AF.Exp,
                               after=mm_toks + av_last[0] + av_last[1])
                    ev_bank_last[bkey] = [ev]
                    ev_toks.append(ev)
                m0 = p.vec("tensor_mul", prb[0][:, :], prb[0][:, :],
                           mask0[:, :], after=[ev_toks[0]] + ct)
                m4 = p.vec("tensor_mul", prb[4][:, :], prb[4][:, :],
                           mask4[:, :], after=[ev_toks[4]] + ct)
                av_toks = []
                for t_ in range(4):
                    pairs = []
                    for d in range(5):
                        kc = 512 * w + 128 * (t_ + d)
                        pairs.append((vt[kc // 128][:, 65 * h:65 * h + 65],
                                      prb[d][:, 128 * t_:128 * t_ + 128]))
                    at = p.mm(C[0:65, 128 * t_:128 * t_ + 128], pairs,
                              after=ev_toks + [m0, m4] + eo_last[q])
                    av_toks.append(at)
                av_last[q] = av_toks[-1:]
                rc = p.vec("reciprocal", invf_q[q][0:1, :], C[64:65, :],
                           after=av_toks)
                rb = p.vec("tensor_copy", invb_q[q][0:1, :],
                           invf_q[q][0:1, :], after=[rc])

                def tail(q=q, h=h, w=w, qsl=qsl, rb=rb, A=A, C=C,
                         eA=ev_bank_last[(q, 0)]):
                    bc = p.mm(A[0:64, :],
                              [(ones64r[:, :], invb_q[q][0:1, :])],
                              after=[rb] + eA)
                    eo = p.act(osc_q[q][0:64, :], C[0:64, :], AF.Identity,
                               after=[bc])
                    fm = p.vec("tensor_mul",
                               otile[h // 2][64 * (h % 2):
                                             64 * (h % 2) + 64, qsl],
                               osc_q[q][0:64, :], A[0:64, :],
                               after=[eo, bc])
                    fm_last[q] = [fm]
                    eo_last[q] = [eo]
                    att_done.append(fm)
                if pend_tail is not None:
                    pend_tail()
                pend_tail = tail
        pend_tail()

        if "fa_dbgo" in dram:
            for i in range(4):
                p.dma("sync", dram["fa_dbgo"][128 * i:128 * i + 128, :],
                      otile[i][:, :], after=att_done, sem="st")
            for h in range(8):
                p.dma("sync", dram["fa_dbgk"][64 * h:64 * h + 64, :],
                      kblk[h][0:64, 0:1024], after=kn_done + qk_done, sem="st")
            p.dma("sync", dram["fa_dbgv"].ap(), vt[0][:, :],
                  after=back_toks, sem="st")
        p.barrier()
        os_ = Streamer(p, wosl)
        os_.load(slab_ap(wo, 0))
        warm2 = p.mm(p0[:, :], [
            (wosl[0][:, 128 * k:128 * k + 128], otile[k][:, 0:512])
            for k in range(4)], after=[os_.load_tok[0]] + att_done)
        os_.note_use(0, warm2)
        os_.load(slab_ap(wo, 1))
        ev_out = None
        st_prev = []
        for d in range(NC_D):
            bi = d % 2
            for g in range(4):
                gsl = slice(g * 512, g * 512 + 512)
                pr = [os_.load_tok[bi]] + att_done
                if ev_out is not None:
                    pr.append(ev_out)
                tmm = p.mm(p0[:, :], [
                    (wosl[bi][:, 128 * k:128 * k + 128], otile[k][:, gsl])
                    for k in range(4)], after=pr)
                os_.note_use(bi, tmm)
                tx = p.dma("sync", xres[:, :],
                           stream_in[128 * d:128 * d + 128, gsl],
                           after=[ev_out] if ev_out else [])
                ev_out = p.vec("tensor_add", osc[:, :], p0[:, :], xres[:, :],
                               after=[tmm, tx] + st_prev)
                st_prev = [p.dma("sync",
                                 stream_out[128 * d:128 * d + 128, gsl],
                                 osc[:, :], after=[ev_out], sem="st")]
            if d + 2 < NC_D:
                os_.load(slab_ap(wo, d + 2))
        p.barrier("sync")
    return


def make_ffn_attn_consts(inputs, sub):
    DH = 64
    WIN = 512
    n = 4096
    half = np.arange(0, DH, 2, dtype=np.float64)
    inv_freq = 1.0 / 10000.0 ** (half / DH)
    base = (half + 0.4 * DH) / (1.4 * DH)

    def tables(t0, ntok):
        t = np.arange(t0, t0 + ntok, dtype=np.float64)
        fr = t[:, None] * inv_freq[None, :]
        fr = np.concatenate([fr, fr], 1)
        power = (t - n // 2) / (WIN // 2)
        sc = base[None, :] ** power[:, None]
        sc = np.concatenate([sc, sc], 1)
        cos, sin = np.cos(fr), np.sin(fr)
        return cos, sin, sc

    def fm(a):
        b = np.ascontiguousarray(a.T.astype(np.float32))
        return np.concatenate([b, b], 0)

    qs = np.asarray(inputs["L0_ffn_q_scale"], np.float32)[sub]
    ks = np.asarray(inputs["L0_ffn_k_scale"], np.float32)[sub]
    qs8 = np.tile(qs * 8.0, 2)[:, None].astype(np.float32)
    ksc = np.tile(ks, 2)[:, None].astype(np.float32)
    P = np.zeros((128, 128), np.float32)
    for i in range(128):
        bb = (i // 64) * 64
        j = i % 64
        if j < 32:
            P[bb + j + 32, i] = -1.0
        else:
            P[bb + j - 32, i] = 1.0
    ql = np.arange(512) % 128
    r = np.arange(128)
    mask0 = (r[:, None] >= ql[None, :]).astype(np.float32)
    mask4 = (r[:, None] <= ql[None, :]).astype(np.float32)
    hd_a = np.zeros((128, 1), np.float32); hd_a[0:64] = 1
    hd_b = np.zeros((128, 1), np.float32); hd_b[64:128] = 1
    ind_a = np.zeros((1, 128), np.float32); ind_a[0, 0:64] = 1
    ind_b = np.zeros((1, 128), np.float32); ind_b[0, 64:128] = 1
    neg1 = np.full((1, T), -1.0, np.float32)
    shared = dict(fa_perm=P.astype(BF), fa_mask0=mask0.astype(BF),
                  fa_mask4=mask4.astype(BF), fa_hda=hd_a.astype(BF),
                  fa_hdb=hd_b.astype(BF), fa_inda=ind_a.astype(BF),
                  fa_indb=ind_b.astype(BF), fa_qs8=qs8, fa_ks=ksc,
                  fa_neg1=neg1.astype(BF))
    per_core = []
    for c in range(8):
        b, h = c // 2, c % 2
        S0 = h * T
        cos, sin, sc = tables(S0, T)
        qcs = fm(cos * sc); qss = fm(sin * sc)
        cos, sin, sc = tables(S0 - 512, 2560)
        kci = fm(cos / sc); ksi = fm(sin / sc)
        kpad = np.zeros((1, 2560), np.float32)
        if h == 0:
            kpad[0, 0:512] = 60000.0
        per_core.append(dict(fa_qcs=qcs, fa_qss=qss, fa_kci=kci, fa_ksi=ksi,
                             fa_kpad=kpad,
                             fa_odd=np.full((128, 1), float(h), np.float32)))
    return shared, per_core


def declare_ffn_attn_dram(nc, p, dbg=False, pfx="", in_h=None, out_h=None):
    dram = {}
    dram["fa_in"] = in_h if in_h is not None else nc.dram_tensor(
        pfx + "fa_in", [D, T], F32, kind="ExternalInput")
    dram["fa_out"] = out_h if out_h is not None else nc.dram_tensor(
        pfx + "fa_out", [D, T], F32, kind="ExternalOutput")
    dram["fa_wqkv"] = nc.dram_tensor(pfx + "fa_wqkv", [1024, 1536], F32,
                                     kind="ExternalInput")
    dram["fa_wo"] = nc.dram_tensor(pfx + "fa_wo", [512, 1024], F32,
                                   kind="ExternalInput")
    for nm, shp, dt in [("fa_qcs", [128, T], F32), ("fa_qss", [128, T], F32),
                        ("fa_kci", [128, 2560], F32),
                        ("fa_ksi", [128, 2560], F32),
                        ("fa_perm", [128, 128], BF16),
                        ("fa_mask0", [128, 512], BF16),
                        ("fa_mask4", [128, 512], BF16),
                        ("fa_hda", [128, 1], BF16), ("fa_hdb", [128, 1], BF16),
                        ("fa_inda", [1, 128], BF16),
                        ("fa_indb", [1, 128], BF16),
                        ("fa_qs8", [128, 1], F32), ("fa_ks", [128, 1], F32),
                        ("fa_odd", [128, 1], F32),
                        ("fa_kpad", [1, 2560], F32),
                        ("fa_neg1", [1, T], BF16)]:
        dram[nm] = nc.dram_tensor(pfx + nm, shp, dt, kind="ExternalInput")
    dram["fa_kpub"] = nc.dram_tensor(pfx + "fa_kpub", [512, 512], BF16)
    dram["fa_kgat"] = nc.dram_tensor(pfx + "fa_kgat", [1024, 512], BF16)
    dram["fa_vpub"] = nc.dram_tensor(pfx + "fa_vpub", [512, 520], BF16)
    dram["fa_vgat"] = nc.dram_tensor(pfx + "fa_vgat", [1024, 520], BF16)
    if dbg:
        dram["fa_dbgo"] = nc.dram_tensor("fa_dbgo", [512, T], BF16,
                                         kind="ExternalOutput")
        dram["fa_dbgk"] = nc.dram_tensor("fa_dbgk", [512, 1024], BF16,
                                         kind="ExternalOutput")
        dram["fa_dbgv"] = nc.dram_tensor("fa_dbgv", [128, 520], BF16,
                                         kind="ExternalOutput")
    dram["fa_cpub"] = nc.dram_tensor(pfx + "fa_cpub", [128, 8], F32)
    dram["fa_cgat"] = nc.dram_tensor(pfx + "fa_cgat", [256, 8], F32)
    return dram


def build_ffn_attn_prologue(nc, ctx, p, dram):
    with ExitStack() as c1:
        col = sb(c1, nc, "fa_pcol", [128, 8])
        toks = []
        for k in range(NC_D):
            toks.append(p.dma("sync", col[:, k:k + 1],
                              dram["fa_in"][128 * k:128 * k + 128,
                                            T - 1:T],
                              allow_slow_non_contiguous=True))
        tp = p.dma("gpsimd", dram["fa_cpub"].ap(), col[:, :], after=toks)
        tc = p.coll("AllGather", ALU.bypass, PAIRS, [dram["fa_cpub"].ap().opt()],
                    [dram["fa_cgat"].ap().opt()], after=[tp])
        dram["fa_cgat_tok"] = tc


def run_ffn_attn(inputs, sub, stream_in_full):
    nc = bass.Bass("TRN2", target_bir_lowering=False)
    with ExitStack() as ctx:
        p = P(nc, ctx)
        dram = declare_ffn_attn_dram(nc, p)
        build_ffn_attn_prologue(nc, ctx, p, dram)
        build_ffn_attn(nc, ctx, p, dram)
        p.finalize()
    shared, per_core = make_ffn_attn_consts(inputs, sub)
    wqkv = np.ascontiguousarray(np.asarray(inputs["L0_ffn_wqkv"], np.float32)[sub])
    wo = np.ascontiguousarray(np.asarray(inputs["L0_ffn_wo"], np.float32)[sub])
    maps = []
    for c in range(8):
        b, h = c // 2, c % 2
        fa_in = np.ascontiguousarray(
            stream_in_full[b, h * T:(h + 1) * T, :].T.astype(np.float32))
        m = dict(fa_in=fa_in, fa_wqkv=wqkv, fa_wo=wo)
        m.update(shared)
        m.update(per_core[c])
        maps.append(m)
    return run_bass_kernel_spmd(nc, maps, core_ids=list(range(8)))



NJ = 22

def build_ffn_ffn(nc, ctx, p, dram, final_ln=False):
    stream_in = dram["ff_in"]
    stream_out = dram["ff_out"]
    w1a, w1g, w2p = dram["ff_w1a"], dram["ff_w1g"], dram["ff_w2p"]

    with ExitStack() as c1:
        h1 = [sb(c1, nc, f"ff_h1{k}", [128, T], BF16) for k in range(NC_D)]
        sqt = sb(c1, nc, "ff_sq", [128, 512], BF16)
        t1t = sb(c1, nc, "ff_t1", [128, 512])
        glt = sb(c1, nc, "ff_gl", [128, 512], BF16)
        att = sb(c1, nc, "ff_at", [128, 512], BF16)
        xres = sb(c1, nc, "ff_xres", [128, 512])
        xres2 = sb(c1, nc, "ff_xres2", [128, 512])
        osc = sb(c1, nc, "ff_osc", [128, 512])
        halocol = sb(c1, nc, "ff_hcol", [128, 8])
        is_odd = sb(c1, nc, "ff_odd_sb", [128, 1])
        lg = sb(c1, nc, "ff_lg_sb", [128, NC_D])
        lb = sb(c1, nc, "ff_lb_sb", [128, NC_D])
        b1a = sb(c1, nc, "ff_b1a_sb", [128, NJ])
        b1g = sb(c1, nc, "ff_b1g_sb", [128, NJ])
        b2c = sb(c1, nc, "ff_b2_sb", [128, NC_D])
        ones_col = sb(c1, nc, "ff_oc", [128, 1], BF16)
        ones_row = sb(c1, nc, "ff_or", [1, 128], BF16)
        r_tiles = [sb(c1, nc, f"ff_rt{d}", [128, 512]) for d in range(NC_D)]
        ln_scr = make_ln_scratch(c1, nc, "ffln")
        wsl_a = [sb(c1, nc, f"ff_wa{i}", [128, 1024], BF16) for i in range(2)]
        wsl_g = [sb(c1, nc, f"ff_wg{i}", [128, 1024], BF16) for i in range(2)]
        wsl_2 = [sb(c1, nc, f"ff_w2{i}", [128, 2816], BF16) for i in range(2)]

        p0 = psum(c1, nc, "ff_p0", [128, 512])
        p1 = psum(c1, nc, "ff_p1", [128, 512])
        pA = psum(c1, nc, "ff_pA", [128, 512])
        pB = psum(c1, nc, "ff_pB", [128, 512])
        psA = psum(c1, nc, "ff_psA", [1, 512])
        psB = psum(c1, nc, "ff_psB", [1, 512])

        ct = []
        ct.append(p.dma("sync", is_odd[:, :], dram["ff_odd"].ap()))
        ct.append(p.dma("sync", lg[:, :], dram["ff_lg"].ap()))
        ct.append(p.dma("sync", lb[:, :], dram["ff_lb"].ap()))
        ct.append(p.dma("sync", b1a[:, :], dram["ff_b1a"].ap()))
        ct.append(p.dma("sync", b1g[:, :], dram["ff_b1g"].ap()))
        ct.append(p.dma("sync", b2c[:, :], dram["ff_b2"].ap()))
        ct.append(p.vec("memset", ones_col[:, :], 1.0))
        ct.append(p.vec("memset", ones_row[:, :], 1.0))

        with ExitStack() as c2:
            xb = [sb(c2, nc, f"ff_xb{k}", [128, T + 1], BF16)
                  for k in range(NC_D)]
            xb_toks = []
            for k in range(NC_D):
                xb_toks.append(p.dma("gpsimd", xb[k][:, 1:T + 1],
                                     stream_in[128 * k:128 * k + 128, :]))
            thc = p.dma("gpsimd", halocol[:, :], dram["ff_cgat"][0:128, :],
                        after=[dram["ff_cgat_tok"]])
            for k in range(NC_D):
                xb_toks.append(p.vec("tensor_scalar_mul", xb[k][:, 0:1],
                                     halocol[:, k:k + 1], is_odd[:, :],
                                     after=[thc, ct[0]]))
            p.barrier("vector")

            def xsh(k, c0, n):
                if k < 4:
                    return xb[k][:, 1 + c0:1 + c0 + n]
                return xb[k][:, c0:c0 + n]

            warm = p.mm(psA[:, :], [(ones_col[:, :], xsh(k, 0, 512))
                                    for k in range(NC_D)],
                        after=xb_toks + ct)
            prev = [warm]
            for g in range(4):
                gsl = slice(g * 512, g * 512 + 512)
                ta = tb = None
                for d in range(NC_D):
                    sq = p.act(sqt[:, :], xsh(d, g * 512, 512), AF.Square,
                               after=xb_toks + prev)
                    ta = p.mm(psA[:, :], [(ones_col[:, :],
                                           xsh(d, g * 512, 512))],
                              after=xb_toks + prev, start=(d == 0),
                              stop=(d == NC_D - 1))
                    tb = p.mm(psB[:, :], [(ones_col[:, :], sqt[:, :])],
                              after=[sq], start=(d == 0),
                              stop=(d == NC_D - 1))
                    prev = [ta, tb]
                r0, r1_, r2, r3 = (ln_scr["r0"], ln_scr["r1"], ln_scr["r2"],
                                   ln_scr["r3"])
                brow, a_bf, b_bf = (ln_scr["brow"], ln_scr["a_bf"],
                                    ln_scr["b_bf"])
                t1 = p.act(r0[:, :], psA[0:1, :], AF.Identity, scale=1.0 / D,
                           after=[ta, tb])
                t2 = p.act(r1_[:, :], psB[0:1, :], AF.Identity,
                           scale=1.0 / D, after=[ta, tb])
                t3 = p.act(r2[:, :], r0[:, :], AF.Square, after=[t1])
                t4 = p.vec("tensor_sub", r1_[:, :], r1_[:, :], r2[:, :],
                           after=[t2, t3])
                t5 = p.act(r2[:, :], r1_[:, :], AF.Ln, bias=1e-5, after=[t4])
                t6 = p.act(r3[:, :], r2[:, :], AF.Exp, scale=-0.5,
                           after=[t5])
                t7 = p.vec("tensor_mul", brow[:, :], r0[:, :], r3[:, :],
                           after=[t6])
                t8 = p.vec("tensor_scalar_mul", brow[:, :], brow[:, :],
                           -1.0, after=[t7])
                t9 = p.vec("tensor_copy", a_bf[:, :], r3[:, :], after=[t6])
                t10 = p.vec("tensor_copy", b_bf[:, :], brow[:, :],
                            after=[t8])
                bA = p.mm(pA[:, :], [(ones_row[:, :], a_bf[:, :])],
                          after=[t9] + prev)
                bB = p.mm(pB[:, :], [(ones_row[:, :], b_bf[:, :])],
                          after=[t10])
                ht = None
                for d in range(NC_D):
                    v1 = p.vec("tensor_mul", t1t[:, :],
                               xsh(d, g * 512, 512), pA[:, :],
                               after=[bA, bB] + ([ht] if ht else []))
                    v2 = p.vec("tensor_add", t1t[:, :], t1t[:, :], pB[:, :],
                               after=[v1])
                    ht = p.act(h1[d][:, gsl], t1t[:, :], AF.Identity,
                               scale=lg[:, d:d + 1], bias=lb[:, d:d + 1],
                               after=[v2] + ct)
                prev = [ta, tb, ht, bA, bB]
            h1_done = list(prev)
            if "ff_dbgh" in dram:
                for k in range(NC_D):
                    p.dma("sync", dram["ff_dbgh"][128 * k:128 * k + 128, :],
                          h1[k][:, :], after=h1_done, sem="st")
            p.barrier()

        ag = [sb(c1, nc, f"ff_ag{j}", [128, T], BF16) for j in range(NJ)]
        sa = Streamer(p, wsl_a)
        sg_ = Streamer(p, wsl_g)
        sa.load(slab_ap(w1a, 0))
        sg_.load(slab_ap(w1g, 0))
        warm2 = p.mm(p0[:, :], [
            (wsl_a[0][:, 128 * k:128 * k + 128], h1[k][:, 0:512])
            for k in range(NC_D)], after=[sa.load_tok[0]] + h1_done)
        sa.note_use(0, warm2)
        sa.load(slab_ap(w1a, 1))
        sg_.load(slab_ap(w1g, 1))
        ev_a = ev_g = mul_prev = None
        ag_done = []
        for j in range(NJ):
            bi = j % 2
            for g in range(4):
                gsl = slice(g * 512, g * 512 + 512)
                pr = [sa.load_tok[bi]] + h1_done
                if ev_a is not None:
                    pr.append(ev_a)
                tma = p.mm(p0[:, :], [
                    (wsl_a[bi][:, 128 * k:128 * k + 128], h1[k][:, gsl])
                    for k in range(NC_D)], after=pr)
                sa.note_use(bi, tma)
                pr = [sg_.load_tok[bi]] + h1_done
                if ev_g is not None:
                    pr.append(ev_g)
                tmg = p.mm(p1[:, :], [
                    (wsl_g[bi][:, 128 * k:128 * k + 128], h1[k][:, gsl])
                    for k in range(NC_D)], after=pr)
                sg_.note_use(bi, tmg)
                ga = [tma] + ([mul_prev] if mul_prev else [])
                ev_a = p.act(att[:, :], p0[:, :], AF.Identity,
                             bias=b1a[:, j:j + 1], after=ga + ct)
                gg_ = [tmg] + ([mul_prev] if mul_prev else [])
                ev_g = p.act(glt[:, :], p1[:, :], AF.Gelu,
                             bias=b1g[:, j:j + 1], after=gg_ + ct)
                mul_prev = p.vec("tensor_mul", ag[j][:, gsl], att[:, :],
                                 glt[:, :], after=[ev_a, ev_g])
                ag_done.append(mul_prev)
            if j + 2 < NJ:
                sa.load(slab_ap(w1a, j + 2))
                sg_.load(slab_ap(w1g, j + 2))

        if "ff_dbga" in dram:
            for j in range(4):
                p.dma("sync", dram["ff_dbga"][128 * j:128 * j + 128, :],
                      ag[j][:, :], after=ag_done, sem="st")
        s2 = Streamer(p, wsl_2)
        s2.load(slab_ap(w2p, 0), after=ag_done[-2:])
        warm3 = p.mm(p1[:, :], [
            (wsl_2[0][:, 128 * k:128 * k + 128], ag[k][:, 0:512])
            for k in range(NJ)], after=[s2.load_tok[0]] + ag_done)
        s2.note_use(0, warm3)
        s2.load(slab_ap(w2p, 1))
        ev_out = None
        st_prev = []
        for d in range(NC_D):
            bi = d % 2
            for g in range(4):
                gsl = slice(g * 512, g * 512 + 512)
                pr = [s2.load_tok[bi]] + ag_done
                if ev_out is not None:
                    pr.append(ev_out)
                tmm = p.mm(p1[:, :], [
                    (wsl_2[bi][:, 128 * k:128 * k + 128], ag[k][:, gsl])
                    for k in range(NJ)], after=pr)
                s2.note_use(bi, tmm)
                tx = p.dma("sync", xres[:, :],
                           stream_in[128 * d:128 * d + 128, gsl],
                           after=[ev_out] if ev_out else [])
                eb = p.act(osc[:, :], p1[:, :], AF.Identity,
                           bias=b2c[:, d:d + 1],
                           after=[tmm] + ct + st_prev)
                ev_out = p.vec("tensor_add", osc[:, :], osc[:, :],
                               xres[:, :], after=[eb, tx] + st_prev)
                st_prev = [p.dma("sync",
                                 stream_out[128 * d:128 * d + 128, gsl],
                                 osc[:, :], after=[ev_out], sem="st")]
            if d + 2 < NC_D:
                s2.load(slab_ap(w2p, d + 2))
        p.barrier()

        if final_ln:
            out2 = dram["ff_out2"]
            ln_guard = []
            for g in range(4):
                gsl = slice(g * 512, g * 512 + 512)
                r_done = []
                rx_prev = []
                for d in range(NC_D):
                    t1_ = p.dma("sync", xres[:, :],
                                stream_out[128 * d:128 * d + 128, gsl],
                                after=rx_prev + st_prev)
                    t2_ = p.dma("sync", xres2[:, :],
                                dram["ff_res2"][128 * d:128 * d + 128, gsl],
                                after=rx_prev)
                    rv = p.vec("tensor_add", r_tiles[d][:, :], xres[:, :],
                               xres2[:, :], after=[t1_, t2_] + ln_guard)
                    rx_prev = [rv]
                    r_done.append(rv)

                def writer(dd, _s, src_ap, toks, g=g):
                    return p.dma("sync",
                                 out2[128 * dd:128 * dd + 128, gsl],
                                 src_ap, after=toks, sem="st")
                stoks, ln_guard = layernorm_fm(
                    p, ln_scr, r_tiles, 512, ones_col, ones_row,
                    psA, psB, pA, pB, r_done, writer, guard=ln_guard)
                ln_guard = ln_guard + stoks
        p.barrier("sync")


def declare_ffn_ffn_dram(nc, final_ln=False, dbg=False, pfx="", in_h=None,
                         out_h=None, res2_h=None, out2_h=None):
    dram = {}
    dram["ff_in"] = in_h if in_h is not None else nc.dram_tensor(
        pfx + "ff_in", [D, T], F32, kind="ExternalInput")
    dram["ff_out"] = out_h if out_h is not None else nc.dram_tensor(
        pfx + "ff_out", [D, T], F32, kind="ExternalOutput")
    for nm, shp in [("ff_w1a", [1024, 2816]), ("ff_w1g", [1024, 2816]),
                    ("ff_w2p", [2816, 1024]), ("ff_lg", [128, NC_D]),
                    ("ff_lb", [128, NC_D]), ("ff_b1a", [128, NJ]),
                    ("ff_b1g", [128, NJ]), ("ff_b2", [128, NC_D]),
                    ("ff_odd", [128, 1])]:
        dram[nm] = nc.dram_tensor(pfx + nm, shp, F32, kind="ExternalInput")
    if final_ln:
        dram["ff_res2"] = res2_h if res2_h is not None else nc.dram_tensor(
            pfx + "ff_res2", [D, T], F32, kind="ExternalInput")
        dram["ff_out2"] = out2_h if out2_h is not None else nc.dram_tensor(
            pfx + "ff_out2", [D, T], F32, kind="ExternalOutput")
    if dbg:
        dram["ff_dbgh"] = nc.dram_tensor("ff_dbgh", [D, T], BF16,
                                         kind="ExternalOutput")
        dram["ff_dbga"] = nc.dram_tensor("ff_dbga", [512, T], BF16,
                                         kind="ExternalOutput")
    dram["ff_cpub"] = nc.dram_tensor(pfx + "ff_cpub", [128, 8], F32)
    dram["ff_cgat"] = nc.dram_tensor(pfx + "ff_cgat", [256, 8], F32)
    return dram


def build_col_prologue(nc, ctx, p, dram, in_name, pub, gat, tokname):
    with ExitStack() as c1:
        col = sb(c1, nc, tokname + "_col", [128, 8])
        toks = []
        for k in range(NC_D):
            toks.append(p.dma("sync", col[:, k:k + 1],
                              dram[in_name][128 * k:128 * k + 128, T - 1:T],
                              allow_slow_non_contiguous=True))
        tp = p.dma("gpsimd", dram[pub].ap(), col[:, :], after=toks)
        tc = p.coll("AllGather", ALU.bypass, PAIRS, [dram[pub].ap().opt()],
                    [dram[gat].ap().opt()], after=[tp])
        dram[tokname] = tc


def pack_ffn_ffn_inputs(inputs, sub, stream_in_full, res2_full=None):
    w1 = np.asarray(inputs["L0_ffn_w1"], np.float32)[sub]
    b1 = np.asarray(inputs["L0_ffn_b1"], np.float32)[sub]
    w2 = np.asarray(inputs["L0_ffn_w2"], np.float32)[sub]
    b2 = np.asarray(inputs["L0_ffn_b2"], np.float32)[sub]
    lgv = np.asarray(inputs["L0_ffn_ln_g"], np.float32)[sub]
    lbv = np.asarray(inputs["L0_ffn_ln_b"], np.float32)[sub]
    HH = 2730
    PAD = NJ * 128
    w1a = np.zeros((1024, PAD), np.float32); w1a[:, :HH] = w1[:, :HH]
    w1g = np.zeros((1024, PAD), np.float32); w1g[:, :HH] = w1[:, HH:]
    w2p = np.zeros((PAD, 1024), np.float32); w2p[:HH] = w2
    b1a = np.zeros(PAD, np.float32); b1a[:HH] = b1[:HH]
    b1g = np.zeros(PAD, np.float32); b1g[:HH] = b1[HH:]
    shared = dict(ff_w1a=w1a, ff_w1g=w1g, ff_w2p=w2p,
                  ff_b1a=cols128(b1a), ff_b1g=cols128(b1g),
                  ff_b2=cols128(b2), ff_lg=cols128(lgv), ff_lb=cols128(lbv))
    maps = []
    for c in range(8):
        b, h = c // 2, c % 2
        m = dict(shared)
        m["ff_in"] = np.ascontiguousarray(
            stream_in_full[b, h * T:(h + 1) * T, :].T.astype(np.float32))
        if res2_full is not None:
            m["ff_res2"] = np.ascontiguousarray(
                res2_full[b, h * T:(h + 1) * T, :].T.astype(np.float32))
        m["ff_odd"] = np.full((128, 1), float(h), np.float32)
        maps.append(m)
    return maps


def run_ffn_ffn(inputs, sub, stream_in_full, res2_full=None):
    final_ln = res2_full is not None
    nc = bass.Bass("TRN2", target_bir_lowering=False)
    with ExitStack() as ctx:
        p = P(nc, ctx)
        dram = declare_ffn_ffn_dram(nc, final_ln)
        build_col_prologue(nc, ctx, p, dram, "ff_in", "ff_cpub", "ff_cgat",
                           "ff_cgat_tok")
        build_ffn_ffn(nc, ctx, p, dram, final_ln)
        p.finalize()
    maps = pack_ffn_ffn_inputs(inputs, sub, stream_in_full, res2_full)
    return run_bass_kernel_spmd(nc, maps, core_ids=list(range(8)))



def build_l1_attn(nc, ctx, p, dram):
    stream_in = dram["la_in"]
    stream_out = dram["la_out"]
    winT = dram["la_winT"]
    woutT = dram["la_woutT"]
    winGT = dram["la_winGT"]
    woutGT = dram["la_woutGT"]
    S8 = 1.0 / np.sqrt(128.0)
    SG = 1.0 / np.sqrt(256.0)

    with ExitStack() as c1:
        otile = [sb(c1, nc, f"la_o{d}", [128, T], BF16) for d in range(NC_D)]
        meansf = [sb(c1, nc, f"la_mn{d}", [128, 32]) for d in range(NC_D)]
        mall = [sb(c1, nc, f"la_ma{d}", [128, 64], BF16) for d in range(NC_D)]
        bql = sb(c1, nc, "la_bql_s", [128, NC_D])
        bkl = sb(c1, nc, "la_bkl_s", [128, NC_D])
        bvl_r = sb(c1, nc, "la_bvl_s", [1, 1024], BF16)
        bol = sb(c1, nc, "la_bol_s", [128, NC_D])
        bqg = sb(c1, nc, "la_bqg_s", [128, NC_D])
        bkg = sb(c1, nc, "la_bkg_s", [128, NC_D])
        bvg_r = sb(c1, nc, "la_bvg_s", [1, 1024], BF16)
        bog = sb(c1, nc, "la_bog_s", [128, NC_D])
        is_odd = sb(c1, nc, "la_odd_sb", [128, 1])
        is_even = sb(c1, nc, "la_evn_sb", [128, 1])
        ones_col = sb(c1, nc, "la_oc", [128, 1], BF16)
        ones_row = sb(c1, nc, "la_or", [1, 128], BF16)
        ones64c = sb(c1, nc, "la_o64c", [64, 1], BF16)
        ones64r = sb(c1, nc, "la_o64r", [1, 64], BF16)
        mcols = sb(c1, nc, "la_mcols", [128, 8])
        mcolsb = sb(c1, nc, "la_mcolsb", [128, 8], BF16)
        qabs = sb(c1, nc, "la_qabs", [128, 512], BF16)
        negB = sb(c1, nc, "la_negB", [1, 512], BF16)
        probs = sb(c1, nc, "la_probs", [128, 512], BF16)
        invb = sb(c1, nc, "la_invb", [1, 512], BF16)
        osc = sb(c1, nc, "la_osc", [128, 512])
        xres = sb(c1, nc, "la_xres", [128, 512])
        gsel = sb(c1, nc, "la_gsel", [128, 32])
        gselb = sb(c1, nc, "la_gselb", [128, 32], BF16)
        ident = sb(c1, nc, "la_id", [128, 128], BF16)
        gT = [sb(c1, nc, f"la_gT{d}", [32, 128], BF16) for d in range(NC_D)]

        p0 = psum(c1, nc, "la_p0", [128, 512])
        p1 = psum(c1, nc, "la_p1", [128, 512])
        p2 = psum(c1, nc, "la_p2", [128, 512])
        pav = psum(c1, nc, "la_pav", [128, 512])
        pbc = psum(c1, nc, "la_pbc", [128, 512])
        prow = psum(c1, nc, "la_prow", [1, 512])
        psA_ = psum(c1, nc, "la_psA", [1, 512])
        ptb = psum(c1, nc, "la_ptb", [128, 128], BF16)

        ct = []
        ct.append(p.dma("sync", is_odd[:, :], dram["la_odd"].ap()))
        ct.append(p.dma("sync", ident[:, :], dram["la_ident"].ap()))
        for nm, t in [("la_bql", bql), ("la_bkl", bkl), ("la_bol", bol),
                      ("la_bqg", bqg), ("la_bkg", bkg), ("la_bog", bog)]:
            ct.append(p.dma("sync", t[:, :], dram[nm].ap()))
        ct.append(p.dma("gpsimd", bvl_r[:, :], dram["la_bvl"].ap()))
        ct.append(p.dma("gpsimd", bvg_r[:, :], dram["la_bvg"].ap()))
        ct.append(p.vec("memset", ones_col[:, :], 1.0))
        ct.append(p.vec("memset", ones_row[:, :], 1.0))
        ct.append(p.vec("memset", ones64c[:, :], 1.0))
        ct.append(p.vec("memset", ones64r[:, :], 1.0))
        ct.append(p.vec("tensor_scalar", is_even[:, :], is_odd[:, :], -1.0,
                        1.0, ALU.mult, ALU.add, after=[ct[0]]))
        ct.append(p.vec("tensor_scalar_mul", bql[:, :], bql[:, :], S8,
                        after=ct[3:4]))
        ct.append(p.vec("tensor_scalar_mul", bqg[:, :], bqg[:, :], SG,
                        after=ct[6:7]))

        with ExitStack() as c2:
            qh = [sb(c2, nc, f"la_q{h}", [128, T], BF16) for h in range(8)]
            kh = [sb(c2, nc, f"la_k{h}", [128, T], BF16) for h in range(8)]
            vt = [sb(c2, nc, f"la_v{i}", [128, 1024], BF16)
                  for i in range(16)]
            wsl = [sb(c2, nc, f"la_wsl{i}", [128, 1024], BF16)
                   for i in range(2)]
            wvsl = sb(c2, nc, "la_wvsl", [128, 4096], BF16)
            c3 = ExitStack()
            xb = [sb(c3, nc, f"la_xb{k}", [128, T], BF16)
                  for k in range(NC_D)]

            xb_toks = [p.dma("gpsimd", xb[k][:, :],
                             stream_in[128 * k:128 * k + 128, :])
                      for k in range(NC_D)]
            p.barrier("vector")

            ws = Streamer(p, wsl)
            ws.load(slab_ap(winT, 0))
            warm = p.mm(p0[:, :], [
                (wsl[0][:, 128 * k:128 * k + 128], xb[k][:, 0:512])
                for k in range(NC_D)], after=[ws.load_tok[0]] + xb_toks)
            ws.note_use(0, warm)
            ws.load(slab_ap(winT, 1))
            qk_done = []
            ev = None
            for F in range(16):
                bi = F % 2
                is_q = F < 8
                h = F % 8
                for g in range(4):
                    gsl = slice(g * 512, g * 512 + 512)
                    pr = [ws.load_tok[bi]] + xb_toks
                    if ev is not None:
                        pr.append(ev)
                    tmm = p.mm(p0[:, :], [
                        (wsl[bi][:, 128 * k:128 * k + 128], xb[k][:, gsl])
                        for k in range(NC_D)], after=pr)
                    ws.note_use(bi, tmm)
                    dst = (qh if is_q else kh)[h][:, gsl]
                    bias = (bql if is_q else bkl)[:, h:h + 1]
                    ev = p.act(dst, p0[:, :], AF.Identity,
                               scale=(S8 if is_q else 1.0), bias=bias,
                               after=[tmm] + ct)
                    qk_done.append(ev)
                if F + 2 < 16:
                    ws.load(slab_ap(winT, F + 2))
            ev_v = None
            v_done = []
            for half in range(2):
                wvt = p.dma("gpsimd",
                            wvsl[:, :].rearrange("q (a j) -> q a j", j=512),
                            winT[:, 2048 + 512 * half:2560 + 512 * half]
                            .rearrange("(a q) j -> q a j", q=128),
                            after=v_done[-1:])
                for tc in range(16):
                    pr = [wvt] + xb_toks + ct
                    if ev_v is not None:
                        pr.append(ev_v)
                    tmm = p.mm(p2[:, :], [
                        (ones_row[:, :],
                         bvl_r[0:1, 512 * half:512 * half + 512])] + [
                        (xb[k][:, 128 * tc:128 * tc + 128],
                         wvsl[:, 512 * k:512 * k + 512])
                        for k in range(NC_D)], after=pr)
                    ev_v = p.act(vt[tc][:, 512 * half:512 * half + 512],
                                 p2[:, :], AF.Identity, after=[tmm])
                    v_done.append(ev_v)
            mn_done = []
            for d in range(NC_D):
                t1 = p.vec("reduce_sum", meansf[d][:, :],
                           xb[d][:, :].rearrange("q (s t) -> q s t", t=64),
                           AX.X, after=xb_toks)
                t2 = p.vec("tensor_scalar_mul", meansf[d][:, :],
                           meansf[d][:, :], 1.0 / 64.0, after=[t1])
                mn_done.append(p.dma("gpsimd", dram["la_mpub"][:, 32 * d:32 * d + 32],
                                     meansf[d][:, :], after=[t2]))
            cm = p.coll("AllGather", ALU.bypass, PAIRS,
                        [dram["la_mpub"].ap().opt()],
                        [dram["la_mgat"].ap().opt()], after=mn_done)
            ma_toks = []
            for d in range(NC_D):
                ma_toks.append(p.dma("gpsimd", mall[d][:, 0:32],
                                     dram["la_mgat"][0:128, 32 * d:32 * d + 32],
                                     after=[cm]))
                ma_toks.append(p.dma("gpsimd", mall[d][:, 32:64],
                                     dram["la_mgat"][128:256, 32 * d:32 * d + 32],
                                     after=[cm]))

            p.barrier()
            c3.close()
            att_done = []
            for h in range(8):
                for sgg in range(4):
                    gsl = slice(512 * sgg, 512 * sgg + 512)
                    sq = p.act(qabs[:, :], qh[h][:, gsl], AF.Abs,
                               after=qk_done + att_done[-1:])
                    rm_toks = []
                    for s8 in range(8):
                        t1 = p.vec("reduce_max", mcols[:, s8:s8 + 1],
                                   kh[h][:, 512 * sgg + 64 * s8:
                                         512 * sgg + 64 * s8 + 64],
                                   AX.X, apply_absolute_value=True,
                                   after=qk_done)
                        rm_toks.append(t1)
                    cv = p.vec("tensor_copy", mcolsb[:, :], mcols[:, :],
                               after=rm_toks)
                    nb_toks = []
                    for s8 in range(8):
                        t2 = p.mm(prow[0:1, 64 * s8:64 * s8 + 64],
                                  [(mcolsb[:, s8:s8 + 1],
                                    qabs[:, 64 * s8:64 * s8 + 64])],
                                  after=[cv, sq] + att_done[-1:])
                        nb_toks.append(t2)
                    nbv = p.vec("tensor_scalar_mul", negB[0:1, :],
                                prow[0:1, 0:512], -1.0, after=nb_toks)
                    sc_toks = []
                    for s8 in range(8):
                        ssl = slice(512 * sgg + 64 * s8,
                                    512 * sgg + 64 * s8 + 64)
                        m1 = p.mm(p0[0:64, 64 * s8:64 * s8 + 64],
                                  [(kh[h][:, ssl], qh[h][:, ssl])],
                                  after=[nbv] + att_done[-1:], stop=False)
                        m2 = p.mm(p0[0:64, 64 * s8:64 * s8 + 64],
                                  [(ones64r[:, :],
                                    negB[0:1, 64 * s8:64 * s8 + 64])],
                                  start=False)
                        sc_toks.append(m2)
                    mz = p.vec("memset", probs[:, :], 0.0,
                               after=att_done[-1:])
                    ev_toks = []
                    for s8 in range(8):
                        seg = 8 * sgg + s8
                        rows = slice(64 * (seg % 2), 64 * (seg % 2) + 64)
                        ev1 = p.act(probs[rows, 64 * s8:64 * s8 + 64],
                                    p0[0:64, 64 * s8:64 * s8 + 64], AF.Exp,
                                    after=sc_toks + [mz])
                        ev_toks.append(ev1)
                    evp = ev_toks[-1]
                    dn_toks = []
                    for s8 in range(8):
                        d1 = p.mm(prow[0:1, 64 * s8:64 * s8 + 64],
                                  [(ones_col[:, :],
                                    probs[:, 64 * s8:64 * s8 + 64])],
                                  after=ev_toks + [mz])
                        dn_toks.append(d1)
                    rc = p.vec("reciprocal", osc[0:1, 0:512],
                               prow[0:1, 0:512], after=dn_toks)
                    rb = p.vec("tensor_copy", invb[0:1, :], osc[0:1, 0:512],
                               after=[rc])
                    bc = p.mm(pbc[:, :], [(ones_row[:, :], invb[0:1, :])],
                              after=[rb])
                    av_toks = []
                    for s8 in range(8):
                        seg = 8 * sgg + s8
                        vtile = vt[seg // 2]
                        a1 = p.mm(pav[:, 64 * s8:64 * s8 + 64],
                                  [(vtile[:, 128 * h:128 * h + 128],
                                    probs[:, 64 * s8:64 * s8 + 64])],
                                  after=ev_toks)
                        av_toks.append(a1)
                    eo = p.act(osc[:, :], pav[:, :], AF.Identity,
                               after=av_toks + [bc])
                    fm = p.vec("tensor_mul", otile[h][:, gsl], osc[:, :],
                               pbc[:, :], after=[eo])
                    att_done.append(fm)

            p.barrier()
            os_ = Streamer(p, wsl)
            os_.load(slab_ap(woutT, 0))
            warm2 = p.mm(p0[:, :], [
                (wsl[0][:, 128 * k:128 * k + 128], otile[k][:, 0:512])
                for k in range(NC_D)], after=[os_.load_tok[0]] + att_done)
            os_.note_use(0, warm2)
            os_.load(slab_ap(woutT, 1))
            ev_out = None
            loc_st = []
            for d in range(NC_D):
                bi = d % 2
                for g in range(4):
                    gsl = slice(g * 512, g * 512 + 512)
                    pr = [os_.load_tok[bi]] + att_done
                    if ev_out is not None:
                        pr.append(ev_out)
                    tmm = p.mm(p0[:, :], [
                        (wsl[bi][:, 128 * k:128 * k + 128],
                         otile[k][:, gsl]) for k in range(NC_D)], after=pr)
                    os_.note_use(bi, tmm)
                    tx = p.dma("sync", xres[:, :],
                               stream_in[128 * d:128 * d + 128, gsl],
                               after=[ev_out] if ev_out else [])
                    eb = p.act(osc[:, :], p0[:, :], AF.Identity,
                               bias=bol[:, d:d + 1],
                               after=[tmm] + ct + loc_st)
                    ev_out = p.vec("tensor_add", osc[:, :],
                                   osc[:, :], xres[:, :],
                                   after=[eb, tx] + loc_st)
                    loc_st = [p.dma("sync",
                                    dram["la_loc"][128 * d:128 * d + 128, gsl],
                                    osc[:, :], after=[ev_out], sem="st")]
                    if "la_dbgl" in dram:
                        loc_st.append(p.dma(
                            "sync",
                            dram["la_dbgl"][128 * d:128 * d + 128, gsl],
                            osc[:, :], after=[ev_out], sem="st"))
                if d + 2 < NC_D:
                    os_.load(slab_ap(woutT, d + 2))
            local_done = [ev_out] + loc_st

            p.barrier()
            mb_toks = []
            qg = [sb(c2, nc, f"la_qg{d}", [128, 64], BF16)
                  for d in range(NC_D)]
            kg = [sb(c2, nc, f"la_kg{d}", [128, 64], BF16)
                  for d in range(NC_D)]
            vg = sb(c2, nc, "la_vg", [64, 1024], BF16)
            og = [sb(c2, nc, f"la_og{d}", [128, 64], BF16)
                  for d in range(NC_D)]
            globf = [sb(c2, nc, f"la_gf{d}", [128, 64]) for d in range(NC_D)]
            for d in range(NC_D):
                mb_toks.append(p.vec("tensor_copy", mall[d][:, :],
                                     mall[d][:, :], after=ma_toks))
            gw = Streamer(p, wsl)
            gw.load(slab_ap(winGT, 0), after=att_done)
            warm3 = p.mm(p0[:, 0:64], [
                (wsl[0][:, 128 * k:128 * k + 128], mall[k][:, :])
                for k in range(NC_D)], after=[gw.load_tok[0]] + mb_toks)
            gw.note_use(0, warm3)
            gw.load(slab_ap(winGT, 1))
            ev = None
            gqk_done = []
            for F in range(16):
                bi = F % 2
                is_q = F < 8
                h = F % 8
                pr = [gw.load_tok[bi]] + mb_toks
                if ev is not None:
                    pr.append(ev)
                tmm = p.mm(p0[:, 0:64], [
                    (wsl[bi][:, 128 * k:128 * k + 128], mall[k][:, :])
                    for k in range(NC_D)], after=pr)
                gw.note_use(bi, tmm)
                dst = (qg if is_q else kg)[h][:, :]
                bias = (bqg if is_q else bkg)[:, h:h + 1]
                ev = p.act(dst, p0[:, 0:64], AF.Identity,
                           scale=(SG if is_q else 1.0), bias=bias,
                           after=[tmm] + ct)
                gqk_done.append(ev)
                if F + 2 < 16:
                    gw.load(slab_ap(winGT, F + 2))
            ev_v = None
            for half in range(2):
                wvg = p.dma("gpsimd",
                            wvsl[:, :].rearrange("q (a j) -> q a j", j=512),
                            winGT[:, 2048 + 512 * half:2560 + 512 * half]
                            .rearrange("(a q) j -> q a j", q=128),
                            after=(att_done + v_done) if half == 0
                            else [ev_v])
                pr = [wvg] + mb_toks + ct
                if ev_v is not None:
                    pr.append(ev_v)
                tmm = p.mm(p2[0:64, :], [
                    (ones_row[:, 0:64],
                     bvg_r[0:1, 512 * half:512 * half + 512])] + [
                    (mall[k][:, :], wvsl[:, 512 * k:512 * k + 512])
                    for k in range(NC_D)], after=pr)
                ev_v = p.act(vg[:, 512 * half:512 * half + 512], p2[0:64, :],
                             AF.Identity, after=[tmm])
            gv_done = [ev_v]
            g_att = []
            for h in range(4):
                sq = p.act(qabs[:, 0:64], qg[2 * h][:, :], AF.Abs,
                           after=gqk_done + g_att[-1:])
                sq2 = p.act(qabs[:, 64:128], qg[2 * h + 1][:, :], AF.Abs,
                            after=gqk_done)
                r1t = p.vec("reduce_max", mcols[:, 0:1], kg[2 * h][:, :],
                            AX.X, apply_absolute_value=True, after=gqk_done)
                r2t = p.vec("reduce_max", mcols[:, 1:2], kg[2 * h + 1][:, :],
                            AX.X, apply_absolute_value=True, after=gqk_done)
                cv = p.vec("tensor_copy", mcolsb[:, 0:2], mcols[:, 0:2],
                           after=[r1t, r2t])
                b1t = p.mm(prow[0:1, 0:64],
                           [(mcolsb[:, 0:1], qabs[:, 0:64]),
                            (mcolsb[:, 1:2], qabs[:, 64:128])],
                           after=[cv, sq, sq2] + g_att[-1:])
                nbv = p.vec("tensor_scalar_mul", negB[0:1, 0:64],
                            prow[0:1, 0:64], -1.0, after=[b1t])
                m1 = p.mm(p0[0:64, 0:64],
                          [(kg[2 * h][:, :], qg[2 * h][:, :]),
                           (kg[2 * h + 1][:, :], qg[2 * h + 1][:, :])],
                          after=[nbv] + g_att[-1:], stop=False)
                m2 = p.mm(p0[0:64, 0:64],
                          [(ones64r[:, :], negB[0:1, 0:64])], start=False)
                evp = p.act(probs[0:64, 0:64], p0[0:64, 0:64], AF.Exp,
                            after=[m2])
                d1 = p.mm(prow[0:1, 0:64],
                          [(ones64c[:, :], probs[0:64, 0:64])], after=[evp])
                rc = p.vec("reciprocal", osc[0:1, 0:64], prow[0:1, 0:64],
                           after=[d1])
                rb = p.vec("tensor_copy", invb[0:1, 0:64], osc[0:1, 0:64],
                           after=[rc])
                bc = p.mm(pbc[:, 0:64], [(ones_row[:, :], invb[0:1, 0:64])],
                          after=[rb] + g_att[-1:])
                prev_eo = []
                for half in range(2):
                    dd = 2 * h + half
                    a1 = p.mm(pav[:, 0:64],
                              [(vg[:, 128 * dd:128 * dd + 128],
                                probs[0:64, 0:64])],
                              after=[evp] + gv_done + prev_eo)
                    eo = p.act(osc[:, 0:64], pav[:, 0:64], AF.Identity,
                               after=[a1, bc])
                    prev_eo = [eo]
                    fm = p.vec("tensor_mul", og[dd][:, :], osc[:, 0:64],
                               pbc[:, 0:64], after=[eo])
                    g_att.append(fm)
            gow = Streamer(p, wsl)
            gow.load(slab_ap(woutGT, 0))
            gow.load(slab_ap(woutGT, 1))
            gw_done = []
            ev = None
            for d in range(NC_D):
                bi = d % 2
                pr = [gow.load_tok[bi]] + g_att
                if ev is not None:
                    pr.append(ev)
                tmm = p.mm(p0[:, 0:64], [
                    (wsl[bi][:, 128 * k:128 * k + 128], og[k][:, :])
                    for k in range(NC_D)], after=pr)
                gow.note_use(bi, tmm)
                ev = p.act(globf[d][:, :], p0[:, 0:64], AF.Identity,
                           bias=bog[:, d:d + 1], after=[tmm] + ct)
                gw_done.append(ev)
                if d + 2 < NC_D:
                    gow.load(slab_ap(woutGT, d + 2))

            if "la_dbgg" in dram:
                for d in range(NC_D):
                    p.dma("sync", dram["la_dbgg"][128 * d:128 * d + 128, :],
                          globf[d][:, :], after=gw_done, sem="st")
                    p.dma("gpsimd",
                          dram["la_dbgm"][128 * d:128 * d + 128, :],
                          mall[d][:, :], after=ma_toks + mb_toks, sem="st")

            if "la_dbgq" in dram:
                for d in range(NC_D):
                    p.dma("sync", dram["la_dbgq"][128 * d:128 * d + 128, :],
                          qg[d][:, :], after=gqk_done, sem="st")
                    p.dma("sync", dram["la_dbgk"][128 * d:128 * d + 128, :],
                          kg[d][:, :], after=gqk_done, sem="st")
                    p.dma("sync", dram["la_dbgog"][128 * d:128 * d + 128, :],
                          og[d][:, :], after=g_att, sem="st")
                p.dma("sync", dram["la_dbgv"].ap(), vg[:, :], after=gv_done,
                      sem="st")
            gT_done = []
            for d in range(NC_D):
                t1 = p.vec("tensor_scalar_mul", gsel[:, :],
                           globf[d][:, 0:32], is_even[:, :],
                           after=gw_done + ct)
                t2 = p.vec("scalar_tensor_tensor", gsel[:, :],
                           globf[d][:, 32:64], is_odd[:, :], gsel[:, :],
                           ALU.mult, ALU.add, after=[t1])
                t3 = p.vec("tensor_copy", gselb[:, :], gsel[:, :],
                           after=[t2])
                tt = p.trans(ptb[0:32, 0:128], gselb[:, 0:32], ident[:, :],
                             after=[t3] + gT_done)
                te = p.vec("tensor_copy", gT[d][:, :], ptb[0:32, 0:128],
                           after=[tt])
                gT_done.append(te)

        p.barrier()
        R32 = sb(c1, nc, "la_R32_s", [32, T], BF16)
        ln_scr = make_ln_scratch(c1, nc, "laln")
        r_tiles = [sb(c1, nc, f"la_rt{d}", [128, 512]) for d in range(NC_D)]
        tr32 = p.dma("sync", R32[:, :], dram["la_R32"].ap())
        lng = [tr32]
        for g in range(4):
            gsl = slice(512 * g, 512 * g + 512)
            r_done = []
            rv = None
            for d in range(NC_D):
                tl = p.dma("sync", r_tiles[d][:, :],
                           dram["la_loc"][128 * d:128 * d + 128, gsl],
                           after=lng)
                rp = p.mm(pbc[:, :], [(gT[d][0:32, :], R32[0:32, gsl])],
                          after=gT_done + lng + [tr32] +
                          ([rv] if rv else []))
                rv = p.vec("tensor_add", r_tiles[d][:, :],
                           r_tiles[d][:, :], pbc[:, :], after=[tl, rp] + lng)
                r_done.append(rv)

            def writer(dd, _s, src_ap, toks, g=g):
                return p.dma("sync",
                             stream_out[128 * dd:128 * dd + 128, gsl],
                             src_ap, after=toks, sem="st")
            stoks, lng = layernorm_fm(
                p, ln_scr, r_tiles, 512, ones_col, ones_row,
                psA_, prow, p1, pav, r_done, writer, guard=lng)
            lng = lng + stoks
        p.barrier("sync")


def declare_l1_attn_dram(nc, dbg=False, in_h=None, out_h=None):
    dram = {}
    dram["la_in"] = in_h if in_h is not None else nc.dram_tensor(
        "la_in", [D, T], F32, kind="ExternalInput")
    dram["la_out"] = out_h if out_h is not None else nc.dram_tensor(
        "la_out", [D, T], F32, kind="ExternalOutput")
    for nm, shp, dt in [("la_winT", [1024, 3072], F32),
                        ("la_woutT", [1024, 1024], F32),
                        ("la_winGT", [1024, 3072], F32),
                        ("la_woutGT", [1024, 1024], F32),
                        ("la_bql", [128, NC_D], F32),
                        ("la_bkl", [128, NC_D], F32),
                        ("la_bvl", [1, 1024], F32),
                        ("la_bol", [128, NC_D], F32),
                        ("la_bqg", [128, NC_D], F32),
                        ("la_bkg", [128, NC_D], F32),
                        ("la_bvg", [1, 1024], F32),
                        ("la_bog", [128, NC_D], F32),
                        ("la_odd", [128, 1], F32),
                        ("la_ident", [128, 128], BF16),
                        ("la_R32", [32, T], BF16)]:
        dram[nm] = nc.dram_tensor(nm, shp, dt, kind="ExternalInput")
    dram["la_loc"] = nc.dram_tensor("la_loc", [D, T], F32)
    if dbg:
        dram["la_dbgl"] = nc.dram_tensor("la_dbgl", [D, T], F32,
                                         kind="ExternalOutput")
        dram["la_dbgg"] = nc.dram_tensor("la_dbgg", [D, 64], F32,
                                         kind="ExternalOutput")
        dram["la_dbgm"] = nc.dram_tensor("la_dbgm", [D, 64], F32,
                                         kind="ExternalOutput")
        dram["la_dbgq"] = nc.dram_tensor("la_dbgq", [D, 64], BF16,
                                         kind="ExternalOutput")
        dram["la_dbgk"] = nc.dram_tensor("la_dbgk", [D, 64], BF16,
                                         kind="ExternalOutput")
        dram["la_dbgv"] = nc.dram_tensor("la_dbgv", [64, 1024], BF16,
                                         kind="ExternalOutput")
        dram["la_dbgog"] = nc.dram_tensor("la_dbgog", [D, 64], BF16,
                                          kind="ExternalOutput")
    dram["la_mpub"] = nc.dram_tensor("la_mpub", [128, 256], F32)
    dram["la_mgat"] = nc.dram_tensor("la_mgat", [256, 256], F32)
    return dram


def pack_l1_attn_inputs(inputs, stream_in_full):
    winT = np.ascontiguousarray(
        np.asarray(inputs["L1_attn_w_in_local"], np.float32).T)
    woutT = np.ascontiguousarray(
        np.asarray(inputs["L1_attn_w_out_local"], np.float32).T)
    winGT = np.ascontiguousarray(
        np.asarray(inputs["L1_attn_w_in_global"], np.float32).T)
    woutGT = np.ascontiguousarray(
        np.asarray(inputs["L1_attn_w_out_global"], np.float32).T)
    bl = np.asarray(inputs["L1_attn_b_in_local"], np.float32)
    bg = np.asarray(inputs["L1_attn_b_in_global"], np.float32)
    R32 = np.zeros((32, T), np.float32)
    for col in range(T):
        R32[col // 64, col] = 1.0
    shared = dict(la_winT=winT, la_woutT=woutT, la_winGT=winGT,
                  la_woutGT=woutGT,
                  la_bql=cols128(bl[0:1024]), la_bkl=cols128(bl[1024:2048]),
                  la_bvl=bl[2048:3072][None, :],
                  la_bol=cols128(inputs["L1_attn_b_out_local"]),
                  la_bqg=cols128(bg[0:1024]), la_bkg=cols128(bg[1024:2048]),
                  la_bvg=bg[2048:3072][None, :],
                  la_bog=cols128(inputs["L1_attn_b_out_global"]),
                  la_ident=np.eye(128, dtype=np.float32).astype(BF),
                  la_R32=R32.astype(BF))
    maps = []
    for c in range(8):
        b, h = c // 2, c % 2
        m = dict(shared)
        m["la_in"] = np.ascontiguousarray(
            stream_in_full[b, h * T:(h + 1) * T, :].T.astype(np.float32))
        m["la_odd"] = np.full((128, 1), float(h), np.float32)
        maps.append(m)
    return maps


def run_l1_attn(inputs, stream_in_full):
    nc = bass.Bass("TRN2", target_bir_lowering=False)
    with ExitStack() as ctx:
        p = P(nc, ctx)
        dram = declare_l1_attn_dram(nc)
        build_l1_attn(nc, ctx, p, dram)
        p.finalize()
    maps = pack_l1_attn_inputs(inputs, stream_in_full)
    return run_bass_kernel_spmd(nc, maps, core_ids=list(range(8)))



def build_l1_ffn(nc, ctx, p, dram):
    stream_in = dram["lf_in"]
    out_fm = dram["lf_out"]

    with ExitStack() as c1:
        ident = sb(c1, nc, "lf_id", [128, 128], BF16)
        ones_col = sb(c1, nc, "lf_oc", [128, 1], BF16)
        ones_row = sb(c1, nc, "lf_or", [1, 128], BF16)
        o32r = sb(c1, nc, "lf_o32r", [1, 32], BF16)
        phib = sb(c1, nc, "lf_phib_s", [1, 2048], BF16)
        R64 = sb(c1, nc, "lf_R64_s", [64, T], BF16)
        R32C = sb(c1, nc, "lf_R32C_s", [32, T], BF16)

        p0 = psum(c1, nc, "lf_p0", [128, 512])
        p1 = psum(c1, nc, "lf_p1", [128, 512])
        p2 = psum(c1, nc, "lf_p2", [128, 512])
        p3 = psum(c1, nc, "lf_p3", [128, 512])
        ptb = psum(c1, nc, "lf_ptb", [128, 128], BF16)
        ptbB = psum(c1, nc, "lf_ptbB", [128, 128], BF16)

        ct = []
        ct.append(p.dma("sync", ident[:, :], dram["lf_ident"].ap()))
        ct.append(p.dma("sync", R64[:, :], dram["lf_R64"].ap()))
        ct.append(p.dma("sync", R32C[:, :], dram["lf_R32C"].ap()))
        ct.append(p.dma("gpsimd", phib[:, :], dram["lf_phib"].ap()))
        ct.append(p.vec("memset", ones_col[:, :], 1.0))
        ct.append(p.vec("memset", ones_row[:, :], 1.0))
        ct.append(p.vec("memset", o32r[:, :], 1.0))

        with ExitStack() as c2:
            Dt = [sb(c2, nc, f"lf_D{tc}", [128, T], BF16) for tc in range(16)]
            xt = [sb(c2, nc, f"lf_xt{tc}", [128, 1024], BF16)
                  for tc in range(16)]
            phis = [sb(c2, nc, f"lf_phi{j}", [128, 4096], BF16)
                    for j in range(4)]
            logit_p = [sb(c2, nc, f"lf_logit{i}", [128, T])
                       for i in range(2)]
            expc_p = [sb(c2, nc, f"lf_expc{i}", [128, T], BF16)
                      for i in range(2)]
            mrow_p = [sb(c2, nc, f"lf_mrow{i}", [128, 64]) for i in range(2)]
            mrowb_p = [sb(c2, nc, f"lf_mrowb{i}", [128, 64], BF16)
                       for i in range(2)]
            mT_p = [sb(c2, nc, f"lf_mT{i}", [64, 128], BF16)
                    for i in range(2)]
            srow_p = [sb(c2, nc, f"lf_srow{i}", [128, 64]) for i in range(2)]
            scr_p = [sb(c2, nc, f"lf_scr{i}", [128, 512]) for i in range(2)]
            xstage_p = [sb(c2, nc, f"lf_xst{i}", [128, 512])
                        for i in range(2)]
            logit = logit_p[0]
            scr = scr_p[0]
            ctstage_p = [sb(c2, nc, f"lf_cts{i}", [128, 256], BF16)
                         for i in range(2)]
            ptbC = psum(c2, nc, "lf_ptbC", [128, 128], BF16)
            ptbD = psum(c2, nc, "lf_ptbD", [128, 128], BF16)
            ptb_p = [ptb, ptbB]
            ptbX_p = [ptbC, ptbD]
            p0_p = [p0, p2]
            p1_p = [p1, p3]
            c3 = ExitStack()
            xb = [sb(c3, nc, f"lf_xb{k}", [128, T], BF16)
                  for k in range(NC_D)]

            xb_toks = [p.dma("gpsimd", xb[k][:, :],
                             stream_in[128 * k:128 * k + 128, :])
                       for k in range(NC_D)]
            ph_toks = [p.dma("gpsimd",
                             phis[j][:, :].rearrange("q (a j) -> q a j",
                                                     j=512),
                             dram["lf_phiT"][:, 512 * j:512 * j + 512]
                             .rearrange("(a q) j -> q a j", q=128))
                       for j in range(4)]
            p.barrier("vector")
            if LF_PART == -1:
                for k in range(NC_D):
                    vv = p.vec("tensor_copy", logit[:, 0:T], xb[k][:, :],
                               after=xb_toks)
                    p.dma("sync", dram["lf_out"][128 * k:128 * k + 128, :],
                          logit[:, 0:T], after=[vv], sem="st")
                c3.close()
                p.barrier("sync")
                return
            warm = p.mm(p1[0:1, :], [(ones_col[:, :], xb[k][:, 0:512])
                                     for k in range(NC_D)],
                        after=xb_toks + ct)
            warm2 = p.mm(p0[:, :], [
                (xb[k][:, 0:128], phis[0][:, 512 * k:512 * k + 512])
                for k in range(NC_D)], after=ph_toks + [warm])

            prev_ev = [[warm2], [warm2]]
            for tc in range(16):
                for d in range(NC_D):
                    q = (tc * NC_D + d) % 2
                    tt = p.trans(ptb_p[q][:, :],
                                 xb[d][:, 128 * tc:128 * tc + 128],
                                 ident[:, :], after=xb_toks + ct + prev_ev[q])
                    te = p.vec("tensor_copy", xt[tc][:, 128 * d:128 * d + 128],
                               ptb_p[q][:, :], after=[tt])
                    prev_ev[q] = [te]
            xt_done = prev_ev[0] + prev_ev[1]
            if LF_PART == -2:
                for k in range(NC_D):
                    p.dma("gpsimd", dram["lf_out"][128 * k:128 * k + 128,
                                                   0:1024],
                          xt[k][:, :], after=xt_done, sem="st")
                c3.close()
                p.barrier("gpsimd")
                return

            Dt_done = []
            Dt_done_p = [[], []]
            Ct_st_p = [[], []]
            lg_last_p = [[], []]
            st = {}

            def stage_logits(tc):
                par = tc % 2
                logit = logit_p[par]
                p0 = p0_p[par]
                lg_toks = []
                for j in range(4):
                    bias_pairs = ([(ones_row[:, :],
                                    phib[0:1, 512 * j:512 * j + 512])]
                                  if LF_BIAS else [])
                    mm_t = p.mm(p0[:, :], bias_pairs + [
                        (xb[k][:, 128 * tc:128 * tc + 128],
                         phis[j][:, 512 * k:512 * k + 512])
                        for k in range(NC_D)],
                        after=ph_toks + xb_toks + Dt_done_p[par][-1:] +
                        Ct_st_p[par][-1:] + (lg_toks[-1:] or lg_last_p[par]))
                    ev = p.act(logit[:, 512 * j:512 * j + 512], p0[:, :],
                               AF.Identity,
                               after=[mm_t] + Dt_done_p[par][-1:])
                    lg_toks.append(ev)
                lg_last_p[par] = lg_toks[-1:]
                st[tc] = dict(lg=lg_toks)

            def stage_D(tc):
                par = tc % 2
                logit = logit_p[par]
                mrow = mrow_p[par]
                mrowb = mrowb_p[par]
                mT = mT_p[par]
                srow = srow_p[par]
                scr = scr_p[par]
                ptb = ptb_p[par]
                p1 = p1_p[par]
                lg_toks = st[tc]["lg"]
                m1 = p.vec("reduce_max", mrow[:, :],
                           logit[:, :].rearrange("q (a b) -> q a b", b=32),
                           AX.X, negate=True, after=lg_toks)
                m2 = p.vec("tensor_copy", mrowb[:, :], mrow[:, :],
                           after=[m1])
                t1 = p.trans(ptb[0:64, :], mrowb[:, :], ident[:, :],
                             after=[m2])
                t2 = p.vec("tensor_copy", mT[:, :], ptb[0:64, 0:128],
                           after=[t1])
                sub_toks = []
                for j in range(4):
                    rp = p.mm(p1[:, :], [(mT[:, :],
                                          R64[:, 512 * j:512 * j + 512])],
                              after=[t2, ct[1]] + sub_toks[-1:])
                    sv = p.vec("tensor_add", scr[:, :],
                               logit[:, 512 * j:512 * j + 512], p1[:, :],
                               after=[rp] + lg_toks)
                    ev = p.act(Dt[tc][:, 512 * j:512 * j + 512], scr[:, :],
                               AF.Exp, after=[sv])
                    sub_toks.append(ev)
                s1 = p.vec("reduce_sum", srow[:, :],
                           Dt[tc][:, :].rearrange("q (a b) -> q a b", b=32),
                           AX.X, after=sub_toks)
                s2 = p.vec("reciprocal", srow[:, :], srow[:, :], after=[s1])
                s3 = p.vec("tensor_copy", mrowb[:, :], srow[:, :],
                           after=[s2])
                t3 = p.trans(ptb[0:64, :], mrowb[:, :], ident[:, :],
                             after=[s3])
                t4 = p.vec("tensor_copy", mT[:, :], ptb[0:64, 0:128],
                           after=[t3])
                dn_toks = []
                for j in range(4):
                    rp = p.mm(p1[:, :], [(mT[:, :],
                                          R64[:, 512 * j:512 * j + 512])],
                              after=[t4] + dn_toks[-1:])
                    dv = p.vec("tensor_mul",
                               Dt[tc][:, 512 * j:512 * j + 512],
                               Dt[tc][:, 512 * j:512 * j + 512], p1[:, :],
                               after=[rp] + sub_toks)
                    dn_toks.append(dv)
                Dt_done_p[par] = dn_toks
                st[tc]["dn"] = dn_toks

            def stage_C(tc):
                par = tc % 2
                logit = logit_p[par]
                expc = expc_p[par]
                mrow = mrow_p[par]
                mrowb = mrowb_p[par]
                mT = mT_p[par]
                srow = srow_p[par]
                scr = scr_p[par]
                ptb = ptb_p[par]
                p1 = p1_p[par]
                lg_toks = st[tc]["lg"]
                Ct_st = Ct_st_p[par]
                cm1 = p.vec("reduce_max", mrow[:, 0:32],
                            logit[:, :].rearrange("q (a b) -> q b a", b=32),
                            AX.X, negate=True, after=lg_toks + Ct_st[-1:])
                cm2 = p.vec("tensor_copy", mrowb[:, 0:32], mrow[:, 0:32],
                            after=[cm1])
                ct1 = p.trans(ptb[0:32, :], mrowb[:, 0:32], ident[:, :],
                              after=[cm2])
                ct2 = p.vec("tensor_copy", mT[0:32, :], ptb[0:32, 0:128],
                            after=[ct1])
                csub = []
                for j in range(4):
                    rp = p.mm(p1[:, :], [(mT[0:32, :],
                                          R32C[:, 512 * j:512 * j + 512])],
                              after=[ct2, ct[2]] + csub[-1:] +
                              st[tc]["dn"][-1:])
                    sv = p.vec("tensor_add", scr[:, :],
                               logit[:, 512 * j:512 * j + 512], p1[:, :],
                               after=[rp] + lg_toks)
                    ev = p.act(expc[:, 512 * j:512 * j + 512], scr[:, :],
                               AF.Exp, after=[sv])
                    csub.append(ev)
                cs1 = p.vec("reduce_sum", srow[:, 0:32],
                            expc[:, :].rearrange("q (a b) -> q b a", b=32),
                            AX.X, after=csub)
                cs2 = p.vec("reciprocal", srow[:, 0:32], srow[:, 0:32],
                            after=[cs1])
                cs3 = p.vec("tensor_copy", mrowb[:, 0:32], srow[:, 0:32],
                            after=[cs2])
                ct3 = p.trans(ptb[0:32, :], mrowb[:, 0:32], ident[:, :],
                              after=[cs3])
                ct4 = p.vec("tensor_copy", mT[0:32, :], ptb[0:32, 0:128],
                            after=[ct3])
                cdn = []
                for j in range(4):
                    rp = p.mm(p1[:, :], [(mT[0:32, :],
                                          R32C[:, 512 * j:512 * j + 512])],
                              after=[ct4] + cdn[-1:])
                    dv = p.vec("tensor_mul",
                               expc[:, 512 * j:512 * j + 512],
                               expc[:, 512 * j:512 * j + 512], p1[:, :],
                               after=[rp] + csub)
                    cdn.append(dv)
                st[tc]["cdn"] = cdn

            def stage_Ct(tc):
                par = tc % 2
                expc = expc_p[par]
                ptb = ptb_p[par]
                Ct_st = Ct_st_p[par]
                cdn = st[tc]["cdn"]
                PTs = [ptb, ptbX_p[par]]
                cts = ctstage_p[par]
                tes = []
                for npc in range(16):
                    tt = p.trans(PTs[npc % 2][:, :],
                                 expc[:, 128 * npc:128 * npc + 128],
                                 ident[:, :],
                                 after=cdn + tes[-2:-1])
                    te = p.vec("tensor_copy",
                               cts[:, 128 * (npc % 2):
                                   128 * (npc % 2) + 128],
                               PTs[npc % 2][:, :],
                               after=[tt] + Ct_st[-2:-1])
                    tes.append(te)
                    Ct_st.append(p.dma(
                        "sync",
                        dram["lf_Ct"][128 * npc:128 * npc + 128,
                                      128 * tc:128 * tc + 128],
                        cts[:, 128 * (npc % 2):128 * (npc % 2) + 128],
                        after=[te], sem="st"))

            if LF_PART == 0:
                for tc in range(LF_NTC):
                    stage_logits(tc)
                    par = tc % 2
                    for j4 in range(4):
                        Ct_st_p[par].append(p.dma(
                            "sync",
                            dram["lf_out"][0:128, 512 * j4:512 * j4 + 512],
                            logit_p[par][:, 512 * j4:512 * j4 + 512],
                            after=st[tc]["lg"], sem="st"))
                Dt_done = st[LF_NTC - 1]["lg"]
            else:
                for pp in range(LF_NTC // 2):
                    a, b = 2 * pp, 2 * pp + 1
                    stage_logits(a)
                    stage_logits(b)
                    stage_D(a)
                    stage_D(b)
                    stage_C(a)
                    stage_C(b)
                    stage_Ct(a)
                    stage_Ct(b)
                Dt_done = Dt_done_p[0][-1:] + Dt_done_p[1][-1:]

            if LF_PART == 0:
                c3.close()
                p.barrier("sync")
                p.barrier("gpsimd")
                return
            if "lf_dbgd" in dram:
                p.dma("sync", dram["lf_dbgd"].ap(), Dt[0][:, :],
                      after=Dt_done, sem="st")

            dn_all = Dt_done_p[0][-1:] + Dt_done_p[1][-1:] + \
                lg_last_p[0] + lg_last_p[1] + Ct_st_p[0][-1:] + \
                Ct_st_p[1][-1:]
            xp_st = [[], []]
            ev_x = [[], []]
            for npc in range(16):
                for half in range(2):
                    q = (2 * npc + half) % 2
                    pr = Dt_done + xt_done + dn_all + ev_x[q][-1:] + \
                        xp_st[q][-1:]
                    tmm = p.mm(p0_p[q][:, :], [
                        (Dt[tc2][:, 128 * npc:128 * npc + 128],
                         xt[tc2][:, 512 * half:512 * half + 512])
                        for tc2 in range(16)], after=pr)
                    ev_x[q].append(p.act(
                        xstage_p[q][:, :],
                        p0_p[q][:, :], AF.Identity, after=[tmm]))
                    xp_st[q].append(p.dma(
                        "gpsimd",
                        dram["lf_Xp"][128 * npc:128 * npc + 128,
                                      512 * half:512 * half + 512],
                        xstage_p[q][:, :], after=ev_x[q][-1:], sem="st"))
            xp_st = xp_st[0] + xp_st[1]
            c3.close()

        if LF_PART < 2:
            p.barrier("sync")
            p.barrier("gpsimd")
            return
        rs = p.coll("ReduceScatter", ALU.add, ALLC,
                    [dram["lf_Xp"].ap().opt()], [dram["lf_Xs"].ap().opt()],
                    after=xp_st)

        with ExitStack() as c2:
            Xe = [sb(c2, nc, f"lf_Xe{i}", [128, 1024]) for i in range(2)]
            Xeb = [sb(c2, nc, f"lf_Xeb{i}", [128, 1024], BF16)
                   for i in range(2)]
            XeT = [sb(c2, nc, f"lf_XeT{d}", [128, 256], BF16)
                   for d in range(NC_D)]
            col1 = sb(c2, nc, "lf_col1", [128, 1])
            col2 = sb(c2, nc, "lf_col2", [128, 1])
            sqs = sb(c2, nc, "lf_sqs", [128, 1024])
            ebrow = sb(c2, nc, "lf_ebrow", [1, 8192], BF16)
            ytst = sb(c2, nc, "lf_ytst", [128, 1024], BF16)
            ewsl = [sb(c2, nc, f"lf_ew{i}", [128, 8192], BF16)
                    for i in range(2)]

            xs_toks = [p.dma("sync", Xe[i][:, :],
                             dram["lf_Xs"][128 * i:128 * i + 128, :],
                             after=[rs]) for i in range(2)]
            teb = p.dma("gpsimd", ebrow[:, :],
                        dram["lf_eb"].ap().rearrange("a j -> (a j)")
                        .unsqueeze(0), after=[rs])
            ln_toks = []
            for i in range(2):
                cl = p.vec("tensor_scalar", Xe[i][:, :], Xe[i][:, :],
                           65000.0, -33000.0, ALU.min, ALU.max,
                           after=xs_toks)
                r1 = p.vec("reduce_sum", col1[:, :], Xe[i][:, :], AX.X,
                           after=[cl])
                r2 = p.vec("tensor_scalar_mul", col1[:, :], col1[:, :],
                           1.0 / 1024.0, after=[r1])
                c1_ = p.vec("tensor_scalar_sub", Xe[i][:, :], Xe[i][:, :],
                            col1[:, :], after=[r2])
                sq = p.act(sqs[:, :], Xe[i][:, :], AF.Square, after=[c1_])
                r3 = p.vec("reduce_sum", col2[:, :], sqs[:, :], AX.X,
                           after=[sq])
                r4 = p.act(col2[:, :], col2[:, :], AF.Ln, bias=1e-5,
                           scale=1.0 / 1024.0, after=[r3])
                r5 = p.act(col2[:, :], col2[:, :], AF.Exp, scale=-0.5,
                           after=[r4])
                mu = p.vec("tensor_scalar_mul", Xe[i][:, :], Xe[i][:, :],
                           col2[:, :], after=[r5])
                th = p.act(Xeb[i][:, :], Xe[i][:, :], AF.Tanh, after=[mu])
                ln_toks.append(th)
            if "lf_dbgx" in dram:
                for i in range(2):
                    p.dma("sync", dram["lf_dbgx"][128 * i:128 * i + 128, :],
                          Xeb[i][:, :], after=ln_toks, sem="st")
            tr_toks = []
            for i in range(2):
                for d in range(NC_D):
                    tt = p.trans(ptb[:, :],
                                 Xeb[i][:, 128 * d:128 * d + 128],
                                 ident[:, :], after=ln_toks + tr_toks[-1:])
                    te = p.vec("tensor_copy",
                               XeT[d][:, 128 * i:128 * i + 128], ptb[:, :],
                               after=[tt])
                    tr_toks.append(te)
            ew_tok = [None, None]
            ew_use = [None, None]

            def ew_load(e):
                i = e % 2
                aft = [ew_use[i]] if ew_use[i] else []
                ew_tok[i] = p.dma(
                    "gpsimd",
                    ewsl[i][:, :].rearrange("q (a j) -> q a j", j=1024),
                    dram["lf_ew"][e, :, :].rearrange("(a q) j -> q a j",
                                                     q=128), after=aft)

            ew_load(0)
            warm3 = p.mm(p3[0:32, :], [
                (XeT[k][:, 0:32], ewsl[0][:, 1024 * k:1024 * k + 512])
                for k in range(NC_D)], after=[ew_tok[0]] + tr_toks)
            ew_use[0] = warm3
            ew_load(1)
            yt_st = []
            ev_y = None
            for e in range(8):
                bi = e % 2
                for half in range(2):
                    pr = [ew_tok[bi], teb] + tr_toks
                    if ev_y is not None:
                        pr.append(ev_y)
                    tmm = p.mm(p3[0:32, :], [
                        (o32r[:, :],
                         ebrow[0:1, 1024 * e + 512 * half:
                               1024 * e + 512 * half + 512])] + [
                        (XeT[k][:, 32 * e:32 * e + 32],
                         ewsl[bi][:, 1024 * k + 512 * half:
                                  1024 * k + 512 * half + 512])
                        for k in range(NC_D)], after=pr)
                    ew_use[bi] = tmm
                    ev_y = p.act(ytst[32 * (e % 4):32 * (e % 4) + 32,
                                      512 * half:512 * half + 512],
                                 p3[0:32, :], AF.Identity, after=[tmm])
                if e % 4 == 3:
                    yt_st.append(p.dma("gpsimd",
                                       dram["lf_Ytp"][128 * (e // 4):
                                                      128 * (e // 4) + 128, :],
                                       ytst[:, :], after=[ev_y], sem="st"))
                    if "lf_dbgy" in dram:
                        yt_st.append(p.dma(
                            "gpsimd",
                            dram["lf_dbgy"][128 * (e // 4):
                                            128 * (e // 4) + 128, :],
                            ytst[:, :], after=[ev_y], sem="st"))
                    ev_y = yt_st[-1]
                if e + 2 < 8:
                    ew_load(e + 2)
            ag = p.coll("AllGather", ALU.bypass, ALLC,
                        [dram["lf_Ytp"].ap().opt()],
                        [dram["lf_Ytg"].ap().opt()], after=yt_st)

        if LF_PART < 3:
            p.barrier("sync")
            p.barrier("gpsimd")
            return
        with ExitStack() as c2:
            yt = [sb(c2, nc, f"lf_yt{i}", [128, 1024], BF16)
                  for i in range(16)]
            Ctt = [sb(c2, nc, f"lf_ct{i}", [128, T], BF16)
                   for i in range(16)]
            xres_p = [sb(c2, nc, f"lf_xres{i}", [128, 512])
                      for i in range(2)]
            osc = sb(c2, nc, "lf_osc", [128, 512])
            ln_scr = make_ln_scratch(c2, nc, "lfln")
            psA = psum(c2, nc, "lf_psA", [1, 512])
            psB = psum(c2, nc, "lf_psB", [1, 512])
            r_tiles = [sb(c2, nc, f"lf_rt{d}", [128, 512])
                       for d in range(NC_D)]
            yt_toks = [p.dma("gpsimd", yt[i][:, :],
                             dram["lf_Ytg"][128 * i:128 * i + 128, :],
                             after=[ag]) for i in range(16)]
            ct_toks = [p.dma("sync", Ctt[i][:, :],
                             dram["lf_Ct"][128 * i:128 * i + 128, :],
                             after=p.now())
                       for i in range(16)]
            if "lf_dbgc" in dram:
                p.dma("sync", dram["lf_dbgc"].ap(), Ctt[0][:, :],
                      after=ct_toks, sem="st")
            warm4 = p.mm(p0[:, :], [(yt[k][:, 0:128], Ctt[k][:, 0:512])
                                    for k in range(16)],
                         after=yt_toks + ct_toks)
            lng = [warm4]
            pC = [p0, p3]
            for g in range(4):
                gsl = slice(512 * g, 512 * g + 512)
                r_done = []
                ev_q = [[], []]
                tx_q = [[], []]
                for d in range(NC_D):
                    q = d % 2
                    pr = yt_toks + ct_toks + lng + ev_q[q][-1:]
                    tmm = p.mm(pC[q][:, :], [
                        (yt[k][:, 128 * d:128 * d + 128], Ctt[k][:, gsl])
                        for k in range(16)], after=pr)
                    tx = p.dma("sync", xres_p[q][:, :],
                               stream_in[128 * d:128 * d + 128, gsl],
                               after=ev_q[q][-1:])
                    ev_o = p.vec("tensor_add", r_tiles[d][:, :],
                                 pC[q][:, :], xres_p[q][:, :],
                                 after=[tmm, tx] + lng)
                    ev_q[q].append(ev_o)
                    r_done.append(ev_o)

                if g == 0 and "lf_dbgr" in dram:
                    for dd in range(NC_D):
                        p.dma("sync",
                              dram["lf_dbgr"][128 * dd:128 * dd + 128, :],
                              r_tiles[dd][:, :], after=r_done, sem="st")

                def writer(dd, _s, src_ap, toks, g=g):
                    return p.dma("sync",
                                 out_fm[128 * dd:128 * dd + 128, gsl],
                                 src_ap, after=toks, sem="st")
                stoks, lng = layernorm_fm(
                    p, ln_scr, r_tiles, 512, ones_col, ones_row,
                    psA, psB, p1, p2, r_done, writer, guard=lng)
                lng = lng + stoks
        p.barrier("sync")


def declare_l1_ffn_dram(nc, dbg=False, in_h=None):
    dram = {}
    dram["lf_in"] = in_h if in_h is not None else nc.dram_tensor(
        "lf_in", [D, T], F32, kind="ExternalInput")
    dram["lf_out"] = nc.dram_tensor("lf_out", [D, T], F32,
                                    kind="ExternalOutput")
    for nm, shp, dt in [("lf_phiT", [1024, 2048], F32),
                        ("lf_phib", [1, 2048], F32),
                        ("lf_ew", [8, 1024, 1024], F32),
                        ("lf_eb", [8, 1024], F32),
                        ("lf_ident", [128, 128], BF16),
                        ("lf_R64", [64, T], BF16),
                        ("lf_R32C", [32, T], BF16)]:
        dram[nm] = nc.dram_tensor(nm, shp, dt, kind="ExternalInput")
    if dbg:
        dram["lf_dbgx"] = nc.dram_tensor("lf_dbgx", [256, 1024], BF16,
                                         kind="ExternalOutput")
        dram["lf_dbgy"] = nc.dram_tensor("lf_dbgy", [256, 1024], BF16,
                                         kind="ExternalOutput")
        dram["lf_dbgc"] = nc.dram_tensor("lf_dbgc", [128, T], BF16,
                                         kind="ExternalOutput")
        dram["lf_dbge"] = nc.dram_tensor("lf_dbge", [128, T], BF16,
                                         kind="ExternalOutput")
        dram["lf_dbgr"] = nc.dram_tensor("lf_dbgr", [D, 512], F32,
                                         kind="ExternalOutput")
        dram["lf_dbgd"] = nc.dram_tensor("lf_dbgd", [128, T], BF16,
                                         kind="ExternalOutput")
    dram["lf_Ct"] = nc.dram_tensor("lf_Ct", [T, T], BF16)
    dram["lf_Xp"] = nc.dram_tensor("lf_Xp", [2048, 1024], F32)
    dram["lf_Xs"] = nc.dram_tensor("lf_Xs", [256, 1024], F32)
    dram["lf_Ytp"] = nc.dram_tensor("lf_Ytp", [256, 1024], BF16)
    dram["lf_Ytg"] = nc.dram_tensor("lf_Ytg", [2048, 1024], BF16)
    return dram


def pack_l1_ffn_inputs(inputs, stream_in_full):
    phiT = np.ascontiguousarray(
        np.asarray(inputs["L1_ffn_phi_w"], np.float32).T)
    phib = np.asarray(inputs["L1_ffn_phi_b"], np.float32)[None, :]
    ew = np.asarray(inputs["L1_ffn_expert_w"], np.float32)
    eb = np.asarray(inputs["L1_ffn_expert_b"], np.float32)
    R64 = np.zeros((64, T), np.float32)
    for col in range(T):
        R64[col // 32, col] = 1.0
    R32C = np.zeros((32, T), np.float32)
    for col in range(T):
        R32C[col % 32, col] = 1.0
    shared = dict(lf_phiT=phiT, lf_phib=phib,
                  lf_ident=np.eye(128, dtype=np.float32).astype(BF),
                  lf_R64=R64.astype(BF), lf_R32C=R32C.astype(BF))
    maps = []
    for c in range(8):
        b, h = c // 2, c % 2
        m = dict(shared)
        m["lf_in"] = np.ascontiguousarray(
            stream_in_full[b, h * T:(h + 1) * T, :].T.astype(np.float32))
        m["lf_ew"] = np.ascontiguousarray(ew[8 * c:8 * c + 8])
        m["lf_eb"] = np.ascontiguousarray(eb[8 * c:8 * c + 8])
        maps.append(m)
    return maps


def run_l1_ffn(inputs, stream_in_full):
    nc = bass.Bass("TRN2", target_bir_lowering=False)
    with ExitStack() as ctx:
        p = P(nc, ctx)
        dram = declare_l1_ffn_dram(nc)
        build_l1_ffn(nc, ctx, p, dram)
        p.finalize()
    maps = pack_l1_ffn_inputs(inputs, stream_in_full)
    return run_bass_kernel_spmd(nc, maps, core_ids=list(range(8)))


def kernel_unfused(**inputs):
    res = run_hawk(inputs)
    y1 = _gather_fm(res, "y1")
    res = run_ffn_attn(inputs, 0, y1)
    x2 = _gather_fm(res, "fa_out")
    res = run_ffn_ffn(inputs, 0, x2)
    x3 = _gather_fm(res, "ff_out")
    res = run_ffn_attn(inputs, 1, x3)
    x4 = _gather_fm(res, "fa_out")
    res = run_ffn_ffn(inputs, 1, x4, res2_full=y1)
    y2 = _gather_fm(res, "ff_out2")
    res = run_l1_attn(inputs, y2)
    y3 = _gather_fm(res, "la_out")
    res = run_l1_ffn(inputs, y3)
    return _gather_fm(res, "lf_out")



def build_fused():
    nc = bass.Bass("TRN2", target_bir_lowering=False)
    with ExitStack() as ctx:
        p = P(nc, ctx)
        senti = sb(ctx, nc, "senti", [1, 1])

        def sentinel(i):
            p.vec("memset", senti[0:1, 0:1], 601.0 + i)
        dram_h = declare_hawk_dram(nc, stream_out_kind="Internal")
        y1 = dram_h["y1"]
        p.set_phase("hawk")
        sentinel(0)
        with ExitStack() as ch:
            build_hawk(nc, ch, p, dram_h)
        p.barrier()

        p.set_phase("fa0")
        sentinel(1)
        s_x2 = nc.dram_tensor("s_x2", [D, T], F32)
        d = declare_ffn_attn_dram(nc, p, pfx="a0_", in_h=y1, out_h=s_x2)
        build_ffn_attn_prologue(nc, ctx, p, d)
        build_ffn_attn(nc, ctx, p, d)
        p.barrier()

        p.set_phase("ff0")
        sentinel(2)
        s_x3 = nc.dram_tensor("s_x3", [D, T], F32)
        d = declare_ffn_ffn_dram(nc, False, pfx="f0_", in_h=s_x2, out_h=s_x3)
        build_col_prologue(nc, ctx, p, d, "ff_in", "ff_cpub", "ff_cgat",
                           "ff_cgat_tok")
        build_ffn_ffn(nc, ctx, p, d, False)
        p.barrier()

        p.set_phase("fa1")
        sentinel(3)
        s_x4 = nc.dram_tensor("s_x4", [D, T], F32)
        d = declare_ffn_attn_dram(nc, p, pfx="a1_", in_h=s_x3, out_h=s_x4)
        build_ffn_attn_prologue(nc, ctx, p, d)
        build_ffn_attn(nc, ctx, p, d)
        p.barrier()

        p.set_phase("ff1")
        sentinel(4)
        s_y2 = nc.dram_tensor("s_y2", [D, T], F32)
        s_x5 = nc.dram_tensor("s_x5", [D, T], F32)
        d = declare_ffn_ffn_dram(nc, True, pfx="f1_", in_h=s_x4, out_h=s_x5,
                                 res2_h=y1, out2_h=s_y2)
        build_col_prologue(nc, ctx, p, d, "ff_in", "ff_cpub", "ff_cgat",
                           "ff_cgat_tok")
        build_ffn_ffn(nc, ctx, p, d, True)
        p.barrier()

        p.set_phase("la")
        sentinel(5)
        s_y3 = nc.dram_tensor("s_y3", [D, T], F32)
        d = declare_l1_attn_dram(nc, in_h=s_y2, out_h=s_y3)
        build_l1_attn(nc, ctx, p, d)
        p.barrier()

        p.set_phase("lf")
        sentinel(6)
        d = declare_l1_ffn_dram(nc, in_h=s_y3)
        build_l1_ffn(nc, ctx, p, d)
        p.finalize()
    return nc


def pack_fused_inputs(inputs):
    maps = pack_hawk_inputs(inputs)
    Z = np.zeros((4, 4096, D), np.float32)
    for sub, pfx in ((0, "a0_"), (1, "a1_")):
        shared, per_core = make_ffn_attn_consts(inputs, sub)
        wqkv = np.ascontiguousarray(
            np.asarray(inputs["L0_ffn_wqkv"], np.float32)[sub])
        wo = np.ascontiguousarray(
            np.asarray(inputs["L0_ffn_wo"], np.float32)[sub])
        for c in range(8):
            m = dict(fa_wqkv=wqkv, fa_wo=wo)
            m.update(shared)
            m.update(per_core[c])
            maps[c].update({pfx + k: v for k, v in m.items()})
    for sub, pfx in ((0, "f0_"), (1, "f1_")):
        fmaps = pack_ffn_ffn_inputs(inputs, sub, Z)
        for c in range(8):
            maps[c].update({pfx + k: v for k, v in fmaps[c].items()
                            if k not in ("ff_in", "ff_res2")})
    lmaps = pack_l1_attn_inputs(inputs, Z)
    for c in range(8):
        maps[c].update({k: v for k, v in lmaps[c].items() if k != "la_in"})
    fmaps = pack_l1_ffn_inputs(inputs, Z)
    for c in range(8):
        maps[c].update({k: v for k, v in fmaps[c].items() if k != "lf_in"})
    return maps


def kernel(**inputs):
    nc = build_fused()
    maps = pack_fused_inputs(inputs)
    res = run_bass_kernel_spmd(nc, maps, core_ids=list(range(8)))
    return _gather_fm(res, "lf_out")
